# version 04
# baseline (speedup 1.0000x reference)
import numpy as np
from contextlib import ExitStack
import concourse.bass as bass
import concourse.mybir as mybir

F32 = mybir.dt.float32
BF16 = mybir.dt.bfloat16
I32 = mybir.dt.int32
AF = mybir.ActivationFunctionType
ALU = mybir.AluOpType
AX = mybir.AxisListType

ENGINES = ("y", "v", "s", "t", "g")
ENGINE_ATTR = {"y": "sync", "v": "vector", "s": "scalar", "t": "tensor", "g": "gpsimd"}

C = 2048
NCH = 16
GELU_K = 0.7978845608028654


class Sem:
    __slots__ = ("h", "n", "name")
    def __init__(self, h, name):
        self.h = h
        self.n = 0
        self.name = name
    def bump(self, k=1):
        self.n += k
        return self.n


class Prog:
    def __init__(self, nc, es):
        self.nc = nc
        self.es = es
        self.plan = {e: [] for e in ENGINES}
        self.sems = {}
        self._barn = 0
        self.last_inc = {e: None for e in ENGINES}
        self.dirty = {e: False for e in ENGINES}
        self._dummy = None

    def sem(self, name):
        s = Sem(self.es.enter_context(self.nc.semaphore(name)), name)
        self.sems[name] = s
        return s

    def const_f32(self, *values):
        if not hasattr(self, "_const_sem"):
            self._const_sem = self.sem("constf")
        for v in values:
            t = self.es.enter_context(self.nc.sbuf_tensor(f"cf_{str(v).replace('.','_').replace('-','m')}", [128, 1], F32))
            self.op("v", lambda E, t=t, v=v: E.memset(t[:], float(v)), inc=self._const_sem)
            self.nc.const_aps.register(F32, float(v), t[:])
        return self._const_sem

    def op(self, eng, fn, inc=None, k=1):
        if inc is None:
            self.plan[eng].append(fn)
            self.dirty[eng] = True
            return None
        n = inc.bump(k)
        self.plan[eng].append(lambda E, fn=fn: fn(E).then_inc(inc.h, k))
        self.last_inc[eng] = (inc, n)
        self.dirty[eng] = False
        return n

    def wait(self, eng, sem, n):
        if n > 0:
            self.plan[eng].append(lambda E: E.wait_ge(sem.h, n))

    def coll(self, fn, sem):
        n = sem.bump(1)
        self.plan["g"].append(lambda E, fn=fn: fn(E).then_inc(sem.h))
        self.last_inc["g"] = (sem, n)
        self.dirty["g"] = False
        return n

    def dma(self, eng, out, in_, sem, **kw):
        n = sem.bump(16)
        self.plan[eng].append(
            lambda E: E.dma_start(out=out, in_=in_, **kw).then_inc(sem.h, 16))
        self.last_inc[eng] = (sem, n)
        self.dirty[eng] = False
        return n

    def barrier(self, engs=("v", "s", "t", "g")):
        if self._dummy is None:
            self._dummy = self.es.enter_context(
                self.nc.sbuf_tensor("bar_dummy", [128, 8], F32))
            self._bars = {e: self.sem(f"bar_{e}") for e in ENGINES}
        d = self._dummy
        for e in engs:
            if self.dirty[e]:
                s = self._bars[e]
                if e == "v":
                    self.op("v", lambda E: E.memset(d[:, 0:2], 0.0), inc=s)
                elif e == "s":
                    self.op("s", lambda E: E.mul(out=d[:, 2:4], in_=d[:, 2:4], mul=1.0), inc=s)
                elif e == "g":
                    self.op("g", lambda E: E.memset(d[:, 4:6], 0.0), inc=s)
                else:
                    raise AssertionError(f"dirty engine {e} needs a real inc'd op before barrier")
        self._barn += 1
        for e in tuple(engs) + ("y",):
            for o in engs:
                if o != e and self.last_inc[o] is not None:
                    sem, n = self.last_inc[o]
                    self.wait(e, sem, n)

    def finalize(self, block):
        for e in ENGINES:
            fns = self.plan[e]
            if not fns:
                continue
            def runner(E, fns=fns):
                for f in fns:
                    f(E)
            getattr(block, ENGINE_ATTR[e])(runner)


class Bufs:
    def __init__(self, nc, es, T_CORE, want=("ffn",)):
        sb = lambda name, shape, dt=F32: es.enter_context(nc.sbuf_tensor(name, shape, dt))
        ps = lambda name, shape, dt=F32: es.enter_context(nc.psum_tensor(name, shape, dt))
        T = T_CORE
        self.T = T
        self.T2 = T // 2
        self.stream = sb("stream", [128, NCH, T])
        self.x_ln = sb("x_ln", [128, NCH, T], BF16)
        self.big_a = sb("big_a", [128, NCH, T], BF16)
        self.rows = sb("rows", [1, 4, self.T2])
        self.ones_col16 = sb("ones_col16", [128, 1], BF16)
        self.ones_row32 = sb("ones_row32", [1, 128])
        self.ident16 = sb("ident16", [128, 128], BF16)
        self.xb16 = self.big_a[:, :, 0:self.T2]
        self.xsq16 = self.big_a[:, :, self.T2:2 * self.T2]
        if "ffn" in want:
            self.wg_buf = sb("wg_buf", [128, 2, NCH, 128], BF16)
            self.wu_buf = sb("wu_buf", [128, 2, NCH, 128], BF16)
            self.wd_buf = sb("wd_buf", [128, 2, NCH, 128], BF16)
            self.gelu_sb = sb("gelu_sb", [128, 2, self.T2], BF16)
            self.ds_acc = sb("ds_acc", [128, NCH, self.T2])
        self.ps = [ps(f"ps{i}", [128, 512]) for i in range(6)]
        self.ps_small = ps("ps_small", [1, 512])
        self.pt = ps("pt", [128, 1024], BF16)


def emit_consts(P, B):
    sg = P.sem("const_g")
    sv = P.sem("const_v")
    P.op("v", lambda E: E.memset(B.ones_col16[:], 1.0))
    P.op("v", lambda E: E.memset(B.ones_row32[:], 1.0), inc=sv)
    P.op("g", lambda E: E.memset(B.ident16[:], 0.0), inc=sg)
    P.wait("g", sg, 1)
    P.op("g", lambda E: E.affine_select(
        out=B.ident16[:], in_=B.ident16[:], compare_op=ALU.not_equal,
        fill=1.0, base=0, pattern=[[-1, 128]], channel_multiplier=1), inc=sg)
    P.wait("t", sv, 1)
    P.wait("t", sg, 2)
    return sg, sv


def emit_ln(P, B, half, sems, war=(), dst_t0=None):
    T2 = B.T2
    t0 = half * T2
    dt0 = t0 if dst_t0 is None else dst_t0
    ss, sv, st = sems["ln_s"], sems["ln_v"], sems["ln_t"]
    src = B.stream[:, :, t0:t0 + T2]
    dst = B.x_ln[:, :, dt0:dt0 + T2]
    negmean = B.rows[:, 0, :]
    msq = B.rows[:, 1, :]
    var = B.rows[:, 2, :]
    rstd = B.rows[:, 3, :]
    lnv = B.rows[:, 1, :]
    pbc = B.ps[5][:, 0:T2]
    bc = pbc.unsqueeze(1).broadcast_to([128, NCH, T2])

    P.wait("s", st, st.n)
    P.wait("t", sv, sv.n)
    for (sem, n) in war:
        P.wait("s", sem, n)
        P.wait("v", sem, n)

    n_cast = P.op("s", lambda E: E.copy(out=B.xb16, in_=src), inc=ss)
    n_sq = P.op("s", lambda E: E.activation(out=B.xsq16, in_=src, func=AF.Square), inc=ss)

    P.wait("t", ss, n_cast)
    for i in range(NCH):
        rhs = B.xb16[:, i, :]
        P.op("t", lambda E, rhs=rhs, i=i: E.matmul(
            B.ps_small[:, 0:T2], B.ones_col16[:], rhs,
            start=(i == 0), stop=(i == NCH - 1)), inc=(st if i == NCH - 1 else None))
    n_sums = st.n
    P.wait("v", st, n_sums)
    n_nm = P.op("v", lambda E: E.tensor_single_scalar(
        out=negmean, in_=B.ps_small[:, 0:T2], scalar=-1.0 / C, op=ALU.mult), inc=sv)
    P.wait("t", sv, n_nm)
    P.wait("t", ss, n_sq)
    for i in range(NCH):
        rhs = B.xsq16[:, i, :]
        P.op("t", lambda E, rhs=rhs, i=i: E.matmul(
            B.ps_small[:, 0:T2], B.ones_col16[:], rhs,
            start=(i == 0), stop=(i == NCH - 1)), inc=(st if i == NCH - 1 else None))
    n_sumsq = st.n
    P.wait("v", st, n_sumsq)
    P.op("v", lambda E: E.tensor_scalar(
        out=msq, in0=B.ps_small[:, 0:T2], scalar1=1.0 / C, scalar2=1e-5,
        op0=ALU.mult, op1=ALU.add), inc=sv)
    P.wait("v", sv, sv.n)
    P.op("v", lambda E: E.tensor_tensor(out=var, in0=negmean, in1=negmean, op=ALU.mult), inc=sv)
    P.wait("v", sv, sv.n)
    n_var = P.op("v", lambda E: E.tensor_sub(out=var, in0=msq, in1=var), inc=sv)
    P.wait("s", sv, n_var)
    P.op("s", lambda E: E.activation(out=lnv, in_=var, func=AF.Ln), inc=ss)
    P.wait("s", ss, ss.n)
    n_rstd = P.op("s", lambda E: E.activation(out=rstd, in_=lnv, func=AF.Exp, scale=-0.5), inc=ss)
    P.wait("t", sv, n_nm)
    n_bc_m = P.op("t", lambda E: E.matmul(
        pbc, B.ones_row32[:], negmean, start=True, stop=True), inc=st)
    P.wait("v", st, n_bc_m)
    n_xm = P.op("v", lambda E: E.tensor_tensor(out=dst, in0=src, in1=bc, op=ALU.add), inc=sv)
    P.wait("t", sv, n_xm)
    P.wait("t", ss, n_rstd)
    n_bc_r = P.op("t", lambda E: E.matmul(
        pbc, B.ones_row32[:], rstd, start=True, stop=True), inc=st)
    P.wait("v", st, n_bc_r)
    n_out = P.op("v", lambda E: E.tensor_tensor(out=dst, in0=dst, in1=bc, op=ALU.mult), inc=sv)
    return n_out


def emit_ffn0(P, B, w, sems, n_mlp=16384, xln_half=False):
    NC = n_mlp // 128
    NGRP = NC // NCH
    T2 = B.T2
    ss, sv, st = sems["f_s"], sems["f_v"], sems["f_t"]
    wsem = sems["f_w"]
    dsem = sems["f_d"]
    lnv = sems["ln_v"]

    for half in (0, 1):
        n_ln = emit_ln(P, B, half, sems, war=[(st, st.n)],
                       dst_t0=(0 if xln_half else None))
        P.wait("t", lnv, n_ln)
        t0 = half * T2
        x0 = 0 if xln_half else t0
        w_ready = {}
        mm_done = {}
        gel_done = {}
        ev_done = {}
        dmm_done = {}
        dev_done = {}

        def load_w(jg):
            P.dma("y", B.wg_buf[:, jg % 2], w["wg_t"](jg), wsem[jg % 2])
            w_ready[jg] = P.dma("y", B.wu_buf[:, jg % 2], w["wu_t"](jg), wsem[jg % 2])

        for g in range(NGRP):
            for j in range(NCH):
                jg = g * NCH + j
                par = jg % 2
                if jg == 0:
                    load_w(0)
                    if NC > 1:
                        load_w(1)
                psg, psu = B.ps[0 + par][:, 0:T2], B.ps[2 + par][:, 0:T2]
                P.wait("t", wsem[jg % 2], w_ready[jg])
                if jg >= 2:
                    P.wait("t", sv, ev_done[jg - 2])
                    P.wait("t", ss, gel_done[jg - 2])
                for cc in range(NCH):
                    lhsT = B.wg_buf[:, par, cc, :]
                    rhs = B.x_ln[:, cc, x0:x0 + T2]
                    P.op("t", lambda E, lhsT=lhsT, rhs=rhs, cc=cc, psg=psg: E.matmul(
                        psg, lhsT, rhs, start=(cc == 0), stop=(cc == NCH - 1)))
                for cc in range(NCH):
                    lhsT = B.wu_buf[:, par, cc, :]
                    rhs = B.x_ln[:, cc, x0:x0 + T2]
                    P.op("t", lambda E, lhsT=lhsT, rhs=rhs, cc=cc, psu=psu: E.matmul(
                        psu, lhsT, rhs, start=(cc == 0), stop=(cc == NCH - 1)),
                        inc=(st if cc == NCH - 1 else None))
                mm_done[jg] = st.n
                if jg + 2 < NC:
                    P.wait("y", st, mm_done[jg])
                    load_w(jg + 2)
                P.wait("s", st, mm_done[jg])
                if jg >= 2:
                    P.wait("s", sv, ev_done[jg - 2])
                gdst = B.gelu_sb[:, par, :]
                gel_done[jg] = P.op("s", lambda E, gdst=gdst, psg=psg: E.activation(
                    out=gdst, in_=psg, func=AF.Gelu_apprx_tanh), inc=ss)
                P.wait("v", ss, gel_done[jg])
                P.wait("v", st, mm_done[jg])
                if g > 0 and j == 0:
                    P.wait("v", st, dmm_done[NCH - 1])
                idst = B.big_a[:, j, t0:t0 + T2]
                ev_done[jg] = P.op("v", lambda E, idst=idst, gdst=gdst, psu=psu: E.tensor_tensor(
                    out=idst, in0=gdst, in1=psu, op=ALU.mult), inc=sv)
            if "wd_gate" in sems:
                gsem, gn = sems.pop("wd_gate")
                P.wait("y", gsem, gn)
            n_grp_ev = sv.n
            for dc in range(NCH):
                par = dc % 2
                psd = B.ps[4][:, 0:T2]
                if dc >= 2:
                    P.wait("y", st, dmm_done[dc - 2])
                ndw = P.dma("y", B.wd_buf[:, par], w["wd_t"](dc, g), dsem[par])
                P.wait("t", dsem[par], ndw)
                P.wait("t", sv, n_grp_ev)
                if dc >= 1:
                    P.wait("t", sv, dev_done[dc - 1])
                for j in range(NCH):
                    lhsT = B.wd_buf[:, par, j, :]
                    rhs = B.big_a[:, j, t0:t0 + T2]
                    P.op("t", lambda E, lhsT=lhsT, rhs=rhs, j=j, psd=psd: E.matmul(
                        psd, lhsT, rhs, start=(j == 0), stop=(j == NCH - 1)),
                        inc=(st if j == NCH - 1 else None))
                dmm_done[dc] = st.n
                P.wait("v", st, dmm_done[dc])
                ddst = B.ds_acc[:, dc, :]
                if g == 0:
                    dev_done[dc] = P.op("v", lambda E, ddst=ddst, psd=psd: E.tensor_copy(
                        out=ddst, in_=psd), inc=sv)
                else:
                    dev_done[dc] = P.op("v", lambda E, ddst=ddst, psd=psd: E.tensor_tensor(
                        out=ddst, in0=ddst, in1=psd, op=ALU.add), inc=sv)
        P.wait("v", sv, sv.n)
        sdst = B.stream[:, :, t0:t0 + T2]
        P.op("v", lambda E, sdst=sdst: E.scalar_tensor_tensor(
            out=sdst, in0=B.ds_acc[:], scalar=0.5, in1=sdst,
            op0=ALU.mult, op1=ALU.add), inc=sv)
    return sv.n


def make_ffn0_sems(P):
    return {
        "ln_s": P.sem("ln_s"), "ln_v": P.sem("ln_v"), "ln_t": P.sem("ln_t"),
        "f_s": P.sem("f_s"), "f_v": P.sem("f_v"), "f_t": P.sem("f_t"),
        "f_w": [P.sem("f_w0"), P.sem("f_w1")],
        "f_d": [P.sem("f_d0"), P.sem("f_d1")],
    }


"""attn0a: MLA attention, head-sharded. Per core: 8 heads x full 2048-token causal.
in: xln_all [2, 2048, 1024] bf16 ([t-half][c][t_local]); weight slices bf16.
out: y_mine [1024, 2048] bf16 (my 8 head-chunks' output rows, all tokens)."""

FILL_L0 = -1e30

NH_LOC = 8
NQT = 16
TFULL = 2048
RSQ = 1.0 / np.sqrt(128.0)


def emit_attn_core(P, B, sems, scores_mm, av_lhsT, out_dma, fill, nheads=NH_LOC):
    ss, sv, st, sg = sems["a_s"], sems["a_v"], sems["a_t"], sems["a_g"]
    so = sems["a_o"]
    scores = B["scores"]
    attn_b = B["attn_b"]
    attnT = B["attnT"]
    rowm = B["rowm"]
    pt = B["pt"]
    sc_ps = B["sc_ps"]
    y_ps = B["y_ps"]
    ystage = B["ystage"]
    dmask = B["dmask"]
    war_pt = B.setdefault("_war_pt", {})
    war_sc = B.setdefault("_war_sc", {})
    war_y = B.setdefault("_war_y", {})
    for h in range(nheads):
        for quad in range(4):
            nw = quad + 1
            nk = nw * 512
            for j in range(4):
                qt = quad * 4 + j
                P.wait("s", sv, sv.n)
                for w in range(nw):
                    par = w % 2
                    ps = sc_ps[par]
                    we, wn = war_sc.get(par, (None, 0))
                    if we is not None:
                        P.wait("t", we, wn)
                    n_mm = scores_mm(h, qt, w, ps[:, 0:512])
                    dst = scores[:, w * 512:(w + 1) * 512]
                    if w == quad:
                        P.wait("v", st, n_mm)
                        msk = dmask[:, qt % 4, :]
                        n_e = P.op("v", lambda E, dst=dst, ps=ps, msk=msk: E.tensor_tensor(
                            out=dst, in0=ps[:, 0:512], in1=msk, op=ALU.add), inc=sv)
                        war_sc[par] = (sv, n_e)
                    else:
                        P.wait("s", st, n_mm)
                        n_e = P.op("s", lambda E, dst=dst, ps=ps: E.copy(
                            out=dst, in_=ps[:, 0:512]), inc=ss)
                        war_sc[par] = (ss, n_e)
                P.wait("v", sv, sv.n)
                P.wait("v", ss, ss.n)
                n_max = P.op("v", lambda E, nk=nk: E.reduce_max(
                    out=rowm[:, 0:1], in_=scores[:, 0:nk], axis=AX.X, negate=True), inc=sv)
                P.wait("s", sv, n_max)
                P.wait("s", st, st.n)
                n_exp = P.op("s", lambda E, nk=nk: E.activation(
                    out=attn_b[:, 0:nk], in_=scores[:, 0:nk], func=AF.Exp,
                    bias=rowm[:, 0:1], accum_out=rowm[:, 1:2]), inc=ss)
                P.wait("v", ss, n_exp)
                n_rec = P.op("v", lambda E: E.reciprocal(out=rowm[:, 1:2], in_=rowm[:, 1:2]), inc=sv)
                P.wait("v", sv, n_rec)
                n_nrm = P.op("v", lambda E, nk=nk: E.tensor_scalar_mul(
                    out=attn_b[:, 0:nk], in0=attn_b[:, 0:nk], scalar1=rowm[:, 1:2]), inc=sv)
                for kc in range(nw * 4):
                    tpar = kc % 2
                    src = attn_b[:, kc * 128:(kc + 1) * 128]
                    P.wait("t", sv, n_nrm)
                    P.wait("t", ss, war_pt.get(tpar, 0))
                    n_tp = P.op("t", lambda E, tpar=tpar, src=src: E.transpose(
                        pt[tpar][:, 0:128], src, B["ident16"][:]), inc=st)
                    P.wait("s", st, n_tp)
                    dstT = attnT[:, kc, j * 128:(j + 1) * 128]
                    war_pt[tpar] = P.op("s", lambda E, dstT=dstT, tpar=tpar: E.copy(
                        out=dstT, in_=pt[tpar][:, 0:128]), inc=ss)
            ypar = (quad % 2) if not B.get("y_single") else 0
            yp = y_ps[ypar]
            P.wait("t", ss, ss.n)
            P.wait("t", so[ypar], war_y.get(ypar, 0))
            for kc in range(nw * 4):
                lhsT = av_lhsT(h, kc)
                rhs = attnT[:, kc, :]
                P.op("t", lambda E, lhsT=lhsT, rhs=rhs, kc=kc, yp=yp, nw=nw: E.matmul(
                    yp[:, 0:512], lhsT, rhs, start=(kc == 0), stop=(kc == nw * 4 - 1)),
                    inc=(st if kc == nw * 4 - 1 else None))
            n_av = st.n
            P.wait("s", st, n_av)
            P.wait("s", so[ypar], war_y.get(ypar, 0))
            n_ye = P.op("s", lambda E, ypar=ypar, yp=yp: E.copy(
                out=ystage[:, ypar, :], in_=yp[:, 0:512]), inc=ss)
            P.wait("g", ss, n_ye)
            war_y[ypar] = out_dma(h, quad, ystage[:, ypar, :])
    return


def build_attn0(T_CORE=1024):
    TF = 2 * T_CORE
    nc = bass.Bass("TRN2", target_bir_lowering=False, debug=False)
    xln_all = nc.dram_tensor("xln_all", [2, 2048, T_CORE], BF16, kind="ExternalInput").ap()
    wq_nat_s = nc.dram_tensor("wq_nat_s", [2048, 1024], BF16, kind="ExternalInput").ap()
    wuk = nc.dram_tensor("wuk", [2048, 512], BF16, kind="ExternalInput").ap()
    wqT_s = nc.dram_tensor("wqT_s", [2048, 1024], BF16, kind="ExternalInput").ap()
    wuv = nc.dram_tensor("wuv", [2048, 512], BF16, kind="ExternalInput").ap()
    woT_s = nc.dram_tensor("woT_s", [2048, 1024], BF16, kind="ExternalInput").ap()
    wdkvT = nc.dram_tensor("wdkvT", [2048, 512], BF16, kind="ExternalInput").ap()
    y_mine = nc.dram_tensor("y_mine", [1024, TF], BF16, kind="ExternalOutput").ap()

    with ExitStack() as es:
        P = Prog(nc, es)
        sb = lambda name, shape, dt=F32: es.enter_context(nc.sbuf_tensor(name, shape, dt))
        ps = lambda name, shape, dt=F32: es.enter_context(nc.psum_tensor(name, shape, dt))
        xla = sb("xla", [128, NCH, TF], BF16)
        big1 = sb("big1", [128, NCH, TF // 2], BF16)
        cv = sb("cv", [128, NCH, 1024], BF16)
        ckv = sb("ckv", [128, 4, TF], BF16)
        keff = sb("keff", [128, NH_LOC, 512], BF16)
        qlat = sb("qlat", [128, 4, TF], BF16)
        scores = sb("scores", [128, 2048])
        attn_b = sb("attn_b", [128, 2048], BF16)
        attnT = sb("attnT", [128, 16, 512], BF16)
        rowm = sb("rowm", [128, 2])
        wst = sb("wst", [128, 1, NCH, 128], BF16)
        wbig = cv[:, :, 0:512]
        ystage = sb("ystage", [128, 2, 512], BF16)
        ident16 = sb("ident16", [128, 128], BF16)
        dmask = sb("dmask", [128, 4, 512], BF16)
        sc_ps = [ps("sc_ps0", [128, 512]), ps("sc_ps1", [128, 512])]
        y_ps = [ps("y_ps0", [128, 512]), ps("y_ps1", [128, 512])]
        pt = [ps("pt0", [128, 1024], BF16), ps("pt1", [128, 1024], BF16)]

        sems = {k: P.sem(k) for k in ("a_s", "a_v", "a_t", "a_g",
                                      "w0", "w1", "io")}
        sems["a_o"] = [P.sem("a_o0"), P.sem("a_o1")]
        ss, sv, st, sg = sems["a_s"], sems["a_v"], sems["a_t"], sems["a_g"]
        so = sems["a_o"]
        io = sems["io"]
        wsem = [sems["w0"], sems["w1"]]

        csg = P.sem("csg")
        P.op("g", lambda E: E.memset(ident16[:], 0.0), inc=csg)
        P.op("g", lambda E: E.memset(dmask[:], 0.0), inc=csg)
        P.wait("g", csg, 2)
        P.op("g", lambda E: E.affine_select(
            out=ident16[:], in_=ident16[:], compare_op=ALU.not_equal,
            fill=1.0, base=0, pattern=[[-1, 128]], channel_multiplier=1), inc=csg)
        for m in range(4):
            sl = dmask[:, m, :]
            P.op("g", lambda E, sl=sl, m=m: E.affine_select(
                out=sl, in_=sl, compare_op=ALU.is_ge, fill=FILL_L0,
                base=m * 128, pattern=[[-1, 512]], channel_multiplier=1), inc=csg)
        P.wait("t", csg, 7)
        P.wait("v", csg, 7)

        for r in range(2):
            P.dma("y", xla[:, :, r * T_CORE:(r + 1) * T_CORE],
                  xln_all[r].rearrange("(cc p) t -> p cc t", p=128), io)
        n_x = io.n
        n_wuk = P.dma("y", wbig, wuk.rearrange("(cc p) l -> p cc l", p=128), io)
        P.wait("t", io, n_wuk)

        wq_r = wq_nat_s.rearrange("(dc p) (sc s) -> sc p dc s", p=128, s=128)
        war_w = {}
        war_sc_ev = {}
        for sc in range(NH_LOC):
            if sc >= 1:
                P.wait("y", st, war_w[sc - 1])
            n_w = P.dma("y", wst[:, 0], wq_r[sc], wsem[0])
            P.wait("t", wsem[0], n_w)
            pp = sc_ps[sc % 2]
            if sc >= 2:
                P.wait("t", ss, war_sc_ev[sc - 2])
            for dc in range(NCH):
                P.op("t", lambda E, dc=dc, pp=pp: E.matmul(
                    pp[:, 0:512], wst[:, 0, dc, :], wbig[:, dc, :],
                    start=(dc == 0), stop=(dc == NCH - 1)),
                    inc=(st if dc == NCH - 1 else None))
            war_w[sc] = st.n
            P.wait("s", st, st.n)
            war_sc_ev[sc] = P.op("s", lambda E, sc=sc, pp=pp: E.mul(
                out=keff[:, sc, :], in_=pp[:, 0:512], mul=RSQ), inc=ss)
        P.barrier()

        P.wait("y", st, st.n)
        n_wuv = P.dma("y", wbig, wuv.rearrange("(dc p) l -> p dc l", p=128), io)
        woT_r = woT_s.rearrange("(dc p) (cw c) -> cw p dc c", p=128, c=512)
        veff = big1[:, 12:16, :].rearrange("p a t -> p a t")
        wo_stage = big1[:, 0:8, :].rearrange("p a t -> p (a t)").rearrange(
            "p (dc c) -> p dc c", c=512)
        P.wait("t", io, n_wuv)
        for cw in range(2):
            P.wait("y", st, st.n)
            n_w = P.dma("y", wo_stage, woT_r[cw], wsem[cw % 2])
            P.wait("t", wsem[cw % 2], n_w)
            for lc in range(4):
                pp = sc_ps[lc % 2]
                P.wait("t", ss, ss.n)
                for dc in range(NCH):
                    P.op("t", lambda E, dc=dc, lc=lc, pp=pp: E.matmul(
                        pp[:, 0:512], wbig[:, dc, lc * 128:(lc + 1) * 128],
                        wo_stage[:, dc, :], start=(dc == 0), stop=(dc == NCH - 1)),
                        inc=(st if dc == NCH - 1 else None))
                P.wait("s", st, st.n)
                dst = veff[:, lc, cw * 512:(cw + 1) * 512]
                P.op("s", lambda E, dst=dst, pp=pp: E.copy(out=dst, in_=pp[:, 0:512]), inc=ss)
            P.barrier()

        P.wait("y", st, st.n)
        n_wk = P.dma("y", wbig, wdkvT.rearrange("(cc p) l -> p cc l", p=128), io)
        P.wait("t", io, n_wk)
        for lc in range(4):
            for tw in range(4):
                pp = sc_ps[tw % 2]
                P.wait("t", ss, ss.n)
                for cc in range(NCH):
                    P.op("t", lambda E, cc=cc, lc=lc, tw=tw, pp=pp: E.matmul(
                        pp[:, 0:512], wbig[:, cc, lc * 128:(lc + 1) * 128],
                        xla[:, cc, tw * 512:(tw + 1) * 512],
                        start=(cc == 0), stop=(cc == NCH - 1)),
                        inc=(st if cc == NCH - 1 else None))
                P.wait("s", st, st.n)
                dst = ckv[:, lc, tw * 512:(tw + 1) * 512]
                P.op("s", lambda E, dst=dst, pp=pp: E.copy(out=dst, in_=pp[:, 0:512]), inc=ss)
        P.barrier()

        for kc in range(16):
            for cwi in range(2):
                pp = sc_ps[cwi]
                P.wait("t", ss, ss.n)
                for lc in range(4):
                    P.op("t", lambda E, lc=lc, kc=kc, cwi=cwi, pp=pp: E.matmul(
                        pp[:, 0:512], ckv[:, lc, kc * 128:(kc + 1) * 128],
                        veff[:, lc, cwi * 512:(cwi + 1) * 512],
                        start=(lc == 0), stop=(lc == 3)),
                        inc=(st if lc == 3 else None))
                P.wait("s", st, st.n)
                dst = cv[:, kc, cwi * 512:(cwi + 1) * 512]
                P.op("s", lambda E, dst=dst, pp=pp: E.copy(out=dst, in_=pp[:, 0:512]), inc=ss)
        P.barrier()

        q = big1[:].rearrange("p a t -> p (a t)").rearrange("p (h t) -> p h t", t=TF)
        wqT_r = wqT_s.rearrange("(cc p) (dq d) -> dq p cc d", p=128, d=128)
        for dqc in range(NH_LOC):
            P.wait("y", st, st.n)
            n_w = P.dma("y", wst[:, 0], wqT_r[dqc], wsem[0])
            P.wait("t", wsem[0], n_w)
            for tw in range(4):
                pp = sc_ps[tw % 2]
                P.wait("t", ss, ss.n)
                for cc in range(NCH):
                    P.op("t", lambda E, cc=cc, tw=tw, pp=pp: E.matmul(
                        pp[:, 0:512], wst[:, 0, cc, :],
                        xla[:, cc, tw * 512:(tw + 1) * 512],
                        start=(cc == 0), stop=(cc == NCH - 1)),
                        inc=(st if cc == NCH - 1 else None))
                P.wait("s", st, st.n)
                dst = q[:, dqc, tw * 512:(tw + 1) * 512]
                P.op("s", lambda E, dst=dst, pp=pp: E.copy(out=dst, in_=pp[:, 0:512]), inc=ss)
            P.barrier()

        def mk_qlat(h):
            for lc in range(4):
                for tw in range(4):
                    pp = sc_ps[tw % 2]
                    P.wait("t", ss, ss.n)
                    P.op("t", lambda E, lc=lc, tw=tw, h=h, pp=pp: E.matmul(
                        pp[:, 0:512], keff[:, h, lc * 128:(lc + 1) * 128],
                        q[:, h, tw * 512:(tw + 1) * 512],
                        start=True, stop=True), inc=st)
                    P.wait("s", st, st.n)
                    dst = qlat[:, lc, tw * 512:(tw + 1) * 512]
                    P.op("s", lambda E, dst=dst, pp=pp: E.copy(out=dst, in_=pp[:, 0:512]), inc=ss)

        B = dict(scores=scores[:], attn_b=attn_b[:], attnT=attnT,
                 rowm=rowm, pt=pt, sc_ps=sc_ps, y_ps=y_ps, ystage=ystage,
                 ident16=ident16, dmask=dmask)

        def scores_mm(h, qt, w, ps_ap):
            for lc in range(4):
                P.op("t", lambda E, lc=lc, qt=qt, w=w, ps_ap=ps_ap: E.matmul(
                    ps_ap, qlat[:, lc, qt * 128:(qt + 1) * 128],
                    ckv[:, lc, w * 512:(w + 1) * 512],
                    start=(lc == 0), stop=(lc == 3)),
                    inc=(st if lc == 3 else None))
            return st.n

        def av_lhsT(h, kc):
            return cv[:, kc, h * 128:(h + 1) * 128]

        def out_dma(h, quad, src):
            return P.dma("g", y_mine[h * 128:(h + 1) * 128,
                                     quad * 512:(quad + 1) * 512], src, so[quad % 2])

        asems = dict(a_s=ss, a_v=sv, a_t=st, a_g=sg, a_o=so)
        for h in range(NH_LOC):
            mk_qlat(h)
            P.barrier()
            _run_head(P, B, asems, h, scores_mm, av_lhsT, out_dma, -1e30)
            P.barrier()

        P.wait("g", so[0], so[0].n)
        P.wait("g", so[1], so[1].n)
        with nc.Block() as block:
            P.finalize(block)
    return nc


def _run_head(P, B, sems, h, scores_mm, av_lhsT, out_dma, fill):
    emit_attn_core(P, B, sems,
                   lambda _h, qt, w, ps: scores_mm(h, qt, w, ps),
                   lambda _h, kc: av_lhsT(h, kc),
                   lambda _h, quad, src: out_dma(h, quad, src),
                   fill, nheads=1)


"""attn1a: GQA attention, head-sharded. Per core: 8 q-heads (2 kv-heads), full causal.
in: xln_all [2, 2048, 1024] bf16; wqT_s [2048,1024], wkT_s [2048,256], wvT_s [2048,256] bf16
out: ao_mine [1024, 2048] bf16 (pre-out-projection attention output rows)."""

NH_LOC = 8
FILL_L1 = -3.0e38


def build_attn1(T_CORE=1024):
    TF = 2 * T_CORE
    nc = bass.Bass("TRN2", target_bir_lowering=False, debug=False)
    xln_all = nc.dram_tensor("xln_all", [2, 2048, T_CORE], BF16, kind="ExternalInput").ap()
    wqT_s = nc.dram_tensor("wqT_s", [2048, 1024], BF16, kind="ExternalInput").ap()
    wkT_s = nc.dram_tensor("wkT_s", [2048, 256], BF16, kind="ExternalInput").ap()
    wvT_s = nc.dram_tensor("wvT_s", [2048, 256], BF16, kind="ExternalInput").ap()
    ao_mine = nc.dram_tensor("ao_mine", [1024, TF], BF16, kind="ExternalOutput").ap()

    with ExitStack() as es:
        P = Prog(nc, es)
        sb = lambda name, shape, dt=F32: es.enter_context(nc.sbuf_tensor(name, shape, dt))
        ps = lambda name, shape, dt=F32: es.enter_context(nc.psum_tensor(name, shape, dt))
        xla = sb("xla", [128, NCH, TF], BF16)
        q = sb("q", [128, NH_LOC, TF], BF16)
        kf = sb("kf", [128, 2, TF], BF16)
        vf = sb("vf", [128, 2, TF], BF16)
        v_tm = sb("v_tm", [128, 16, 256], BF16)
        scores = sb("scores", [128, 2048])
        attn_b = sb("attn_b", [128, 2048], BF16)
        attnT = sb("attnT", [128, 16, 512], BF16)
        rowm = sb("rowm", [128, 2])
        wst = sb("wst", [128, 1, NCH, 128], BF16)
        ystage = sb("ystage", [128, 2, 512], BF16)
        ident16 = sb("ident16", [128, 128], BF16)
        dmask = sb("dmask", [128, 4, 512], BF16)
        sc_ps = [ps("sc_ps0", [128, 512]), ps("sc_ps1", [128, 512])]
        y_ps = [ps("y_ps0", [128, 512]), ps("y_ps1", [128, 512])]
        pt = [ps("pt0", [128, 1024], BF16), ps("pt1", [128, 1024], BF16)]

        sems = {k: P.sem(k) for k in ("a_s", "a_v", "a_t", "a_g", "w0", "io")}
        sems["a_o"] = [P.sem("a_o0"), P.sem("a_o1")]
        ss, sv, st, sg = sems["a_s"], sems["a_v"], sems["a_t"], sems["a_g"]
        so = sems["a_o"]
        io = sems["io"]
        wsem = sems["w0"]

        csg = P.sem("csg")
        P.op("g", lambda E: E.memset(ident16[:], 0.0), inc=csg)
        P.op("g", lambda E: E.memset(dmask[:], 0.0), inc=csg)
        P.wait("g", csg, 2)
        P.op("g", lambda E: E.affine_select(
            out=ident16[:], in_=ident16[:], compare_op=ALU.not_equal,
            fill=1.0, base=0, pattern=[[-1, 128]], channel_multiplier=1), inc=csg)
        for m in range(4):
            sl = dmask[:, m, :]
            P.op("g", lambda E, sl=sl, m=m: E.affine_select(
                out=sl, in_=sl, compare_op=ALU.is_ge, fill=FILL_L1,
                base=m * 128, pattern=[[-1, 512]], channel_multiplier=1), inc=csg)
        P.wait("t", csg, 7)
        P.wait("v", csg, 7)

        for r in range(2):
            P.dma("y", xla[:, :, r * T_CORE:(r + 1) * T_CORE],
                  xln_all[r].rearrange("(cc p) t -> p cc t", p=128), io)
        n_x = io.n
        P.wait("t", io, n_x)

        wqT_r = wqT_s.rearrange("(cc p) (dq d) -> dq p cc d", p=128, d=128)
        for dqc in range(NH_LOC):
            P.wait("y", st, st.n)
            n_w = P.dma("y", wst[:, 0], wqT_r[dqc], wsem)
            P.wait("t", wsem, n_w)
            for tw in range(4):
                pp = sc_ps[tw % 2]
                P.wait("t", ss, ss.n)
                for cc in range(NCH):
                    P.op("t", lambda E, cc=cc, tw=tw, pp=pp: E.matmul(
                        pp[:, 0:512], wst[:, 0, cc, :],
                        xla[:, cc, tw * 512:(tw + 1) * 512],
                        start=(cc == 0), stop=(cc == NCH - 1)),
                        inc=(st if cc == NCH - 1 else None))
                P.wait("s", st, st.n)
                dst = q[:, dqc, tw * 512:(tw + 1) * 512]
                P.op("s", lambda E, dst=dst, pp=pp: E.copy(out=dst, in_=pp[:, 0:512]), inc=ss)
            P.barrier()

        wkT_r = wkT_s.rearrange("(cc p) (kv d) -> kv p cc d", p=128, d=128)
        wvT_r = wvT_s.rearrange("(cc p) (kv d) -> kv p cc d", p=128, d=128)
        for (wr, dstb) in ((wkT_r, kf), (wvT_r, vf)):
            for kv in range(2):
                P.wait("y", st, st.n)
                n_w = P.dma("y", wst[:, 0], wr[kv], wsem)
                P.wait("t", wsem, n_w)
                for tw in range(4):
                    pp = sc_ps[tw % 2]
                    P.wait("t", ss, ss.n)
                    for cc in range(NCH):
                        P.op("t", lambda E, cc=cc, tw=tw, pp=pp: E.matmul(
                            pp[:, 0:512], wst[:, 0, cc, :],
                            xla[:, cc, tw * 512:(tw + 1) * 512],
                            start=(cc == 0), stop=(cc == NCH - 1)),
                            inc=(st if cc == NCH - 1 else None))
                    P.wait("s", st, st.n)
                    dst = dstb[:, kv, tw * 512:(tw + 1) * 512]
                    P.op("s", lambda E, dst=dst, pp=pp: E.copy(out=dst, in_=pp[:, 0:512]), inc=ss)
                P.barrier()

        for kv in range(2):
            for kc in range(16):
                tpar = kc % 2
                src = vf[:, kv, kc * 128:(kc + 1) * 128]
                P.wait("t", ss, ss.n)
                n_tp = P.op("t", lambda E, tpar=tpar, src=src: E.transpose(
                    pt[tpar][:, 0:128], src, ident16[:]), inc=st)
                P.wait("s", st, n_tp)
                dst = v_tm[:, kc, kv * 128:(kv + 1) * 128]
                P.op("s", lambda E, dst=dst, tpar=tpar: E.copy(
                    out=dst, in_=pt[tpar][:, 0:128]), inc=ss)
        P.barrier()

        B = dict(scores=scores[:], attn_b=attn_b[:], attnT=attnT,
                 rowm=rowm, pt=pt, sc_ps=sc_ps, y_ps=y_ps, ystage=ystage,
                 ident16=ident16, dmask=dmask)

        def scores_mm(h, qt, w, ps_ap):
            P.op("t", lambda E, h=h, qt=qt, w=w, ps_ap=ps_ap: E.matmul(
                ps_ap, q[:, h, qt * 128:(qt + 1) * 128],
                kf[:, h // 4, w * 512:(w + 1) * 512],
                start=True, stop=True), inc=st)
            return st.n

        def av_lhsT(h, kc):
            return v_tm[:, kc, (h // 4) * 128:(h // 4 + 1) * 128]

        def out_dma(h, quad, src):
            return P.dma("g", ao_mine[h * 128:(h + 1) * 128,
                                      quad * 512:(quad + 1) * 512], src, so[quad % 2])

        asems = dict(a_s=ss, a_v=sv, a_t=st, a_g=sg, a_o=so)
        for h in range(NH_LOC):
            emit_attn_core(P, B, asems,
                           lambda _h, qt, w, ps, h=h: scores_mm(h, qt, w, ps),
                           lambda _h, kc, h=h: av_lhsT(h, kc),
                           lambda _h, quad, src, h=h: out_dma(h, quad, src),
                           FILL_L1, nheads=1)
            P.barrier()

        P.wait("g", so[0], so[0].n)
        P.wait("g", so[1], so[1].n)
        with nc.Block() as block:
            P.finalize(block)
    return nc


"""FUSED megakernel: the whole macaron-d2 forward in ONE program per core.

Phases: prep (dequant+casts+AllGather) | ffn0 | xln->gather | attn0 (stream
spilled) | y-add | ffn0 | moe | xln->gather | attn1 (spilled) | outproj | moe
| store. Residual stream stays in SBUF except during the two attention phases
(spilled to private DRAM; reloaded). All cross-core traffic via pair/all-8
AllGather collectives. Program is identical on all 8 cores; per-core behavior
comes only from input data (incl. 0/1 half-selection masks cfgm).

SBUF map (bytes/partition): [16384,81920) stream | [81920,215808) phase arena
| [215808,229344) perm smalls. PSUM: ps0..ps4+ps_small (left,f32), pt0/pt1
(right,bf16).
"""
from types import SimpleNamespace

NH_LOC = 8
NE = 8
NIC = 11
T = 1024
T2 = 512
TF = 2048

WL_SPECS = [
    ("wq_nat_s", (2048, 1024)), ("wqT_s", (2048, 1024)), ("woT_s", (2048, 1024)),
    ("wuk", (2048, 512)), ("wuv", (2048, 512)), ("wdkvT", (2048, 512)),
    ("wqT1_s", (2048, 1024)), ("wkT_s", (2048, 256)), ("wvT_s", (2048, 256)),
    ("woT1", (2048, 2048)), ("wr", (2048, 8)),
]
WL_OFF = {}
_off = 0
for _n, _s in WL_SPECS:
    WL_OFF[_n] = _off
    _off += _s[0] * _s[1]
WL_ELEMS = _off
WL_PAD = ((WL_ELEMS + 262143) // 262144) * 262144

GW_TILE = 128 * 16 * 128
C_WG = 0
C_WU = 16 * GW_TILE
GU_ELEMS = 32 * GW_TILE
D_ELEMS = 16 * GW_TILE
MOE_ELEMS = 3 * 2048 * 1408
MOE_PAD = ((MOE_ELEMS + 262143) // 262144) * 262144


def build_fused(n_mlp=16384, debug=False, stop=None, unit_groups=False):
    NCHK = n_mlp // 8 // 128
    NGRP = n_mlp // 128 // NCH
    nc = bass.Bass("TRN2", target_bir_lowering=False, debug=False)

    x_in = nc.dram_tensor("x_in", [2048, T], F32, kind="ExternalInput").ap()
    wl_f32 = nc.dram_tensor("wl_f32", [WL_PAD // 512, 512], F32, kind="ExternalInput").ap()
    moe_f32 = nc.dram_tensor("moe_f32", [MOE_PAD // 512, 512], F32, kind="ExternalInput").ap()
    gate_p = nc.dram_tensor("gate_p", [n_mlp // 8, 1024], I32, kind="ExternalInput").ap()
    gate_s = nc.dram_tensor("gate_s", [n_mlp // 8, 64], F32, kind="ExternalInput").ap()
    up_p = nc.dram_tensor("up_p", [n_mlp // 8, 1024], I32, kind="ExternalInput").ap()
    up_s = nc.dram_tensor("up_s", [n_mlp // 8, 64], F32, kind="ExternalInput").ap()
    down_p = nc.dram_tensor("down_p", [256, n_mlp // 2], I32, kind="ExternalInput").ap()
    down_s = nc.dram_tensor("down_s", [256, n_mlp // 32], F32, kind="ExternalInput").ap()
    bo_in = nc.dram_tensor("bo_in", [2048], F32, kind="ExternalInput").ap()
    cfg_in = nc.dram_tensor("cfg_in", [128, 2], F32, kind="ExternalInput").ap()
    out_t = nc.dram_tensor("out_t", [2048, T], F32, kind="ExternalOutput").ap()

    dbg = {}
    if debug:
        for i in range(5):
            dbg[i] = nc.dram_tensor(f"dbg{i}", [2048, T], F32, kind="ExternalOutput").ap()
        dbg["y0"] = nc.dram_tensor("dbg_y0", [1024, TF], BF16, kind="ExternalOutput").ap()
        dbg["xln0"] = nc.dram_tensor("dbg_xln0", [2048, T], BF16, kind="ExternalOutput").ap()
        dbg["gw0"] = nc.dram_tensor("dbg_gw0", [128, 16, 128], BF16, kind="ExternalOutput").ap()
        dbg["gw17"] = nc.dram_tensor("dbg_gw17", [128, 16, 128], BF16, kind="ExternalOutput").ap()
        dbg["cw0"] = nc.dram_tensor("dbg_cw0", [128, 16, 128], BF16, kind="ExternalOutput").ap()
        dbg["gm0"] = nc.dram_tensor("dbg_gm0", [64, 1408], BF16, kind="ExternalOutput").ap()
        dbg["wl0"] = nc.dram_tensor("dbg_wl0", [64, 1024], BF16, kind="ExternalOutput").ap()
    wl_bf = nc.dram_tensor("wl_bf", [WL_PAD // 512, 512], BF16)
    contrib = nc.dram_tensor("contrib", [GU_ELEMS // 512, 512], BF16)
    gath = nc.dram_tensor("gath", [8 * GU_ELEMS // 512, 512], BF16)
    contrib_d = nc.dram_tensor("contrib_d", [D_ELEMS // 512, 512], BF16)
    gath_d = nc.dram_tensor("gath_d", [8 * D_ELEMS // 512, 512], BF16)
    contrib_m = nc.dram_tensor("contrib_m", [MOE_PAD // 512, 512], BF16)
    gath_m = nc.dram_tensor("gath_m", [8 * MOE_PAD // 512, 512], BF16)
    xln_o = nc.dram_tensor("xln_o", [2048, T], BF16)
    xg = nc.dram_tensor("xg", [2 * 2048, T], BF16)
    y_o = nc.dram_tensor("y_o", [1024, TF], BF16)
    yg = nc.dram_tensor("yg", [2 * 1024, TF], BF16)
    spill = nc.dram_tensor("spill", [2048, T], F32)

    flatC = contrib.ap().rearrange("a b -> (a b)")
    flatCD = contrib_d.ap().rearrange("a b -> (a b)")
    flatG = gath.ap().rearrange("a b -> (a b)")
    flatGD = gath_d.ap().rearrange("a b -> (a b)")
    flatGM = gath_m.ap().rearrange("a b -> (a b)")
    flatWL = wl_bf.ap().rearrange("a b -> (a b)")

    def wl_ap(name):
        r, c = dict(WL_SPECS)[name]
        o = WL_OFF[name]
        return flatWL[o:o + r * c].rearrange("(a b) -> a b", b=c)

    def gath_wg(jg, base):
        r, jl = divmod(jg, NCHK)
        o = r * GU_ELEMS + base + jl * GW_TILE
        return flatG[o:o + GW_TILE].rearrange("(p cc n) -> p cc n", p=128, cc=16)

    def gath_wd(dc, g):
        r, dl = divmod(dc, 2)
        o = r * D_ELEMS + (dl * NGRP + g) * GW_TILE
        return flatGD[o:o + GW_TILE].rearrange("(p cc n) -> p cc n", p=128, cc=16)

    def gath_moe(e, which):
        o = e * MOE_PAD + which * 2048 * 1408
        if which < 2:
            return flatGM[o:o + 2048 * 1408].rearrange("(a b) -> a b", b=1408)
        return flatGM[o:o + 1408 * 2048].rearrange("(a b) -> a b", b=2048)

    CCV = 128 if unit_groups else 1
    CCP = 32 if unit_groups else 1
    with ExitStack() as es:
        P = Prog(nc, es)

        def at(name, shape, dt, off):
            return nc.alloc_sbuf_tensor_at(name, shape, dt, offset=off)

        stream = at("stream", [128, NCH, T], F32, 18432)
        AR = 83968
        SM = 215808
        ones_col16 = at("ones_col16", [128, 1], BF16, SM); SM += 32
        ones_row32 = at("ones_row32", [1, 128], F32, SM); SM += 512
        ones_row16 = at("ones_row16", [1, 128], BF16, SM); SM += 256
        ident16 = at("ident16", [128, 128], BF16, SM); SM += 256
        ident32 = at("ident32", [128, 128], F32, SM); SM += 512
        wr_sb = at("wr_sb", [128, NCH, 8], BF16, SM); SM += 256
        bo_sb = at("bo_sb", [128, NCH], F32, SM); SM += 64
        rowm = at("rowm", [128, 2], F32, SM); SM += 32
        ystage = at("ystage", [128, 2, 512], BF16, SM); SM += 2048
        ce_all = at("ce_all", [8, 512], BF16, SM); SM += 1024
        ce_rows = at("ce_rows", [1, 8, 512], BF16, SM); SM += 8192
        er = at("er", [128, 4, 8], F32, SM); SM += 128
        rsc = at("rsc", [128, 8], F32, SM); SM += 32
        comb = at("comb", [128, 8], F32, SM); SM += 32
        bar_dummy = at("bar_dummy", [128, 8], F32, SM); SM += 32
        cfgm = at("cfgm", [128, 2], F32, SM); SM += 32
        cf_a = at("cf_a", [128, 1], F32, SM); SM += 32
        cf_b = at("cf_b", [128, 1], F32, SM); SM += 32
        assert SM <= 229344, SM
        P._dummy = bar_dummy
        P._bars = {e: P.sem(f"bar_{e}") for e in ("y", "v", "s", "t", "g")}
        nc.const_aps.register(F32, -2.0, cf_a[:])
        nc.const_aps.register(F32, -2.5, cf_b[:])

        ps = [es.enter_context(nc.psum_tensor(f"ps{i}", [128, 512], F32)) for i in range(5)]
        ps_small = es.enter_context(nc.psum_tensor("ps_small", [1, 512], F32))
        pt = [es.enter_context(nc.psum_tensor("pt0", [128, 1024], BF16, side="right")),
              es.enter_context(nc.psum_tensor("pt1", [128, 1024], BF16, side="right"))]

        S = {k: P.sem(k) for k in (
            "io", "sp",
            "d_v", "d_g", "d_s", "d_t", "dio0", "dio1", "dso0", "dso1",
            "cA_in0", "cA_in1", "cA_out0", "cA_out1", "cA_s",
            "cc0", "cc0b", "cc0d", "cc1", "cc2", "cc3", "cc4",
            "ln_s", "ln_v", "ln_t",
            "f_s", "f_v", "f_t", "f_w0", "f_w1", "f_d0", "f_d1",
            "a_s", "a_v", "a_t", "a_g", "a_o0", "a_o1", "a_w",
            "m_s", "m_v", "m_t", "m_g", "m_w0", "m_w1", "m_wd", "m_wd1", "m_ce",
            "o_w0", "o_w1", "yio0", "yio1", "oio0", "oio1",
        )}
        csg = P.sem("csg")
        csv = P.sem("csv")


        def gather(in_ap, out_full_ap, rows_in, sem, groups):
            if unit_groups:
                nrep = out_full_ap.shape[0] // rows_in
                for r in range(nrep - 1):
                    P.dma("g", out_full_ap[(r + 1) * rows_in:(r + 2) * rows_in], in_ap, sem)
                P.dma("g", out_full_ap[0:rows_in], in_ap, sem)
            else:
                P.coll(lambda E, in_ap=in_ap, out_full_ap=out_full_ap, groups=groups:
                       E.collective_compute("AllGather", ALU.bypass,
                                            replica_groups=groups,
                                            ins=[in_ap], outs=[out_full_ap]), sem)

        P.op("v", lambda E: E.memset(ones_col16[:], 1.0))
        P.op("v", lambda E: E.memset(ones_row32[:], 1.0))
        P.op("v", lambda E: E.memset(ones_row16[:], 1.0), inc=csv)
        P.op("v", lambda E: E.memset(cf_a[:], -2.0))
        P.op("v", lambda E: E.memset(cf_b[:], -2.5), inc=csv)
        P.op("g", lambda E: E.memset(ident16[:], 0.0))
        P.op("g", lambda E: E.memset(ident32[:], 0.0), inc=csg)
        P.wait("g", csg, 1)
        P.op("g", lambda E: E.affine_select(
            out=ident16[:], in_=ident16[:], compare_op=ALU.not_equal,
            fill=1.0, base=0, pattern=[[-1, 128]], channel_multiplier=1), inc=csg)
        P.op("g", lambda E: E.affine_select(
            out=ident32[:], in_=ident32[:], compare_op=ALU.not_equal,
            fill=1.0, base=0, pattern=[[-1, 128]], channel_multiplier=1), inc=csg)
        P.wait("t", csg, 3)
        P.wait("t", csv, 2)
        P.wait("s", csv, 2)
        n_x = P.dma("g", stream[:], x_in.rearrange("(cc p) t -> p cc t", p=128), S["io"])
        P.dma("g", cfgm[:], cfg_in, S["io"])
        for cc in range(NCH):
            P.dma("g", bo_sb[:, cc:cc + 1], bo_in[cc * 128:(cc + 1) * 128].unsqueeze(1), S["io"])
        n_io0 = S["io"].n

        PSUB = 512
        pk_b = at("pk_b", [128, 2, PSUB], I32, AR)
        sc_b = at("sc_b", [128, 2, 32], F32, AR + 4096)
        scr_names = ("c_i", "m_i", "o_i", "s_i", "m_f", "o_f", "s_f",
                     "t0", "t1", "t2", "t3", "t4", "t5")
        scratch = {}
        o = AR + 4096 + 256
        for k in scr_names:
            scratch[k] = at(k, [128, 2, PSUB], I32 if k.endswith("_i") else F32, o)
            o += 2 * PSUB * 4
        val16 = at("val16", [128, 2, 2 * PSUB], BF16, o); o += 2 * 2 * PSUB * 2
        wtile = at("wtile", [128, 2, 128], BF16, o); o += 512
        cbuf32 = at("cbuf32", [128, 2, 2048], F32, o); o += 16384
        cbuf16 = at("cbuf16", [128, 2, 2048], BF16, o); o += 8192
        assert o <= 215808, o

        _war_cast = {}

        def emit_cast(src_flat, dst_flat, rows_total, pre):
            c_s = S["cA_s"]
            war_cast = _war_cast
            CH = 512
            nch = rows_total // CH
            assert nch * CH == rows_total
            for i in range(nch):
                par = i % 2
                in_sem = S[f"cA_in{par}"]
                out_sem = S[f"cA_out{par}"]
                r0 = i * CH
                vi = src_flat[r0:r0 + CH].rearrange("(a p) w -> p a w", p=128)
                vo = dst_flat[r0:r0 + CH].rearrange("(a p) w -> p a w", p=128)
                P.wait("y", out_sem, out_sem.n)
                P.wait("y", c_s, war_cast.get(par, 0))
                n_in = P.dma("y", cbuf32[:, par].rearrange("p (a w) -> p a w", w=512), vi, in_sem)
                P.wait("s", in_sem, n_in)
                n_c = P.op("s", lambda E, par=par: E.copy(
                    out=cbuf16[:, par], in_=cbuf32[:, par]), inc=c_s)
                war_cast[par] = n_c
                P.wait("y", c_s, n_c)
                P.dma("y", vo, cbuf16[:, par].rearrange("p (a w) -> p a w", w=512), out_sem)

        sv_d, sg_d, ss_d, st_d = S["d_v"], S["d_g"], S["d_s"], S["d_t"]
        war_tp = {}
        war_decode = {}

        def emit_decode(pk, par, out_view, sc_bc):
            for nib in range(2):
                P.wait("v", sg_d, war_decode.get(("g", par), 0))
                P.wait("v", sv_d, war_decode.get(("v", par), 0) if nib == 0 else sv_d.n)
                c_i, m_i, o_i, s_i = (scratch[k][:, par] for k in ("c_i", "m_i", "o_i", "s_i"))
                m_f, o_f, s_f = (scratch[k][:, par] for k in ("m_f", "o_f", "s_f"))
                t0_, t1_, t2_, t3_ = (scratch[k][:, par] for k in ("t0", "t1", "t2", "t3"))
                t4_, t5_ = scratch["t4"][:, par], scratch["t5"][:, par]
                if nib == 0:
                    P.op("v", lambda E, c_i=c_i, pk=pk: E.tensor_single_scalar(
                        out=c_i, in_=pk, scalar=15, op=ALU.bitwise_and), inc=sv_d)
                else:
                    P.op("v", lambda E, c_i=c_i, pk=pk: E.tensor_single_scalar(
                        out=c_i, in_=pk, scalar=4, op=ALU.logical_shift_right), inc=sv_d)
                    P.wait("v", sv_d, sv_d.n)
                    P.op("v", lambda E, c_i=c_i: E.tensor_single_scalar(
                        out=c_i, in_=c_i, scalar=15, op=ALU.bitwise_and), inc=sv_d)
                P.wait("v", sv_d, sv_d.n)
                P.op("v", lambda E, m_i=m_i, c_i=c_i: E.tensor_single_scalar(
                    out=m_i, in_=c_i, scalar=7, op=ALU.bitwise_and), inc=sv_d)
                P.op("v", lambda E, o_i=o_i, c_i=c_i: E.tensor_single_scalar(
                    out=o_i, in_=c_i, scalar=1, op=ALU.bitwise_and), inc=sv_d)
                n_int = P.op("v", lambda E, s_i=s_i, c_i=c_i: E.tensor_single_scalar(
                    out=s_i, in_=c_i, scalar=3, op=ALU.logical_shift_right), inc=sv_d)
                P.wait("s", sv_d, n_int)
                P.wait("s", ss_d, war_decode.get(("s", par), 0))
                P.op("s", lambda E, m_f=m_f, m_i=m_i: E.copy(out=m_f, in_=m_i), inc=ss_d)
                P.op("s", lambda E, o_f=o_f, o_i=o_i: E.copy(out=o_f, in_=o_i), inc=ss_d)
                P.op("s", lambda E, s_f=s_f, s_i=s_i: E.copy(out=s_f, in_=s_i), inc=ss_d)
                P.wait("s", ss_d, ss_d.n)
                P.op("s", lambda E, t0_=t0_, m_f=m_f: E.mul(out=t0_, in_=m_f, mul=0.5), inc=ss_d)
                P.op("s", lambda E, t4_=t4_, m_f=m_f: E.add(out=t4_, in_=m_f, add=-2.0), inc=ss_d)
                P.wait("s", ss_d, ss_d.n)
                n_aff = P.op("s", lambda E, t5_=t5_, t0_=t0_: E.add(out=t5_, in_=t0_, add=-2.5), inc=ss_d)
                P.wait("v", ss_d, n_aff)
                P.op("v", lambda E, t1_=t1_, o_f=o_f, t5_=t5_: E.tensor_tensor(
                    out=t1_, in0=o_f, in1=t5_, op=ALU.mult), inc=sv_d)
                P.op("v", lambda E, t2_=t2_, m_f=m_f: E.tensor_single_scalar(
                    out=t2_, in_=m_f, scalar=4.0, op=ALU.is_ge), inc=sv_d)
                P.wait("v", sv_d, sv_d.n)
                P.op("v", lambda E, t1_=t1_, t4_=t4_: E.tensor_tensor(
                    out=t1_, in0=t4_, in1=t1_, op=ALU.add), inc=sv_d)
                P.wait("v", sv_d, sv_d.n)
                P.op("v", lambda E, t1_=t1_, t0_=t0_: E.tensor_tensor(
                    out=t1_, in0=t1_, in1=t0_, op=ALU.subtract), inc=sv_d)
                P.wait("v", sv_d, sv_d.n)
                P.op("v", lambda E, t1_=t1_, t2_=t2_: E.tensor_tensor(
                    out=t1_, in0=t2_, in1=t1_, op=ALU.mult), inc=sv_d)
                P.wait("v", sv_d, sv_d.n)
                n_mag = P.op("v", lambda E, t0_=t0_, t1_=t1_: E.tensor_tensor(
                    out=t0_, in0=t0_, in1=t1_, op=ALU.add), inc=sv_d)
                P.wait("g", sv_d, n_mag)
                P.wait("g", sg_d, war_decode.get(("g2", par), 0))
                P.op("g", lambda E, t3_=t3_, s_f=s_f, t0_=t0_: E.tensor_tensor(
                    out=t3_, in0=s_f, in1=t0_, op=ALU.mult), inc=sg_d)
                P.wait("g", sg_d, sg_d.n)
                P.op("g", lambda E, t1_=t1_, t0_=t0_, t3_=t3_: E.tensor_tensor(
                    out=t1_, in0=t0_, in1=t3_, op=ALU.subtract), inc=sg_d)
                P.wait("g", sg_d, sg_d.n)
                n_sgn = P.op("g", lambda E, t1_=t1_, t3_=t3_: E.tensor_tensor(
                    out=t1_, in0=t1_, in1=t3_, op=ALU.subtract), inc=sg_d)
                war_decode[("g2", par)] = n_sgn
                P.wait("v", sg_d, n_sgn)
                ov = out_view[nib]
                n_out = P.op("v", lambda E, ov=ov, t1_=t1_: E.tensor_tensor(
                    out=ov, in0=t1_.rearrange("p (b k) -> p b k", k=16),
                    in1=sc_bc, op=ALU.mult), inc=sv_d)
            war_decode[("v", par)] = n_out
            war_decode[("s", par)] = ss_d.n
            war_decode[("g", par)] = sg_d.n
            return n_out

        def do_dequant(pk_ap, sc_ap, nchk, pw, out_tile_fn):
            nsub = pw // PSUB
            for ch in range(nchk):
                for sub in range(nsub):
                    par = (ch * nsub + sub) % 2
                    iosem = S[f"dio{par}"]
                    P.wait("y", sv_d, war_decode.get(("v", par), 0))
                    P.dma("y", pk_b[:, par], pk_ap[ch * 128:(ch + 1) * 128, sub * PSUB:(sub + 1) * PSUB], iosem)
                    nbl = PSUB // 16
                    n_sc = P.dma("y", sc_b[:, par, 0:nbl],
                                 sc_ap[ch * 128:(ch + 1) * 128, sub * nbl:(sub + 1) * nbl], iosem)
                    P.wait("v", iosem, n_sc)
                    pk = pk_b[:, par]
                    vv = val16[:, par].rearrange("p (a b) -> p a b", b=2)
                    sc_bc = sc_b[:, par, 0:nbl].unsqueeze(2).broadcast_to([128, nbl, 16])
                    lo = vv[:, :, 0].rearrange("p (b k) -> p b k", k=16)
                    hi = vv[:, :, 1].rearrange("p (b k) -> p b k", k=16)
                    P.wait("v", st_d, war_tp.get(("val", par), 0))
                    n_dec = emit_decode(pk, par, (lo, hi), sc_bc)
                    ncc = 2 * PSUB // 128
                    for q in range(ncc):
                        cc_glob = sub * ncc + q
                        tpar = cc_glob % 2
                        src = val16[:, par, q * 128:(q + 1) * 128]
                        P.wait("t", sv_d, n_dec)
                        P.wait("t", ss_d, war_tp.get(tpar, 0))
                        n_tp = P.op("t", lambda E, tpar=tpar, src=src: E.transpose(
                            pt[tpar][:, 0:128], src, ident16[:]), inc=st_d)
                        war_tp[("val", par)] = n_tp
                        P.wait("s", st_d, n_tp)
                        sosem = S[f"dso{tpar}"]
                        P.wait("s", sosem, sosem.n)
                        n_ev = P.op("s", lambda E, tpar=tpar: E.copy(
                            out=wtile[:, tpar], in_=pt[tpar][:, 0:128]), inc=ss_d)
                        war_tp[tpar] = n_ev
                        P.wait("y", ss_d, n_ev)
                        P.dma("y", out_tile_fn(ch, cc_glob), wtile[:, tpar], sosem)

        def vC4(base, nchk):
            return flatC[base:base + nchk * GW_TILE].rearrange(
                "(jl c cc n) -> jl c cc n", c=128, cc=16, n=128)

        vg = vC4(C_WG, NCHK)
        vu = vC4(C_WU, NCHK)
        vdn = flatCD[0:16 * GW_TILE].rearrange(
            "(dl g c cc n) -> dl g c cc n", g=NGRP, c=128, cc=16, n=128)
        do_dequant(gate_p, gate_s, NCHK, 1024, lambda ch, ccg: vg[ch, :, ccg, :])
        do_dequant(up_p, up_s, NCHK, 1024, lambda ch, ccg: vu[ch, :, ccg, :])
        for s in ("dso0", "dso1"):
            P.wait("g", S[s], S[s].n)
        gather(contrib.ap(), gath.ap(), GU_ELEMS // 512, S["cc0"], [list(range(8))])
        do_dequant(down_p, down_s, 2, n_mlp // 2,
                   lambda ch, nchunk: vdn[ch, nchunk // NCH, :, nchunk % NCH, :])
        for s in ("dso0", "dso1"):
            P.wait("g", S[s], S[s].n)
        gather(contrib_d.ap(), gath_d.ap(), D_ELEMS // 512, S["cc0d"], [list(range(8))])
        emit_cast(wl_f32, wl_bf.ap(), WL_PAD // 512, "cA")
        emit_cast(moe_f32, contrib_m.ap(), MOE_PAD // 512, "cB")
        for s in ("cA_out0", "cA_out1"):
            P.wait("g", S[s], S[s].n)
        gather(contrib_m.ap(), gath_m.ap(), MOE_PAD // 512, S["cc0b"], [list(range(8))])
        P.barrier()

        ffnB = SimpleNamespace(
            stream=stream, T=T, T2=T2,
            x_ln=at("f_xln", [128, NCH, T2], BF16, AR),
            big_a=at("f_biga", [128, NCH, 1024], BF16, AR + 16384),
            wg_buf=at("f_wg", [128, 2, NCH, 128], BF16, AR + 49152),
            wu_buf=at("f_wu", [128, 2, NCH, 128], BF16, AR + 57344),
            wd_buf=at("f_wd", [128, 2, NCH, 128], BF16, AR + 65536),
            gelu_sb=at("f_gelu", [128, 2, T2], BF16, AR + 73728),
            ds_acc=at("f_ds", [128, NCH, T2], F32, AR + 75776),
            rows=at("f_rows", [1, 4, T2], F32, AR + 108544),
            ones_col16=ones_col16, ones_row32=ones_row32,
            ps=[ps[0], ps[1], ps[2], ps[3], ps[4], ps[4]],
            ps_small=ps_small,
        )
        ffnB.xb16 = ffnB.big_a[:, :, 0:T2]
        ffnB.xsq16 = ffnB.big_a[:, :, T2:1024]
        fw = dict(wg_t=lambda jg: gath_wg(jg, C_WG),
                  wu_t=lambda jg: gath_wg(jg, C_WU),
                  wd_t=gath_wd)
        fsems = dict(ln_s=S["ln_s"], ln_v=S["ln_v"], ln_t=S["ln_t"],
                     f_s=S["f_s"], f_v=S["f_v"], f_t=S["f_t"],
                     f_w=[S["f_w0"], S["f_w1"]], f_d=[S["f_d0"], S["f_d1"]])

        if debug:
            P.wait("g", S["cc0"], CCV)
            P.wait("g", S["cc0b"], CCV)
            vC0 = flatC[0:GW_TILE].rearrange("(p cc n) -> p cc n", p=128, cc=16)
            for (tag, src_ap) in (("gw0", gath_wg(0, C_WG)), ("gw17", gath_wg(17, C_WG)),
                                  ("cw0", vC0)):
                n_d = P.dma("g", dbg[tag], src_ap, S["io"])
            n_d = P.dma("g", dbg["gm0"], gath_moe(0, 0)[0:64, :], S["io"])
            n_d = P.dma("g", dbg["wl0"], wl_ap("wq_nat_s")[0:64, :], S["io"])
            P.wait("g", S["io"], n_d)

        def run_ffn():
            P.wait("y", S["cc0"], CCV)
            fsems["wd_gate"] = (S["cc0d"], CCV)
            P.wait("s", S["io"], n_io0)
            P.wait("v", S["io"], n_io0)
            emit_ffn0(P, ffnB, fw, fsems, n_mlp=n_mlp, xln_half=True)
            P.barrier()

        run_ffn()

        def dump_stream(i):
            if debug:
                P.wait("g", S["f_v"], S["f_v"].n)
                P.wait("g", S["m_v"], S["m_v"].n)
                P.wait("g", S["a_v"], S["a_v"].n)
                n_d = P.dma("g", dbg[i].rearrange("(cc p) t -> p cc t", p=128), stream[:], S["io"])
                P.wait("g", S["io"], n_d)
        dump_stream(0)
        if stop == "x1":
            P.wait("g", S["f_v"], S["f_v"].n)
            n_o = P.dma("g", out_t.rearrange("(cc p) t -> p cc t", p=128), stream[:], S["io"])
            P.wait("g", S["io"], n_o)
            with nc.Block() as block:
                P.finalize(block)
            return nc

        xlnB = SimpleNamespace(
            stream=stream, T=T, T2=T2,
            x_ln=ffnB.x_ln, rows=ffnB.rows,
            xb16=ffnB.xb16, xsq16=ffnB.xsq16,
            ones_col16=ones_col16, ones_row32=ones_row32,
            ps=[None, None, None, None, None, ps[4]], ps_small=ps_small,
        )
        lsems = dict(ln_s=S["ln_s"], ln_v=S["ln_v"], ln_t=S["ln_t"])

        def run_xln(ccsem):
            for half in (0, 1):
                n_ln = emit_ln(P, xlnB, half, lsems, dst_t0=0,
                               war=[(S["io"], S["io"].n)])
                P.wait("g", lsems["ln_v"], n_ln)
                P.dma("g", xln_o.ap()[:, half * T2:(half + 1) * T2].rearrange(
                    "(cc p) t -> p cc t", p=128), xlnB.x_ln[:], S["io"])
            P.wait("g", S["io"], S["io"].n)
            gather(xln_o.ap(), xg.ap(), 2048, ccsem, [[0, 1], [2, 3], [4, 5], [6, 7]])
            P.barrier()

        run_xln(S["cc1"])
        if debug:
            n_d = P.dma("g", dbg["xln0"].rearrange("(cc p) t -> p cc t", p=128),
                        xln_o.ap().rearrange("(cc p) t -> p cc t", p=128), S["io"])
            P.wait("g", S["io"], n_d)

        n_sp = P.dma("g", spill.ap().rearrange("(cc p) t -> p cc t", p=128), stream[:], S["sp"])

        AB = 18432
        a_xst = at("a_xst", [128, NCH, T2], BF16, AB)
        a_wf = at("a_wf", [128, NCH, 1024], BF16, AB + 16384)
        a_q = at("a_q", [128, NH_LOC, TF], BF16, AB + 49152)
        a_cv = at("a_cv", [128, NCH, 1024], BF16, AB + 81920)
        a_ckv = at("a_ckv", [128, 4, TF], BF16, AB + 114688)
        a_keff = at("a_keff", [128, NH_LOC, 512], BF16, AB + 131072)
        a_qlat = at("a_qlat", [128, 4, TF], BF16, AB + 139264)
        a_scores = at("a_scores", [128, 2048], F32, AB + 155648)
        a_attnb = at("a_attnb", [128, 2048], BF16, AB + 163840)
        a_attnT = at("a_attnT", [128, 16, 512], BF16, AB + 167936)
        a_dm0 = at("a_dm0", [128, 4, 512], BF16, AB + 184320)
        a_veff = at("a_veff", [128, 4, 1024], BF16, AB + 188416)
        a_kf = at("a_kf", [128, 2, TF], BF16, AB + 81920)
        a_vf = at("a_vf", [128, 2, TF], BF16, AB + 90112)
        a_vtm = at("a_vtm", [128, 16, 256], BF16, AB + 98304)
        a_dm1 = at("a_dm1", [128, 4, 512], BF16, AB + 106496)

        ss_a, sv_a, st_a, sg_a = S["a_s"], S["a_v"], S["a_t"], S["a_g"]
        so_a = [S["a_o0"], S["a_o1"]]
        aw = S["a_w"]

        war_pp = {}
        aB = dict(scores=a_scores[:], attn_b=a_attnb[:], attnT=a_attnT,
                  rowm=rowm, pt=pt, sc_ps=[ps[0], ps[1]], y_ps=[ps[2], ps[2]],
                  ystage=ystage, ident16=ident16, y_single=True)
        asems = dict(a_s=ss_a, a_v=sv_a, a_t=st_a, a_g=sg_a, a_o=so_a)

        def mk_dmask(dm, fill):
            nmk = P.op("g", lambda E, dm=dm: E.memset(dm[:], 0.0), inc=sg_a)
            P.wait("g", sg_a, nmk)
            for m in range(4):
                sl = dm[:, m, :]
                nmk = P.op("g", lambda E, sl=sl, m=m, fill=fill: E.affine_select(
                    out=sl, in_=sl, compare_op=ALU.is_ge, fill=fill,
                    base=m * 128, pattern=[[-1, 512]], channel_multiplier=1), inc=sg_a)
            P.wait("t", sg_a, sg_a.n)
            P.wait("v", sg_a, sg_a.n)

        def load_xst(tw):
            r, twl = divmod(tw, 2)
            src = xg.ap()[r * 2048:(r + 1) * 2048, twl * T2:(twl + 1) * T2].rearrange(
                "(cc p) t -> p cc t", p=128)
            return P.dma("y", a_xst[:], src, aw)

        def proj_pass(wl_name, ncols, dst_fn, extra_lhs_rearr=None):
            nocc = ncols // 128
            wv = wl_ap(wl_name).rearrange("(cc p) d -> p cc d", p=128)
            P.wait("y", st_a, st_a.n)
            n_w = P.dma("y", a_wf[:, :, 0:ncols], wv, aw)
            for tw in range(4):
                P.wait("y", st_a, st_a.n)
                n_x = load_xst(tw)
                P.wait("t", aw, n_x)
                for oc in range(nocc):
                    pp = ps[oc % 2]
                    P.wait("t", ss_a, war_pp.get(id(pp), 0))
                    for cc in range(NCH):
                        lhsT = a_wf[:, cc, oc * 128:(oc + 1) * 128]
                        rhs = a_xst[:, cc, :]
                        P.op("t", lambda E, lhsT=lhsT, rhs=rhs, cc=cc, pp=pp: E.matmul(
                            pp[:, 0:512], lhsT, rhs, start=(cc == 0), stop=(cc == NCH - 1)),
                            inc=(st_a if cc == NCH - 1 else None))
                    P.wait("s", st_a, st_a.n)
                    dst = dst_fn(oc, tw)
                    war_pp[id(pp)] = P.op("s", lambda E, dst=dst, pp=pp: E.copy(out=dst, in_=pp[:, 0:512]), inc=ss_a)
            P.barrier()

        def run_attn0():
            mk_dmask(a_dm0, FILL_L0)
            aB["dmask"] = a_dm0
            for e in ("y", "g", "s", "t", "v"):
                P.wait(e, S["sp"], n_sp)
            P.wait("y", S["cc1"], CCP)
            P.wait("y", S["cA_out0"], S["cA_out0"].n)
            P.wait("y", S["cA_out1"], S["cA_out1"].n)
            P.barrier()
            n_wuk = P.dma("y", a_xst[:], wl_ap("wuk").rearrange("(dc p) l -> p dc l", p=128), aw)
            wqv = wl_ap("wq_nat_s").rearrange("(dc p) s -> p dc s", p=128)
            n_wq = P.dma("y", a_wf[:, :, 0:1024], wqv, aw)
            P.wait("t", aw, n_wq)
            for sc in range(NH_LOC):
                pp = ps[sc % 2]
                P.wait("t", ss_a, war_pp.get(id(pp), 0))
                for dc in range(NCH):
                    lhsT = a_wf[:, dc, sc * 128:(sc + 1) * 128]
                    P.op("t", lambda E, lhsT=lhsT, dc=dc, pp=pp: E.matmul(
                        pp[:, 0:512], lhsT, a_xst[:, dc, :],
                        start=(dc == 0), stop=(dc == NCH - 1)),
                        inc=(st_a if dc == NCH - 1 else None))
                P.wait("s", st_a, st_a.n)
                dst = a_keff[:, sc, :]
                war_pp[id(pp)] = P.op("s", lambda E, dst=dst, pp=pp: E.mul(out=dst, in_=pp[:, 0:512], mul=RSQ), inc=ss_a)
            P.barrier()
            n_wuv = P.dma("y", a_xst[:], wl_ap("wuv").rearrange("(dc p) l -> p dc l", p=128), aw)
            wov = wl_ap("woT_s").rearrange("(dc p) c -> p dc c", p=128)
            n_wo = P.dma("y", a_wf[:, :, 0:1024], wov, aw)
            P.wait("t", aw, n_wo)
            for cw in range(2):
                for lc in range(4):
                    pp = ps[lc % 2]
                    P.wait("t", ss_a, war_pp.get(id(pp), 0))
                    for dc in range(NCH):
                        lhsT = a_xst[:, dc, lc * 128:(lc + 1) * 128]
                        rhs = a_wf[:, dc, cw * 512:(cw + 1) * 512]
                        P.op("t", lambda E, lhsT=lhsT, rhs=rhs, dc=dc, pp=pp: E.matmul(
                            pp[:, 0:512], lhsT, rhs, start=(dc == 0), stop=(dc == NCH - 1)),
                            inc=(st_a if dc == NCH - 1 else None))
                    P.wait("s", st_a, st_a.n)
                    dst = a_veff[:, lc, cw * 512:(cw + 1) * 512]
                    war_pp[id(pp)] = P.op("s", lambda E, dst=dst, pp=pp: E.copy(out=dst, in_=pp[:, 0:512]), inc=ss_a)
            P.barrier()
            n_wk = P.dma("y", a_wf[:, :, 0:512], wl_ap("wdkvT").rearrange(
                "(cc p) l -> p cc l", p=128), aw)
            for tw in range(4):
                P.wait("y", st_a, st_a.n)
                n_x = load_xst(tw)
                P.wait("t", aw, n_x)
                for lc in range(4):
                    pp = ps[lc % 2]
                    P.wait("t", ss_a, war_pp.get(id(pp), 0))
                    for cc in range(NCH):
                        lhsT = a_wf[:, cc, lc * 128:(lc + 1) * 128]
                        P.op("t", lambda E, lhsT=lhsT, cc=cc, pp=pp: E.matmul(
                            pp[:, 0:512], lhsT, a_xst[:, cc, :],
                            start=(cc == 0), stop=(cc == NCH - 1)),
                            inc=(st_a if cc == NCH - 1 else None))
                    P.wait("s", st_a, st_a.n)
                    dst = a_ckv[:, lc, tw * 512:(tw + 1) * 512]
                    war_pp[id(pp)] = P.op("s", lambda E, dst=dst, pp=pp: E.copy(out=dst, in_=pp[:, 0:512]), inc=ss_a)
            P.barrier()
            for kc in range(16):
                for cwi in range(2):
                    pp = ps[cwi]
                    P.wait("t", ss_a, war_pp.get(id(pp), 0))
                    for lc in range(4):
                        lhsT = a_ckv[:, lc, kc * 128:(kc + 1) * 128]
                        rhs = a_veff[:, lc, cwi * 512:(cwi + 1) * 512]
                        P.op("t", lambda E, lhsT=lhsT, rhs=rhs, lc=lc, pp=pp: E.matmul(
                            pp[:, 0:512], lhsT, rhs, start=(lc == 0), stop=(lc == 3)),
                            inc=(st_a if lc == 3 else None))
                    P.wait("s", st_a, st_a.n)
                    dst = a_cv[:, kc, cwi * 512:(cwi + 1) * 512]
                    war_pp[id(pp)] = P.op("s", lambda E, dst=dst, pp=pp: E.copy(out=dst, in_=pp[:, 0:512]), inc=ss_a)
            P.barrier()
            proj_pass("wqT_s", 1024,
                      lambda oc, tw: a_q[:, oc, tw * 512:(tw + 1) * 512])
            for h in range(NH_LOC):
                for lc in range(4):
                    for tw in range(4):
                        pp = ps[tw % 2]
                        P.wait("t", ss_a, war_pp.get(id(pp), 0))
                        lhsT = a_keff[:, h, lc * 128:(lc + 1) * 128]
                        rhs = a_q[:, h, tw * 512:(tw + 1) * 512]
                        P.op("t", lambda E, lhsT=lhsT, rhs=rhs, pp=pp: E.matmul(
                            pp[:, 0:512], lhsT, rhs, start=True, stop=True), inc=st_a)
                        P.wait("s", st_a, st_a.n)
                        dst = a_qlat[:, lc, tw * 512:(tw + 1) * 512]
                        war_pp[id(pp)] = P.op("s", lambda E, dst=dst, pp=pp: E.copy(out=dst, in_=pp[:, 0:512]), inc=ss_a)
                P.barrier()

                def scores_mm(_h, qt, w, ps_ap):
                    for lc in range(4):
                        lhsT = a_qlat[:, lc, qt * 128:(qt + 1) * 128]
                        rhs = a_ckv[:, lc, w * 512:(w + 1) * 512]
                        P.op("t", lambda E, lhsT=lhsT, rhs=rhs, lc=lc, ps_ap=ps_ap: E.matmul(
                            ps_ap, lhsT, rhs, start=(lc == 0), stop=(lc == 3)),
                            inc=(st_a if lc == 3 else None))
                    return st_a.n

                emit_attn_core(P, aB, asems, scores_mm,
                               lambda _h, kc, h=h: a_cv[:, kc, h * 128:(h + 1) * 128],
                               lambda _h, quad, src, h=h: P.dma(
                                   "g", y_o.ap()[h * 128:(h + 1) * 128,
                                                 quad * 512:(quad + 1) * 512],
                                   src, so_a[0]),
                               FILL_L0, nheads=1)
                P.barrier()
            P.wait("g", so_a[0], so_a[0].n)
            P.wait("g", so_a[1], so_a[1].n)
            gather(y_o.ap(), yg.ap(), 1024, S["cc2"], [[0, 1], [2, 3], [4, 5], [6, 7]])
            P.barrier()

        run_attn0()

        n_rl = P.dma("g", stream[:], spill.ap().rearrange("(cc p) t -> p cc t", p=128), S["sp"])
        ystg = at("ystg", [128, 2, 2, 1024], BF16, AR + 16384)
        war_add = {}

        def run_yadd(ccsem):
            P.wait("g", ccsem, CCP)
            P.wait("g", S["sp"], S["sp"].n)
            for cc in range(NCH):
                par = cc % 2
                r, hcl = divmod(cc, 8)
                src = yg.ap()[r * 1024 + hcl * 128: r * 1024 + (hcl + 1) * 128, :].rearrange(
                    "p (h t) -> p h t", t=T)
                P.wait("g", S["f_v"], war_add.get(par, 0))
                n_l = P.dma("g", ystg[:, par], src, S[f"yio{par}"])
                P.wait("v", S[f"yio{par}"], n_l)
                P.wait("v", S["sp"], S["sp"].n)
                sdst = stream[:, cc, :]
                P.op("v", lambda E, sdst=sdst, par=par: E.scalar_tensor_tensor(
                    out=sdst, in0=ystg[:, par, 0], scalar=cfgm[:, 0:1], in1=sdst,
                    op0=ALU.mult, op1=ALU.add), inc=S["f_v"])
                P.wait("v", S["f_v"], S["f_v"].n)
                war_add[par] = P.op("v", lambda E, sdst=sdst, par=par: E.scalar_tensor_tensor(
                    out=sdst, in0=ystg[:, par, 1], scalar=cfgm[:, 1:2], in1=sdst,
                    op0=ALU.mult, op1=ALU.add), inc=S["f_v"])
            P.barrier()

        if debug:
            P.wait("g", S["cc2"], CCP)
            n_d = P.dma("g", dbg["y0"], y_o.ap(), S["io"])
            P.wait("g", S["io"], n_d)
        run_yadd(S["cc2"])
        dump_stream(1)

        run_ffn()
        dump_stream(2)

        m_xln = ffnB.x_ln
        m_hall = at("m_hall", [128, NE, NIC, T2], BF16, AR + 16384)
        m_wgst = at("m_wgst", [128, 2, NCH, 128], BF16, AR + 106496)
        m_wust = at("m_wust", [128, 2, NCH, 128], BF16, AR + 114688)
        m_wdst = at("m_wdst", [128, 2, NIC, 128], BF16, AR + 122880)
        m_gsil = at("m_gsil", [128, 2, T2], BF16, AR + 122880)
        m_rows = at("m_rows", [1, 4, T2], F32, AR + 106496)
        assert AR + 125696 <= 215808

        moeB = SimpleNamespace(
            stream=stream, T=T, T2=T2, x_ln=m_xln, rows=m_rows,
            xb16=m_hall[:, 0:2].rearrange("p e a t -> p (e a t)")[:, 0:16 * T2].rearrange(
                "p (a t) -> p a t", t=T2),
            xsq16=m_hall[:, 2:4].rearrange("p e a t -> p (e a t)")[:, 0:16 * T2].rearrange(
                "p (a t) -> p a t", t=T2),
            ones_col16=ones_col16, ones_row32=ones_row32,
            ps=[None, None, None, None, None, ps[3]], ps_small=ps_small,
        )
        msems = dict(ln_s=S["ln_s"], ln_v=S["ln_v"], ln_t=S["ln_t"])
        ss_m, sv_m, st_m = S["m_s"], S["m_v"], S["m_t"]

        P.wait("g", S["cA_out0"], S["cA_out0"].n)
        P.wait("g", S["cA_out1"], S["cA_out1"].n)
        n_wr = P.dma("g", wr_sb[:], wl_ap("wr").rearrange("(cc p) e -> p cc e", p=128), S["io"])

        def run_moe():
            P.wait("s", S["io"], n_wr)
            P.wait("v", S["io"], n_wr)
            P.wait("t", S["io"], n_wr)
            pbc_m = ps[3]
            ps_r = ps[0]
            for half in (0, 1):
                t0 = half * T2
                n_ln = emit_ln(P, moeB, half, msems, war=[(st_m, st_m.n), (sv_m, sv_m.n)], dst_t0=0)
                P.wait("t", msems["ln_v"], n_ln)
                for tc in range(4):
                    P.wait("t", sv_m, sv_m.n)
                    P.wait("t", ss_m, ss_m.n)
                    for cc in range(NCH):
                        lhsT = m_xln[:, cc, tc * 128:(tc + 1) * 128]
                        P.op("t", lambda E, lhsT=lhsT, cc=cc: E.matmul(
                            ps_r[:, 0:8], lhsT, wr_sb[:, cc, :],
                            start=(cc == 0), stop=(cc == NCH - 1)),
                            inc=(st_m if cc == NCH - 1 else None))
                    n_lg = st_m.n
                    P.wait("v", st_m, st_m.n)
                    P.wait("v", ss_m, ss_m.n)
                    er0, er2 = er[:, 0, :], er[:, 1, :]
                    msk1, msk2 = er[:, 2, :], er[:, 3, :]
                    m1 = rsc[:, 0:1]; m2 = rsc[:, 1:2]; s12 = rsc[:, 2:3]
                    w1 = rsc[:, 4:5]; w2 = rsc[:, 5:6]; nmx = rsc[:, 6:7]
                    P.op("v", lambda E: E.reduce_max(out=nmx, in_=ps_r[:, 0:8], axis=AX.X, negate=True), inc=sv_m)
                    P.wait("s", sv_m, sv_m.n)
                    P.op("s", lambda E: E.activation(out=er0, in_=ps_r[:, 0:8], func=AF.Exp, bias=nmx), inc=ss_m)
                    P.wait("v", ss_m, ss_m.n)
                    P.op("v", lambda E: E.reduce_max(out=m1, in_=er0, axis=AX.X), inc=sv_m)
                    P.wait("v", sv_m, sv_m.n)
                    P.op("v", lambda E: E.tensor_scalar(out=msk1, in0=er0, scalar1=m1, scalar2=None, op0=ALU.is_equal), inc=sv_m)
                    P.wait("v", sv_m, sv_m.n)
                    P.op("v", lambda E: E.scalar_tensor_tensor(out=er2, in0=msk1, scalar=-4.0, in1=er0, op0=ALU.mult, op1=ALU.add), inc=sv_m)
                    P.wait("v", sv_m, sv_m.n)
                    P.op("v", lambda E: E.reduce_max(out=m2, in_=er2, axis=AX.X), inc=sv_m)
                    P.wait("v", sv_m, sv_m.n)
                    P.op("v", lambda E: E.tensor_scalar(out=msk2, in0=er2, scalar1=m2, scalar2=None, op0=ALU.is_equal), inc=sv_m)
                    P.op("v", lambda E: E.tensor_tensor(out=s12, in0=m1, in1=m2, op=ALU.add), inc=sv_m)
                    P.wait("v", sv_m, sv_m.n)
                    P.op("v", lambda E: E.reciprocal(out=s12, in_=s12), inc=sv_m)
                    P.wait("v", sv_m, sv_m.n)
                    P.op("v", lambda E: E.tensor_tensor(out=w1, in0=m1, in1=s12, op=ALU.mult), inc=sv_m)
                    P.op("v", lambda E: E.tensor_tensor(out=w2, in0=m2, in1=s12, op=ALU.mult), inc=sv_m)
                    P.wait("v", sv_m, sv_m.n)
                    P.op("v", lambda E: E.tensor_scalar(out=msk1, in0=msk1, scalar1=w1, scalar2=None, op0=ALU.mult), inc=sv_m)
                    P.op("v", lambda E: E.tensor_scalar(out=msk2, in0=msk2, scalar1=w2, scalar2=None, op0=ALU.mult), inc=sv_m)
                    P.wait("v", sv_m, sv_m.n)
                    n_cmb = P.op("v", lambda E: E.tensor_tensor(out=comb[:], in0=msk1, in1=msk2, op=ALU.add), inc=sv_m)
                    P.wait("t", sv_m, n_cmb)
                    n_tp = P.op("t", lambda E: E.transpose(ps_r[0:8, 0:128], comb[:], ident32[:]), inc=st_m)
                    P.wait("s", st_m, n_tp)
                    P.wait("s", S["m_ce"], S["m_ce"].n)
                    P.op("s", lambda E, tc=tc: E.copy(
                        out=ce_all[0:8, tc * 128:(tc + 1) * 128], in_=ps_r[0:8, 0:128]), inc=ss_m)
                P.wait("g", ss_m, ss_m.n)
                for e in range(NE):
                    P.dma("g", ce_rows[:, e, :], ce_all[e:e + 1, :], S["m_ce"])
                P.wait("t", S["m_ce"], S["m_ce"].n)
                war_w = {}
                war_h = {}
                P.wait("y", st_m, st_m.n)
                P.wait("y", S["ln_v"], S["ln_v"].n)
                P.wait("y", S["ln_s"], S["ln_s"].n)
                P.wait("y", S["cc0b"], CCV)
                for e in range(NE):
                    P.wait("t", sv_m, sv_m.n)
                    n_bc = P.op("t", lambda E, e=e: E.matmul(
                        pbc_m[:, 0:512], ones_row16[:], ce_rows[:, e, :],
                        start=True, stop=True), inc=st_m)
                    for ic in range(NIC):
                        par = (e * NIC + ic) % 2
                        idx = e * NIC + ic
                        wgsem = S[f"m_w{par}"]
                        if idx >= 2:
                            P.wait("y", st_m, war_w[idx - 2])
                        P.dma("y", m_wgst[:, par], gath_moe(e, 0)[:, ic * 128:(ic + 1) * 128].rearrange(
                            "(cc p) i -> p cc i", p=128), wgsem)
                        n_w = P.dma("y", m_wust[:, par], gath_moe(e, 1)[:, ic * 128:(ic + 1) * 128].rearrange(
                            "(cc p) i -> p cc i", p=128), wgsem)
                        P.wait("t", wgsem, n_w)
                        pg, pu = (ps[0], ps[2]) if idx % 2 == 0 else (ps[1], ps[4])
                        if idx >= 2:
                            P.wait("t", ss_m, war_h.get((idx - 2, "s"), 0))
                            P.wait("t", sv_m, war_h.get((idx - 2, "v"), 0))
                        for cc in range(NCH):
                            P.op("t", lambda E, cc=cc, par=par, pg=pg: E.matmul(
                                pg[:, 0:512], m_wgst[:, par, cc, :], m_xln[:, cc, 0:T2],
                                start=(cc == 0), stop=(cc == NCH - 1)))
                        for cc in range(NCH):
                            P.op("t", lambda E, cc=cc, par=par, pu=pu: E.matmul(
                                pu[:, 0:512], m_wust[:, par, cc, :], m_xln[:, cc, 0:T2],
                                start=(cc == 0), stop=(cc == NCH - 1)),
                                inc=(st_m if cc == NCH - 1 else None))
                        n_mm = st_m.n
                        war_w[idx] = n_mm
                        P.wait("s", st_m, n_mm)
                        if idx >= 2:
                            P.wait("s", sv_m, war_h.get((idx - 2, "v"), 0))
                        gpar = idx % 2
                        n_sil = P.op("s", lambda E, gpar=gpar, pg=pg: E.activation(
                            out=m_gsil[:, gpar, :], in_=pg[:, 0:512], func=AF.Silu), inc=ss_m)
                        war_h[(idx, "s")] = n_sil
                        P.wait("v", ss_m, n_sil)
                        P.wait("v", st_m, n_mm)
                        dst = m_hall[:, e, ic, :]
                        P.op("v", lambda E, dst=dst, gpar=gpar, pu=pu: E.tensor_tensor(
                            out=dst, in0=m_gsil[:, gpar, :], in1=pu[:, 0:512], op=ALU.mult), inc=sv_m)
                        P.wait("v", sv_m, sv_m.n)
                        war_h[(idx, "v")] = P.op("v", lambda E, dst=dst: E.tensor_tensor(
                            out=dst, in0=dst, in1=pbc_m[:, 0:512], op=ALU.mult), inc=sv_m)
                P.barrier()
                war_d = {}
                war_de = {}
                for dc in range(NCH):
                    pd = ps[dc % 2]
                    if dc >= 2:
                        P.wait("t", sv_m, war_de[dc - 2])
                    for e in range(NE):
                        idx = dc * NE + e
                        wpar = idx % 2
                        wdsem = S["m_wd"] if wpar == 0 else S["m_wd1"]
                        if idx >= 2:
                            P.wait("y", st_m, war_d[idx - 2])
                        n_w = P.dma("y", m_wdst[:, wpar], gath_moe(e, 2)[:, dc * 128:(dc + 1) * 128].rearrange(
                            "(ic p) d -> p ic d", p=128), wdsem)
                        P.wait("t", wdsem, n_w)
                        for ic in range(NIC):
                            P.op("t", lambda E, e=e, ic=ic, pd=pd, wpar=wpar: E.matmul(
                                pd[:, 0:512], m_wdst[:, wpar, ic, :], m_hall[:, e, ic, :],
                                start=(e == 0 and ic == 0), stop=(e == NE - 1 and ic == NIC - 1)),
                                inc=(st_m if ic == NIC - 1 else None))
                        war_d[idx] = st_m.n
                    n_dmm = st_m.n
                    P.wait("v", st_m, n_dmm)
                    sdst = stream[:, dc, t0:t0 + T2]
                    war_de[dc] = P.op("v", lambda E, sdst=sdst, pd=pd: E.scalar_tensor_tensor(
                        out=sdst, in0=pd[:, 0:512], scalar=0.5, in1=sdst,
                        op0=ALU.mult, op1=ALU.add), inc=sv_m)
                P.barrier()

        run_moe()
        dump_stream(3)

        run_xln(S["cc3"])
        n_sp2 = P.dma("g", spill.ap().rearrange("(cc p) t -> p cc t", p=128), stream[:], S["sp"])

        def run_attn1():
            mk_dmask(a_dm1, FILL_L1)
            aB["dmask"] = a_dm1
            for e in ("y", "g", "s", "t", "v"):
                P.wait(e, S["sp"], S["sp"].n)
            P.wait("y", S["cc3"], CCP)
            P.barrier()
            proj_pass("wqT1_s", 1024,
                      lambda oc, tw: a_q[:, oc, tw * 512:(tw + 1) * 512])
            proj_pass("wkT_s", 256,
                      lambda oc, tw: a_kf[:, oc, tw * 512:(tw + 1) * 512])
            proj_pass("wvT_s", 256,
                      lambda oc, tw: a_vf[:, oc, tw * 512:(tw + 1) * 512])
            for kv in range(2):
                for kc in range(16):
                    tpar = kc % 2
                    src = a_vf[:, kv, kc * 128:(kc + 1) * 128]
                    P.wait("t", ss_a, ss_a.n)
                    n_tp = P.op("t", lambda E, tpar=tpar, src=src: E.transpose(
                        pt[tpar][:, 0:128], src, ident16[:]), inc=st_a)
                    P.wait("s", st_a, n_tp)
                    dst = a_vtm[:, kc, kv * 128:(kv + 1) * 128]
                    P.op("s", lambda E, dst=dst, tpar=tpar: E.copy(
                        out=dst, in_=pt[tpar][:, 0:128]), inc=ss_a)
            P.barrier()
            for h in range(NH_LOC):
                def scores_mm(_h, qt, w, ps_ap, h=h):
                    lhsT = a_q[:, h, qt * 128:(qt + 1) * 128]
                    rhs = a_kf[:, h // 4, w * 512:(w + 1) * 512]
                    P.op("t", lambda E, lhsT=lhsT, rhs=rhs, ps_ap=ps_ap: E.matmul(
                        ps_ap, lhsT, rhs, start=True, stop=True), inc=st_a)
                    return st_a.n

                emit_attn_core(P, aB, asems, scores_mm,
                               lambda _h, kc, h=h: a_vtm[:, kc, (h // 4) * 128:(h // 4 + 1) * 128],
                               lambda _h, quad, src, h=h: P.dma(
                                   "g", y_o.ap()[h * 128:(h + 1) * 128,
                                                 quad * 512:(quad + 1) * 512],
                                   src, so_a[0]),
                               FILL_L1, nheads=1)
                P.barrier()
            P.wait("g", so_a[0], so_a[0].n)
            P.wait("g", so_a[1], so_a[1].n)
            gather(y_o.ap(), yg.ap(), 1024, S["cc4"], [[0, 1], [2, 3], [4, 5], [6, 7]])
            P.barrier()

        run_attn1()

        P.dma("g", stream[:], spill.ap().rearrange("(cc p) t -> p cc t", p=128), S["sp"])
        o_ao = at("o_ao", [128, NCH, T], BF16, AR)
        o_wst = at("o_wst", [128, 2, NCH, 128], BF16, AR + 32768)
        o_stg = at("o_stg", [128, 2, 2, 1024], BF16, AR + 40960)

        def run_outproj():
            P.wait("g", S["cc4"], CCP)
            P.wait("g", S["sp"], S["sp"].n)
            war_st = {}
            for cc in range(NCH):
                par = cc % 2
                r, hcl = divmod(cc, 8)
                src = yg.ap()[r * 1024 + hcl * 128: r * 1024 + (hcl + 1) * 128, :].rearrange(
                    "p (h t) -> p h t", t=T)
                P.wait("g", sv_a, war_st.get(par, 0))
                n_l = P.dma("g", o_stg[:, par], src, S[f"oio{par}"])
                P.wait("v", S[f"oio{par}"], n_l)
                dst = o_ao[:, cc, :]
                P.op("v", lambda E, dst=dst, par=par: E.tensor_scalar(
                    out=dst, in0=o_stg[:, par, 0], scalar1=cfgm[:, 0:1], scalar2=None,
                    op0=ALU.mult), inc=sv_a)
                P.wait("v", sv_a, sv_a.n)
                war_st[par] = P.op("v", lambda E, dst=dst, par=par: E.scalar_tensor_tensor(
                    out=dst, in0=o_stg[:, par, 1], scalar=cfgm[:, 1:2], in1=dst,
                    op0=ALU.mult, op1=ALU.add), inc=sv_a)
            P.barrier()
            wo_r = wl_ap("woT1").rearrange("(dc p) (cc c) -> cc p dc c", p=128, c=128)
            war_w = {}
            war_ev = {}
            for cc in range(NCH):
                par = cc % 2
                if cc >= 2:
                    P.wait("y", st_a, war_w[cc - 2])
                n_w = P.dma("y", o_wst[:, par], wo_r[cc], S[f"o_w{par}"])
                P.wait("t", S[f"o_w{par}"], n_w)
                for tw in range(2):
                    pp = ps[tw % 2]
                    if cc >= 1:
                        P.wait("t", sv_a, war_ev[(cc - 1, tw)])
                    for dc in range(NCH):
                        lhsT = o_wst[:, par, dc, :]
                        rhs = o_ao[:, dc, tw * 512:(tw + 1) * 512]
                        P.op("t", lambda E, lhsT=lhsT, rhs=rhs, dc=dc, pp=pp: E.matmul(
                            pp[:, 0:512], lhsT, rhs, start=(dc == 0), stop=(dc == NCH - 1)),
                            inc=(st_a if dc == NCH - 1 else None))
                    n_mm = st_a.n
                    war_w[cc] = n_mm
                    P.wait("v", st_a, n_mm)
                    P.wait("v", S["sp"], S["sp"].n)
                    sdst = stream[:, cc, tw * 512:(tw + 1) * 512]
                    bslice = bo_sb[:, cc:cc + 1]
                    war_ev[(cc, tw)] = P.op("v", lambda E, sdst=sdst, pp=pp, bslice=bslice: E.scalar_tensor_tensor(
                        out=sdst, in0=pp[:, 0:512], scalar=bslice, in1=sdst,
                        op0=ALU.add, op1=ALU.add), inc=sv_a)
            P.barrier()

        run_outproj()
        dump_stream(4)

        run_moe()

        P.wait("g", S["m_v"], S["m_v"].n)
        n_out = P.dma("g", out_t.rearrange("(cc p) t -> p cc t", p=128), stream[:], S["io"])
        P.wait("g", S["io"], n_out)

        with nc.Block() as block:
            P.finalize(block)
    return nc


try:
    from prog_fused import WL_PAD, MOE_PAD, build_fused
except ImportError:
    pass
from concourse.bass_utils import run_bass_kernel_spmd

_fused_nc = [None]
_last_results = [None]


def kernel(**inputs):
    f = np.float32
    x = np.asarray(inputs["L0_attn_x"], dtype=f)
    Wq0 = np.asarray(inputs["L0_attn_Wq_w"], f)
    Wdkv = np.asarray(inputs["L0_attn_Wdkv_w"], f)
    Wuk = np.asarray(inputs["L0_attn_Wuk_w"], f)
    Wuv = np.asarray(inputs["L0_attn_Wuv_w"], f)
    Wo0 = np.asarray(inputs["L0_attn_Wo_w"], f)
    Wq1 = np.asarray(inputs["L1_attn_Wq"], f)
    Wk1 = np.asarray(inputs["L1_attn_Wk"], f)
    Wv1 = np.asarray(inputs["L1_attn_Wv"], f)
    Wo1 = np.asarray(inputs["L1_attn_Wo"], f)
    bo = np.asarray(inputs["L1_attn_bo"], f)
    Wr = np.asarray(inputs["L1_ffn_Wr"], f)
    Wg1 = np.asarray(inputs["L1_ffn_Wg"], f)
    Wu1 = np.asarray(inputs["L1_ffn_Wu"], f)
    Wd1 = np.asarray(inputs["L1_ffn_Wd"], f)
    gate_p = np.asarray(inputs["L0_ffn_gate_W_packed"], np.int32)
    gate_s = np.asarray(inputs["L0_ffn_gate_scales"], f)
    up_p = np.asarray(inputs["L0_ffn_up_W_packed"], np.int32)
    up_s = np.asarray(inputs["L0_ffn_up_scales"], f)
    down_p = np.asarray(inputs["L0_ffn_down_W_packed"], np.int32)
    down_s = np.asarray(inputs["L0_ffn_down_scales"], f)

    Wq0T = np.ascontiguousarray(Wq0.T)
    Wo0T = np.ascontiguousarray(Wo0.T)
    WdkvT = np.ascontiguousarray(Wdkv.T)
    Wq1T = np.ascontiguousarray(Wq1.T)
    Wk1T = np.ascontiguousarray(Wk1.T)
    Wv1T = np.ascontiguousarray(Wv1.T)
    Wo1T = np.ascontiguousarray(Wo1.T)

    NLOC = 16384 // 8
    in_maps = []
    for c in range(8):
        b, half = divmod(c, 2)
        hb = half * 8
        kvb = half * 2
        wl_parts = [
            Wq0[:, hb * 128:(hb + 8) * 128],
            Wq0T[:, hb * 128:(hb + 8) * 128],
            Wo0T[:, hb * 128:(hb + 8) * 128],
            Wuk, Wuv, WdkvT,
            Wq1T[:, hb * 128:(hb + 8) * 128],
            Wk1T[:, kvb * 128:(kvb + 2) * 128],
            Wv1T[:, kvb * 128:(kvb + 2) * 128],
            Wo1T, Wr,
        ]
        wl_flat = np.zeros(WL_PAD, f)
        o = 0
        for p_ in wl_parts:
            wl_flat[o:o + p_.size] = np.ascontiguousarray(p_).ravel()
            o += p_.size
        moe_flat = np.zeros(MOE_PAD, f)
        o = 0
        for p_ in (Wg1[c], Wu1[c], Wd1[c]):
            moe_flat[o:o + p_.size] = p_.ravel()
            o += p_.size
        cfg = np.zeros((128, 2), f)
        cfg[:, 0] = 1.0 if half == 0 else 0.0
        cfg[:, 1] = 1.0 - cfg[:, 0]
        in_maps.append(dict(
            x_in=np.ascontiguousarray(x[b, half * 1024:(half + 1) * 1024, :].T),
            wl_f32=wl_flat.reshape(WL_PAD // 512, 512),
            moe_f32=moe_flat.reshape(MOE_PAD // 512, 512),
            gate_p=gate_p[c * NLOC:(c + 1) * NLOC], gate_s=gate_s[c * NLOC:(c + 1) * NLOC],
            up_p=up_p[c * NLOC:(c + 1) * NLOC], up_s=up_s[c * NLOC:(c + 1) * NLOC],
            down_p=down_p[c * 256:(c + 1) * 256], down_s=down_s[c * 256:(c + 1) * 256],
            bo_in=bo, cfg_in=cfg,
        ))

    if _fused_nc[0] is None:
        _fused_nc[0] = build_fused()
    res = run_bass_kernel_spmd(_fused_nc[0], in_maps, core_ids=list(range(8)))
    _last_results[0] = res.results
    out = np.empty((4, 2048, 2048), f)
    for c in range(8):
        b, half = divmod(c, 2)
        out[b, half * 1024:(half + 1) * 1024, :] = np.asarray(res.results[c]["out_t"]).T
    return out


# version 06
# speedup vs baseline: 1.0837x; 1.0837x over previous
import numpy as np
from contextlib import ExitStack
import concourse.bass as bass
import concourse.mybir as mybir

F32 = mybir.dt.float32
BF16 = mybir.dt.bfloat16
I32 = mybir.dt.int32
AF = mybir.ActivationFunctionType
ALU = mybir.AluOpType
AX = mybir.AxisListType

ENGINES = ("y", "v", "s", "t", "g")
ENGINE_ATTR = {"y": "sync", "v": "vector", "s": "scalar", "t": "tensor", "g": "gpsimd"}

C = 2048
NCH = 16
GELU_K = 0.7978845608028654


class Sem:
    __slots__ = ("h", "n", "name")
    def __init__(self, h, name):
        self.h = h
        self.n = 0
        self.name = name
    def bump(self, k=1):
        self.n += k
        return self.n


class Prog:
    def __init__(self, nc, es):
        self.nc = nc
        self.es = es
        self.plan = {e: [] for e in ENGINES}
        self.sems = {}
        self._barn = 0
        self.last_inc = {e: None for e in ENGINES}
        self.dirty = {e: False for e in ENGINES}
        self._dummy = None

    def sem(self, name):
        s = Sem(self.es.enter_context(self.nc.semaphore(name)), name)
        self.sems[name] = s
        return s

    def const_f32(self, *values):
        if not hasattr(self, "_const_sem"):
            self._const_sem = self.sem("constf")
        for v in values:
            t = self.es.enter_context(self.nc.sbuf_tensor(f"cf_{str(v).replace('.','_').replace('-','m')}", [128, 1], F32))
            self.op("v", lambda E, t=t, v=v: E.memset(t[:], float(v)), inc=self._const_sem)
            self.nc.const_aps.register(F32, float(v), t[:])
        return self._const_sem

    def op(self, eng, fn, inc=None, k=1):
        if inc is None:
            self.plan[eng].append(fn)
            self.dirty[eng] = True
            return None
        n = inc.bump(k)
        self.plan[eng].append(lambda E, fn=fn: fn(E).then_inc(inc.h, k))
        self.last_inc[eng] = (inc, n)
        self.dirty[eng] = False
        return n

    def wait(self, eng, sem, n):
        if n > 0:
            self.plan[eng].append(lambda E: E.wait_ge(sem.h, n))

    def coll(self, fn, sem):
        n = sem.bump(1)
        self.plan["g"].append(lambda E, fn=fn: fn(E).then_inc(sem.h))
        self.last_inc["g"] = (sem, n)
        self.dirty["g"] = False
        return n

    def dma(self, eng, out, in_, sem, **kw):
        n = sem.bump(16)
        self.plan[eng].append(
            lambda E: E.dma_start(out=out, in_=in_, **kw).then_inc(sem.h, 16))
        self.last_inc[eng] = (sem, n)
        self.dirty[eng] = False
        return n

    def barrier(self, engs=("v", "s", "t", "g")):
        if self._dummy is None:
            self._dummy = self.es.enter_context(
                self.nc.sbuf_tensor("bar_dummy", [128, 8], F32))
            self._bars = {e: self.sem(f"bar_{e}") for e in ENGINES}
        d = self._dummy
        for e in engs:
            if self.dirty[e]:
                s = self._bars[e]
                if e == "v":
                    self.op("v", lambda E: E.memset(d[:, 0:2], 0.0), inc=s)
                elif e == "s":
                    self.op("s", lambda E: E.mul(out=d[:, 2:4], in_=d[:, 2:4], mul=1.0), inc=s)
                elif e == "g":
                    self.op("g", lambda E: E.memset(d[:, 4:6], 0.0), inc=s)
                else:
                    raise AssertionError(f"dirty engine {e} needs a real inc'd op before barrier")
        self._barn += 1
        for e in tuple(engs) + ("y",):
            for o in engs:
                if o != e and self.last_inc[o] is not None:
                    sem, n = self.last_inc[o]
                    self.wait(e, sem, n)

    def finalize(self, block):
        for e in ENGINES:
            fns = self.plan[e]
            if not fns:
                continue
            def runner(E, fns=fns):
                for f in fns:
                    f(E)
            getattr(block, ENGINE_ATTR[e])(runner)


class Bufs:
    def __init__(self, nc, es, T_CORE, want=("ffn",)):
        sb = lambda name, shape, dt=F32: es.enter_context(nc.sbuf_tensor(name, shape, dt))
        ps = lambda name, shape, dt=F32: es.enter_context(nc.psum_tensor(name, shape, dt))
        T = T_CORE
        self.T = T
        self.T2 = T // 2
        self.stream = sb("stream", [128, NCH, T])
        self.x_ln = sb("x_ln", [128, NCH, T], BF16)
        self.big_a = sb("big_a", [128, NCH, T], BF16)
        self.rows = sb("rows", [1, 4, self.T2])
        self.ones_col16 = sb("ones_col16", [128, 1], BF16)
        self.ones_row32 = sb("ones_row32", [1, 128])
        self.ident16 = sb("ident16", [128, 128], BF16)
        self.xb16 = self.big_a[:, :, 0:self.T2]
        self.xsq16 = self.big_a[:, :, self.T2:2 * self.T2]
        if "ffn" in want:
            self.wg_buf = sb("wg_buf", [128, 2, NCH, 128], BF16)
            self.wu_buf = sb("wu_buf", [128, 2, NCH, 128], BF16)
            self.wd_buf = sb("wd_buf", [128, 2, NCH, 128], BF16)
            self.gelu_sb = sb("gelu_sb", [128, 2, self.T2], BF16)
            self.ds_acc = sb("ds_acc", [128, NCH, self.T2])
        self.ps = [ps(f"ps{i}", [128, 512]) for i in range(6)]
        self.ps_small = ps("ps_small", [1, 512])
        self.pt = ps("pt", [128, 1024], BF16)


def emit_consts(P, B):
    sg = P.sem("const_g")
    sv = P.sem("const_v")
    P.op("v", lambda E: E.memset(B.ones_col16[:], 1.0))
    P.op("v", lambda E: E.memset(B.ones_row32[:], 1.0), inc=sv)
    P.op("g", lambda E: E.memset(B.ident16[:], 0.0), inc=sg)
    P.wait("g", sg, 1)
    P.op("g", lambda E: E.affine_select(
        out=B.ident16[:], in_=B.ident16[:], compare_op=ALU.not_equal,
        fill=1.0, base=0, pattern=[[-1, 128]], channel_multiplier=1), inc=sg)
    P.wait("t", sv, 1)
    P.wait("t", sg, 2)
    return sg, sv


def emit_ln(P, B, half, sems, war=(), dst_t0=None):
    T2 = B.T2
    t0 = half * T2
    dt0 = t0 if dst_t0 is None else dst_t0
    ss, sv, st = sems["ln_s"], sems["ln_v"], sems["ln_t"]
    src = B.stream[:, :, t0:t0 + T2]
    dst = B.x_ln[:, :, dt0:dt0 + T2]
    negmean = B.rows[:, 0, :]
    msq = B.rows[:, 1, :]
    var = B.rows[:, 2, :]
    rstd = B.rows[:, 3, :]
    lnv = B.rows[:, 1, :]
    pbc = B.ps[5][:, 0:T2]
    bc = pbc.unsqueeze(1).broadcast_to([128, NCH, T2])

    P.wait("s", st, st.n)
    P.wait("t", sv, sv.n)
    for (sem, n) in war:
        P.wait("s", sem, n)
        P.wait("v", sem, n)

    n_cast = P.op("s", lambda E: E.copy(out=B.xb16, in_=src), inc=ss)
    n_sq = P.op("s", lambda E: E.activation(out=B.xsq16, in_=src, func=AF.Square), inc=ss)

    P.wait("t", ss, n_cast)
    for i in range(NCH):
        rhs = B.xb16[:, i, :]
        P.op("t", lambda E, rhs=rhs, i=i: E.matmul(
            B.ps_small[:, 0:T2], B.ones_col16[:], rhs,
            start=(i == 0), stop=(i == NCH - 1)), inc=(st if i == NCH - 1 else None))
    n_sums = st.n
    P.wait("v", st, n_sums)
    n_nm = P.op("v", lambda E: E.tensor_single_scalar(
        out=negmean, in_=B.ps_small[:, 0:T2], scalar=-1.0 / C, op=ALU.mult), inc=sv)
    P.wait("t", sv, n_nm)
    P.wait("t", ss, n_sq)
    for i in range(NCH):
        rhs = B.xsq16[:, i, :]
        P.op("t", lambda E, rhs=rhs, i=i: E.matmul(
            B.ps_small[:, 0:T2], B.ones_col16[:], rhs,
            start=(i == 0), stop=(i == NCH - 1)), inc=(st if i == NCH - 1 else None))
    n_sumsq = st.n
    P.wait("v", st, n_sumsq)
    P.op("v", lambda E: E.tensor_scalar(
        out=msq, in0=B.ps_small[:, 0:T2], scalar1=1.0 / C, scalar2=1e-5,
        op0=ALU.mult, op1=ALU.add), inc=sv)
    P.wait("v", sv, sv.n)
    P.op("v", lambda E: E.tensor_tensor(out=var, in0=negmean, in1=negmean, op=ALU.mult), inc=sv)
    P.wait("v", sv, sv.n)
    n_var = P.op("v", lambda E: E.tensor_sub(out=var, in0=msq, in1=var), inc=sv)
    P.wait("s", sv, n_var)
    P.op("s", lambda E: E.activation(out=lnv, in_=var, func=AF.Ln), inc=ss)
    P.wait("s", ss, ss.n)
    n_rstd = P.op("s", lambda E: E.activation(out=rstd, in_=lnv, func=AF.Exp, scale=-0.5), inc=ss)
    P.wait("t", sv, n_nm)
    n_bc_m = P.op("t", lambda E: E.matmul(
        pbc, B.ones_row32[:], negmean, start=True, stop=True), inc=st)
    P.wait("v", st, n_bc_m)
    n_xm = P.op("v", lambda E: E.tensor_tensor(out=dst, in0=src, in1=bc, op=ALU.add), inc=sv)
    P.wait("t", sv, n_xm)
    P.wait("t", ss, n_rstd)
    n_bc_r = P.op("t", lambda E: E.matmul(
        pbc, B.ones_row32[:], rstd, start=True, stop=True), inc=st)
    P.wait("v", st, n_bc_r)
    n_out = P.op("v", lambda E: E.tensor_tensor(out=dst, in0=dst, in1=bc, op=ALU.mult), inc=sv)
    return n_out


def emit_ffn0(P, B, w, sems, n_mlp=16384, xln_half=False):
    NC = n_mlp // 128
    NGRP = NC // NCH
    T2 = B.T2
    ss, sv, st = sems["f_s"], sems["f_v"], sems["f_t"]
    wsem = sems["f_w"]
    dsem = sems["f_d"]
    lnv = sems["ln_v"]

    for half in (0, 1):
        n_ln = emit_ln(P, B, half, sems, war=[(st, st.n)],
                       dst_t0=(0 if xln_half else None))
        P.wait("t", lnv, n_ln)
        t0 = half * T2
        x0 = 0 if xln_half else t0
        w_ready = {}
        mm_done = {}
        gel_done = {}
        ev_done = {}
        dmm_done = {}
        dev_done = {}

        def load_w(jg):
            if "wg_gate" in sems:
                sems["wg_gate"](jg)
            P.dma("y", B.wg_buf[:, jg % 2], w["wg_t"](jg), wsem[jg % 2])
            w_ready[jg] = P.dma("y", B.wu_buf[:, jg % 2], w["wu_t"](jg), wsem[jg % 2])

        for g in range(NGRP):
            for j in range(NCH):
                jg = g * NCH + j
                par = jg % 2
                if jg == 0:
                    load_w(0)
                    if NC > 1:
                        load_w(1)
                psg, psu = B.ps[0 + par][:, 0:T2], B.ps[2 + par][:, 0:T2]
                P.wait("t", wsem[jg % 2], w_ready[jg])
                if jg >= 2:
                    P.wait("t", sv, ev_done[jg - 2])
                    P.wait("t", ss, gel_done[jg - 2])
                for cc in range(NCH):
                    lhsT = B.wg_buf[:, par, cc, :]
                    rhs = B.x_ln[:, cc, x0:x0 + T2]
                    P.op("t", lambda E, lhsT=lhsT, rhs=rhs, cc=cc, psg=psg: E.matmul(
                        psg, lhsT, rhs, start=(cc == 0), stop=(cc == NCH - 1)))
                for cc in range(NCH):
                    lhsT = B.wu_buf[:, par, cc, :]
                    rhs = B.x_ln[:, cc, x0:x0 + T2]
                    P.op("t", lambda E, lhsT=lhsT, rhs=rhs, cc=cc, psu=psu: E.matmul(
                        psu, lhsT, rhs, start=(cc == 0), stop=(cc == NCH - 1)),
                        inc=(st if cc == NCH - 1 else None))
                mm_done[jg] = st.n
                if jg + 2 < NC:
                    P.wait("y", st, mm_done[jg])
                    load_w(jg + 2)
                P.wait("s", st, mm_done[jg])
                if jg >= 2:
                    P.wait("s", sv, ev_done[jg - 2])
                gdst = B.gelu_sb[:, par, :]
                gel_done[jg] = P.op("s", lambda E, gdst=gdst, psg=psg: E.activation(
                    out=gdst, in_=psg, func=AF.Gelu_apprx_tanh), inc=ss)
                P.wait("v", ss, gel_done[jg])
                P.wait("v", st, mm_done[jg])
                if g > 0 and j == 0:
                    P.wait("v", st, dmm_done[NCH - 1])
                idst = B.big_a[:, j, t0:t0 + T2]
                ev_done[jg] = P.op("v", lambda E, idst=idst, gdst=gdst, psu=psu: E.tensor_tensor(
                    out=idst, in0=gdst, in1=psu, op=ALU.mult), inc=sv)
            if "wd_gate" in sems:
                gsem, gn = sems.pop("wd_gate")
                P.wait("y", gsem, gn)
            n_grp_ev = sv.n
            for dc in range(NCH):
                par = dc % 2
                psd = B.ps[4][:, 0:T2]
                if dc >= 2:
                    P.wait("y", st, dmm_done[dc - 2])
                ndw = P.dma("y", B.wd_buf[:, par], w["wd_t"](dc, g), dsem[par])
                P.wait("t", dsem[par], ndw)
                P.wait("t", sv, n_grp_ev)
                if dc >= 1:
                    P.wait("t", sv, dev_done[dc - 1])
                for j in range(NCH):
                    lhsT = B.wd_buf[:, par, j, :]
                    rhs = B.big_a[:, j, t0:t0 + T2]
                    P.op("t", lambda E, lhsT=lhsT, rhs=rhs, j=j, psd=psd: E.matmul(
                        psd, lhsT, rhs, start=(j == 0), stop=(j == NCH - 1)),
                        inc=(st if j == NCH - 1 else None))
                dmm_done[dc] = st.n
                P.wait("v", st, dmm_done[dc])
                ddst = B.ds_acc[:, dc, :]
                if g == 0:
                    dev_done[dc] = P.op("v", lambda E, ddst=ddst, psd=psd: E.tensor_copy(
                        out=ddst, in_=psd), inc=sv)
                else:
                    dev_done[dc] = P.op("v", lambda E, ddst=ddst, psd=psd: E.tensor_tensor(
                        out=ddst, in0=ddst, in1=psd, op=ALU.add), inc=sv)
        P.wait("v", sv, sv.n)
        sdst = B.stream[:, :, t0:t0 + T2]
        P.op("v", lambda E, sdst=sdst: E.scalar_tensor_tensor(
            out=sdst, in0=B.ds_acc[:], scalar=0.5, in1=sdst,
            op0=ALU.mult, op1=ALU.add), inc=sv)
    return sv.n


def make_ffn0_sems(P):
    return {
        "ln_s": P.sem("ln_s"), "ln_v": P.sem("ln_v"), "ln_t": P.sem("ln_t"),
        "f_s": P.sem("f_s"), "f_v": P.sem("f_v"), "f_t": P.sem("f_t"),
        "f_w": [P.sem("f_w0"), P.sem("f_w1")],
        "f_d": [P.sem("f_d0"), P.sem("f_d1")],
    }


"""attn0a: MLA attention, head-sharded. Per core: 8 heads x full 2048-token causal.
in: xln_all [2, 2048, 1024] bf16 ([t-half][c][t_local]); weight slices bf16.
out: y_mine [1024, 2048] bf16 (my 8 head-chunks' output rows, all tokens)."""

FILL_L0 = -1e30

NH_LOC = 8
NQT = 16
TFULL = 2048
RSQ = 1.0 / np.sqrt(128.0)


def emit_attn_core(P, B, sems, scores_mm, av_lhsT, out_dma, fill, nheads=NH_LOC):
    ss, sv, st, sg = sems["a_s"], sems["a_v"], sems["a_t"], sems["a_g"]
    so = sems["a_o"]
    scores = B["scores"]
    attn_b = B["attn_b"]
    attnT = B["attnT"]
    rowm = B["rowm"]
    pt = B["pt"]
    sc_ps = B["sc_ps"]
    y_ps = B["y_ps"]
    ystage = B["ystage"]
    dmask = B["dmask"]
    war_pt = B.setdefault("_war_pt", {})
    war_sc = B.setdefault("_war_sc", {})
    war_y = B.setdefault("_war_y", {})
    for h in range(nheads):
        for quad in range(4):
            nw = quad + 1
            nk = nw * 512
            for j in range(4):
                qt = quad * 4 + j
                P.wait("s", sv, sv.n)
                for w in range(nw):
                    par = w % 2
                    ps = sc_ps[par]
                    we, wn = war_sc.get(par, (None, 0))
                    if we is not None:
                        P.wait("t", we, wn)
                    n_mm = scores_mm(h, qt, w, ps[:, 0:512])
                    dst = scores[:, w * 512:(w + 1) * 512]
                    if w == quad:
                        P.wait("v", st, n_mm)
                        msk = dmask[:, qt % 4, :]
                        n_e = P.op("v", lambda E, dst=dst, ps=ps, msk=msk: E.tensor_tensor(
                            out=dst, in0=ps[:, 0:512], in1=msk, op=ALU.add), inc=sv)
                        war_sc[par] = (sv, n_e)
                    else:
                        P.wait("s", st, n_mm)
                        n_e = P.op("s", lambda E, dst=dst, ps=ps: E.copy(
                            out=dst, in_=ps[:, 0:512]), inc=ss)
                        war_sc[par] = (ss, n_e)
                P.wait("v", sv, sv.n)
                P.wait("v", ss, ss.n)
                n_max = P.op("v", lambda E, nk=nk: E.reduce_max(
                    out=rowm[:, 0:1], in_=scores[:, 0:nk], axis=AX.X, negate=True), inc=sv)
                P.wait("s", sv, n_max)
                P.wait("s", st, st.n)
                n_exp = P.op("s", lambda E, nk=nk: E.activation(
                    out=attn_b[:, 0:nk], in_=scores[:, 0:nk], func=AF.Exp,
                    bias=rowm[:, 0:1], accum_out=rowm[:, 1:2]), inc=ss)
                P.wait("v", ss, n_exp)
                n_rec = P.op("v", lambda E: E.reciprocal(out=rowm[:, 1:2], in_=rowm[:, 1:2]), inc=sv)
                P.wait("v", sv, n_rec)
                n_nrm = P.op("v", lambda E, nk=nk: E.tensor_scalar_mul(
                    out=attn_b[:, 0:nk], in0=attn_b[:, 0:nk], scalar1=rowm[:, 1:2]), inc=sv)
                for kc in range(nw * 4):
                    tpar = kc % 2
                    src = attn_b[:, kc * 128:(kc + 1) * 128]
                    P.wait("t", sv, n_nrm)
                    P.wait("t", ss, war_pt.get(tpar, 0))
                    n_tp = P.op("t", lambda E, tpar=tpar, src=src: E.transpose(
                        pt[tpar][:, 0:128], src, B["ident16"][:]), inc=st)
                    P.wait("s", st, n_tp)
                    dstT = attnT[:, kc, j * 128:(j + 1) * 128]
                    war_pt[tpar] = P.op("s", lambda E, dstT=dstT, tpar=tpar: E.copy(
                        out=dstT, in_=pt[tpar][:, 0:128]), inc=ss)
            ypar = (quad % 2) if not B.get("y_single") else 0
            yp = y_ps[ypar]
            P.wait("t", ss, ss.n)
            P.wait("t", so[ypar], war_y.get(ypar, 0))
            for kc in range(nw * 4):
                lhsT = av_lhsT(h, kc)
                rhs = attnT[:, kc, :]
                P.op("t", lambda E, lhsT=lhsT, rhs=rhs, kc=kc, yp=yp, nw=nw: E.matmul(
                    yp[:, 0:512], lhsT, rhs, start=(kc == 0), stop=(kc == nw * 4 - 1)),
                    inc=(st if kc == nw * 4 - 1 else None))
            n_av = st.n
            P.wait("s", st, n_av)
            P.wait("s", so[ypar], war_y.get(ypar, 0))
            n_ye = P.op("s", lambda E, ypar=ypar, yp=yp: E.copy(
                out=ystage[:, ypar, :], in_=yp[:, 0:512]), inc=ss)
            P.wait("g", ss, n_ye)
            war_y[ypar] = out_dma(h, quad, ystage[:, ypar, :])
    return


def build_attn0(T_CORE=1024):
    TF = 2 * T_CORE
    nc = bass.Bass("TRN2", target_bir_lowering=False, debug=False)
    xln_all = nc.dram_tensor("xln_all", [2, 2048, T_CORE], BF16, kind="ExternalInput").ap()
    wq_nat_s = nc.dram_tensor("wq_nat_s", [2048, 1024], BF16, kind="ExternalInput").ap()
    wuk = nc.dram_tensor("wuk", [2048, 512], BF16, kind="ExternalInput").ap()
    wqT_s = nc.dram_tensor("wqT_s", [2048, 1024], BF16, kind="ExternalInput").ap()
    wuv = nc.dram_tensor("wuv", [2048, 512], BF16, kind="ExternalInput").ap()
    woT_s = nc.dram_tensor("woT_s", [2048, 1024], BF16, kind="ExternalInput").ap()
    wdkvT = nc.dram_tensor("wdkvT", [2048, 512], BF16, kind="ExternalInput").ap()
    y_mine = nc.dram_tensor("y_mine", [1024, TF], BF16, kind="ExternalOutput").ap()

    with ExitStack() as es:
        P = Prog(nc, es)
        sb = lambda name, shape, dt=F32: es.enter_context(nc.sbuf_tensor(name, shape, dt))
        ps = lambda name, shape, dt=F32: es.enter_context(nc.psum_tensor(name, shape, dt))
        xla = sb("xla", [128, NCH, TF], BF16)
        big1 = sb("big1", [128, NCH, TF // 2], BF16)
        cv = sb("cv", [128, NCH, 1024], BF16)
        ckv = sb("ckv", [128, 4, TF], BF16)
        keff = sb("keff", [128, NH_LOC, 512], BF16)
        qlat = sb("qlat", [128, 4, TF], BF16)
        scores = sb("scores", [128, 2048])
        attn_b = sb("attn_b", [128, 2048], BF16)
        attnT = sb("attnT", [128, 16, 512], BF16)
        rowm = sb("rowm", [128, 2])
        wst = sb("wst", [128, 1, NCH, 128], BF16)
        wbig = cv[:, :, 0:512]
        ystage = sb("ystage", [128, 2, 512], BF16)
        ident16 = sb("ident16", [128, 128], BF16)
        dmask = sb("dmask", [128, 4, 512], BF16)
        sc_ps = [ps("sc_ps0", [128, 512]), ps("sc_ps1", [128, 512])]
        y_ps = [ps("y_ps0", [128, 512]), ps("y_ps1", [128, 512])]
        pt = [ps("pt0", [128, 1024], BF16), ps("pt1", [128, 1024], BF16)]

        sems = {k: P.sem(k) for k in ("a_s", "a_v", "a_t", "a_g",
                                      "w0", "w1", "io")}
        sems["a_o"] = [P.sem("a_o0"), P.sem("a_o1")]
        ss, sv, st, sg = sems["a_s"], sems["a_v"], sems["a_t"], sems["a_g"]
        so = sems["a_o"]
        io = sems["io"]
        wsem = [sems["w0"], sems["w1"]]

        csg = P.sem("csg")
        P.op("g", lambda E: E.memset(ident16[:], 0.0), inc=csg)
        P.op("g", lambda E: E.memset(dmask[:], 0.0), inc=csg)
        P.wait("g", csg, 2)
        P.op("g", lambda E: E.affine_select(
            out=ident16[:], in_=ident16[:], compare_op=ALU.not_equal,
            fill=1.0, base=0, pattern=[[-1, 128]], channel_multiplier=1), inc=csg)
        for m in range(4):
            sl = dmask[:, m, :]
            P.op("g", lambda E, sl=sl, m=m: E.affine_select(
                out=sl, in_=sl, compare_op=ALU.is_ge, fill=FILL_L0,
                base=m * 128, pattern=[[-1, 512]], channel_multiplier=1), inc=csg)
        P.wait("t", csg, 7)
        P.wait("v", csg, 7)

        for r in range(2):
            P.dma("y", xla[:, :, r * T_CORE:(r + 1) * T_CORE],
                  xln_all[r].rearrange("(cc p) t -> p cc t", p=128), io)
        n_x = io.n
        n_wuk = P.dma("y", wbig, wuk.rearrange("(cc p) l -> p cc l", p=128), io)
        P.wait("t", io, n_wuk)

        wq_r = wq_nat_s.rearrange("(dc p) (sc s) -> sc p dc s", p=128, s=128)
        war_w = {}
        war_sc_ev = {}
        for sc in range(NH_LOC):
            if sc >= 1:
                P.wait("y", st, war_w[sc - 1])
            n_w = P.dma("y", wst[:, 0], wq_r[sc], wsem[0])
            P.wait("t", wsem[0], n_w)
            pp = sc_ps[sc % 2]
            if sc >= 2:
                P.wait("t", ss, war_sc_ev[sc - 2])
            for dc in range(NCH):
                P.op("t", lambda E, dc=dc, pp=pp: E.matmul(
                    pp[:, 0:512], wst[:, 0, dc, :], wbig[:, dc, :],
                    start=(dc == 0), stop=(dc == NCH - 1)),
                    inc=(st if dc == NCH - 1 else None))
            war_w[sc] = st.n
            P.wait("s", st, st.n)
            war_sc_ev[sc] = P.op("s", lambda E, sc=sc, pp=pp: E.mul(
                out=keff[:, sc, :], in_=pp[:, 0:512], mul=RSQ), inc=ss)
        P.barrier()

        P.wait("y", st, st.n)
        n_wuv = P.dma("y", wbig, wuv.rearrange("(dc p) l -> p dc l", p=128), io)
        woT_r = woT_s.rearrange("(dc p) (cw c) -> cw p dc c", p=128, c=512)
        veff = big1[:, 12:16, :].rearrange("p a t -> p a t")
        wo_stage = big1[:, 0:8, :].rearrange("p a t -> p (a t)").rearrange(
            "p (dc c) -> p dc c", c=512)
        P.wait("t", io, n_wuv)
        for cw in range(2):
            P.wait("y", st, st.n)
            n_w = P.dma("y", wo_stage, woT_r[cw], wsem[cw % 2])
            P.wait("t", wsem[cw % 2], n_w)
            for lc in range(4):
                pp = sc_ps[lc % 2]
                P.wait("t", ss, ss.n)
                for dc in range(NCH):
                    P.op("t", lambda E, dc=dc, lc=lc, pp=pp: E.matmul(
                        pp[:, 0:512], wbig[:, dc, lc * 128:(lc + 1) * 128],
                        wo_stage[:, dc, :], start=(dc == 0), stop=(dc == NCH - 1)),
                        inc=(st if dc == NCH - 1 else None))
                P.wait("s", st, st.n)
                dst = veff[:, lc, cw * 512:(cw + 1) * 512]
                P.op("s", lambda E, dst=dst, pp=pp: E.copy(out=dst, in_=pp[:, 0:512]), inc=ss)
            P.barrier()

        P.wait("y", st, st.n)
        n_wk = P.dma("y", wbig, wdkvT.rearrange("(cc p) l -> p cc l", p=128), io)
        P.wait("t", io, n_wk)
        for lc in range(4):
            for tw in range(4):
                pp = sc_ps[tw % 2]
                P.wait("t", ss, ss.n)
                for cc in range(NCH):
                    P.op("t", lambda E, cc=cc, lc=lc, tw=tw, pp=pp: E.matmul(
                        pp[:, 0:512], wbig[:, cc, lc * 128:(lc + 1) * 128],
                        xla[:, cc, tw * 512:(tw + 1) * 512],
                        start=(cc == 0), stop=(cc == NCH - 1)),
                        inc=(st if cc == NCH - 1 else None))
                P.wait("s", st, st.n)
                dst = ckv[:, lc, tw * 512:(tw + 1) * 512]
                P.op("s", lambda E, dst=dst, pp=pp: E.copy(out=dst, in_=pp[:, 0:512]), inc=ss)
        P.barrier()

        for kc in range(16):
            for cwi in range(2):
                pp = sc_ps[cwi]
                P.wait("t", ss, ss.n)
                for lc in range(4):
                    P.op("t", lambda E, lc=lc, kc=kc, cwi=cwi, pp=pp: E.matmul(
                        pp[:, 0:512], ckv[:, lc, kc * 128:(kc + 1) * 128],
                        veff[:, lc, cwi * 512:(cwi + 1) * 512],
                        start=(lc == 0), stop=(lc == 3)),
                        inc=(st if lc == 3 else None))
                P.wait("s", st, st.n)
                dst = cv[:, kc, cwi * 512:(cwi + 1) * 512]
                P.op("s", lambda E, dst=dst, pp=pp: E.copy(out=dst, in_=pp[:, 0:512]), inc=ss)
        P.barrier()

        q = big1[:].rearrange("p a t -> p (a t)").rearrange("p (h t) -> p h t", t=TF)
        wqT_r = wqT_s.rearrange("(cc p) (dq d) -> dq p cc d", p=128, d=128)
        for dqc in range(NH_LOC):
            P.wait("y", st, st.n)
            n_w = P.dma("y", wst[:, 0], wqT_r[dqc], wsem[0])
            P.wait("t", wsem[0], n_w)
            for tw in range(4):
                pp = sc_ps[tw % 2]
                P.wait("t", ss, ss.n)
                for cc in range(NCH):
                    P.op("t", lambda E, cc=cc, tw=tw, pp=pp: E.matmul(
                        pp[:, 0:512], wst[:, 0, cc, :],
                        xla[:, cc, tw * 512:(tw + 1) * 512],
                        start=(cc == 0), stop=(cc == NCH - 1)),
                        inc=(st if cc == NCH - 1 else None))
                P.wait("s", st, st.n)
                dst = q[:, dqc, tw * 512:(tw + 1) * 512]
                P.op("s", lambda E, dst=dst, pp=pp: E.copy(out=dst, in_=pp[:, 0:512]), inc=ss)
            P.barrier()

        def mk_qlat(h):
            for lc in range(4):
                for tw in range(4):
                    pp = sc_ps[tw % 2]
                    P.wait("t", ss, ss.n)
                    P.op("t", lambda E, lc=lc, tw=tw, h=h, pp=pp: E.matmul(
                        pp[:, 0:512], keff[:, h, lc * 128:(lc + 1) * 128],
                        q[:, h, tw * 512:(tw + 1) * 512],
                        start=True, stop=True), inc=st)
                    P.wait("s", st, st.n)
                    dst = qlat[:, lc, tw * 512:(tw + 1) * 512]
                    P.op("s", lambda E, dst=dst, pp=pp: E.copy(out=dst, in_=pp[:, 0:512]), inc=ss)

        B = dict(scores=scores[:], attn_b=attn_b[:], attnT=attnT,
                 rowm=rowm, pt=pt, sc_ps=sc_ps, y_ps=y_ps, ystage=ystage,
                 ident16=ident16, dmask=dmask)

        def scores_mm(h, qt, w, ps_ap):
            for lc in range(4):
                P.op("t", lambda E, lc=lc, qt=qt, w=w, ps_ap=ps_ap: E.matmul(
                    ps_ap, qlat[:, lc, qt * 128:(qt + 1) * 128],
                    ckv[:, lc, w * 512:(w + 1) * 512],
                    start=(lc == 0), stop=(lc == 3)),
                    inc=(st if lc == 3 else None))
            return st.n

        def av_lhsT(h, kc):
            return cv[:, kc, h * 128:(h + 1) * 128]

        def out_dma(h, quad, src):
            return P.dma("g", y_mine[h * 128:(h + 1) * 128,
                                     quad * 512:(quad + 1) * 512], src, so[quad % 2])

        asems = dict(a_s=ss, a_v=sv, a_t=st, a_g=sg, a_o=so)
        for h in range(NH_LOC):
            mk_qlat(h)
            P.barrier()
            _run_head(P, B, asems, h, scores_mm, av_lhsT, out_dma, -1e30)
            P.barrier()

        P.wait("g", so[0], so[0].n)
        P.wait("g", so[1], so[1].n)
        with nc.Block() as block:
            P.finalize(block)
    return nc


def _run_head(P, B, sems, h, scores_mm, av_lhsT, out_dma, fill):
    emit_attn_core(P, B, sems,
                   lambda _h, qt, w, ps: scores_mm(h, qt, w, ps),
                   lambda _h, kc: av_lhsT(h, kc),
                   lambda _h, quad, src: out_dma(h, quad, src),
                   fill, nheads=1)


"""attn1a: GQA attention, head-sharded. Per core: 8 q-heads (2 kv-heads), full causal.
in: xln_all [2, 2048, 1024] bf16; wqT_s [2048,1024], wkT_s [2048,256], wvT_s [2048,256] bf16
out: ao_mine [1024, 2048] bf16 (pre-out-projection attention output rows)."""

NH_LOC = 8
FILL_L1 = -3.0e38


def build_attn1(T_CORE=1024):
    TF = 2 * T_CORE
    nc = bass.Bass("TRN2", target_bir_lowering=False, debug=False)
    xln_all = nc.dram_tensor("xln_all", [2, 2048, T_CORE], BF16, kind="ExternalInput").ap()
    wqT_s = nc.dram_tensor("wqT_s", [2048, 1024], BF16, kind="ExternalInput").ap()
    wkT_s = nc.dram_tensor("wkT_s", [2048, 256], BF16, kind="ExternalInput").ap()
    wvT_s = nc.dram_tensor("wvT_s", [2048, 256], BF16, kind="ExternalInput").ap()
    ao_mine = nc.dram_tensor("ao_mine", [1024, TF], BF16, kind="ExternalOutput").ap()

    with ExitStack() as es:
        P = Prog(nc, es)
        sb = lambda name, shape, dt=F32: es.enter_context(nc.sbuf_tensor(name, shape, dt))
        ps = lambda name, shape, dt=F32: es.enter_context(nc.psum_tensor(name, shape, dt))
        xla = sb("xla", [128, NCH, TF], BF16)
        q = sb("q", [128, NH_LOC, TF], BF16)
        kf = sb("kf", [128, 2, TF], BF16)
        vf = sb("vf", [128, 2, TF], BF16)
        v_tm = sb("v_tm", [128, 16, 256], BF16)
        scores = sb("scores", [128, 2048])
        attn_b = sb("attn_b", [128, 2048], BF16)
        attnT = sb("attnT", [128, 16, 512], BF16)
        rowm = sb("rowm", [128, 2])
        wst = sb("wst", [128, 1, NCH, 128], BF16)
        ystage = sb("ystage", [128, 2, 512], BF16)
        ident16 = sb("ident16", [128, 128], BF16)
        dmask = sb("dmask", [128, 4, 512], BF16)
        sc_ps = [ps("sc_ps0", [128, 512]), ps("sc_ps1", [128, 512])]
        y_ps = [ps("y_ps0", [128, 512]), ps("y_ps1", [128, 512])]
        pt = [ps("pt0", [128, 1024], BF16), ps("pt1", [128, 1024], BF16)]

        sems = {k: P.sem(k) for k in ("a_s", "a_v", "a_t", "a_g", "w0", "io")}
        sems["a_o"] = [P.sem("a_o0"), P.sem("a_o1")]
        ss, sv, st, sg = sems["a_s"], sems["a_v"], sems["a_t"], sems["a_g"]
        so = sems["a_o"]
        io = sems["io"]
        wsem = sems["w0"]

        csg = P.sem("csg")
        P.op("g", lambda E: E.memset(ident16[:], 0.0), inc=csg)
        P.op("g", lambda E: E.memset(dmask[:], 0.0), inc=csg)
        P.wait("g", csg, 2)
        P.op("g", lambda E: E.affine_select(
            out=ident16[:], in_=ident16[:], compare_op=ALU.not_equal,
            fill=1.0, base=0, pattern=[[-1, 128]], channel_multiplier=1), inc=csg)
        for m in range(4):
            sl = dmask[:, m, :]
            P.op("g", lambda E, sl=sl, m=m: E.affine_select(
                out=sl, in_=sl, compare_op=ALU.is_ge, fill=FILL_L1,
                base=m * 128, pattern=[[-1, 512]], channel_multiplier=1), inc=csg)
        P.wait("t", csg, 7)
        P.wait("v", csg, 7)

        for r in range(2):
            P.dma("y", xla[:, :, r * T_CORE:(r + 1) * T_CORE],
                  xln_all[r].rearrange("(cc p) t -> p cc t", p=128), io)
        n_x = io.n
        P.wait("t", io, n_x)

        wqT_r = wqT_s.rearrange("(cc p) (dq d) -> dq p cc d", p=128, d=128)
        for dqc in range(NH_LOC):
            P.wait("y", st, st.n)
            n_w = P.dma("y", wst[:, 0], wqT_r[dqc], wsem)
            P.wait("t", wsem, n_w)
            for tw in range(4):
                pp = sc_ps[tw % 2]
                P.wait("t", ss, ss.n)
                for cc in range(NCH):
                    P.op("t", lambda E, cc=cc, tw=tw, pp=pp: E.matmul(
                        pp[:, 0:512], wst[:, 0, cc, :],
                        xla[:, cc, tw * 512:(tw + 1) * 512],
                        start=(cc == 0), stop=(cc == NCH - 1)),
                        inc=(st if cc == NCH - 1 else None))
                P.wait("s", st, st.n)
                dst = q[:, dqc, tw * 512:(tw + 1) * 512]
                P.op("s", lambda E, dst=dst, pp=pp: E.copy(out=dst, in_=pp[:, 0:512]), inc=ss)
            P.barrier()

        wkT_r = wkT_s.rearrange("(cc p) (kv d) -> kv p cc d", p=128, d=128)
        wvT_r = wvT_s.rearrange("(cc p) (kv d) -> kv p cc d", p=128, d=128)
        for (wr, dstb) in ((wkT_r, kf), (wvT_r, vf)):
            for kv in range(2):
                P.wait("y", st, st.n)
                n_w = P.dma("y", wst[:, 0], wr[kv], wsem)
                P.wait("t", wsem, n_w)
                for tw in range(4):
                    pp = sc_ps[tw % 2]
                    P.wait("t", ss, ss.n)
                    for cc in range(NCH):
                        P.op("t", lambda E, cc=cc, tw=tw, pp=pp: E.matmul(
                            pp[:, 0:512], wst[:, 0, cc, :],
                            xla[:, cc, tw * 512:(tw + 1) * 512],
                            start=(cc == 0), stop=(cc == NCH - 1)),
                            inc=(st if cc == NCH - 1 else None))
                    P.wait("s", st, st.n)
                    dst = dstb[:, kv, tw * 512:(tw + 1) * 512]
                    P.op("s", lambda E, dst=dst, pp=pp: E.copy(out=dst, in_=pp[:, 0:512]), inc=ss)
                P.barrier()

        for kv in range(2):
            for kc in range(16):
                tpar = kc % 2
                src = vf[:, kv, kc * 128:(kc + 1) * 128]
                P.wait("t", ss, ss.n)
                n_tp = P.op("t", lambda E, tpar=tpar, src=src: E.transpose(
                    pt[tpar][:, 0:128], src, ident16[:]), inc=st)
                P.wait("s", st, n_tp)
                dst = v_tm[:, kc, kv * 128:(kv + 1) * 128]
                P.op("s", lambda E, dst=dst, tpar=tpar: E.copy(
                    out=dst, in_=pt[tpar][:, 0:128]), inc=ss)
        P.barrier()

        B = dict(scores=scores[:], attn_b=attn_b[:], attnT=attnT,
                 rowm=rowm, pt=pt, sc_ps=sc_ps, y_ps=y_ps, ystage=ystage,
                 ident16=ident16, dmask=dmask)

        def scores_mm(h, qt, w, ps_ap):
            P.op("t", lambda E, h=h, qt=qt, w=w, ps_ap=ps_ap: E.matmul(
                ps_ap, q[:, h, qt * 128:(qt + 1) * 128],
                kf[:, h // 4, w * 512:(w + 1) * 512],
                start=True, stop=True), inc=st)
            return st.n

        def av_lhsT(h, kc):
            return v_tm[:, kc, (h // 4) * 128:(h // 4 + 1) * 128]

        def out_dma(h, quad, src):
            return P.dma("g", ao_mine[h * 128:(h + 1) * 128,
                                      quad * 512:(quad + 1) * 512], src, so[quad % 2])

        asems = dict(a_s=ss, a_v=sv, a_t=st, a_g=sg, a_o=so)
        for h in range(NH_LOC):
            emit_attn_core(P, B, asems,
                           lambda _h, qt, w, ps, h=h: scores_mm(h, qt, w, ps),
                           lambda _h, kc, h=h: av_lhsT(h, kc),
                           lambda _h, quad, src, h=h: out_dma(h, quad, src),
                           FILL_L1, nheads=1)
            P.barrier()

        P.wait("g", so[0], so[0].n)
        P.wait("g", so[1], so[1].n)
        with nc.Block() as block:
            P.finalize(block)
    return nc


"""FUSED megakernel: the whole macaron-d2 forward in ONE program per core.

Phases: prep (dequant+casts+AllGather) | ffn0 | xln->gather | attn0 (stream
spilled) | y-add | ffn0 | moe | xln->gather | attn1 (spilled) | outproj | moe
| store. Residual stream stays in SBUF except during the two attention phases
(spilled to private DRAM; reloaded). All cross-core traffic via pair/all-8
AllGather collectives. Program is identical on all 8 cores; per-core behavior
comes only from input data (incl. 0/1 half-selection masks cfgm).

SBUF map (bytes/partition): [16384,81920) stream | [81920,215808) phase arena
| [215808,229344) perm smalls. PSUM: ps0..ps4+ps_small (left,f32), pt0/pt1
(right,bf16).
"""
from types import SimpleNamespace

NH_LOC = 8
NE = 8
NIC = 11
T = 1024
T2 = 512
TF = 2048

WL_SPECS = [
    ("wq_nat_s", (2048, 1024)), ("wqT_s", (2048, 1024)), ("woT_s", (2048, 1024)),
    ("wuk", (2048, 512)), ("wuv", (2048, 512)), ("wdkvT", (2048, 512)),
    ("wqT1_s", (2048, 1024)), ("wkT_s", (2048, 256)), ("wvT_s", (2048, 256)),
    ("woT1", (2048, 2048)), ("wr", (2048, 8)),
]
WL_OFF = {}
_off = 0
for _n, _s in WL_SPECS:
    WL_OFF[_n] = _off
    _off += _s[0] * _s[1]
WL_ELEMS = _off
WL_PAD = ((WL_ELEMS + 262143) // 262144) * 262144

GW_TILE = 128 * 16 * 128
C_WG = 0
C_WU = 16 * GW_TILE
GU_ELEMS = 32 * GW_TILE
CHUNK_G = 8 * GW_TILE
D_ELEMS = 16 * GW_TILE
MOE_ELEMS = 3 * 2048 * 1408
MOE_PAD = ((MOE_ELEMS + 262143) // 262144) * 262144


def build_fused(n_mlp=16384, debug=False, stop=None, unit_groups=False):
    NCHK = n_mlp // 8 // 128
    NGRP = n_mlp // 128 // NCH
    nc = bass.Bass("TRN2", target_bir_lowering=False, debug=False)

    x_in = nc.dram_tensor("x_in", [2048, T], F32, kind="ExternalInput").ap()
    wl_f32 = nc.dram_tensor("wl_f32", [WL_PAD // 512, 512], F32, kind="ExternalInput").ap()
    moe_f32 = nc.dram_tensor("moe_f32", [MOE_PAD // 512, 512], F32, kind="ExternalInput").ap()
    gate_p = nc.dram_tensor("gate_p", [n_mlp // 8, 1024], I32, kind="ExternalInput").ap()
    gate_s = nc.dram_tensor("gate_s", [n_mlp // 8, 64], F32, kind="ExternalInput").ap()
    up_p = nc.dram_tensor("up_p", [n_mlp // 8, 1024], I32, kind="ExternalInput").ap()
    up_s = nc.dram_tensor("up_s", [n_mlp // 8, 64], F32, kind="ExternalInput").ap()
    down_p = nc.dram_tensor("down_p", [256, n_mlp // 2], I32, kind="ExternalInput").ap()
    down_s = nc.dram_tensor("down_s", [256, n_mlp // 32], F32, kind="ExternalInput").ap()
    bo_in = nc.dram_tensor("bo_in", [2048], F32, kind="ExternalInput").ap()
    cfg_in = nc.dram_tensor("cfg_in", [128, 2], F32, kind="ExternalInput").ap()
    out_t = nc.dram_tensor("out_t", [2048, T], F32, kind="ExternalOutput").ap()

    dbg = {}
    if debug:
        for i in range(5):
            dbg[i] = nc.dram_tensor(f"dbg{i}", [2048, T], F32, kind="ExternalOutput").ap()
        dbg["y0"] = nc.dram_tensor("dbg_y0", [1024, TF], BF16, kind="ExternalOutput").ap()
        dbg["xln0"] = nc.dram_tensor("dbg_xln0", [2048, T], BF16, kind="ExternalOutput").ap()
        dbg["gw0"] = nc.dram_tensor("dbg_gw0", [128, 16, 128], BF16, kind="ExternalOutput").ap()
        dbg["gw17"] = nc.dram_tensor("dbg_gw17", [128, 16, 128], BF16, kind="ExternalOutput").ap()
        dbg["cw0"] = nc.dram_tensor("dbg_cw0", [128, 16, 128], BF16, kind="ExternalOutput").ap()
        dbg["gm0"] = nc.dram_tensor("dbg_gm0", [64, 1408], BF16, kind="ExternalOutput").ap()
        dbg["wl0"] = nc.dram_tensor("dbg_wl0", [64, 1024], BF16, kind="ExternalOutput").ap()
    wl_bf = nc.dram_tensor("wl_bf", [WL_PAD // 512, 512], BF16)
    contrib = nc.dram_tensor("contrib", [GU_ELEMS // 512, 512], BF16)
    gath = nc.dram_tensor("gath", [8 * GU_ELEMS // 512, 512], BF16)
    contrib_d = nc.dram_tensor("contrib_d", [D_ELEMS // 512, 512], BF16)
    gath_d = nc.dram_tensor("gath_d", [8 * D_ELEMS // 512, 512], BF16)
    contrib_m = nc.dram_tensor("contrib_m", [MOE_PAD // 512, 512], BF16)
    gath_m = nc.dram_tensor("gath_m", [8 * MOE_PAD // 512, 512], BF16)
    xln_o = nc.dram_tensor("xln_o", [2048, T], BF16)
    xg = nc.dram_tensor("xg", [2 * 2048, T], BF16)
    y_o = nc.dram_tensor("y_o", [1024, TF], BF16)
    yg = nc.dram_tensor("yg", [2 * 1024, TF], BF16)
    spill = nc.dram_tensor("spill", [2048, T], F32)

    flatC = contrib.ap().rearrange("a b -> (a b)")
    flatCD = contrib_d.ap().rearrange("a b -> (a b)")
    flatG = gath.ap().rearrange("a b -> (a b)")
    flatGD = gath_d.ap().rearrange("a b -> (a b)")
    flatGM = gath_m.ap().rearrange("a b -> (a b)")
    flatWL = wl_bf.ap().rearrange("a b -> (a b)")

    def wl_ap(name):
        r, c = dict(WL_SPECS)[name]
        o = WL_OFF[name]
        return flatWL[o:o + r * c].rearrange("(a b) -> a b", b=c)

    def gath_wg(slot, gu):
        g, j = divmod(slot, NCH)
        r = 4 * (g % 2) + j // 4
        jl = 4 * (g // 2) + j % 4
        o = (jl // 4) * 8 * CHUNK_G + r * CHUNK_G + (gu * 4 + jl % 4) * GW_TILE
        return flatG[o:o + GW_TILE].rearrange("(p cc n) -> p cc n", p=128, cc=16)

    def gath_wd(dc, g):
        r, dl = divmod(dc, 2)
        o = r * D_ELEMS + (dl * NGRP + g) * GW_TILE
        return flatGD[o:o + GW_TILE].rearrange("(p cc n) -> p cc n", p=128, cc=16)

    def gath_moe(e, which):
        o = e * MOE_PAD + which * 2048 * 1408
        if which < 2:
            return flatGM[o:o + 2048 * 1408].rearrange("(a b) -> a b", b=1408)
        return flatGM[o:o + 1408 * 2048].rearrange("(a b) -> a b", b=2048)

    CCV = 128 if unit_groups else 1
    CCP = 32 if unit_groups else 1
    with ExitStack() as es:
        P = Prog(nc, es)

        def at(name, shape, dt, off):
            return nc.alloc_sbuf_tensor_at(name, shape, dt, offset=off)

        stream = at("stream", [128, NCH, T], F32, 18432)
        AR = 83968
        SM = 215808
        ones_col16 = at("ones_col16", [128, 1], BF16, SM); SM += 32
        ones_row32 = at("ones_row32", [1, 128], F32, SM); SM += 512
        ones_row16 = at("ones_row16", [1, 128], BF16, SM); SM += 256
        ident16 = at("ident16", [128, 128], BF16, SM); SM += 256
        ident32 = at("ident32", [128, 128], F32, SM); SM += 512
        wr_sb = at("wr_sb", [128, NCH, 8], BF16, SM); SM += 256
        bo_sb = at("bo_sb", [128, NCH], F32, SM); SM += 64
        rowm = at("rowm", [128, 2], F32, SM); SM += 32
        ystage = at("ystage", [128, 2, 512], BF16, SM); SM += 2048
        ce_all = at("ce_all", [8, 512], BF16, SM); SM += 1024
        ce_rows = at("ce_rows", [1, 8, 512], BF16, SM); SM += 8192
        er = at("er", [128, 4, 8], F32, SM); SM += 128
        rsc = at("rsc", [128, 8], F32, SM); SM += 32
        comb = at("comb", [128, 8], F32, SM); SM += 32
        bar_dummy = at("bar_dummy", [128, 8], F32, SM); SM += 32
        cfgm = at("cfgm", [128, 2], F32, SM); SM += 32
        cf_a = at("cf_a", [128, 1], F32, SM); SM += 32
        cf_b = at("cf_b", [128, 1], F32, SM); SM += 32
        assert SM <= 229344, SM
        P._dummy = bar_dummy
        P._bars = {e: P.sem(f"bar_{e}") for e in ("y", "v", "s", "t", "g")}
        nc.const_aps.register(F32, -2.0, cf_a[:])
        nc.const_aps.register(F32, -2.5, cf_b[:])

        ps = [es.enter_context(nc.psum_tensor(f"ps{i}", [128, 512], F32)) for i in range(5)]
        ps_small = es.enter_context(nc.psum_tensor("ps_small", [1, 512], F32))
        pt = [es.enter_context(nc.psum_tensor("pt0", [128, 1024], BF16, side="right")),
              es.enter_context(nc.psum_tensor("pt1", [128, 1024], BF16, side="right"))]

        S = {k: P.sem(k) for k in (
            "io", "sp",
            "d_v", "d_g", "d_s", "d_t", "dio0", "dio1", "dso0", "dso1",
            "cA_in0", "cA_in1", "cA_out0", "cA_out1", "cA_s",
            "cc0", "cc01", "cc02", "cc03", "cc0b", "cc0d", "cc1", "cc2", "cc3", "cc4",
            "ln_s", "ln_v", "ln_t",
            "f_s", "f_v", "f_t", "f_w0", "f_w1", "f_d0", "f_d1",
            "a_s", "a_v", "a_t", "a_g", "a_o0", "a_o1", "a_w",
            "m_s", "m_v", "m_t", "m_g", "m_w0", "m_w1", "m_wd", "m_wd1", "m_ce",
            "o_w0", "o_w1", "yio0", "yio1", "oio0", "oio1",
        )}
        csg = P.sem("csg")
        csv = P.sem("csv")


        def gather(in_ap, out_full_ap, rows_in, sem, groups):
            if unit_groups:
                nrep = out_full_ap.shape[0] // rows_in
                for r in range(nrep - 1):
                    P.dma("g", out_full_ap[(r + 1) * rows_in:(r + 2) * rows_in], in_ap, sem)
                P.dma("g", out_full_ap[0:rows_in], in_ap, sem)
            else:
                P.coll(lambda E, in_ap=in_ap, out_full_ap=out_full_ap, groups=groups:
                       E.collective_compute("AllGather", ALU.bypass,
                                            replica_groups=groups,
                                            ins=[in_ap], outs=[out_full_ap]), sem)

        P.op("v", lambda E: E.memset(ones_col16[:], 1.0))
        P.op("v", lambda E: E.memset(ones_row32[:], 1.0))
        P.op("v", lambda E: E.memset(ones_row16[:], 1.0), inc=csv)
        P.op("v", lambda E: E.memset(cf_a[:], -2.0))
        P.op("v", lambda E: E.memset(cf_b[:], -2.5), inc=csv)
        P.op("g", lambda E: E.memset(ident16[:], 0.0))
        P.op("g", lambda E: E.memset(ident32[:], 0.0), inc=csg)
        P.wait("g", csg, 1)
        P.op("g", lambda E: E.affine_select(
            out=ident16[:], in_=ident16[:], compare_op=ALU.not_equal,
            fill=1.0, base=0, pattern=[[-1, 128]], channel_multiplier=1), inc=csg)
        P.op("g", lambda E: E.affine_select(
            out=ident32[:], in_=ident32[:], compare_op=ALU.not_equal,
            fill=1.0, base=0, pattern=[[-1, 128]], channel_multiplier=1), inc=csg)
        P.wait("t", csg, 3)
        P.wait("t", csv, 2)
        P.wait("s", csv, 2)
        n_x = P.dma("g", stream[:], x_in.rearrange("(cc p) t -> p cc t", p=128), S["io"])
        P.dma("g", cfgm[:], cfg_in, S["io"])
        for cc in range(NCH):
            P.dma("g", bo_sb[:, cc:cc + 1], bo_in[cc * 128:(cc + 1) * 128].unsqueeze(1), S["io"])
        n_io0 = S["io"].n

        PSUB = 1024
        pk_b = at("pk_b", [128, 2, PSUB], I32, AR)
        sc_b = at("sc_b", [128, 2, PSUB // 16], F32, AR + 2 * PSUB * 4)
        scr_names = ("c_i", "s_i", "a_i", "b_i", "bits_i")
        scratch = {}
        o = AR + 2 * PSUB * 4 + 512
        for k in scr_names:
            scratch[k] = at(k, [128, 2, PSUB], I32, o)
            o += 2 * PSUB * 4
        val16 = at("val16", [128, 2, 2 * PSUB], BF16, o); o += 2 * 2 * PSUB * 2
        wtile = at("wtile", [128, 2, 128], BF16, o); o += 512
        cbuf32 = at("cbuf32", [128, 2, 2048], F32, o); o += 16384
        cbuf16 = at("cbuf16", [128, 2, 2048], BF16, o); o += 8192
        assert o <= 215808, o

        _war_cast = {}

        def emit_cast(src_flat, dst_flat, rows_total, pre):
            c_s = S["cA_s"]
            war_cast = _war_cast
            CH = 512
            nch = rows_total // CH
            assert nch * CH == rows_total
            for i in range(nch):
                par = i % 2
                in_sem = S[f"cA_in{par}"]
                out_sem = S[f"cA_out{par}"]
                r0 = i * CH
                vi = src_flat[r0:r0 + CH].rearrange("(a p) w -> p a w", p=128)
                vo = dst_flat[r0:r0 + CH].rearrange("(a p) w -> p a w", p=128)
                P.wait("y", out_sem, out_sem.n)
                P.wait("y", c_s, war_cast.get(par, 0))
                n_in = P.dma("y", cbuf32[:, par].rearrange("p (a w) -> p a w", w=512), vi, in_sem)
                P.wait("s", in_sem, n_in)
                n_c = P.op("s", lambda E, par=par: E.copy(
                    out=cbuf16[:, par], in_=cbuf32[:, par]), inc=c_s)
                war_cast[par] = n_c
                P.wait("y", c_s, n_c)
                P.dma("y", vo, cbuf16[:, par].rearrange("p (a w) -> p a w", w=512), out_sem)

        sv_d, sg_d, ss_d, st_d = S["d_v"], S["d_g"], S["d_s"], S["d_t"]
        war_tp = {}
        war_decode = {}

        def emit_decode(pk, par, out_view, sc_bc):
            c_i, s_i = scratch["c_i"][:, par], scratch["s_i"][:, par]
            a_i, b_i = scratch["a_i"][:, par], scratch["b_i"][:, par]
            bits = scratch["bits_i"][:, par]
            fb = bits.bitcast(F32)

            def vop(fn, inc=True):
                P.wait("v", sv_d, sv_d.n)
                return P.op("v", fn, inc=(sv_d if inc else None))
            for nib in range(2):
                if nib == 0:
                    vop(lambda E, c_i=c_i, pk=pk: E.tensor_single_scalar(
                        out=c_i, in_=pk, scalar=15, op=ALU.bitwise_and))
                else:
                    vop(lambda E, c_i=c_i, pk=pk: E.tensor_single_scalar(
                        out=c_i, in_=pk, scalar=4, op=ALU.logical_shift_right))
                    vop(lambda E, c_i=c_i: E.tensor_single_scalar(
                        out=c_i, in_=c_i, scalar=15, op=ALU.bitwise_and))
                vop(lambda E, s_i=s_i, c_i=c_i: E.tensor_single_scalar(
                    out=s_i, in_=c_i, scalar=3, op=ALU.logical_shift_right))
                vop(lambda E, s_i=s_i: E.tensor_single_scalar(
                    out=s_i, in_=s_i, scalar=31, op=ALU.logical_shift_left))
                vop(lambda E, c_i=c_i: E.tensor_single_scalar(
                    out=c_i, in_=c_i, scalar=7, op=ALU.bitwise_and))
                vop(lambda E, a_i=a_i, c_i=c_i: E.tensor_single_scalar(
                    out=a_i, in_=c_i, scalar=1, op=ALU.min))
                vop(lambda E, b_i=b_i, c_i=c_i: E.tensor_single_scalar(
                    out=b_i, in_=c_i, scalar=1, op=ALU.logical_shift_right))
                vop(lambda E, b_i=b_i: E.tensor_single_scalar(
                    out=b_i, in_=b_i, scalar=1, op=ALU.min))
                vop(lambda E, b_i=b_i, c_i=c_i: E.tensor_tensor(
                    out=b_i, in0=c_i, in1=b_i, op=ALU.mult))
                vop(lambda E, b_i=b_i: E.tensor_single_scalar(
                    out=b_i, in_=b_i, scalar=22, op=ALU.logical_shift_left))
                vop(lambda E, bits=bits, a_i=a_i: E.tensor_single_scalar(
                    out=bits, in_=a_i, scalar=0x3F000000, op=ALU.mult))
                vop(lambda E, bits=bits, b_i=b_i: E.tensor_tensor(
                    out=bits, in0=bits, in1=b_i, op=ALU.add))
                vop(lambda E, bits=bits, s_i=s_i: E.tensor_tensor(
                    out=bits, in0=bits, in1=s_i, op=ALU.bitwise_or))
                ov = out_view[nib]
                n_out = vop(lambda E, ov=ov, fb=fb: E.tensor_tensor(
                    out=ov, in0=fb.rearrange("p (b k) -> p b k", k=16),
                    in1=sc_bc, op=ALU.mult))
            war_decode[("v", par)] = n_out
            return n_out

        def do_dequant(pk_ap, sc_ap, nchk, pw, out_tile_fn):
            nsub = pw // PSUB
            for ch in range(nchk):
                for sub in range(nsub):
                    par = (ch * nsub + sub) % 2
                    iosem = S[f"dio{par}"]
                    P.wait("y", sv_d, war_decode.get(("v", par), 0))
                    P.dma("y", pk_b[:, par], pk_ap[ch * 128:(ch + 1) * 128, sub * PSUB:(sub + 1) * PSUB], iosem)
                    nbl = PSUB // 16
                    n_sc = P.dma("y", sc_b[:, par, 0:nbl],
                                 sc_ap[ch * 128:(ch + 1) * 128, sub * nbl:(sub + 1) * nbl], iosem)
                    P.wait("v", iosem, n_sc)
                    pk = pk_b[:, par]
                    vv = val16[:, par].rearrange("p (a b) -> p a b", b=2)
                    sc_bc = sc_b[:, par, 0:nbl].unsqueeze(2).broadcast_to([128, nbl, 16])
                    lo = vv[:, :, 0].rearrange("p (b k) -> p b k", k=16)
                    hi = vv[:, :, 1].rearrange("p (b k) -> p b k", k=16)
                    P.wait("v", st_d, war_tp.get(("val", par), 0))
                    n_dec = emit_decode(pk, par, (lo, hi), sc_bc)
                    ncc = 2 * PSUB // 128
                    for q in range(ncc):
                        cc_glob = sub * ncc + q
                        tpar = cc_glob % 2
                        src = val16[:, par, q * 128:(q + 1) * 128]
                        P.wait("t", sv_d, n_dec)
                        P.wait("t", ss_d, war_tp.get(tpar, 0))
                        n_tp = P.op("t", lambda E, tpar=tpar, src=src: E.transpose(
                            pt[tpar][:, 0:128], src, ident16[:]), inc=st_d)
                        war_tp[("val", par)] = n_tp
                        P.wait("s", st_d, n_tp)
                        sosem = S[f"dso{tpar}"]
                        P.wait("s", sosem, sosem.n)
                        n_ev = P.op("s", lambda E, tpar=tpar: E.copy(
                            out=wtile[:, tpar], in_=pt[tpar][:, 0:128]), inc=ss_d)
                        war_tp[tpar] = n_ev
                        P.wait("y", ss_d, n_ev)
                        P.dma("y", out_tile_fn(ch, cc_glob), wtile[:, tpar], sosem)

        def vCK(k, gu, ch, ccg):
            o = k * CHUNK_G + (gu * 4 + ch) * GW_TILE
            return flatC[o:o + GW_TILE].rearrange(
                "(c cc n) -> c cc n", c=128, cc=16)[:, ccg, :]

        vdn = flatCD[0:16 * GW_TILE].rearrange(
            "(dl g c cc n) -> dl g c cc n", g=NGRP, c=128, cc=16, n=128)

        def vdn_slot(ch, nchunk):
            r_t, jl = divmod(nchunk, NCH)
            g = 2 * (jl // 4) + (r_t // 4)
            j = 4 * (r_t % 4) + (jl % 4)
            return vdn[ch, g, :, j, :]
        for k in range(4):
            do_dequant(gate_p[512 * k:512 * (k + 1)], gate_s[512 * k:512 * (k + 1)],
                       4, 1024, lambda ch, ccg, k=k: vCK(k, 0, ch, ccg))
            do_dequant(up_p[512 * k:512 * (k + 1)], up_s[512 * k:512 * (k + 1)],
                       4, 1024, lambda ch, ccg, k=k: vCK(k, 1, ch, ccg))
            for s in ("dso0", "dso1"):
                P.wait("g", S[s], S[s].n)
            gather(contrib.ap()[k * (CHUNK_G // 512):(k + 1) * (CHUNK_G // 512)],
                   gath.ap()[k * (8 * CHUNK_G // 512):(k + 1) * (8 * CHUNK_G // 512)],
                   CHUNK_G // 512, S["cc0" if k == 0 else f"cc0{k}"], [list(range(8))])
        do_dequant(down_p, down_s, 2, n_mlp // 2, vdn_slot)
        for s in ("dso0", "dso1"):
            P.wait("g", S[s], S[s].n)
        gather(contrib_d.ap(), gath_d.ap(), D_ELEMS // 512, S["cc0d"], [list(range(8))])
        emit_cast(wl_f32, wl_bf.ap(), WL_PAD // 512, "cA")
        emit_cast(moe_f32, contrib_m.ap(), MOE_PAD // 512, "cB")
        for s in ("cA_out0", "cA_out1"):
            P.wait("g", S[s], S[s].n)
        gather(contrib_m.ap(), gath_m.ap(), MOE_PAD // 512, S["cc0b"], [list(range(8))])
        P.barrier()

        ffnB = SimpleNamespace(
            stream=stream, T=T, T2=T2,
            x_ln=at("f_xln", [128, NCH, T2], BF16, AR),
            big_a=at("f_biga", [128, NCH, 1024], BF16, AR + 16384),
            wg_buf=at("f_wg", [128, 2, NCH, 128], BF16, AR + 49152),
            wu_buf=at("f_wu", [128, 2, NCH, 128], BF16, AR + 57344),
            wd_buf=at("f_wd", [128, 2, NCH, 128], BF16, AR + 65536),
            gelu_sb=at("f_gelu", [128, 2, T2], BF16, AR + 73728),
            ds_acc=at("f_ds", [128, NCH, T2], F32, AR + 75776),
            rows=at("f_rows", [1, 4, T2], F32, AR + 108544),
            ones_col16=ones_col16, ones_row32=ones_row32,
            ps=[ps[0], ps[1], ps[2], ps[3], ps[4], ps[4]],
            ps_small=ps_small,
        )
        ffnB.xb16 = ffnB.big_a[:, :, 0:T2]
        ffnB.xsq16 = ffnB.big_a[:, :, T2:1024]
        fw = dict(wg_t=lambda jg: gath_wg(jg, 0),
                  wu_t=lambda jg: gath_wg(jg, 1),
                  wd_t=gath_wd)
        fsems = dict(ln_s=S["ln_s"], ln_v=S["ln_v"], ln_t=S["ln_t"],
                     f_s=S["f_s"], f_v=S["f_v"], f_t=S["f_t"],
                     f_w=[S["f_w0"], S["f_w1"]], f_d=[S["f_d0"], S["f_d1"]])

        if debug:
            P.wait("g", S["cc0"], CCV)
            P.wait("g", S["cc0b"], CCV)
            vC0 = flatC[0:GW_TILE].rearrange("(p cc n) -> p cc n", p=128, cc=16)
            for (tag, src_ap) in (("gw0", gath_wg(0, 0)), ("gw17", gath_wg(17, 0)),
                                  ("cw0", vC0)):
                n_d = P.dma("g", dbg[tag], src_ap, S["io"])
            n_d = P.dma("g", dbg["gm0"], gath_moe(0, 0)[0:64, :], S["io"])
            n_d = P.dma("g", dbg["wl0"], wl_ap("wq_nat_s")[0:64, :], S["io"])
            P.wait("g", S["io"], n_d)

        def run_ffn():
            P.wait("y", S["cc0"], CCV)
            fsems["wg_gate"] = lambda jg: P.wait(
                "y", S["cc0" if jg < 32 else f"cc0{min(jg // 32, 3)}"], CCV)
            fsems["wd_gate"] = (S["cc0d"], CCV)
            P.wait("s", S["io"], n_io0)
            P.wait("v", S["io"], n_io0)
            emit_ffn0(P, ffnB, fw, fsems, n_mlp=n_mlp, xln_half=True)
            P.barrier()

        run_ffn()

        def dump_stream(i):
            if debug:
                P.wait("g", S["f_v"], S["f_v"].n)
                P.wait("g", S["m_v"], S["m_v"].n)
                P.wait("g", S["a_v"], S["a_v"].n)
                n_d = P.dma("g", dbg[i].rearrange("(cc p) t -> p cc t", p=128), stream[:], S["io"])
                P.wait("g", S["io"], n_d)
        dump_stream(0)
        if stop == "x1":
            P.wait("g", S["f_v"], S["f_v"].n)
            n_o = P.dma("g", out_t.rearrange("(cc p) t -> p cc t", p=128), stream[:], S["io"])
            P.wait("g", S["io"], n_o)
            with nc.Block() as block:
                P.finalize(block)
            return nc

        xlnB = SimpleNamespace(
            stream=stream, T=T, T2=T2,
            x_ln=ffnB.x_ln, rows=ffnB.rows,
            xb16=ffnB.xb16, xsq16=ffnB.xsq16,
            ones_col16=ones_col16, ones_row32=ones_row32,
            ps=[None, None, None, None, None, ps[4]], ps_small=ps_small,
        )
        lsems = dict(ln_s=S["ln_s"], ln_v=S["ln_v"], ln_t=S["ln_t"])

        def run_xln(ccsem):
            for half in (0, 1):
                n_ln = emit_ln(P, xlnB, half, lsems, dst_t0=0,
                               war=[(S["io"], S["io"].n)])
                P.wait("g", lsems["ln_v"], n_ln)
                P.dma("g", xln_o.ap()[:, half * T2:(half + 1) * T2].rearrange(
                    "(cc p) t -> p cc t", p=128), xlnB.x_ln[:], S["io"])
            P.wait("g", S["io"], S["io"].n)
            gather(xln_o.ap(), xg.ap(), 2048, ccsem, [[0, 1], [2, 3], [4, 5], [6, 7]])
            P.barrier()

        run_xln(S["cc1"])
        if debug:
            n_d = P.dma("g", dbg["xln0"].rearrange("(cc p) t -> p cc t", p=128),
                        xln_o.ap().rearrange("(cc p) t -> p cc t", p=128), S["io"])
            P.wait("g", S["io"], n_d)

        n_sp = P.dma("g", spill.ap().rearrange("(cc p) t -> p cc t", p=128), stream[:], S["sp"])

        AB = 18432
        a_xst = at("a_xst", [128, NCH, T2], BF16, AB)
        a_wf = at("a_wf", [128, NCH, 1024], BF16, AB + 16384)
        a_q = at("a_q", [128, NH_LOC, TF], BF16, AB + 49152)
        a_cv = at("a_cv", [128, NCH, 1024], BF16, AB + 81920)
        a_ckv = at("a_ckv", [128, 4, TF], BF16, AB + 114688)
        a_keff = at("a_keff", [128, NH_LOC, 512], BF16, AB + 131072)
        a_qlat = at("a_qlat", [128, 4, TF], BF16, AB + 139264)
        a_scores = at("a_scores", [128, 2048], F32, AB + 155648)
        a_attnb = at("a_attnb", [128, 2048], BF16, AB + 163840)
        a_attnT = at("a_attnT", [128, 16, 512], BF16, AB + 167936)
        a_dm0 = at("a_dm0", [128, 4, 512], BF16, AB + 184320)
        a_veff = at("a_veff", [128, 4, 1024], BF16, AB + 188416)
        a_kf = at("a_kf", [128, 2, TF], BF16, AB + 81920)
        a_vf = at("a_vf", [128, 2, TF], BF16, AB + 90112)
        a_vtm = at("a_vtm", [128, 16, 256], BF16, AB + 98304)
        a_dm1 = at("a_dm1", [128, 4, 512], BF16, AB + 106496)

        ss_a, sv_a, st_a, sg_a = S["a_s"], S["a_v"], S["a_t"], S["a_g"]
        so_a = [S["a_o0"], S["a_o1"]]
        aw = S["a_w"]

        war_pp = {}
        aB = dict(scores=a_scores[:], attn_b=a_attnb[:], attnT=a_attnT,
                  rowm=rowm, pt=pt, sc_ps=[ps[0], ps[1]], y_ps=[ps[2], ps[2]],
                  ystage=ystage, ident16=ident16, y_single=True)
        asems = dict(a_s=ss_a, a_v=sv_a, a_t=st_a, a_g=sg_a, a_o=so_a)

        def mk_dmask(dm, fill):
            nmk = P.op("g", lambda E, dm=dm: E.memset(dm[:], 0.0), inc=sg_a)
            P.wait("g", sg_a, nmk)
            for m in range(4):
                sl = dm[:, m, :]
                nmk = P.op("g", lambda E, sl=sl, m=m, fill=fill: E.affine_select(
                    out=sl, in_=sl, compare_op=ALU.is_ge, fill=fill,
                    base=m * 128, pattern=[[-1, 512]], channel_multiplier=1), inc=sg_a)
            P.wait("t", sg_a, sg_a.n)
            P.wait("v", sg_a, sg_a.n)

        def load_xst(tw):
            r, twl = divmod(tw, 2)
            src = xg.ap()[r * 2048:(r + 1) * 2048, twl * T2:(twl + 1) * T2].rearrange(
                "(cc p) t -> p cc t", p=128)
            return P.dma("y", a_xst[:], src, aw)

        def proj_pass(wl_name, ncols, dst_fn, extra_lhs_rearr=None):
            nocc = ncols // 128
            wv = wl_ap(wl_name).rearrange("(cc p) d -> p cc d", p=128)
            P.wait("y", st_a, st_a.n)
            n_w = P.dma("y", a_wf[:, :, 0:ncols], wv, aw)
            for tw in range(4):
                P.wait("y", st_a, st_a.n)
                n_x = load_xst(tw)
                P.wait("t", aw, n_x)
                for oc in range(nocc):
                    pp = ps[oc % 2]
                    P.wait("t", ss_a, war_pp.get(id(pp), 0))
                    for cc in range(NCH):
                        lhsT = a_wf[:, cc, oc * 128:(oc + 1) * 128]
                        rhs = a_xst[:, cc, :]
                        P.op("t", lambda E, lhsT=lhsT, rhs=rhs, cc=cc, pp=pp: E.matmul(
                            pp[:, 0:512], lhsT, rhs, start=(cc == 0), stop=(cc == NCH - 1)),
                            inc=(st_a if cc == NCH - 1 else None))
                    P.wait("s", st_a, st_a.n)
                    dst = dst_fn(oc, tw)
                    war_pp[id(pp)] = P.op("s", lambda E, dst=dst, pp=pp: E.copy(out=dst, in_=pp[:, 0:512]), inc=ss_a)
            P.barrier()

        def run_attn0():
            mk_dmask(a_dm0, FILL_L0)
            aB["dmask"] = a_dm0
            for e in ("y", "g", "s", "t", "v"):
                P.wait(e, S["sp"], n_sp)
            P.wait("y", S["cc1"], CCP)
            P.wait("y", S["cA_out0"], S["cA_out0"].n)
            P.wait("y", S["cA_out1"], S["cA_out1"].n)
            P.barrier()
            n_wuk = P.dma("y", a_xst[:], wl_ap("wuk").rearrange("(dc p) l -> p dc l", p=128), aw)
            wqv = wl_ap("wq_nat_s").rearrange("(dc p) s -> p dc s", p=128)
            n_wq = P.dma("y", a_wf[:, :, 0:1024], wqv, aw)
            P.wait("t", aw, n_wq)
            for sc in range(NH_LOC):
                pp = ps[sc % 2]
                P.wait("t", ss_a, war_pp.get(id(pp), 0))
                for dc in range(NCH):
                    lhsT = a_wf[:, dc, sc * 128:(sc + 1) * 128]
                    P.op("t", lambda E, lhsT=lhsT, dc=dc, pp=pp: E.matmul(
                        pp[:, 0:512], lhsT, a_xst[:, dc, :],
                        start=(dc == 0), stop=(dc == NCH - 1)),
                        inc=(st_a if dc == NCH - 1 else None))
                P.wait("s", st_a, st_a.n)
                dst = a_keff[:, sc, :]
                war_pp[id(pp)] = P.op("s", lambda E, dst=dst, pp=pp: E.mul(out=dst, in_=pp[:, 0:512], mul=RSQ), inc=ss_a)
            P.barrier()
            n_wuv = P.dma("y", a_xst[:], wl_ap("wuv").rearrange("(dc p) l -> p dc l", p=128), aw)
            wov = wl_ap("woT_s").rearrange("(dc p) c -> p dc c", p=128)
            n_wo = P.dma("y", a_wf[:, :, 0:1024], wov, aw)
            P.wait("t", aw, n_wo)
            for cw in range(2):
                for lc in range(4):
                    pp = ps[lc % 2]
                    P.wait("t", ss_a, war_pp.get(id(pp), 0))
                    for dc in range(NCH):
                        lhsT = a_xst[:, dc, lc * 128:(lc + 1) * 128]
                        rhs = a_wf[:, dc, cw * 512:(cw + 1) * 512]
                        P.op("t", lambda E, lhsT=lhsT, rhs=rhs, dc=dc, pp=pp: E.matmul(
                            pp[:, 0:512], lhsT, rhs, start=(dc == 0), stop=(dc == NCH - 1)),
                            inc=(st_a if dc == NCH - 1 else None))
                    P.wait("s", st_a, st_a.n)
                    dst = a_veff[:, lc, cw * 512:(cw + 1) * 512]
                    war_pp[id(pp)] = P.op("s", lambda E, dst=dst, pp=pp: E.copy(out=dst, in_=pp[:, 0:512]), inc=ss_a)
            P.barrier()
            n_wk = P.dma("y", a_wf[:, :, 0:512], wl_ap("wdkvT").rearrange(
                "(cc p) l -> p cc l", p=128), aw)
            for tw in range(4):
                P.wait("y", st_a, st_a.n)
                n_x = load_xst(tw)
                P.wait("t", aw, n_x)
                for lc in range(4):
                    pp = ps[lc % 2]
                    P.wait("t", ss_a, war_pp.get(id(pp), 0))
                    for cc in range(NCH):
                        lhsT = a_wf[:, cc, lc * 128:(lc + 1) * 128]
                        P.op("t", lambda E, lhsT=lhsT, cc=cc, pp=pp: E.matmul(
                            pp[:, 0:512], lhsT, a_xst[:, cc, :],
                            start=(cc == 0), stop=(cc == NCH - 1)),
                            inc=(st_a if cc == NCH - 1 else None))
                    P.wait("s", st_a, st_a.n)
                    dst = a_ckv[:, lc, tw * 512:(tw + 1) * 512]
                    war_pp[id(pp)] = P.op("s", lambda E, dst=dst, pp=pp: E.copy(out=dst, in_=pp[:, 0:512]), inc=ss_a)
            P.barrier()
            for kc in range(16):
                for cwi in range(2):
                    pp = ps[cwi]
                    P.wait("t", ss_a, war_pp.get(id(pp), 0))
                    for lc in range(4):
                        lhsT = a_ckv[:, lc, kc * 128:(kc + 1) * 128]
                        rhs = a_veff[:, lc, cwi * 512:(cwi + 1) * 512]
                        P.op("t", lambda E, lhsT=lhsT, rhs=rhs, lc=lc, pp=pp: E.matmul(
                            pp[:, 0:512], lhsT, rhs, start=(lc == 0), stop=(lc == 3)),
                            inc=(st_a if lc == 3 else None))
                    P.wait("s", st_a, st_a.n)
                    dst = a_cv[:, kc, cwi * 512:(cwi + 1) * 512]
                    war_pp[id(pp)] = P.op("s", lambda E, dst=dst, pp=pp: E.copy(out=dst, in_=pp[:, 0:512]), inc=ss_a)
            P.barrier()
            proj_pass("wqT_s", 1024,
                      lambda oc, tw: a_q[:, oc, tw * 512:(tw + 1) * 512])
            for h in range(NH_LOC):
                for lc in range(4):
                    for tw in range(4):
                        pp = ps[tw % 2]
                        P.wait("t", ss_a, war_pp.get(id(pp), 0))
                        lhsT = a_keff[:, h, lc * 128:(lc + 1) * 128]
                        rhs = a_q[:, h, tw * 512:(tw + 1) * 512]
                        P.op("t", lambda E, lhsT=lhsT, rhs=rhs, pp=pp: E.matmul(
                            pp[:, 0:512], lhsT, rhs, start=True, stop=True), inc=st_a)
                        P.wait("s", st_a, st_a.n)
                        dst = a_qlat[:, lc, tw * 512:(tw + 1) * 512]
                        war_pp[id(pp)] = P.op("s", lambda E, dst=dst, pp=pp: E.copy(out=dst, in_=pp[:, 0:512]), inc=ss_a)
                P.barrier()

                def scores_mm(_h, qt, w, ps_ap):
                    for lc in range(4):
                        lhsT = a_qlat[:, lc, qt * 128:(qt + 1) * 128]
                        rhs = a_ckv[:, lc, w * 512:(w + 1) * 512]
                        P.op("t", lambda E, lhsT=lhsT, rhs=rhs, lc=lc, ps_ap=ps_ap: E.matmul(
                            ps_ap, lhsT, rhs, start=(lc == 0), stop=(lc == 3)),
                            inc=(st_a if lc == 3 else None))
                    return st_a.n

                emit_attn_core(P, aB, asems, scores_mm,
                               lambda _h, kc, h=h: a_cv[:, kc, h * 128:(h + 1) * 128],
                               lambda _h, quad, src, h=h: P.dma(
                                   "g", y_o.ap()[h * 128:(h + 1) * 128,
                                                 quad * 512:(quad + 1) * 512],
                                   src, so_a[0]),
                               FILL_L0, nheads=1)
                P.barrier()
            P.wait("g", so_a[0], so_a[0].n)
            P.wait("g", so_a[1], so_a[1].n)
            gather(y_o.ap(), yg.ap(), 1024, S["cc2"], [[0, 1], [2, 3], [4, 5], [6, 7]])
            P.barrier()

        run_attn0()

        n_rl = P.dma("g", stream[:], spill.ap().rearrange("(cc p) t -> p cc t", p=128), S["sp"])
        ystg = at("ystg", [128, 2, 2, 1024], BF16, AR + 16384)
        war_add = {}

        def run_yadd(ccsem):
            P.wait("g", ccsem, CCP)
            P.wait("g", S["sp"], S["sp"].n)
            for cc in range(NCH):
                par = cc % 2
                r, hcl = divmod(cc, 8)
                src = yg.ap()[r * 1024 + hcl * 128: r * 1024 + (hcl + 1) * 128, :].rearrange(
                    "p (h t) -> p h t", t=T)
                P.wait("g", S["f_v"], war_add.get(par, 0))
                n_l = P.dma("g", ystg[:, par], src, S[f"yio{par}"])
                P.wait("v", S[f"yio{par}"], n_l)
                P.wait("v", S["sp"], S["sp"].n)
                sdst = stream[:, cc, :]
                P.op("v", lambda E, sdst=sdst, par=par: E.scalar_tensor_tensor(
                    out=sdst, in0=ystg[:, par, 0], scalar=cfgm[:, 0:1], in1=sdst,
                    op0=ALU.mult, op1=ALU.add), inc=S["f_v"])
                P.wait("v", S["f_v"], S["f_v"].n)
                war_add[par] = P.op("v", lambda E, sdst=sdst, par=par: E.scalar_tensor_tensor(
                    out=sdst, in0=ystg[:, par, 1], scalar=cfgm[:, 1:2], in1=sdst,
                    op0=ALU.mult, op1=ALU.add), inc=S["f_v"])
            P.barrier()

        if debug:
            P.wait("g", S["cc2"], CCP)
            n_d = P.dma("g", dbg["y0"], y_o.ap(), S["io"])
            P.wait("g", S["io"], n_d)
        run_yadd(S["cc2"])
        dump_stream(1)

        run_ffn()
        dump_stream(2)

        m_xln = ffnB.x_ln
        m_hall = at("m_hall", [128, NE, NIC, T2], BF16, AR + 16384)
        m_wgst = at("m_wgst", [128, 2, NCH, 128], BF16, AR + 106496)
        m_wust = at("m_wust", [128, 2, NCH, 128], BF16, AR + 114688)
        m_wdst = at("m_wdst", [128, 2, NIC, 128], BF16, AR + 122880)
        m_gsil = at("m_gsil", [128, 2, T2], BF16, AR + 122880)
        m_rows = at("m_rows", [1, 4, T2], F32, AR + 106496)
        assert AR + 125696 <= 215808

        moeB = SimpleNamespace(
            stream=stream, T=T, T2=T2, x_ln=m_xln, rows=m_rows,
            xb16=m_hall[:, 0:2].rearrange("p e a t -> p (e a t)")[:, 0:16 * T2].rearrange(
                "p (a t) -> p a t", t=T2),
            xsq16=m_hall[:, 2:4].rearrange("p e a t -> p (e a t)")[:, 0:16 * T2].rearrange(
                "p (a t) -> p a t", t=T2),
            ones_col16=ones_col16, ones_row32=ones_row32,
            ps=[None, None, None, None, None, ps[3]], ps_small=ps_small,
        )
        msems = dict(ln_s=S["ln_s"], ln_v=S["ln_v"], ln_t=S["ln_t"])
        ss_m, sv_m, st_m = S["m_s"], S["m_v"], S["m_t"]

        P.wait("g", S["cA_out0"], S["cA_out0"].n)
        P.wait("g", S["cA_out1"], S["cA_out1"].n)
        n_wr = P.dma("g", wr_sb[:], wl_ap("wr").rearrange("(cc p) e -> p cc e", p=128), S["io"])

        def run_moe():
            P.wait("s", S["io"], n_wr)
            P.wait("v", S["io"], n_wr)
            P.wait("t", S["io"], n_wr)
            pbc_m = ps[3]
            ps_r = ps[0]
            for half in (0, 1):
                t0 = half * T2
                n_ln = emit_ln(P, moeB, half, msems, war=[(st_m, st_m.n), (sv_m, sv_m.n)], dst_t0=0)
                P.wait("t", msems["ln_v"], n_ln)
                for tc in range(4):
                    P.wait("t", sv_m, sv_m.n)
                    P.wait("t", ss_m, ss_m.n)
                    for cc in range(NCH):
                        lhsT = m_xln[:, cc, tc * 128:(tc + 1) * 128]
                        P.op("t", lambda E, lhsT=lhsT, cc=cc: E.matmul(
                            ps_r[:, 0:8], lhsT, wr_sb[:, cc, :],
                            start=(cc == 0), stop=(cc == NCH - 1)),
                            inc=(st_m if cc == NCH - 1 else None))
                    n_lg = st_m.n
                    P.wait("v", st_m, st_m.n)
                    P.wait("v", ss_m, ss_m.n)
                    er0, er2 = er[:, 0, :], er[:, 1, :]
                    msk1, msk2 = er[:, 2, :], er[:, 3, :]
                    m1 = rsc[:, 0:1]; m2 = rsc[:, 1:2]; s12 = rsc[:, 2:3]
                    w1 = rsc[:, 4:5]; w2 = rsc[:, 5:6]; nmx = rsc[:, 6:7]
                    P.op("v", lambda E: E.reduce_max(out=nmx, in_=ps_r[:, 0:8], axis=AX.X, negate=True), inc=sv_m)
                    P.wait("s", sv_m, sv_m.n)
                    P.op("s", lambda E: E.activation(out=er0, in_=ps_r[:, 0:8], func=AF.Exp, bias=nmx), inc=ss_m)
                    P.wait("v", ss_m, ss_m.n)
                    P.op("v", lambda E: E.reduce_max(out=m1, in_=er0, axis=AX.X), inc=sv_m)
                    P.wait("v", sv_m, sv_m.n)
                    P.op("v", lambda E: E.tensor_scalar(out=msk1, in0=er0, scalar1=m1, scalar2=None, op0=ALU.is_equal), inc=sv_m)
                    P.wait("v", sv_m, sv_m.n)
                    P.op("v", lambda E: E.scalar_tensor_tensor(out=er2, in0=msk1, scalar=-4.0, in1=er0, op0=ALU.mult, op1=ALU.add), inc=sv_m)
                    P.wait("v", sv_m, sv_m.n)
                    P.op("v", lambda E: E.reduce_max(out=m2, in_=er2, axis=AX.X), inc=sv_m)
                    P.wait("v", sv_m, sv_m.n)
                    P.op("v", lambda E: E.tensor_scalar(out=msk2, in0=er2, scalar1=m2, scalar2=None, op0=ALU.is_equal), inc=sv_m)
                    P.op("v", lambda E: E.tensor_tensor(out=s12, in0=m1, in1=m2, op=ALU.add), inc=sv_m)
                    P.wait("v", sv_m, sv_m.n)
                    P.op("v", lambda E: E.reciprocal(out=s12, in_=s12), inc=sv_m)
                    P.wait("v", sv_m, sv_m.n)
                    P.op("v", lambda E: E.tensor_tensor(out=w1, in0=m1, in1=s12, op=ALU.mult), inc=sv_m)
                    P.op("v", lambda E: E.tensor_tensor(out=w2, in0=m2, in1=s12, op=ALU.mult), inc=sv_m)
                    P.wait("v", sv_m, sv_m.n)
                    P.op("v", lambda E: E.tensor_scalar(out=msk1, in0=msk1, scalar1=w1, scalar2=None, op0=ALU.mult), inc=sv_m)
                    P.op("v", lambda E: E.tensor_scalar(out=msk2, in0=msk2, scalar1=w2, scalar2=None, op0=ALU.mult), inc=sv_m)
                    P.wait("v", sv_m, sv_m.n)
                    n_cmb = P.op("v", lambda E: E.tensor_tensor(out=comb[:], in0=msk1, in1=msk2, op=ALU.add), inc=sv_m)
                    P.wait("t", sv_m, n_cmb)
                    n_tp = P.op("t", lambda E: E.transpose(ps_r[0:8, 0:128], comb[:], ident32[:]), inc=st_m)
                    P.wait("s", st_m, n_tp)
                    P.wait("s", S["m_ce"], S["m_ce"].n)
                    P.op("s", lambda E, tc=tc: E.copy(
                        out=ce_all[0:8, tc * 128:(tc + 1) * 128], in_=ps_r[0:8, 0:128]), inc=ss_m)
                P.wait("g", ss_m, ss_m.n)
                for e in range(NE):
                    P.dma("g", ce_rows[:, e, :], ce_all[e:e + 1, :], S["m_ce"])
                P.wait("t", S["m_ce"], S["m_ce"].n)
                war_w = {}
                war_h = {}
                P.wait("y", st_m, st_m.n)
                P.wait("y", S["ln_v"], S["ln_v"].n)
                P.wait("y", S["ln_s"], S["ln_s"].n)
                P.wait("y", S["cc0b"], CCV)
                for e in range(NE):
                    P.wait("t", sv_m, sv_m.n)
                    n_bc = P.op("t", lambda E, e=e: E.matmul(
                        pbc_m[:, 0:512], ones_row16[:], ce_rows[:, e, :],
                        start=True, stop=True), inc=st_m)
                    for ic in range(NIC):
                        par = (e * NIC + ic) % 2
                        idx = e * NIC + ic
                        wgsem = S[f"m_w{par}"]
                        if idx >= 2:
                            P.wait("y", st_m, war_w[idx - 2])
                        P.dma("y", m_wgst[:, par], gath_moe(e, 0)[:, ic * 128:(ic + 1) * 128].rearrange(
                            "(cc p) i -> p cc i", p=128), wgsem)
                        n_w = P.dma("y", m_wust[:, par], gath_moe(e, 1)[:, ic * 128:(ic + 1) * 128].rearrange(
                            "(cc p) i -> p cc i", p=128), wgsem)
                        P.wait("t", wgsem, n_w)
                        pg, pu = (ps[0], ps[2]) if idx % 2 == 0 else (ps[1], ps[4])
                        if idx >= 2:
                            P.wait("t", ss_m, war_h.get((idx - 2, "s"), 0))
                            P.wait("t", sv_m, war_h.get((idx - 2, "v"), 0))
                        for cc in range(NCH):
                            P.op("t", lambda E, cc=cc, par=par, pg=pg: E.matmul(
                                pg[:, 0:512], m_wgst[:, par, cc, :], m_xln[:, cc, 0:T2],
                                start=(cc == 0), stop=(cc == NCH - 1)))
                        for cc in range(NCH):
                            P.op("t", lambda E, cc=cc, par=par, pu=pu: E.matmul(
                                pu[:, 0:512], m_wust[:, par, cc, :], m_xln[:, cc, 0:T2],
                                start=(cc == 0), stop=(cc == NCH - 1)),
                                inc=(st_m if cc == NCH - 1 else None))
                        n_mm = st_m.n
                        war_w[idx] = n_mm
                        P.wait("s", st_m, n_mm)
                        if idx >= 2:
                            P.wait("s", sv_m, war_h.get((idx - 2, "v"), 0))
                        gpar = idx % 2
                        n_sil = P.op("s", lambda E, gpar=gpar, pg=pg: E.activation(
                            out=m_gsil[:, gpar, :], in_=pg[:, 0:512], func=AF.Silu), inc=ss_m)
                        war_h[(idx, "s")] = n_sil
                        P.wait("v", ss_m, n_sil)
                        P.wait("v", st_m, n_mm)
                        dst = m_hall[:, e, ic, :]
                        P.op("v", lambda E, dst=dst, gpar=gpar, pu=pu: E.tensor_tensor(
                            out=dst, in0=m_gsil[:, gpar, :], in1=pu[:, 0:512], op=ALU.mult), inc=sv_m)
                        P.wait("v", sv_m, sv_m.n)
                        war_h[(idx, "v")] = P.op("v", lambda E, dst=dst: E.tensor_tensor(
                            out=dst, in0=dst, in1=pbc_m[:, 0:512], op=ALU.mult), inc=sv_m)
                P.barrier()
                war_d = {}
                war_de = {}
                for dc in range(NCH):
                    pd = ps[dc % 2]
                    if dc >= 2:
                        P.wait("t", sv_m, war_de[dc - 2])
                    for e in range(NE):
                        idx = dc * NE + e
                        wpar = idx % 2
                        wdsem = S["m_wd"] if wpar == 0 else S["m_wd1"]
                        if idx >= 2:
                            P.wait("y", st_m, war_d[idx - 2])
                        n_w = P.dma("y", m_wdst[:, wpar], gath_moe(e, 2)[:, dc * 128:(dc + 1) * 128].rearrange(
                            "(ic p) d -> p ic d", p=128), wdsem)
                        P.wait("t", wdsem, n_w)
                        for ic in range(NIC):
                            P.op("t", lambda E, e=e, ic=ic, pd=pd, wpar=wpar: E.matmul(
                                pd[:, 0:512], m_wdst[:, wpar, ic, :], m_hall[:, e, ic, :],
                                start=(e == 0 and ic == 0), stop=(e == NE - 1 and ic == NIC - 1)),
                                inc=(st_m if ic == NIC - 1 else None))
                        war_d[idx] = st_m.n
                    n_dmm = st_m.n
                    P.wait("v", st_m, n_dmm)
                    sdst = stream[:, dc, t0:t0 + T2]
                    war_de[dc] = P.op("v", lambda E, sdst=sdst, pd=pd: E.scalar_tensor_tensor(
                        out=sdst, in0=pd[:, 0:512], scalar=0.5, in1=sdst,
                        op0=ALU.mult, op1=ALU.add), inc=sv_m)
                P.barrier()

        run_moe()
        dump_stream(3)

        run_xln(S["cc3"])
        n_sp2 = P.dma("g", spill.ap().rearrange("(cc p) t -> p cc t", p=128), stream[:], S["sp"])

        def run_attn1():
            mk_dmask(a_dm1, FILL_L1)
            aB["dmask"] = a_dm1
            for e in ("y", "g", "s", "t", "v"):
                P.wait(e, S["sp"], S["sp"].n)
            P.wait("y", S["cc3"], CCP)
            P.barrier()
            proj_pass("wqT1_s", 1024,
                      lambda oc, tw: a_q[:, oc, tw * 512:(tw + 1) * 512])
            proj_pass("wkT_s", 256,
                      lambda oc, tw: a_kf[:, oc, tw * 512:(tw + 1) * 512])
            proj_pass("wvT_s", 256,
                      lambda oc, tw: a_vf[:, oc, tw * 512:(tw + 1) * 512])
            for kv in range(2):
                for kc in range(16):
                    tpar = kc % 2
                    src = a_vf[:, kv, kc * 128:(kc + 1) * 128]
                    P.wait("t", ss_a, ss_a.n)
                    n_tp = P.op("t", lambda E, tpar=tpar, src=src: E.transpose(
                        pt[tpar][:, 0:128], src, ident16[:]), inc=st_a)
                    P.wait("s", st_a, n_tp)
                    dst = a_vtm[:, kc, kv * 128:(kv + 1) * 128]
                    P.op("s", lambda E, dst=dst, tpar=tpar: E.copy(
                        out=dst, in_=pt[tpar][:, 0:128]), inc=ss_a)
            P.barrier()
            for h in range(NH_LOC):
                def scores_mm(_h, qt, w, ps_ap, h=h):
                    lhsT = a_q[:, h, qt * 128:(qt + 1) * 128]
                    rhs = a_kf[:, h // 4, w * 512:(w + 1) * 512]
                    P.op("t", lambda E, lhsT=lhsT, rhs=rhs, ps_ap=ps_ap: E.matmul(
                        ps_ap, lhsT, rhs, start=True, stop=True), inc=st_a)
                    return st_a.n

                emit_attn_core(P, aB, asems, scores_mm,
                               lambda _h, kc, h=h: a_vtm[:, kc, (h // 4) * 128:(h // 4 + 1) * 128],
                               lambda _h, quad, src, h=h: P.dma(
                                   "g", y_o.ap()[h * 128:(h + 1) * 128,
                                                 quad * 512:(quad + 1) * 512],
                                   src, so_a[0]),
                               FILL_L1, nheads=1)
                P.barrier()
            P.wait("g", so_a[0], so_a[0].n)
            P.wait("g", so_a[1], so_a[1].n)
            gather(y_o.ap(), yg.ap(), 1024, S["cc4"], [[0, 1], [2, 3], [4, 5], [6, 7]])
            P.barrier()

        run_attn1()

        P.dma("g", stream[:], spill.ap().rearrange("(cc p) t -> p cc t", p=128), S["sp"])
        o_ao = at("o_ao", [128, NCH, T], BF16, AR)
        o_wst = at("o_wst", [128, 2, NCH, 128], BF16, AR + 32768)
        o_stg = at("o_stg", [128, 2, 2, 1024], BF16, AR + 40960)

        def run_outproj():
            P.wait("g", S["cc4"], CCP)
            P.wait("g", S["sp"], S["sp"].n)
            war_st = {}
            for cc in range(NCH):
                par = cc % 2
                r, hcl = divmod(cc, 8)
                src = yg.ap()[r * 1024 + hcl * 128: r * 1024 + (hcl + 1) * 128, :].rearrange(
                    "p (h t) -> p h t", t=T)
                P.wait("g", sv_a, war_st.get(par, 0))
                n_l = P.dma("g", o_stg[:, par], src, S[f"oio{par}"])
                P.wait("v", S[f"oio{par}"], n_l)
                dst = o_ao[:, cc, :]
                P.op("v", lambda E, dst=dst, par=par: E.tensor_scalar(
                    out=dst, in0=o_stg[:, par, 0], scalar1=cfgm[:, 0:1], scalar2=None,
                    op0=ALU.mult), inc=sv_a)
                P.wait("v", sv_a, sv_a.n)
                war_st[par] = P.op("v", lambda E, dst=dst, par=par: E.scalar_tensor_tensor(
                    out=dst, in0=o_stg[:, par, 1], scalar=cfgm[:, 1:2], in1=dst,
                    op0=ALU.mult, op1=ALU.add), inc=sv_a)
            P.barrier()
            wo_r = wl_ap("woT1").rearrange("(dc p) (cc c) -> cc p dc c", p=128, c=128)
            war_w = {}
            war_ev = {}
            for cc in range(NCH):
                par = cc % 2
                if cc >= 2:
                    P.wait("y", st_a, war_w[cc - 2])
                n_w = P.dma("y", o_wst[:, par], wo_r[cc], S[f"o_w{par}"])
                P.wait("t", S[f"o_w{par}"], n_w)
                for tw in range(2):
                    pp = ps[tw % 2]
                    if cc >= 1:
                        P.wait("t", sv_a, war_ev[(cc - 1, tw)])
                    for dc in range(NCH):
                        lhsT = o_wst[:, par, dc, :]
                        rhs = o_ao[:, dc, tw * 512:(tw + 1) * 512]
                        P.op("t", lambda E, lhsT=lhsT, rhs=rhs, dc=dc, pp=pp: E.matmul(
                            pp[:, 0:512], lhsT, rhs, start=(dc == 0), stop=(dc == NCH - 1)),
                            inc=(st_a if dc == NCH - 1 else None))
                    n_mm = st_a.n
                    war_w[cc] = n_mm
                    P.wait("v", st_a, n_mm)
                    P.wait("v", S["sp"], S["sp"].n)
                    sdst = stream[:, cc, tw * 512:(tw + 1) * 512]
                    bslice = bo_sb[:, cc:cc + 1]
                    war_ev[(cc, tw)] = P.op("v", lambda E, sdst=sdst, pp=pp, bslice=bslice: E.scalar_tensor_tensor(
                        out=sdst, in0=pp[:, 0:512], scalar=bslice, in1=sdst,
                        op0=ALU.add, op1=ALU.add), inc=sv_a)
            P.barrier()

        run_outproj()
        dump_stream(4)

        run_moe()

        P.wait("g", S["m_v"], S["m_v"].n)
        n_out = P.dma("g", out_t.rearrange("(cc p) t -> p cc t", p=128), stream[:], S["io"])
        P.wait("g", S["io"], n_out)

        with nc.Block() as block:
            P.finalize(block)
    return nc


try:
    from prog_fused import WL_PAD, MOE_PAD, build_fused
except ImportError:
    pass
from concourse.bass_utils import run_bass_kernel_spmd

_fused_nc = [None]
_last_results = [None]


def kernel(**inputs):
    f = np.float32
    x = np.asarray(inputs["L0_attn_x"], dtype=f)
    Wq0 = np.asarray(inputs["L0_attn_Wq_w"], f)
    Wdkv = np.asarray(inputs["L0_attn_Wdkv_w"], f)
    Wuk = np.asarray(inputs["L0_attn_Wuk_w"], f)
    Wuv = np.asarray(inputs["L0_attn_Wuv_w"], f)
    Wo0 = np.asarray(inputs["L0_attn_Wo_w"], f)
    Wq1 = np.asarray(inputs["L1_attn_Wq"], f)
    Wk1 = np.asarray(inputs["L1_attn_Wk"], f)
    Wv1 = np.asarray(inputs["L1_attn_Wv"], f)
    Wo1 = np.asarray(inputs["L1_attn_Wo"], f)
    bo = np.asarray(inputs["L1_attn_bo"], f)
    Wr = np.asarray(inputs["L1_ffn_Wr"], f)
    Wg1 = np.asarray(inputs["L1_ffn_Wg"], f)
    Wu1 = np.asarray(inputs["L1_ffn_Wu"], f)
    Wd1 = np.asarray(inputs["L1_ffn_Wd"], f)
    gate_p = np.asarray(inputs["L0_ffn_gate_W_packed"], np.int32)
    gate_s = np.asarray(inputs["L0_ffn_gate_scales"], f)
    up_p = np.asarray(inputs["L0_ffn_up_W_packed"], np.int32)
    up_s = np.asarray(inputs["L0_ffn_up_scales"], f)
    down_p = np.asarray(inputs["L0_ffn_down_W_packed"], np.int32)
    down_s = np.asarray(inputs["L0_ffn_down_scales"], f)

    Wq0T = np.ascontiguousarray(Wq0.T)
    Wo0T = np.ascontiguousarray(Wo0.T)
    WdkvT = np.ascontiguousarray(Wdkv.T)
    Wq1T = np.ascontiguousarray(Wq1.T)
    Wk1T = np.ascontiguousarray(Wk1.T)
    Wv1T = np.ascontiguousarray(Wv1.T)
    Wo1T = np.ascontiguousarray(Wo1.T)

    NLOC = 16384 // 8
    in_maps = []
    for c in range(8):
        b, half = divmod(c, 2)
        hb = half * 8
        kvb = half * 2
        wl_parts = [
            Wq0[:, hb * 128:(hb + 8) * 128],
            Wq0T[:, hb * 128:(hb + 8) * 128],
            Wo0T[:, hb * 128:(hb + 8) * 128],
            Wuk, Wuv, WdkvT,
            Wq1T[:, hb * 128:(hb + 8) * 128],
            Wk1T[:, kvb * 128:(kvb + 2) * 128],
            Wv1T[:, kvb * 128:(kvb + 2) * 128],
            Wo1T, Wr,
        ]
        wl_flat = np.zeros(WL_PAD, f)
        o = 0
        for p_ in wl_parts:
            wl_flat[o:o + p_.size] = np.ascontiguousarray(p_).ravel()
            o += p_.size
        moe_flat = np.zeros(MOE_PAD, f)
        o = 0
        for p_ in (Wg1[c], Wu1[c], Wd1[c]):
            moe_flat[o:o + p_.size] = p_.ravel()
            o += p_.size
        cfg = np.zeros((128, 2), f)
        cfg[:, 0] = 1.0 if half == 0 else 0.0
        cfg[:, 1] = 1.0 - cfg[:, 0]
        in_maps.append(dict(
            x_in=np.ascontiguousarray(x[b, half * 1024:(half + 1) * 1024, :].T),
            wl_f32=wl_flat.reshape(WL_PAD // 512, 512),
            moe_f32=moe_flat.reshape(MOE_PAD // 512, 512),
            gate_p=gate_p[c * NLOC:(c + 1) * NLOC], gate_s=gate_s[c * NLOC:(c + 1) * NLOC],
            up_p=up_p[c * NLOC:(c + 1) * NLOC], up_s=up_s[c * NLOC:(c + 1) * NLOC],
            down_p=down_p[c * 256:(c + 1) * 256], down_s=down_s[c * 256:(c + 1) * 256],
            bo_in=bo, cfg_in=cfg,
        ))

    if _fused_nc[0] is None:
        _fused_nc[0] = build_fused()
    res = run_bass_kernel_spmd(_fused_nc[0], in_maps, core_ids=list(range(8)))
    _last_results[0] = res.results
    out = np.empty((4, 2048, 2048), f)
    for c in range(8):
        b, half = divmod(c, 2)
        out[b, half * 1024:(half + 1) * 1024, :] = np.asarray(res.results[c]["out_t"]).T
    return out


# version 08
# speedup vs baseline: 1.1844x; 1.0929x over previous
import numpy as np
from contextlib import ExitStack
import concourse.bass as bass
import concourse.mybir as mybir

F32 = mybir.dt.float32
BF16 = mybir.dt.bfloat16
I32 = mybir.dt.int32
AF = mybir.ActivationFunctionType
ALU = mybir.AluOpType
AX = mybir.AxisListType

ENGINES = ("y", "v", "s", "t", "g")
ENGINE_ATTR = {"y": "sync", "v": "vector", "s": "scalar", "t": "tensor", "g": "gpsimd"}

C = 2048
NCH = 16
GELU_K = 0.7978845608028654


class Sem:
    __slots__ = ("h", "n", "name")
    def __init__(self, h, name):
        self.h = h
        self.n = 0
        self.name = name
    def bump(self, k=1):
        self.n += k
        return self.n


class Prog:
    def __init__(self, nc, es):
        self.nc = nc
        self.es = es
        self.plan = {e: [] for e in ENGINES}
        self.sems = {}
        self._barn = 0
        self.last_inc = {e: None for e in ENGINES}
        self.dirty = {e: False for e in ENGINES}
        self._dummy = None

    def sem(self, name):
        s = Sem(self.es.enter_context(self.nc.semaphore(name)), name)
        self.sems[name] = s
        return s

    def const_f32(self, *values):
        if not hasattr(self, "_const_sem"):
            self._const_sem = self.sem("constf")
        for v in values:
            t = self.es.enter_context(self.nc.sbuf_tensor(f"cf_{str(v).replace('.','_').replace('-','m')}", [128, 1], F32))
            self.op("v", lambda E, t=t, v=v: E.memset(t[:], float(v)), inc=self._const_sem)
            self.nc.const_aps.register(F32, float(v), t[:])
        return self._const_sem

    def op(self, eng, fn, inc=None, k=1):
        if inc is None:
            self.plan[eng].append(fn)
            self.dirty[eng] = True
            return None
        n = inc.bump(k)
        self.plan[eng].append(lambda E, fn=fn: fn(E).then_inc(inc.h, k))
        self.last_inc[eng] = (inc, n)
        self.dirty[eng] = False
        return n

    def wait(self, eng, sem, n):
        if n > 0:
            self.plan[eng].append(lambda E: E.wait_ge(sem.h, n))

    def coll(self, fn, sem):
        n = sem.bump(1)
        self.plan["g"].append(lambda E, fn=fn: fn(E).then_inc(sem.h))
        self.last_inc["g"] = (sem, n)
        self.dirty["g"] = False
        return n

    def dma(self, eng, out, in_, sem, **kw):
        n = sem.bump(16)
        self.plan[eng].append(
            lambda E: E.dma_start(out=out, in_=in_, **kw).then_inc(sem.h, 16))
        self.last_inc[eng] = (sem, n)
        self.dirty[eng] = False
        return n

    def barrier(self, engs=("v", "s", "t", "g")):
        if self._dummy is None:
            self._dummy = self.es.enter_context(
                self.nc.sbuf_tensor("bar_dummy", [128, 8], F32))
            self._bars = {e: self.sem(f"bar_{e}") for e in ENGINES}
        d = self._dummy
        for e in engs:
            if self.dirty[e]:
                s = self._bars[e]
                if e == "v":
                    self.op("v", lambda E: E.memset(d[:, 0:2], 0.0), inc=s)
                elif e == "s":
                    self.op("s", lambda E: E.mul(out=d[:, 2:4], in_=d[:, 2:4], mul=1.0), inc=s)
                elif e == "g":
                    self.op("g", lambda E: E.memset(d[:, 4:6], 0.0), inc=s)
                else:
                    raise AssertionError(f"dirty engine {e} needs a real inc'd op before barrier")
        self._barn += 1
        for e in tuple(engs) + ("y",):
            for o in engs:
                if o != e and self.last_inc[o] is not None:
                    sem, n = self.last_inc[o]
                    self.wait(e, sem, n)

    def finalize(self, block):
        for e in ENGINES:
            fns = self.plan[e]
            if not fns:
                continue
            def runner(E, fns=fns):
                for f in fns:
                    f(E)
            getattr(block, ENGINE_ATTR[e])(runner)


class Bufs:
    def __init__(self, nc, es, T_CORE, want=("ffn",)):
        sb = lambda name, shape, dt=F32: es.enter_context(nc.sbuf_tensor(name, shape, dt))
        ps = lambda name, shape, dt=F32: es.enter_context(nc.psum_tensor(name, shape, dt))
        T = T_CORE
        self.T = T
        self.T2 = T // 2
        self.stream = sb("stream", [128, NCH, T])
        self.x_ln = sb("x_ln", [128, NCH, T], BF16)
        self.big_a = sb("big_a", [128, NCH, T], BF16)
        self.rows = sb("rows", [1, 4, self.T2])
        self.ones_col16 = sb("ones_col16", [128, 1], BF16)
        self.ones_row32 = sb("ones_row32", [1, 128])
        self.ident16 = sb("ident16", [128, 128], BF16)
        self.xb16 = self.big_a[:, :, 0:self.T2]
        self.xsq16 = self.big_a[:, :, self.T2:2 * self.T2]
        if "ffn" in want:
            self.wg_buf = sb("wg_buf", [128, 2, NCH, 128], BF16)
            self.wu_buf = sb("wu_buf", [128, 2, NCH, 128], BF16)
            self.wd_buf = sb("wd_buf", [128, 2, NCH, 128], BF16)
            self.gelu_sb = sb("gelu_sb", [128, 2, self.T2], BF16)
            self.ds_acc = sb("ds_acc", [128, NCH, self.T2])
        self.ps = [ps(f"ps{i}", [128, 512]) for i in range(6)]
        self.ps_small = ps("ps_small", [1, 512])
        self.pt = ps("pt", [128, 1024], BF16)


def emit_consts(P, B):
    sg = P.sem("const_g")
    sv = P.sem("const_v")
    P.op("v", lambda E: E.memset(B.ones_col16[:], 1.0))
    P.op("v", lambda E: E.memset(B.ones_row32[:], 1.0), inc=sv)
    P.op("g", lambda E: E.memset(B.ident16[:], 0.0), inc=sg)
    P.wait("g", sg, 1)
    P.op("g", lambda E: E.affine_select(
        out=B.ident16[:], in_=B.ident16[:], compare_op=ALU.not_equal,
        fill=1.0, base=0, pattern=[[-1, 128]], channel_multiplier=1), inc=sg)
    P.wait("t", sv, 1)
    P.wait("t", sg, 2)
    return sg, sv


def emit_ln(P, B, half, sems, war=(), dst_t0=None):
    T2 = B.T2
    t0 = half * T2
    dt0 = t0 if dst_t0 is None else dst_t0
    ss, sv, st = sems["ln_s"], sems["ln_v"], sems["ln_t"]
    src = B.stream[:, :, t0:t0 + T2]
    dst = B.x_ln[:, :, dt0:dt0 + T2]
    negmean = B.rows[:, 0, :]
    msq = B.rows[:, 1, :]
    var = B.rows[:, 2, :]
    rstd = B.rows[:, 3, :]
    lnv = B.rows[:, 1, :]
    pbc = B.ps[5][:, 0:T2]
    bc = pbc.unsqueeze(1).broadcast_to([128, NCH, T2])

    P.wait("s", st, st.n)
    P.wait("t", sv, sv.n)
    for (sem, n) in war:
        P.wait("s", sem, n)
        P.wait("v", sem, n)

    n_cast = P.op("s", lambda E: E.copy(out=B.xb16, in_=src), inc=ss)
    n_sq = P.op("s", lambda E: E.activation(out=B.xsq16, in_=src, func=AF.Square), inc=ss)

    P.wait("t", ss, n_cast)
    for i in range(NCH):
        rhs = B.xb16[:, i, :]
        P.op("t", lambda E, rhs=rhs, i=i: E.matmul(
            B.ps_small[:, 0:T2], B.ones_col16[:], rhs,
            start=(i == 0), stop=(i == NCH - 1)), inc=(st if i == NCH - 1 else None))
    n_sums = st.n
    P.wait("v", st, n_sums)
    n_nm = P.op("v", lambda E: E.tensor_single_scalar(
        out=negmean, in_=B.ps_small[:, 0:T2], scalar=-1.0 / C, op=ALU.mult), inc=sv)
    P.wait("t", sv, n_nm)
    P.wait("t", ss, n_sq)
    for i in range(NCH):
        rhs = B.xsq16[:, i, :]
        P.op("t", lambda E, rhs=rhs, i=i: E.matmul(
            B.ps_small[:, 0:T2], B.ones_col16[:], rhs,
            start=(i == 0), stop=(i == NCH - 1)), inc=(st if i == NCH - 1 else None))
    n_sumsq = st.n
    P.wait("v", st, n_sumsq)
    P.op("v", lambda E: E.tensor_scalar(
        out=msq, in0=B.ps_small[:, 0:T2], scalar1=1.0 / C, scalar2=1e-5,
        op0=ALU.mult, op1=ALU.add), inc=sv)
    P.wait("v", sv, sv.n)
    P.op("v", lambda E: E.tensor_tensor(out=var, in0=negmean, in1=negmean, op=ALU.mult), inc=sv)
    P.wait("v", sv, sv.n)
    n_var = P.op("v", lambda E: E.tensor_sub(out=var, in0=msq, in1=var), inc=sv)
    P.wait("s", sv, n_var)
    P.op("s", lambda E: E.activation(out=lnv, in_=var, func=AF.Ln), inc=ss)
    P.wait("s", ss, ss.n)
    n_rstd = P.op("s", lambda E: E.activation(out=rstd, in_=lnv, func=AF.Exp, scale=-0.5), inc=ss)
    P.wait("t", sv, n_nm)
    n_bc_m = P.op("t", lambda E: E.matmul(
        pbc, B.ones_row32[:], negmean, start=True, stop=True), inc=st)
    P.wait("v", st, n_bc_m)
    n_xm = P.op("v", lambda E: E.tensor_tensor(out=dst, in0=src, in1=bc, op=ALU.add), inc=sv)
    P.wait("t", sv, n_xm)
    P.wait("t", ss, n_rstd)
    n_bc_r = P.op("t", lambda E: E.matmul(
        pbc, B.ones_row32[:], rstd, start=True, stop=True), inc=st)
    P.wait("v", st, n_bc_r)
    n_out = P.op("v", lambda E: E.tensor_tensor(out=dst, in0=dst, in1=bc, op=ALU.mult), inc=sv)
    return n_out


def emit_ffn0(P, B, w, sems, n_mlp=16384, xln_half=False):
    NC = n_mlp // 128
    NGRP = NC // NCH
    T2 = B.T2
    ss, sv, st = sems["f_s"], sems["f_v"], sems["f_t"]
    wsem = sems["f_w"]
    dsem = sems["f_d"]
    lnv = sems["ln_v"]

    for half in (0, 1):
        n_ln = emit_ln(P, B, half, sems, war=[(st, st.n)],
                       dst_t0=(0 if xln_half else None))
        P.wait("t", lnv, n_ln)
        t0 = half * T2
        x0 = 0 if xln_half else t0
        w_ready = {}
        mm_done = {}
        gel_done = {}
        ev_done = {}
        dmm_done = {}
        dev_done = {}

        def load_w(jg):
            if "wg_gate" in sems:
                sems["wg_gate"](jg)
            P.dma("y", B.wg_buf[:, jg % 2], w["wg_t"](jg), wsem[jg % 2])
            w_ready[jg] = P.dma("y", B.wu_buf[:, jg % 2], w["wu_t"](jg), wsem[jg % 2])

        for g in range(NGRP):
            for j in range(NCH):
                jg = g * NCH + j
                par = jg % 2
                if jg == 0:
                    load_w(0)
                    if NC > 1:
                        load_w(1)
                psg, psu = B.ps[0 + par][:, 0:T2], B.ps[2 + par][:, 0:T2]
                P.wait("t", wsem[jg % 2], w_ready[jg])
                if jg >= 2:
                    P.wait("t", sv, ev_done[jg - 2])
                    P.wait("t", ss, gel_done[jg - 2])
                for cc in range(NCH):
                    lhsT = B.wg_buf[:, par, cc, :]
                    rhs = B.x_ln[:, cc, x0:x0 + T2]
                    P.op("t", lambda E, lhsT=lhsT, rhs=rhs, cc=cc, psg=psg: E.matmul(
                        psg, lhsT, rhs, start=(cc == 0), stop=(cc == NCH - 1)))
                for cc in range(NCH):
                    lhsT = B.wu_buf[:, par, cc, :]
                    rhs = B.x_ln[:, cc, x0:x0 + T2]
                    P.op("t", lambda E, lhsT=lhsT, rhs=rhs, cc=cc, psu=psu: E.matmul(
                        psu, lhsT, rhs, start=(cc == 0), stop=(cc == NCH - 1)),
                        inc=(st if cc == NCH - 1 else None))
                mm_done[jg] = st.n
                if jg + 2 < NC:
                    P.wait("y", st, mm_done[jg])
                    load_w(jg + 2)
                P.wait("s", st, mm_done[jg])
                if jg >= 2:
                    P.wait("s", sv, ev_done[jg - 2])
                gdst = B.gelu_sb[:, par, :]
                gel_done[jg] = P.op("s", lambda E, gdst=gdst, psg=psg: E.activation(
                    out=gdst, in_=psg, func=AF.Gelu_apprx_tanh), inc=ss)
                P.wait("v", ss, gel_done[jg])
                P.wait("v", st, mm_done[jg])
                if g > 0 and j == 0:
                    P.wait("v", st, dmm_done[NCH - 1])
                idst = B.big_a[:, j, t0:t0 + T2]
                ev_done[jg] = P.op("v", lambda E, idst=idst, gdst=gdst, psu=psu: E.tensor_tensor(
                    out=idst, in0=gdst, in1=psu, op=ALU.mult), inc=sv)
                if "jg_feeder" in sems:
                    sems["jg_feeder"]()
            if "wd_gate" in sems:
                gsem, gn = sems.pop("wd_gate")
                P.wait("y", gsem, gn)
            n_grp_ev = sv.n
            for dc in range(NCH):
                par = dc % 2
                psd = B.ps[4][:, 0:T2]
                if dc >= 2:
                    P.wait("y", st, dmm_done[dc - 2])
                ndw = P.dma("y", B.wd_buf[:, par], w["wd_t"](dc, g), dsem[par])
                P.wait("t", dsem[par], ndw)
                P.wait("t", sv, n_grp_ev)
                if dc >= 1:
                    P.wait("t", sv, dev_done[dc - 1])
                for j in range(NCH):
                    lhsT = B.wd_buf[:, par, j, :]
                    rhs = B.big_a[:, j, t0:t0 + T2]
                    P.op("t", lambda E, lhsT=lhsT, rhs=rhs, j=j, psd=psd: E.matmul(
                        psd, lhsT, rhs, start=(j == 0), stop=(j == NCH - 1)),
                        inc=(st if j == NCH - 1 else None))
                dmm_done[dc] = st.n
                P.wait("v", st, dmm_done[dc])
                ddst = B.ds_acc[:, dc, :]
                if g == 0:
                    dev_done[dc] = P.op("v", lambda E, ddst=ddst, psd=psd: E.tensor_copy(
                        out=ddst, in_=psd), inc=sv)
                else:
                    dev_done[dc] = P.op("v", lambda E, ddst=ddst, psd=psd: E.tensor_tensor(
                        out=ddst, in0=ddst, in1=psd, op=ALU.add), inc=sv)
        P.wait("v", sv, sv.n)
        sdst = B.stream[:, :, t0:t0 + T2]
        P.op("v", lambda E, sdst=sdst: E.scalar_tensor_tensor(
            out=sdst, in0=B.ds_acc[:], scalar=0.5, in1=sdst,
            op0=ALU.mult, op1=ALU.add), inc=sv)
    return sv.n


def make_ffn0_sems(P):
    return {
        "ln_s": P.sem("ln_s"), "ln_v": P.sem("ln_v"), "ln_t": P.sem("ln_t"),
        "f_s": P.sem("f_s"), "f_v": P.sem("f_v"), "f_t": P.sem("f_t"),
        "f_w": [P.sem("f_w0"), P.sem("f_w1")],
        "f_d": [P.sem("f_d0"), P.sem("f_d1")],
    }


"""attn0a: MLA attention, head-sharded. Per core: 8 heads x full 2048-token causal.
in: xln_all [2, 2048, 1024] bf16 ([t-half][c][t_local]); weight slices bf16.
out: y_mine [1024, 2048] bf16 (my 8 head-chunks' output rows, all tokens)."""

FILL_L0 = -1e30

NH_LOC = 8
NQT = 16
TFULL = 2048
RSQ = 1.0 / np.sqrt(128.0)


def emit_attn_core(P, B, sems, scores_mm, av_lhsT, out_dma, fill, nheads=NH_LOC):
    ss, sv, st, sg = sems["a_s"], sems["a_v"], sems["a_t"], sems["a_g"]
    so = sems["a_o"]
    scores = B["scores"]
    attn_b = B["attn_b"]
    attnT = B["attnT"]
    rowm = B["rowm"]
    pt = B["pt"]
    sc_ps = B["sc_ps"]
    y_ps = B["y_ps"]
    ystage = B["ystage"]
    dmask = B["dmask"]
    war_pt = B.setdefault("_war_pt", {})
    war_sc = B.setdefault("_war_sc", {})
    war_y = B.setdefault("_war_y", {})
    for h in range(nheads):
        for quad in range(4):
            nw = quad + 1
            nk = nw * 512
            for j in range(4):
                qt = quad * 4 + j
                P.wait("s", sv, sv.n)
                for w in range(nw):
                    par = w % 2
                    ps = sc_ps[par]
                    we, wn = war_sc.get(par, (None, 0))
                    if we is not None:
                        P.wait("t", we, wn)
                    n_mm = scores_mm(h, qt, w, ps[:, 0:512])
                    dst = scores[:, w * 512:(w + 1) * 512]
                    if w == quad:
                        P.wait("v", st, n_mm)
                        msk = dmask[:, qt % 4, :]
                        n_e = P.op("v", lambda E, dst=dst, ps=ps, msk=msk: E.tensor_tensor(
                            out=dst, in0=ps[:, 0:512], in1=msk, op=ALU.add), inc=sv)
                        war_sc[par] = (sv, n_e)
                    else:
                        P.wait("s", st, n_mm)
                        n_e = P.op("s", lambda E, dst=dst, ps=ps: E.copy(
                            out=dst, in_=ps[:, 0:512]), inc=ss)
                        war_sc[par] = (ss, n_e)
                P.wait("v", sv, sv.n)
                P.wait("v", ss, ss.n)
                n_max = P.op("v", lambda E, nk=nk: E.reduce_max(
                    out=rowm[:, 0:1], in_=scores[:, 0:nk], axis=AX.X, negate=True), inc=sv)
                P.wait("s", sv, n_max)
                P.wait("s", st, st.n)
                n_exp = P.op("s", lambda E, nk=nk: E.activation(
                    out=attn_b[:, 0:nk], in_=scores[:, 0:nk], func=AF.Exp,
                    bias=rowm[:, 0:1], accum_out=rowm[:, 1:2]), inc=ss)
                P.wait("v", ss, n_exp)
                n_rec = P.op("v", lambda E: E.reciprocal(out=rowm[:, 1:2], in_=rowm[:, 1:2]), inc=sv)
                P.wait("v", sv, n_rec)
                n_nrm = P.op("v", lambda E, nk=nk: E.tensor_scalar_mul(
                    out=attn_b[:, 0:nk], in0=attn_b[:, 0:nk], scalar1=rowm[:, 1:2]), inc=sv)
                for kc in range(nw * 4):
                    tpar = kc % 2
                    src = attn_b[:, kc * 128:(kc + 1) * 128]
                    P.wait("t", sv, n_nrm)
                    P.wait("t", ss, war_pt.get(tpar, 0))
                    n_tp = P.op("t", lambda E, tpar=tpar, src=src: E.transpose(
                        pt[tpar][:, 0:128], src, B["ident16"][:]), inc=st)
                    P.wait("s", st, n_tp)
                    dstT = attnT[:, kc, j * 128:(j + 1) * 128]
                    war_pt[tpar] = P.op("s", lambda E, dstT=dstT, tpar=tpar: E.copy(
                        out=dstT, in_=pt[tpar][:, 0:128]), inc=ss)
            ypar = (quad % 2) if not B.get("y_single") else 0
            yp = y_ps[ypar]
            P.wait("t", ss, ss.n)
            P.wait("t", so[ypar], war_y.get(ypar, 0))
            for kc in range(nw * 4):
                lhsT = av_lhsT(h, kc)
                rhs = attnT[:, kc, :]
                P.op("t", lambda E, lhsT=lhsT, rhs=rhs, kc=kc, yp=yp, nw=nw: E.matmul(
                    yp[:, 0:512], lhsT, rhs, start=(kc == 0), stop=(kc == nw * 4 - 1)),
                    inc=(st if kc == nw * 4 - 1 else None))
            n_av = st.n
            P.wait("s", st, n_av)
            P.wait("s", so[ypar], war_y.get(ypar, 0))
            n_ye = P.op("s", lambda E, ypar=ypar, yp=yp: E.copy(
                out=ystage[:, ypar, :], in_=yp[:, 0:512]), inc=ss)
            P.wait("g", ss, n_ye)
            war_y[ypar] = out_dma(h, quad, ystage[:, ypar, :])
    return


def build_attn0(T_CORE=1024):
    TF = 2 * T_CORE
    nc = bass.Bass("TRN2", target_bir_lowering=False, debug=False)
    xln_all = nc.dram_tensor("xln_all", [2, 2048, T_CORE], BF16, kind="ExternalInput").ap()
    wq_nat_s = nc.dram_tensor("wq_nat_s", [2048, 1024], BF16, kind="ExternalInput").ap()
    wuk = nc.dram_tensor("wuk", [2048, 512], BF16, kind="ExternalInput").ap()
    wqT_s = nc.dram_tensor("wqT_s", [2048, 1024], BF16, kind="ExternalInput").ap()
    wuv = nc.dram_tensor("wuv", [2048, 512], BF16, kind="ExternalInput").ap()
    woT_s = nc.dram_tensor("woT_s", [2048, 1024], BF16, kind="ExternalInput").ap()
    wdkvT = nc.dram_tensor("wdkvT", [2048, 512], BF16, kind="ExternalInput").ap()
    y_mine = nc.dram_tensor("y_mine", [1024, TF], BF16, kind="ExternalOutput").ap()

    with ExitStack() as es:
        P = Prog(nc, es)
        sb = lambda name, shape, dt=F32: es.enter_context(nc.sbuf_tensor(name, shape, dt))
        ps = lambda name, shape, dt=F32: es.enter_context(nc.psum_tensor(name, shape, dt))
        xla = sb("xla", [128, NCH, TF], BF16)
        big1 = sb("big1", [128, NCH, TF // 2], BF16)
        cv = sb("cv", [128, NCH, 1024], BF16)
        ckv = sb("ckv", [128, 4, TF], BF16)
        keff = sb("keff", [128, NH_LOC, 512], BF16)
        qlat = sb("qlat", [128, 4, TF], BF16)
        scores = sb("scores", [128, 2048])
        attn_b = sb("attn_b", [128, 2048], BF16)
        attnT = sb("attnT", [128, 16, 512], BF16)
        rowm = sb("rowm", [128, 2])
        wst = sb("wst", [128, 1, NCH, 128], BF16)
        wbig = cv[:, :, 0:512]
        ystage = sb("ystage", [128, 2, 512], BF16)
        ident16 = sb("ident16", [128, 128], BF16)
        dmask = sb("dmask", [128, 4, 512], BF16)
        sc_ps = [ps("sc_ps0", [128, 512]), ps("sc_ps1", [128, 512])]
        y_ps = [ps("y_ps0", [128, 512]), ps("y_ps1", [128, 512])]
        pt = [ps("pt0", [128, 1024], BF16), ps("pt1", [128, 1024], BF16)]

        sems = {k: P.sem(k) for k in ("a_s", "a_v", "a_t", "a_g",
                                      "w0", "w1", "io")}
        sems["a_o"] = [P.sem("a_o0"), P.sem("a_o1")]
        ss, sv, st, sg = sems["a_s"], sems["a_v"], sems["a_t"], sems["a_g"]
        so = sems["a_o"]
        io = sems["io"]
        wsem = [sems["w0"], sems["w1"]]

        csg = P.sem("csg")
        P.op("g", lambda E: E.memset(ident16[:], 0.0), inc=csg)
        P.op("g", lambda E: E.memset(dmask[:], 0.0), inc=csg)
        P.wait("g", csg, 2)
        P.op("g", lambda E: E.affine_select(
            out=ident16[:], in_=ident16[:], compare_op=ALU.not_equal,
            fill=1.0, base=0, pattern=[[-1, 128]], channel_multiplier=1), inc=csg)
        for m in range(4):
            sl = dmask[:, m, :]
            P.op("g", lambda E, sl=sl, m=m: E.affine_select(
                out=sl, in_=sl, compare_op=ALU.is_ge, fill=FILL_L0,
                base=m * 128, pattern=[[-1, 512]], channel_multiplier=1), inc=csg)
        P.wait("t", csg, 7)
        P.wait("v", csg, 7)

        for r in range(2):
            P.dma("y", xla[:, :, r * T_CORE:(r + 1) * T_CORE],
                  xln_all[r].rearrange("(cc p) t -> p cc t", p=128), io)
        n_x = io.n
        n_wuk = P.dma("y", wbig, wuk.rearrange("(cc p) l -> p cc l", p=128), io)
        P.wait("t", io, n_wuk)

        wq_r = wq_nat_s.rearrange("(dc p) (sc s) -> sc p dc s", p=128, s=128)
        war_w = {}
        war_sc_ev = {}
        for sc in range(NH_LOC):
            if sc >= 1:
                P.wait("y", st, war_w[sc - 1])
            n_w = P.dma("y", wst[:, 0], wq_r[sc], wsem[0])
            P.wait("t", wsem[0], n_w)
            pp = sc_ps[sc % 2]
            if sc >= 2:
                P.wait("t", ss, war_sc_ev[sc - 2])
            for dc in range(NCH):
                P.op("t", lambda E, dc=dc, pp=pp: E.matmul(
                    pp[:, 0:512], wst[:, 0, dc, :], wbig[:, dc, :],
                    start=(dc == 0), stop=(dc == NCH - 1)),
                    inc=(st if dc == NCH - 1 else None))
            war_w[sc] = st.n
            P.wait("s", st, st.n)
            war_sc_ev[sc] = P.op("s", lambda E, sc=sc, pp=pp: E.mul(
                out=keff[:, sc, :], in_=pp[:, 0:512], mul=RSQ), inc=ss)
        P.barrier()

        P.wait("y", st, st.n)
        n_wuv = P.dma("y", wbig, wuv.rearrange("(dc p) l -> p dc l", p=128), io)
        woT_r = woT_s.rearrange("(dc p) (cw c) -> cw p dc c", p=128, c=512)
        veff = big1[:, 12:16, :].rearrange("p a t -> p a t")
        wo_stage = big1[:, 0:8, :].rearrange("p a t -> p (a t)").rearrange(
            "p (dc c) -> p dc c", c=512)
        P.wait("t", io, n_wuv)
        for cw in range(2):
            P.wait("y", st, st.n)
            n_w = P.dma("y", wo_stage, woT_r[cw], wsem[cw % 2])
            P.wait("t", wsem[cw % 2], n_w)
            for lc in range(4):
                pp = sc_ps[lc % 2]
                P.wait("t", ss, ss.n)
                for dc in range(NCH):
                    P.op("t", lambda E, dc=dc, lc=lc, pp=pp: E.matmul(
                        pp[:, 0:512], wbig[:, dc, lc * 128:(lc + 1) * 128],
                        wo_stage[:, dc, :], start=(dc == 0), stop=(dc == NCH - 1)),
                        inc=(st if dc == NCH - 1 else None))
                P.wait("s", st, st.n)
                dst = veff[:, lc, cw * 512:(cw + 1) * 512]
                P.op("s", lambda E, dst=dst, pp=pp: E.copy(out=dst, in_=pp[:, 0:512]), inc=ss)
            P.barrier()

        P.wait("y", st, st.n)
        n_wk = P.dma("y", wbig, wdkvT.rearrange("(cc p) l -> p cc l", p=128), io)
        P.wait("t", io, n_wk)
        for lc in range(4):
            for tw in range(4):
                pp = sc_ps[tw % 2]
                P.wait("t", ss, ss.n)
                for cc in range(NCH):
                    P.op("t", lambda E, cc=cc, lc=lc, tw=tw, pp=pp: E.matmul(
                        pp[:, 0:512], wbig[:, cc, lc * 128:(lc + 1) * 128],
                        xla[:, cc, tw * 512:(tw + 1) * 512],
                        start=(cc == 0), stop=(cc == NCH - 1)),
                        inc=(st if cc == NCH - 1 else None))
                P.wait("s", st, st.n)
                dst = ckv[:, lc, tw * 512:(tw + 1) * 512]
                P.op("s", lambda E, dst=dst, pp=pp: E.copy(out=dst, in_=pp[:, 0:512]), inc=ss)
        P.barrier()

        for kc in range(16):
            for cwi in range(2):
                pp = sc_ps[cwi]
                P.wait("t", ss, ss.n)
                for lc in range(4):
                    P.op("t", lambda E, lc=lc, kc=kc, cwi=cwi, pp=pp: E.matmul(
                        pp[:, 0:512], ckv[:, lc, kc * 128:(kc + 1) * 128],
                        veff[:, lc, cwi * 512:(cwi + 1) * 512],
                        start=(lc == 0), stop=(lc == 3)),
                        inc=(st if lc == 3 else None))
                P.wait("s", st, st.n)
                dst = cv[:, kc, cwi * 512:(cwi + 1) * 512]
                P.op("s", lambda E, dst=dst, pp=pp: E.copy(out=dst, in_=pp[:, 0:512]), inc=ss)
        P.barrier()

        q = big1[:].rearrange("p a t -> p (a t)").rearrange("p (h t) -> p h t", t=TF)
        wqT_r = wqT_s.rearrange("(cc p) (dq d) -> dq p cc d", p=128, d=128)
        for dqc in range(NH_LOC):
            P.wait("y", st, st.n)
            n_w = P.dma("y", wst[:, 0], wqT_r[dqc], wsem[0])
            P.wait("t", wsem[0], n_w)
            for tw in range(4):
                pp = sc_ps[tw % 2]
                P.wait("t", ss, ss.n)
                for cc in range(NCH):
                    P.op("t", lambda E, cc=cc, tw=tw, pp=pp: E.matmul(
                        pp[:, 0:512], wst[:, 0, cc, :],
                        xla[:, cc, tw * 512:(tw + 1) * 512],
                        start=(cc == 0), stop=(cc == NCH - 1)),
                        inc=(st if cc == NCH - 1 else None))
                P.wait("s", st, st.n)
                dst = q[:, dqc, tw * 512:(tw + 1) * 512]
                P.op("s", lambda E, dst=dst, pp=pp: E.copy(out=dst, in_=pp[:, 0:512]), inc=ss)
            P.barrier()

        def mk_qlat(h):
            for lc in range(4):
                for tw in range(4):
                    pp = sc_ps[tw % 2]
                    P.wait("t", ss, ss.n)
                    P.op("t", lambda E, lc=lc, tw=tw, h=h, pp=pp: E.matmul(
                        pp[:, 0:512], keff[:, h, lc * 128:(lc + 1) * 128],
                        q[:, h, tw * 512:(tw + 1) * 512],
                        start=True, stop=True), inc=st)
                    P.wait("s", st, st.n)
                    dst = qlat[:, lc, tw * 512:(tw + 1) * 512]
                    P.op("s", lambda E, dst=dst, pp=pp: E.copy(out=dst, in_=pp[:, 0:512]), inc=ss)

        B = dict(scores=scores[:], attn_b=attn_b[:], attnT=attnT,
                 rowm=rowm, pt=pt, sc_ps=sc_ps, y_ps=y_ps, ystage=ystage,
                 ident16=ident16, dmask=dmask)

        def scores_mm(h, qt, w, ps_ap):
            for lc in range(4):
                P.op("t", lambda E, lc=lc, qt=qt, w=w, ps_ap=ps_ap: E.matmul(
                    ps_ap, qlat[:, lc, qt * 128:(qt + 1) * 128],
                    ckv[:, lc, w * 512:(w + 1) * 512],
                    start=(lc == 0), stop=(lc == 3)),
                    inc=(st if lc == 3 else None))
            return st.n

        def av_lhsT(h, kc):
            return cv[:, kc, h * 128:(h + 1) * 128]

        def out_dma(h, quad, src):
            return P.dma("g", y_mine[h * 128:(h + 1) * 128,
                                     quad * 512:(quad + 1) * 512], src, so[quad % 2])

        asems = dict(a_s=ss, a_v=sv, a_t=st, a_g=sg, a_o=so)
        for h in range(NH_LOC):
            mk_qlat(h)
            P.barrier()
            _run_head(P, B, asems, h, scores_mm, av_lhsT, out_dma, -1e30)
            P.barrier()

        P.wait("g", so[0], so[0].n)
        P.wait("g", so[1], so[1].n)
        with nc.Block() as block:
            P.finalize(block)
    return nc


def _run_head(P, B, sems, h, scores_mm, av_lhsT, out_dma, fill):
    emit_attn_core(P, B, sems,
                   lambda _h, qt, w, ps: scores_mm(h, qt, w, ps),
                   lambda _h, kc: av_lhsT(h, kc),
                   lambda _h, quad, src: out_dma(h, quad, src),
                   fill, nheads=1)


"""attn1a: GQA attention, head-sharded. Per core: 8 q-heads (2 kv-heads), full causal.
in: xln_all [2, 2048, 1024] bf16; wqT_s [2048,1024], wkT_s [2048,256], wvT_s [2048,256] bf16
out: ao_mine [1024, 2048] bf16 (pre-out-projection attention output rows)."""

NH_LOC = 8
FILL_L1 = -3.0e38


def build_attn1(T_CORE=1024):
    TF = 2 * T_CORE
    nc = bass.Bass("TRN2", target_bir_lowering=False, debug=False)
    xln_all = nc.dram_tensor("xln_all", [2, 2048, T_CORE], BF16, kind="ExternalInput").ap()
    wqT_s = nc.dram_tensor("wqT_s", [2048, 1024], BF16, kind="ExternalInput").ap()
    wkT_s = nc.dram_tensor("wkT_s", [2048, 256], BF16, kind="ExternalInput").ap()
    wvT_s = nc.dram_tensor("wvT_s", [2048, 256], BF16, kind="ExternalInput").ap()
    ao_mine = nc.dram_tensor("ao_mine", [1024, TF], BF16, kind="ExternalOutput").ap()

    with ExitStack() as es:
        P = Prog(nc, es)
        sb = lambda name, shape, dt=F32: es.enter_context(nc.sbuf_tensor(name, shape, dt))
        ps = lambda name, shape, dt=F32: es.enter_context(nc.psum_tensor(name, shape, dt))
        xla = sb("xla", [128, NCH, TF], BF16)
        q = sb("q", [128, NH_LOC, TF], BF16)
        kf = sb("kf", [128, 2, TF], BF16)
        vf = sb("vf", [128, 2, TF], BF16)
        v_tm = sb("v_tm", [128, 16, 256], BF16)
        scores = sb("scores", [128, 2048])
        attn_b = sb("attn_b", [128, 2048], BF16)
        attnT = sb("attnT", [128, 16, 512], BF16)
        rowm = sb("rowm", [128, 2])
        wst = sb("wst", [128, 1, NCH, 128], BF16)
        ystage = sb("ystage", [128, 2, 512], BF16)
        ident16 = sb("ident16", [128, 128], BF16)
        dmask = sb("dmask", [128, 4, 512], BF16)
        sc_ps = [ps("sc_ps0", [128, 512]), ps("sc_ps1", [128, 512])]
        y_ps = [ps("y_ps0", [128, 512]), ps("y_ps1", [128, 512])]
        pt = [ps("pt0", [128, 1024], BF16), ps("pt1", [128, 1024], BF16)]

        sems = {k: P.sem(k) for k in ("a_s", "a_v", "a_t", "a_g", "w0", "io")}
        sems["a_o"] = [P.sem("a_o0"), P.sem("a_o1")]
        ss, sv, st, sg = sems["a_s"], sems["a_v"], sems["a_t"], sems["a_g"]
        so = sems["a_o"]
        io = sems["io"]
        wsem = sems["w0"]

        csg = P.sem("csg")
        P.op("g", lambda E: E.memset(ident16[:], 0.0), inc=csg)
        P.op("g", lambda E: E.memset(dmask[:], 0.0), inc=csg)
        P.wait("g", csg, 2)
        P.op("g", lambda E: E.affine_select(
            out=ident16[:], in_=ident16[:], compare_op=ALU.not_equal,
            fill=1.0, base=0, pattern=[[-1, 128]], channel_multiplier=1), inc=csg)
        for m in range(4):
            sl = dmask[:, m, :]
            P.op("g", lambda E, sl=sl, m=m: E.affine_select(
                out=sl, in_=sl, compare_op=ALU.is_ge, fill=FILL_L1,
                base=m * 128, pattern=[[-1, 512]], channel_multiplier=1), inc=csg)
        P.wait("t", csg, 7)
        P.wait("v", csg, 7)

        for r in range(2):
            P.dma("y", xla[:, :, r * T_CORE:(r + 1) * T_CORE],
                  xln_all[r].rearrange("(cc p) t -> p cc t", p=128), io)
        n_x = io.n
        P.wait("t", io, n_x)

        wqT_r = wqT_s.rearrange("(cc p) (dq d) -> dq p cc d", p=128, d=128)
        for dqc in range(NH_LOC):
            P.wait("y", st, st.n)
            n_w = P.dma("y", wst[:, 0], wqT_r[dqc], wsem)
            P.wait("t", wsem, n_w)
            for tw in range(4):
                pp = sc_ps[tw % 2]
                P.wait("t", ss, ss.n)
                for cc in range(NCH):
                    P.op("t", lambda E, cc=cc, tw=tw, pp=pp: E.matmul(
                        pp[:, 0:512], wst[:, 0, cc, :],
                        xla[:, cc, tw * 512:(tw + 1) * 512],
                        start=(cc == 0), stop=(cc == NCH - 1)),
                        inc=(st if cc == NCH - 1 else None))
                P.wait("s", st, st.n)
                dst = q[:, dqc, tw * 512:(tw + 1) * 512]
                P.op("s", lambda E, dst=dst, pp=pp: E.copy(out=dst, in_=pp[:, 0:512]), inc=ss)
            P.barrier()

        wkT_r = wkT_s.rearrange("(cc p) (kv d) -> kv p cc d", p=128, d=128)
        wvT_r = wvT_s.rearrange("(cc p) (kv d) -> kv p cc d", p=128, d=128)
        for (wr, dstb) in ((wkT_r, kf), (wvT_r, vf)):
            for kv in range(2):
                P.wait("y", st, st.n)
                n_w = P.dma("y", wst[:, 0], wr[kv], wsem)
                P.wait("t", wsem, n_w)
                for tw in range(4):
                    pp = sc_ps[tw % 2]
                    P.wait("t", ss, ss.n)
                    for cc in range(NCH):
                        P.op("t", lambda E, cc=cc, tw=tw, pp=pp: E.matmul(
                            pp[:, 0:512], wst[:, 0, cc, :],
                            xla[:, cc, tw * 512:(tw + 1) * 512],
                            start=(cc == 0), stop=(cc == NCH - 1)),
                            inc=(st if cc == NCH - 1 else None))
                    P.wait("s", st, st.n)
                    dst = dstb[:, kv, tw * 512:(tw + 1) * 512]
                    P.op("s", lambda E, dst=dst, pp=pp: E.copy(out=dst, in_=pp[:, 0:512]), inc=ss)
                P.barrier()

        for kv in range(2):
            for kc in range(16):
                tpar = kc % 2
                src = vf[:, kv, kc * 128:(kc + 1) * 128]
                P.wait("t", ss, ss.n)
                n_tp = P.op("t", lambda E, tpar=tpar, src=src: E.transpose(
                    pt[tpar][:, 0:128], src, ident16[:]), inc=st)
                P.wait("s", st, n_tp)
                dst = v_tm[:, kc, kv * 128:(kv + 1) * 128]
                P.op("s", lambda E, dst=dst, tpar=tpar: E.copy(
                    out=dst, in_=pt[tpar][:, 0:128]), inc=ss)
        P.barrier()

        B = dict(scores=scores[:], attn_b=attn_b[:], attnT=attnT,
                 rowm=rowm, pt=pt, sc_ps=sc_ps, y_ps=y_ps, ystage=ystage,
                 ident16=ident16, dmask=dmask)

        def scores_mm(h, qt, w, ps_ap):
            P.op("t", lambda E, h=h, qt=qt, w=w, ps_ap=ps_ap: E.matmul(
                ps_ap, q[:, h, qt * 128:(qt + 1) * 128],
                kf[:, h // 4, w * 512:(w + 1) * 512],
                start=True, stop=True), inc=st)
            return st.n

        def av_lhsT(h, kc):
            return v_tm[:, kc, (h // 4) * 128:(h // 4 + 1) * 128]

        def out_dma(h, quad, src):
            return P.dma("g", ao_mine[h * 128:(h + 1) * 128,
                                      quad * 512:(quad + 1) * 512], src, so[quad % 2])

        asems = dict(a_s=ss, a_v=sv, a_t=st, a_g=sg, a_o=so)
        for h in range(NH_LOC):
            emit_attn_core(P, B, asems,
                           lambda _h, qt, w, ps, h=h: scores_mm(h, qt, w, ps),
                           lambda _h, kc, h=h: av_lhsT(h, kc),
                           lambda _h, quad, src, h=h: out_dma(h, quad, src),
                           FILL_L1, nheads=1)
            P.barrier()

        P.wait("g", so[0], so[0].n)
        P.wait("g", so[1], so[1].n)
        with nc.Block() as block:
            P.finalize(block)
    return nc


"""FUSED megakernel: the whole macaron-d2 forward in ONE program per core.

Phases: prep (dequant+casts+AllGather) | ffn0 | xln->gather | attn0 (stream
spilled) | y-add | ffn0 | moe | xln->gather | attn1 (spilled) | outproj | moe
| store. Residual stream stays in SBUF except during the two attention phases
(spilled to private DRAM; reloaded). All cross-core traffic via pair/all-8
AllGather collectives. Program is identical on all 8 cores; per-core behavior
comes only from input data (incl. 0/1 half-selection masks cfgm).

SBUF map (bytes/partition): [16384,81920) stream | [81920,215808) phase arena
| [215808,229344) perm smalls. PSUM: ps0..ps4+ps_small (left,f32), pt0/pt1
(right,bf16).
"""
from types import SimpleNamespace

NH_LOC = 8
NE = 8
NIC = 11
T = 1024
T2 = 512
TF = 2048

WL_SPECS = [
    ("wq_nat_s", (2048, 1024)), ("wqT_s", (2048, 1024)), ("woT_s", (2048, 1024)),
    ("wuk", (2048, 512)), ("wuv", (2048, 512)), ("wdkvT", (2048, 512)),
    ("wqT1_s", (2048, 1024)), ("wkT_s", (2048, 256)), ("wvT_s", (2048, 256)),
    ("woT1", (2048, 2048)), ("wr", (2048, 8)),
]
WL_OFF = {}
_off = 0
for _n, _s in WL_SPECS:
    WL_OFF[_n] = _off
    _off += _s[0] * _s[1]
WL_ELEMS = _off
WL_PAD = ((WL_ELEMS + 262143) // 262144) * 262144

GW_TILE = 128 * 16 * 128
C_WG = 0
C_WU = 16 * GW_TILE
GU_ELEMS = 32 * GW_TILE
CHUNK_G = 8 * GW_TILE
D_ELEMS = 16 * GW_TILE
MOE_ELEMS = 3 * 2048 * 1408
MOE_PAD = ((MOE_ELEMS + 262143) // 262144) * 262144


def build_fused(n_mlp=16384, debug=False, stop=None, unit_groups=False):
    NCHK = n_mlp // 8 // 128
    NGRP = n_mlp // 128 // NCH
    nc = bass.Bass("TRN2", target_bir_lowering=False, debug=False)

    x_in = nc.dram_tensor("x_in", [2048, T], F32, kind="ExternalInput").ap()
    wl_f32 = nc.dram_tensor("wl_f32", [WL_PAD // 512, 512], F32, kind="ExternalInput").ap()
    moe_f32 = nc.dram_tensor("moe_f32", [MOE_PAD // 512, 512], F32, kind="ExternalInput").ap()
    gate_p = nc.dram_tensor("gate_p", [n_mlp // 8, 1024], I32, kind="ExternalInput").ap()
    gate_s = nc.dram_tensor("gate_s", [n_mlp // 8, 64], F32, kind="ExternalInput").ap()
    up_p = nc.dram_tensor("up_p", [n_mlp // 8, 1024], I32, kind="ExternalInput").ap()
    up_s = nc.dram_tensor("up_s", [n_mlp // 8, 64], F32, kind="ExternalInput").ap()
    down_p = nc.dram_tensor("down_p", [256, n_mlp // 2], I32, kind="ExternalInput").ap()
    down_s = nc.dram_tensor("down_s", [256, n_mlp // 32], F32, kind="ExternalInput").ap()
    bo_in = nc.dram_tensor("bo_in", [2048], F32, kind="ExternalInput").ap()
    cfg_in = nc.dram_tensor("cfg_in", [128, 2], F32, kind="ExternalInput").ap()
    out_t = nc.dram_tensor("out_t", [2048, T], F32, kind="ExternalOutput").ap()

    dbg = {}
    if debug:
        for i in range(5):
            dbg[i] = nc.dram_tensor(f"dbg{i}", [2048, T], F32, kind="ExternalOutput").ap()
        dbg["y0"] = nc.dram_tensor("dbg_y0", [1024, TF], BF16, kind="ExternalOutput").ap()
        dbg["xln0"] = nc.dram_tensor("dbg_xln0", [2048, T], BF16, kind="ExternalOutput").ap()
        dbg["gw0"] = nc.dram_tensor("dbg_gw0", [128, 16, 128], BF16, kind="ExternalOutput").ap()
        dbg["gw17"] = nc.dram_tensor("dbg_gw17", [128, 16, 128], BF16, kind="ExternalOutput").ap()
        dbg["cw0"] = nc.dram_tensor("dbg_cw0", [128, 16, 128], BF16, kind="ExternalOutput").ap()
        dbg["gm0"] = nc.dram_tensor("dbg_gm0", [64, 1408], BF16, kind="ExternalOutput").ap()
        dbg["wl0"] = nc.dram_tensor("dbg_wl0", [64, 1024], BF16, kind="ExternalOutput").ap()
    wl_bf = nc.dram_tensor("wl_bf", [WL_PAD // 512, 512], BF16)
    contrib = nc.dram_tensor("contrib", [GU_ELEMS // 512, 512], BF16)
    gath = nc.dram_tensor("gath", [8 * GU_ELEMS // 512, 512], BF16)
    contrib_d = nc.dram_tensor("contrib_d", [D_ELEMS // 512, 512], BF16)
    gath_d = nc.dram_tensor("gath_d", [8 * D_ELEMS // 512, 512], BF16)
    contrib_m = nc.dram_tensor("contrib_m", [MOE_PAD // 512, 512], BF16)
    gath_m = nc.dram_tensor("gath_m", [8 * MOE_PAD // 512, 512], BF16)
    xln_o = nc.dram_tensor("xln_o", [2048, T], BF16)
    xg = nc.dram_tensor("xg", [2 * 2048, T], BF16)
    y_o = nc.dram_tensor("y_o", [1024, TF], BF16)
    yg = nc.dram_tensor("yg", [2 * 1024, TF], BF16)
    spill = nc.dram_tensor("spill", [2048, T], F32)

    flatC = contrib.ap().rearrange("a b -> (a b)")
    flatCD = contrib_d.ap().rearrange("a b -> (a b)")
    flatG = gath.ap().rearrange("a b -> (a b)")
    flatGD = gath_d.ap().rearrange("a b -> (a b)")
    flatGM = gath_m.ap().rearrange("a b -> (a b)")
    flatWL = wl_bf.ap().rearrange("a b -> (a b)")

    def wl_ap(name):
        r, c = dict(WL_SPECS)[name]
        o = WL_OFF[name]
        return flatWL[o:o + r * c].rearrange("(a b) -> a b", b=c)

    def gath_wg(slot, gu):
        g, j = divmod(slot, NCH)
        r = 4 * (g % 2) + j // 4
        jl = 4 * (g // 2) + j % 4
        o = (jl // 4) * 8 * CHUNK_G + r * CHUNK_G + (gu * 4 + jl % 4) * GW_TILE
        return flatG[o:o + GW_TILE].rearrange("(p cc n) -> p cc n", p=128, cc=16)

    def gath_wd(dc, g):
        r, dl = divmod(dc, 2)
        o = r * D_ELEMS + (dl * NGRP + g) * GW_TILE
        return flatGD[o:o + GW_TILE].rearrange("(p cc n) -> p cc n", p=128, cc=16)

    def gath_moe(e, which):
        o = e * MOE_PAD + which * 2048 * 1408
        if which < 2:
            return flatGM[o:o + 2048 * 1408].rearrange("(a b) -> a b", b=1408)
        return flatGM[o:o + 1408 * 2048].rearrange("(a b) -> a b", b=2048)

    CCV = 128 if unit_groups else 1
    CCP = 32 if unit_groups else 1
    with ExitStack() as es:
        P = Prog(nc, es)

        def at(name, shape, dt, off):
            return nc.alloc_sbuf_tensor_at(name, shape, dt, offset=off)

        stream = at("stream", [128, NCH, T], F32, 18432)
        AR = 83968
        SM = 215808
        ones_col16 = at("ones_col16", [128, 1], BF16, SM); SM += 32
        ones_row32 = at("ones_row32", [1, 128], F32, SM); SM += 512
        ones_row16 = at("ones_row16", [1, 128], BF16, SM); SM += 256
        ident16 = at("ident16", [128, 128], BF16, SM); SM += 256
        ident32 = at("ident32", [128, 128], F32, SM); SM += 512
        wr_sb = at("wr_sb", [128, NCH, 8], BF16, SM); SM += 256
        bo_sb = at("bo_sb", [128, NCH], F32, SM); SM += 64
        rowm = at("rowm", [128, 2], F32, SM); SM += 32
        ystage = at("ystage", [128, 2, 512], BF16, SM); SM += 2048
        ce_all = at("ce_all", [8, 512], BF16, SM); SM += 1024
        ce_rows = at("ce_rows", [1, 8, 512], BF16, SM); SM += 8192
        er = at("er", [128, 4, 8], F32, SM); SM += 128
        rsc = at("rsc", [128, 8], F32, SM); SM += 32
        comb = at("comb", [128, 8], F32, SM); SM += 32
        bar_dummy = at("bar_dummy", [128, 8], F32, SM); SM += 32
        cfgm = at("cfgm", [128, 2], F32, SM); SM += 32
        cf_a = at("cf_a", [128, 1], F32, SM); SM += 32
        cf_b = at("cf_b", [128, 1], F32, SM); SM += 32
        assert SM <= 229344, SM
        P._dummy = bar_dummy
        P._bars = {e: P.sem(f"bar_{e}") for e in ("y", "v", "s", "t", "g")}
        nc.const_aps.register(F32, -2.0, cf_a[:])
        nc.const_aps.register(F32, -2.5, cf_b[:])

        ps = [es.enter_context(nc.psum_tensor(f"ps{i}", [128, 512], F32)) for i in range(5)]
        ps_small = es.enter_context(nc.psum_tensor("ps_small", [1, 512], F32))
        pt = [es.enter_context(nc.psum_tensor("pt0", [128, 1024], BF16, side="right")),
              es.enter_context(nc.psum_tensor("pt1", [128, 1024], BF16, side="right"))]

        S = {k: P.sem(k) for k in (
            "io", "sp",
            "d_v", "d_g", "d_s", "d_t", "dio0", "dio1", "dso0", "dso1",
            "cA_in0", "cA_in1", "cA_out0", "cA_out1", "cA_s",
            "cc0", "cc01", "cc02", "cc03", "cc0b", "cc0d", "cc1", "cc2", "cc3", "cc4",
            "ln_s", "ln_v", "ln_t",
            "f_s", "f_v", "f_t", "f_w0", "f_w1", "f_d0", "f_d1",
            "a_s", "a_v", "a_t", "a_g", "a_o0", "a_o1", "a_w",
            "m_s", "m_v", "m_t", "m_g", "m_w0", "m_w1", "m_wd", "m_wd1", "m_ce",
            "o_w0", "o_w1", "yio0", "yio1", "oio0", "oio1",
        )}
        csg = P.sem("csg")
        csv = P.sem("csv")


        def gather(in_ap, out_full_ap, rows_in, sem, groups):
            if unit_groups:
                nrep = out_full_ap.shape[0] // rows_in
                for r in range(nrep - 1):
                    P.dma("g", out_full_ap[(r + 1) * rows_in:(r + 2) * rows_in], in_ap, sem)
                P.dma("g", out_full_ap[0:rows_in], in_ap, sem)
            else:
                P.coll(lambda E, in_ap=in_ap, out_full_ap=out_full_ap, groups=groups:
                       E.collective_compute("AllGather", ALU.bypass,
                                            replica_groups=groups,
                                            ins=[in_ap], outs=[out_full_ap]), sem)

        P.op("v", lambda E: E.memset(ones_col16[:], 1.0))
        P.op("v", lambda E: E.memset(ones_row32[:], 1.0))
        P.op("v", lambda E: E.memset(ones_row16[:], 1.0), inc=csv)
        P.op("v", lambda E: E.memset(cf_a[:], -2.0))
        P.op("v", lambda E: E.memset(cf_b[:], -2.5), inc=csv)
        P.op("g", lambda E: E.memset(ident16[:], 0.0))
        P.op("g", lambda E: E.memset(ident32[:], 0.0), inc=csg)
        P.wait("g", csg, 1)
        P.op("g", lambda E: E.affine_select(
            out=ident16[:], in_=ident16[:], compare_op=ALU.not_equal,
            fill=1.0, base=0, pattern=[[-1, 128]], channel_multiplier=1), inc=csg)
        P.op("g", lambda E: E.affine_select(
            out=ident32[:], in_=ident32[:], compare_op=ALU.not_equal,
            fill=1.0, base=0, pattern=[[-1, 128]], channel_multiplier=1), inc=csg)
        P.wait("t", csg, 3)
        P.wait("t", csv, 2)
        P.wait("s", csv, 2)
        n_x = P.dma("g", stream[:], x_in.rearrange("(cc p) t -> p cc t", p=128), S["io"])
        P.dma("g", cfgm[:], cfg_in, S["io"])
        for cc in range(NCH):
            P.dma("g", bo_sb[:, cc:cc + 1], bo_in[cc * 128:(cc + 1) * 128].unsqueeze(1), S["io"])
        n_io0 = S["io"].n

        pk_b = at("pk_b", [128, 2, 2048], I32, AR)
        sc_b = at("sc_b", [128, 2, 128], F32, AR + 16384)
        c4 = at("c4", [128, 4096], I32, AR + 17408)
        s4 = at("s4", [128, 4096], I32, AR + 33792)
        b4 = at("b4", [128, 4096], I32, AR + 50176)
        x4 = at("x4", [128, 4096], I32, AR + 66560)
        val16 = at("val16", [128, 2, 4096], BF16, AR + 82944)
        wtile = at("wtile", [128, 2, 512], BF16, AR + 99328)
        cbuf32 = at("cbuf32", [128, 2, 1024], F32, AR + 116736)
        cbuf16 = at("cbuf16", [128, 2, 1024], BF16, AR + 124928)
        assert AR + 124928 + 4096 <= 215808

        _war_cast = {}

        _cast_par = [0]

        def emit_cast_chunk(src_flat, dst_flat, i):
            c_s = S["cA_s"]
            war_cast = _war_cast
            par = _cast_par[0] % 2
            _cast_par[0] += 1
            in_sem = S[f"cA_in{par}"]
            out_sem = S[f"cA_out{par}"]
            r0 = i * 256
            vi = src_flat[r0:r0 + 256].rearrange("(a p) w -> p a w", p=128)
            vo = dst_flat[r0:r0 + 256].rearrange("(a p) w -> p a w", p=128)
            P.wait("g", out_sem, out_sem.n)
            P.wait("g", c_s, war_cast.get(par, 0))
            n_in = P.dma("g", cbuf32[:, par].rearrange("p (a w) -> p a w", w=512), vi, in_sem)
            P.wait("s", in_sem, n_in)
            n_c = P.op("s", lambda E, par=par: E.copy(
                out=cbuf16[:, par], in_=cbuf32[:, par]), inc=c_s)
            war_cast[par] = n_c
            P.wait("g", c_s, n_c)
            P.dma("g", vo, cbuf16[:, par].rearrange("p (a w) -> p a w", w=512), out_sem)

        sv_d, sg_d, ss_d, st_d = S["d_v"], S["d_g"], S["d_s"], S["d_t"]
        war_tp = {}
        war_decode = {}

        def emit_decode(pk, par, W, lo, hi, sc_bc):
            cW, cLo, cHi = c4[:, 0:2 * W], c4[:, 0:W], c4[:, W:2 * W]
            sW, bW, xW = s4[:, 0:2 * W], b4[:, 0:2 * W], x4[:, 0:2 * W]
            fbL = x4.bitcast(F32)[:, 0:W].rearrange("p (b k) -> p b k", k=16)
            fbH = x4.bitcast(F32)[:, W:2 * W].rearrange("p (b k) -> p b k", k=16)

            def vop(fn):
                P.wait("v", sv_d, sv_d.n)
                return P.op("v", fn, inc=sv_d)
            vop(lambda E, cLo=cLo, pk=pk: E.tensor_single_scalar(
                out=cLo, in_=pk, scalar=15, op=ALU.bitwise_and))
            vop(lambda E, cHi=cHi, pk=pk: E.tensor_single_scalar(
                out=cHi, in_=pk, scalar=4, op=ALU.logical_shift_right))
            vop(lambda E, cHi=cHi: E.tensor_single_scalar(
                out=cHi, in_=cHi, scalar=15, op=ALU.bitwise_and))
            vop(lambda E, sW=sW, cW=cW: E.tensor_single_scalar(
                out=sW, in_=cW, scalar=3, op=ALU.logical_shift_right))
            vop(lambda E, sW=sW: E.tensor_single_scalar(
                out=sW, in_=sW, scalar=31, op=ALU.logical_shift_left))
            vop(lambda E, cW=cW: E.tensor_single_scalar(
                out=cW, in_=cW, scalar=7, op=ALU.bitwise_and))
            vop(lambda E, xW=xW, cW=cW: E.tensor_single_scalar(
                out=xW, in_=cW, scalar=1, op=ALU.min))
            vop(lambda E, bW=bW, cW=cW: E.tensor_single_scalar(
                out=bW, in_=cW, scalar=1, op=ALU.logical_shift_right))
            vop(lambda E, bW=bW: E.tensor_single_scalar(
                out=bW, in_=bW, scalar=1, op=ALU.min))
            vop(lambda E, bW=bW, cW=cW: E.tensor_tensor(
                out=bW, in0=cW, in1=bW, op=ALU.mult))
            vop(lambda E, bW=bW: E.tensor_single_scalar(
                out=bW, in_=bW, scalar=22, op=ALU.logical_shift_left))
            vop(lambda E, xW=xW: E.tensor_single_scalar(
                out=xW, in_=xW, scalar=0x3F000000, op=ALU.mult))
            vop(lambda E, xW=xW, bW=bW: E.tensor_tensor(
                out=xW, in0=xW, in1=bW, op=ALU.add))
            vop(lambda E, xW=xW, sW=sW: E.tensor_tensor(
                out=xW, in0=xW, in1=sW, op=ALU.bitwise_or))
            vop(lambda E, lo=lo, fbL=fbL: E.tensor_tensor(
                out=lo, in0=fbL, in1=sc_bc, op=ALU.mult))
            n_out = vop(lambda E, hi=hi, fbH=fbH: E.tensor_tensor(
                out=hi, in0=fbH, in1=sc_bc, op=ALU.mult))
            war_decode[("v", par)] = n_out
            return n_out

        def do_dequant(pk_ap, sc_ap, nchk, pw, out_tile_fn4, W):
            nsub = pw // W
            nbl = W // 16
            for ch in range(nchk):
                for sub in range(nsub):
                    blk = ch * nsub + sub
                    par = blk % 2
                    iosem = S[f"dio{par}"]
                    P.wait("y", sv_d, war_decode.get(("v", par), 0))
                    P.dma("y", pk_b[:, par, 0:W],
                          pk_ap[ch * 128:(ch + 1) * 128, sub * W:(sub + 1) * W], iosem)
                    n_sc = P.dma("y", sc_b[:, par, 0:nbl],
                                 sc_ap[ch * 128:(ch + 1) * 128, sub * nbl:(sub + 1) * nbl], iosem)
                    P.wait("v", iosem, n_sc)
                    pk = pk_b[:, par, 0:W]
                    vv = val16[:, par, 0:2 * W].rearrange("p (a b) -> p a b", b=2)
                    sc_bc = sc_b[:, par, 0:nbl].unsqueeze(2).broadcast_to([128, nbl, 16])
                    lo = vv[:, :, 0].rearrange("p (b k) -> p b k", k=16)
                    hi = vv[:, :, 1].rearrange("p (b k) -> p b k", k=16)
                    P.wait("v", st_d, war_tp.get(("val", par), 0))
                    n_dec = emit_decode(pk, par, W, lo, hi, sc_bc)
                    nq4 = 2 * W // 512
                    for q4 in range(nq4):
                        tpar = (blk * nq4 + q4) % 2
                        P.wait("t", sv_d, n_dec)
                        P.wait("t", ss_d, war_tp.get(tpar, 0))
                        for u in range(4):
                            q = q4 * 4 + u
                            src = val16[:, par, q * 128:(q + 1) * 128]
                            n_tp = P.op("t", lambda E, tpar=tpar, src=src, u=u: E.transpose(
                                pt[tpar][:, u * 128:(u + 1) * 128], src, ident16[:]),
                                inc=(st_d if u == 3 else None))
                        war_tp[("val", par)] = n_tp
                        P.wait("s", st_d, n_tp)
                        sosem = S[f"dso{tpar}"]
                        P.wait("s", sosem, sosem.n)
                        n_ev = P.op("s", lambda E, tpar=tpar: E.copy(
                            out=wtile[:, tpar], in_=pt[tpar][:, 0:512]), inc=ss_d)
                        war_tp[tpar] = n_ev
                        P.wait("y", ss_d, n_ev)
                        P.dma("y", out_tile_fn4(ch, sub, q4), wtile[:, tpar].rearrange(
                            "p (a n) -> p a n", n=128), sosem)

        def vCK4(k, gu, ch, q4):
            o = k * CHUNK_G + (gu * 4 + ch) * GW_TILE
            return flatC[o:o + GW_TILE].rearrange(
                "(c cc n) -> c cc n", c=128, cc=16)[:, q4 * 4:(q4 + 1) * 4, :]

        vdn = flatCD[0:16 * GW_TILE].rearrange(
            "(dl g c cc n) -> dl g c cc n", g=NGRP, c=128, cc=16, n=128)

        def vdn_slot4(ch, sub, q4):
            r_t = 2 * sub + q4 // 4
            g = 2 * (q4 % 4) + r_t // 4
            j0 = 4 * (r_t % 4)
            return vdn[ch, g, :, j0:j0 + 4, :]
        for k in range(4):
            do_dequant(gate_p[512 * k:512 * (k + 1)], gate_s[512 * k:512 * (k + 1)],
                       4, 1024, lambda ch, sub, q4, k=k: vCK4(k, 0, ch, q4), 1024)
            do_dequant(up_p[512 * k:512 * (k + 1)], up_s[512 * k:512 * (k + 1)],
                       4, 1024, lambda ch, sub, q4, k=k: vCK4(k, 1, ch, q4), 1024)
            for s in ("dso0", "dso1"):
                P.wait("g", S[s], S[s].n)
            gather(contrib.ap()[k * (CHUNK_G // 512):(k + 1) * (CHUNK_G // 512)],
                   gath.ap()[k * (8 * CHUNK_G // 512):(k + 1) * (8 * CHUNK_G // 512)],
                   CHUNK_G // 512, S["cc0" if k == 0 else f"cc0{k}"], [list(range(8))])
        do_dequant(down_p, down_s, 2, n_mlp // 2, vdn_slot4, 2048)
        for s in ("dso0", "dso1"):
            P.wait("g", S[s], S[s].n)
        gather(contrib_d.ap(), gath_d.ap(), D_ELEMS // 512, S["cc0d"], [list(range(8))])
        cast_jobs = []
        for (cs, cd, rt) in ((wl_f32, wl_bf.ap(), WL_PAD // 512),
                             (moe_f32, contrib_m.ap(), MOE_PAD // 512)):
            cast_jobs += [(cs, cd, rt, i) for i in range(rt // 256)]
        cast_pos = [0]

        def cast_feeder(n=1):
            for _ in range(n):
                if cast_pos[0] >= len(cast_jobs):
                    return
                cs, cd, rt, i = cast_jobs[cast_pos[0]]
                cast_pos[0] += 1
                emit_cast_chunk(cs, cd, i)

        ffnB = SimpleNamespace(
            stream=stream, T=T, T2=T2,
            x_ln=at("f_xln", [128, NCH, T2], BF16, AR),
            big_a=at("f_biga", [128, NCH, 1024], BF16, AR + 16384),
            wg_buf=at("f_wg", [128, 2, NCH, 128], BF16, AR + 49152),
            wu_buf=at("f_wu", [128, 2, NCH, 128], BF16, AR + 57344),
            wd_buf=at("f_wd", [128, 2, NCH, 128], BF16, AR + 65536),
            gelu_sb=at("f_gelu", [128, 2, T2], BF16, AR + 73728),
            ds_acc=at("f_ds", [128, NCH, T2], F32, AR + 75776),
            rows=at("f_rows", [1, 4, T2], F32, AR + 108544),
            ones_col16=ones_col16, ones_row32=ones_row32,
            ps=[ps[0], ps[1], ps[2], ps[3], ps[4], ps[4]],
            ps_small=ps_small,
        )
        ffnB.xb16 = ffnB.big_a[:, :, 0:T2]
        ffnB.xsq16 = ffnB.big_a[:, :, T2:1024]
        fw = dict(wg_t=lambda jg: gath_wg(jg, 0),
                  wu_t=lambda jg: gath_wg(jg, 1),
                  wd_t=gath_wd)
        fsems = dict(ln_s=S["ln_s"], ln_v=S["ln_v"], ln_t=S["ln_t"],
                     f_s=S["f_s"], f_v=S["f_v"], f_t=S["f_t"],
                     f_w=[S["f_w0"], S["f_w1"]], f_d=[S["f_d0"], S["f_d1"]])

        if debug:
            P.wait("g", S["cc0"], CCV)
            P.wait("g", S["cc0b"], CCV)
            vC0 = flatC[0:GW_TILE].rearrange("(p cc n) -> p cc n", p=128, cc=16)
            for (tag, src_ap) in (("gw0", gath_wg(0, 0)), ("gw17", gath_wg(17, 0)),
                                  ("cw0", vC0)):
                n_d = P.dma("g", dbg[tag], src_ap, S["io"])
            n_d = P.dma("g", dbg["gm0"], gath_moe(0, 0)[0:64, :], S["io"])
            n_d = P.dma("g", dbg["wl0"], wl_ap("wq_nat_s")[0:64, :], S["io"])
            P.wait("g", S["io"], n_d)

        def run_ffn():
            for e in ("v", "s", "t", "y"):
                P.wait(e, S["dso0"], S["dso0"].n)
                P.wait(e, S["dso1"], S["dso1"].n)
            fsems["jg_feeder"] = cast_feeder
            P.wait("y", S["cc0"], CCV)
            fsems["wg_gate"] = lambda jg: P.wait(
                "y", S["cc0" if jg < 32 else f"cc0{min(jg // 32, 3)}"], CCV)
            fsems["wd_gate"] = (S["cc0d"], CCV)
            P.wait("s", S["io"], n_io0)
            P.wait("v", S["io"], n_io0)
            emit_ffn0(P, ffnB, fw, fsems, n_mlp=n_mlp, xln_half=True)
            P.barrier()

        run_ffn()
        cast_feeder(len(cast_jobs))
        ncA0, ncA1 = S["cA_out0"].n, S["cA_out1"].n
        for s in ("cA_out0", "cA_out1"):
            P.wait("g", S[s], S[s].n)
        gather(contrib_m.ap(), gath_m.ap(), MOE_PAD // 512, S["cc0b"], [list(range(8))])

        def dump_stream(i):
            if debug:
                P.wait("g", S["f_v"], S["f_v"].n)
                P.wait("g", S["m_v"], S["m_v"].n)
                P.wait("g", S["a_v"], S["a_v"].n)
                n_d = P.dma("g", dbg[i].rearrange("(cc p) t -> p cc t", p=128), stream[:], S["io"])
                P.wait("g", S["io"], n_d)
        dump_stream(0)
        if stop == "x1":
            P.wait("g", S["f_v"], S["f_v"].n)
            n_o = P.dma("g", out_t.rearrange("(cc p) t -> p cc t", p=128), stream[:], S["io"])
            P.wait("g", S["io"], n_o)
            with nc.Block() as block:
                P.finalize(block)
            return nc

        for e in ("y", "g"):
            P.wait(e, S["cA_out0"], ncA0)
            P.wait(e, S["cA_out1"], ncA1)
        xlnB = SimpleNamespace(
            stream=stream, T=T, T2=T2,
            x_ln=ffnB.x_ln, rows=ffnB.rows,
            xb16=ffnB.xb16, xsq16=ffnB.xsq16,
            ones_col16=ones_col16, ones_row32=ones_row32,
            ps=[None, None, None, None, None, ps[4]], ps_small=ps_small,
        )
        lsems = dict(ln_s=S["ln_s"], ln_v=S["ln_v"], ln_t=S["ln_t"])

        def run_xln(ccsem):
            for half in (0, 1):
                n_ln = emit_ln(P, xlnB, half, lsems, dst_t0=0,
                               war=[(S["io"], S["io"].n)])
                P.wait("g", lsems["ln_v"], n_ln)
                P.dma("g", xln_o.ap()[:, half * T2:(half + 1) * T2].rearrange(
                    "(cc p) t -> p cc t", p=128), xlnB.x_ln[:], S["io"])
            P.wait("g", S["io"], S["io"].n)
            gather(xln_o.ap(), xg.ap(), 2048, ccsem, [[0, 1], [2, 3], [4, 5], [6, 7]])
            P.barrier()

        run_xln(S["cc1"])
        if debug:
            n_d = P.dma("g", dbg["xln0"].rearrange("(cc p) t -> p cc t", p=128),
                        xln_o.ap().rearrange("(cc p) t -> p cc t", p=128), S["io"])
            P.wait("g", S["io"], n_d)

        n_sp = P.dma("g", spill.ap().rearrange("(cc p) t -> p cc t", p=128), stream[:], S["sp"])

        AB = 18432
        a_xst = at("a_xst", [128, NCH, T2], BF16, AB)
        a_wf = at("a_wf", [128, NCH, 1024], BF16, AB + 16384)
        a_q = at("a_q", [128, NH_LOC, TF], BF16, AB + 49152)
        a_cv = at("a_cv", [128, NCH, 1024], BF16, AB + 81920)
        a_ckv = at("a_ckv", [128, 4, TF], BF16, AB + 114688)
        a_keff = at("a_keff", [128, NH_LOC, 512], BF16, AB + 131072)
        a_qlat = at("a_qlat", [128, 4, TF], BF16, AB + 139264)
        a_scores = at("a_scores", [128, 2048], F32, AB + 155648)
        a_attnb = at("a_attnb", [128, 2048], BF16, AB + 163840)
        a_attnT = at("a_attnT", [128, 16, 512], BF16, AB + 167936)
        a_dm0 = at("a_dm0", [128, 4, 512], BF16, AB + 184320)
        a_veff = at("a_veff", [128, 4, 1024], BF16, AB + 188416)
        a_kf = at("a_kf", [128, 2, TF], BF16, AB + 81920)
        a_vf = at("a_vf", [128, 2, TF], BF16, AB + 90112)
        a_vtm = at("a_vtm", [128, 16, 256], BF16, AB + 98304)
        a_dm1 = at("a_dm1", [128, 4, 512], BF16, AB + 106496)

        ss_a, sv_a, st_a, sg_a = S["a_s"], S["a_v"], S["a_t"], S["a_g"]
        so_a = [S["a_o0"], S["a_o1"]]
        aw = S["a_w"]

        war_pp = {}
        aB = dict(scores=a_scores[:], attn_b=a_attnb[:], attnT=a_attnT,
                  rowm=rowm, pt=pt, sc_ps=[ps[0], ps[1]], y_ps=[ps[2], ps[2]],
                  ystage=ystage, ident16=ident16, y_single=True)
        asems = dict(a_s=ss_a, a_v=sv_a, a_t=st_a, a_g=sg_a, a_o=so_a)

        def mk_dmask(dm, fill):
            nmk = P.op("g", lambda E, dm=dm: E.memset(dm[:], 0.0), inc=sg_a)
            P.wait("g", sg_a, nmk)
            for m in range(4):
                sl = dm[:, m, :]
                nmk = P.op("g", lambda E, sl=sl, m=m, fill=fill: E.affine_select(
                    out=sl, in_=sl, compare_op=ALU.is_ge, fill=fill,
                    base=m * 128, pattern=[[-1, 512]], channel_multiplier=1), inc=sg_a)
            P.wait("t", sg_a, sg_a.n)
            P.wait("v", sg_a, sg_a.n)

        def load_xst(tw):
            r, twl = divmod(tw, 2)
            src = xg.ap()[r * 2048:(r + 1) * 2048, twl * T2:(twl + 1) * T2].rearrange(
                "(cc p) t -> p cc t", p=128)
            return P.dma("y", a_xst[:], src, aw)

        def proj_pass(wl_name, ncols, dst_fn, extra_lhs_rearr=None):
            nocc = ncols // 128
            wv = wl_ap(wl_name).rearrange("(cc p) d -> p cc d", p=128)
            P.wait("y", st_a, st_a.n)
            n_w = P.dma("y", a_wf[:, :, 0:ncols], wv, aw)
            for tw in range(4):
                P.wait("y", st_a, st_a.n)
                n_x = load_xst(tw)
                P.wait("t", aw, n_x)
                for oc in range(nocc):
                    pp = ps[oc % 2]
                    P.wait("t", ss_a, war_pp.get(id(pp), 0))
                    for cc in range(NCH):
                        lhsT = a_wf[:, cc, oc * 128:(oc + 1) * 128]
                        rhs = a_xst[:, cc, :]
                        P.op("t", lambda E, lhsT=lhsT, rhs=rhs, cc=cc, pp=pp: E.matmul(
                            pp[:, 0:512], lhsT, rhs, start=(cc == 0), stop=(cc == NCH - 1)),
                            inc=(st_a if cc == NCH - 1 else None))
                    P.wait("s", st_a, st_a.n)
                    dst = dst_fn(oc, tw)
                    war_pp[id(pp)] = P.op("s", lambda E, dst=dst, pp=pp: E.copy(out=dst, in_=pp[:, 0:512]), inc=ss_a)
            P.barrier()

        def run_attn0():
            mk_dmask(a_dm0, FILL_L0)
            aB["dmask"] = a_dm0
            for e in ("y", "g", "s", "t", "v"):
                P.wait(e, S["sp"], n_sp)
            P.wait("y", S["cc1"], CCP)
            P.wait("y", S["cA_out0"], S["cA_out0"].n)
            P.wait("y", S["cA_out1"], S["cA_out1"].n)
            P.barrier()
            n_wuk = P.dma("y", a_xst[:], wl_ap("wuk").rearrange("(dc p) l -> p dc l", p=128), aw)
            wqv = wl_ap("wq_nat_s").rearrange("(dc p) s -> p dc s", p=128)
            n_wq = P.dma("y", a_wf[:, :, 0:1024], wqv, aw)
            P.wait("t", aw, n_wq)
            for sc in range(NH_LOC):
                pp = ps[sc % 2]
                P.wait("t", ss_a, war_pp.get(id(pp), 0))
                for dc in range(NCH):
                    lhsT = a_wf[:, dc, sc * 128:(sc + 1) * 128]
                    P.op("t", lambda E, lhsT=lhsT, dc=dc, pp=pp: E.matmul(
                        pp[:, 0:512], lhsT, a_xst[:, dc, :],
                        start=(dc == 0), stop=(dc == NCH - 1)),
                        inc=(st_a if dc == NCH - 1 else None))
                P.wait("s", st_a, st_a.n)
                dst = a_keff[:, sc, :]
                war_pp[id(pp)] = P.op("s", lambda E, dst=dst, pp=pp: E.mul(out=dst, in_=pp[:, 0:512], mul=RSQ), inc=ss_a)
            P.barrier()
            n_wuv = P.dma("y", a_xst[:], wl_ap("wuv").rearrange("(dc p) l -> p dc l", p=128), aw)
            wov = wl_ap("woT_s").rearrange("(dc p) c -> p dc c", p=128)
            n_wo = P.dma("y", a_wf[:, :, 0:1024], wov, aw)
            P.wait("t", aw, n_wo)
            for cw in range(2):
                for lc in range(4):
                    pp = ps[lc % 2]
                    P.wait("t", ss_a, war_pp.get(id(pp), 0))
                    for dc in range(NCH):
                        lhsT = a_xst[:, dc, lc * 128:(lc + 1) * 128]
                        rhs = a_wf[:, dc, cw * 512:(cw + 1) * 512]
                        P.op("t", lambda E, lhsT=lhsT, rhs=rhs, dc=dc, pp=pp: E.matmul(
                            pp[:, 0:512], lhsT, rhs, start=(dc == 0), stop=(dc == NCH - 1)),
                            inc=(st_a if dc == NCH - 1 else None))
                    P.wait("s", st_a, st_a.n)
                    dst = a_veff[:, lc, cw * 512:(cw + 1) * 512]
                    war_pp[id(pp)] = P.op("s", lambda E, dst=dst, pp=pp: E.copy(out=dst, in_=pp[:, 0:512]), inc=ss_a)
            P.barrier()
            n_wk = P.dma("y", a_wf[:, :, 0:512], wl_ap("wdkvT").rearrange(
                "(cc p) l -> p cc l", p=128), aw)
            for tw in range(4):
                P.wait("y", st_a, st_a.n)
                n_x = load_xst(tw)
                P.wait("t", aw, n_x)
                for lc in range(4):
                    pp = ps[lc % 2]
                    P.wait("t", ss_a, war_pp.get(id(pp), 0))
                    for cc in range(NCH):
                        lhsT = a_wf[:, cc, lc * 128:(lc + 1) * 128]
                        P.op("t", lambda E, lhsT=lhsT, cc=cc, pp=pp: E.matmul(
                            pp[:, 0:512], lhsT, a_xst[:, cc, :],
                            start=(cc == 0), stop=(cc == NCH - 1)),
                            inc=(st_a if cc == NCH - 1 else None))
                    P.wait("s", st_a, st_a.n)
                    dst = a_ckv[:, lc, tw * 512:(tw + 1) * 512]
                    war_pp[id(pp)] = P.op("s", lambda E, dst=dst, pp=pp: E.copy(out=dst, in_=pp[:, 0:512]), inc=ss_a)
            P.barrier()
            for kc in range(16):
                for cwi in range(2):
                    pp = ps[cwi]
                    P.wait("t", ss_a, war_pp.get(id(pp), 0))
                    for lc in range(4):
                        lhsT = a_ckv[:, lc, kc * 128:(kc + 1) * 128]
                        rhs = a_veff[:, lc, cwi * 512:(cwi + 1) * 512]
                        P.op("t", lambda E, lhsT=lhsT, rhs=rhs, lc=lc, pp=pp: E.matmul(
                            pp[:, 0:512], lhsT, rhs, start=(lc == 0), stop=(lc == 3)),
                            inc=(st_a if lc == 3 else None))
                    P.wait("s", st_a, st_a.n)
                    dst = a_cv[:, kc, cwi * 512:(cwi + 1) * 512]
                    war_pp[id(pp)] = P.op("s", lambda E, dst=dst, pp=pp: E.copy(out=dst, in_=pp[:, 0:512]), inc=ss_a)
            P.barrier()
            proj_pass("wqT_s", 1024,
                      lambda oc, tw: a_q[:, oc, tw * 512:(tw + 1) * 512])
            for h in range(NH_LOC):
                for lc in range(4):
                    for tw in range(4):
                        pp = ps[tw % 2]
                        P.wait("t", ss_a, war_pp.get(id(pp), 0))
                        lhsT = a_keff[:, h, lc * 128:(lc + 1) * 128]
                        rhs = a_q[:, h, tw * 512:(tw + 1) * 512]
                        P.op("t", lambda E, lhsT=lhsT, rhs=rhs, pp=pp: E.matmul(
                            pp[:, 0:512], lhsT, rhs, start=True, stop=True), inc=st_a)
                        P.wait("s", st_a, st_a.n)
                        dst = a_qlat[:, lc, tw * 512:(tw + 1) * 512]
                        war_pp[id(pp)] = P.op("s", lambda E, dst=dst, pp=pp: E.copy(out=dst, in_=pp[:, 0:512]), inc=ss_a)
                P.barrier()

                def scores_mm(_h, qt, w, ps_ap):
                    for lc in range(4):
                        lhsT = a_qlat[:, lc, qt * 128:(qt + 1) * 128]
                        rhs = a_ckv[:, lc, w * 512:(w + 1) * 512]
                        P.op("t", lambda E, lhsT=lhsT, rhs=rhs, lc=lc, ps_ap=ps_ap: E.matmul(
                            ps_ap, lhsT, rhs, start=(lc == 0), stop=(lc == 3)),
                            inc=(st_a if lc == 3 else None))
                    return st_a.n

                emit_attn_core(P, aB, asems, scores_mm,
                               lambda _h, kc, h=h: a_cv[:, kc, h * 128:(h + 1) * 128],
                               lambda _h, quad, src, h=h: P.dma(
                                   "g", y_o.ap()[h * 128:(h + 1) * 128,
                                                 quad * 512:(quad + 1) * 512],
                                   src, so_a[0]),
                               FILL_L0, nheads=1)
                P.barrier()
            P.wait("g", so_a[0], so_a[0].n)
            P.wait("g", so_a[1], so_a[1].n)
            gather(y_o.ap(), yg.ap(), 1024, S["cc2"], [[0, 1], [2, 3], [4, 5], [6, 7]])
            P.barrier()

        run_attn0()

        n_rl = P.dma("g", stream[:], spill.ap().rearrange("(cc p) t -> p cc t", p=128), S["sp"])
        ystg = at("ystg", [128, 2, 2, 1024], BF16, AR + 16384)
        war_add = {}

        def run_yadd(ccsem):
            P.wait("g", ccsem, CCP)
            P.wait("g", S["sp"], S["sp"].n)
            for cc in range(NCH):
                par = cc % 2
                r, hcl = divmod(cc, 8)
                src = yg.ap()[r * 1024 + hcl * 128: r * 1024 + (hcl + 1) * 128, :].rearrange(
                    "p (h t) -> p h t", t=T)
                P.wait("g", S["f_v"], war_add.get(par, 0))
                n_l = P.dma("g", ystg[:, par], src, S[f"yio{par}"])
                P.wait("v", S[f"yio{par}"], n_l)
                P.wait("v", S["sp"], S["sp"].n)
                sdst = stream[:, cc, :]
                P.op("v", lambda E, sdst=sdst, par=par: E.scalar_tensor_tensor(
                    out=sdst, in0=ystg[:, par, 0], scalar=cfgm[:, 0:1], in1=sdst,
                    op0=ALU.mult, op1=ALU.add), inc=S["f_v"])
                P.wait("v", S["f_v"], S["f_v"].n)
                war_add[par] = P.op("v", lambda E, sdst=sdst, par=par: E.scalar_tensor_tensor(
                    out=sdst, in0=ystg[:, par, 1], scalar=cfgm[:, 1:2], in1=sdst,
                    op0=ALU.mult, op1=ALU.add), inc=S["f_v"])
            P.barrier()

        if debug:
            P.wait("g", S["cc2"], CCP)
            n_d = P.dma("g", dbg["y0"], y_o.ap(), S["io"])
            P.wait("g", S["io"], n_d)
        run_yadd(S["cc2"])
        dump_stream(1)

        run_ffn()
        dump_stream(2)

        m_xln = ffnB.x_ln
        m_hall = at("m_hall", [128, NE, NIC, T2], BF16, AR + 16384)
        m_wgst = at("m_wgst", [128, 2, NCH, 128], BF16, AR + 106496)
        m_wust = at("m_wust", [128, 2, NCH, 128], BF16, AR + 114688)
        m_wdst = at("m_wdst", [128, 2, NIC, 128], BF16, AR + 122880)
        m_gsil = at("m_gsil", [128, 2, T2], BF16, AR + 122880)
        m_rows = at("m_rows", [1, 4, T2], F32, AR + 106496)
        assert AR + 125696 <= 215808

        moeB = SimpleNamespace(
            stream=stream, T=T, T2=T2, x_ln=m_xln, rows=m_rows,
            xb16=m_hall[:, 0:2].rearrange("p e a t -> p (e a t)")[:, 0:16 * T2].rearrange(
                "p (a t) -> p a t", t=T2),
            xsq16=m_hall[:, 2:4].rearrange("p e a t -> p (e a t)")[:, 0:16 * T2].rearrange(
                "p (a t) -> p a t", t=T2),
            ones_col16=ones_col16, ones_row32=ones_row32,
            ps=[None, None, None, None, None, ps[3]], ps_small=ps_small,
        )
        msems = dict(ln_s=S["ln_s"], ln_v=S["ln_v"], ln_t=S["ln_t"])
        ss_m, sv_m, st_m = S["m_s"], S["m_v"], S["m_t"]

        P.wait("g", S["cA_out0"], S["cA_out0"].n)
        P.wait("g", S["cA_out1"], S["cA_out1"].n)
        n_wr = P.dma("g", wr_sb[:], wl_ap("wr").rearrange("(cc p) e -> p cc e", p=128), S["io"])

        def run_moe():
            P.wait("s", S["io"], n_wr)
            P.wait("v", S["io"], n_wr)
            P.wait("t", S["io"], n_wr)
            pbc_m = ps[3]
            ps_r = ps[0]
            for half in (0, 1):
                t0 = half * T2
                n_ln = emit_ln(P, moeB, half, msems, war=[(st_m, st_m.n), (sv_m, sv_m.n)], dst_t0=0)
                P.wait("t", msems["ln_v"], n_ln)
                for tc in range(4):
                    P.wait("t", sv_m, sv_m.n)
                    P.wait("t", ss_m, ss_m.n)
                    for cc in range(NCH):
                        lhsT = m_xln[:, cc, tc * 128:(tc + 1) * 128]
                        P.op("t", lambda E, lhsT=lhsT, cc=cc: E.matmul(
                            ps_r[:, 0:8], lhsT, wr_sb[:, cc, :],
                            start=(cc == 0), stop=(cc == NCH - 1)),
                            inc=(st_m if cc == NCH - 1 else None))
                    n_lg = st_m.n
                    P.wait("v", st_m, st_m.n)
                    P.wait("v", ss_m, ss_m.n)
                    er0, er2 = er[:, 0, :], er[:, 1, :]
                    msk1, msk2 = er[:, 2, :], er[:, 3, :]
                    m1 = rsc[:, 0:1]; m2 = rsc[:, 1:2]; s12 = rsc[:, 2:3]
                    w1 = rsc[:, 4:5]; w2 = rsc[:, 5:6]; nmx = rsc[:, 6:7]
                    P.op("v", lambda E: E.reduce_max(out=nmx, in_=ps_r[:, 0:8], axis=AX.X, negate=True), inc=sv_m)
                    P.wait("s", sv_m, sv_m.n)
                    P.op("s", lambda E: E.activation(out=er0, in_=ps_r[:, 0:8], func=AF.Exp, bias=nmx), inc=ss_m)
                    P.wait("v", ss_m, ss_m.n)
                    P.op("v", lambda E: E.reduce_max(out=m1, in_=er0, axis=AX.X), inc=sv_m)
                    P.wait("v", sv_m, sv_m.n)
                    P.op("v", lambda E: E.tensor_scalar(out=msk1, in0=er0, scalar1=m1, scalar2=None, op0=ALU.is_equal), inc=sv_m)
                    P.wait("v", sv_m, sv_m.n)
                    P.op("v", lambda E: E.scalar_tensor_tensor(out=er2, in0=msk1, scalar=-4.0, in1=er0, op0=ALU.mult, op1=ALU.add), inc=sv_m)
                    P.wait("v", sv_m, sv_m.n)
                    P.op("v", lambda E: E.reduce_max(out=m2, in_=er2, axis=AX.X), inc=sv_m)
                    P.wait("v", sv_m, sv_m.n)
                    P.op("v", lambda E: E.tensor_scalar(out=msk2, in0=er2, scalar1=m2, scalar2=None, op0=ALU.is_equal), inc=sv_m)
                    P.op("v", lambda E: E.tensor_tensor(out=s12, in0=m1, in1=m2, op=ALU.add), inc=sv_m)
                    P.wait("v", sv_m, sv_m.n)
                    P.op("v", lambda E: E.reciprocal(out=s12, in_=s12), inc=sv_m)
                    P.wait("v", sv_m, sv_m.n)
                    P.op("v", lambda E: E.tensor_tensor(out=w1, in0=m1, in1=s12, op=ALU.mult), inc=sv_m)
                    P.op("v", lambda E: E.tensor_tensor(out=w2, in0=m2, in1=s12, op=ALU.mult), inc=sv_m)
                    P.wait("v", sv_m, sv_m.n)
                    P.op("v", lambda E: E.tensor_scalar(out=msk1, in0=msk1, scalar1=w1, scalar2=None, op0=ALU.mult), inc=sv_m)
                    P.op("v", lambda E: E.tensor_scalar(out=msk2, in0=msk2, scalar1=w2, scalar2=None, op0=ALU.mult), inc=sv_m)
                    P.wait("v", sv_m, sv_m.n)
                    n_cmb = P.op("v", lambda E: E.tensor_tensor(out=comb[:], in0=msk1, in1=msk2, op=ALU.add), inc=sv_m)
                    P.wait("t", sv_m, n_cmb)
                    n_tp = P.op("t", lambda E: E.transpose(ps_r[0:8, 0:128], comb[:], ident32[:]), inc=st_m)
                    P.wait("s", st_m, n_tp)
                    P.wait("s", S["m_ce"], S["m_ce"].n)
                    P.op("s", lambda E, tc=tc: E.copy(
                        out=ce_all[0:8, tc * 128:(tc + 1) * 128], in_=ps_r[0:8, 0:128]), inc=ss_m)
                P.wait("g", ss_m, ss_m.n)
                for e in range(NE):
                    P.dma("g", ce_rows[:, e, :], ce_all[e:e + 1, :], S["m_ce"])
                P.wait("t", S["m_ce"], S["m_ce"].n)
                war_w = {}
                war_h = {}
                P.wait("y", st_m, st_m.n)
                P.wait("y", S["ln_v"], S["ln_v"].n)
                P.wait("y", S["ln_s"], S["ln_s"].n)
                P.wait("y", S["cc0b"], CCV)
                for e in range(NE):
                    P.wait("t", sv_m, sv_m.n)
                    n_bc = P.op("t", lambda E, e=e: E.matmul(
                        pbc_m[:, 0:512], ones_row16[:], ce_rows[:, e, :],
                        start=True, stop=True), inc=st_m)
                    for ic in range(NIC):
                        par = (e * NIC + ic) % 2
                        idx = e * NIC + ic
                        wgsem = S[f"m_w{par}"]
                        if idx >= 2:
                            P.wait("y", st_m, war_w[idx - 2])
                        P.dma("y", m_wgst[:, par], gath_moe(e, 0)[:, ic * 128:(ic + 1) * 128].rearrange(
                            "(cc p) i -> p cc i", p=128), wgsem)
                        n_w = P.dma("y", m_wust[:, par], gath_moe(e, 1)[:, ic * 128:(ic + 1) * 128].rearrange(
                            "(cc p) i -> p cc i", p=128), wgsem)
                        P.wait("t", wgsem, n_w)
                        pg, pu = (ps[0], ps[2]) if idx % 2 == 0 else (ps[1], ps[4])
                        if idx >= 2:
                            P.wait("t", ss_m, war_h.get((idx - 2, "s"), 0))
                            P.wait("t", sv_m, war_h.get((idx - 2, "v"), 0))
                        for cc in range(NCH):
                            P.op("t", lambda E, cc=cc, par=par, pg=pg: E.matmul(
                                pg[:, 0:512], m_wgst[:, par, cc, :], m_xln[:, cc, 0:T2],
                                start=(cc == 0), stop=(cc == NCH - 1)))
                        for cc in range(NCH):
                            P.op("t", lambda E, cc=cc, par=par, pu=pu: E.matmul(
                                pu[:, 0:512], m_wust[:, par, cc, :], m_xln[:, cc, 0:T2],
                                start=(cc == 0), stop=(cc == NCH - 1)),
                                inc=(st_m if cc == NCH - 1 else None))
                        n_mm = st_m.n
                        war_w[idx] = n_mm
                        P.wait("s", st_m, n_mm)
                        if idx >= 2:
                            P.wait("s", sv_m, war_h.get((idx - 2, "v"), 0))
                        gpar = idx % 2
                        n_sil = P.op("s", lambda E, gpar=gpar, pg=pg: E.activation(
                            out=m_gsil[:, gpar, :], in_=pg[:, 0:512], func=AF.Silu), inc=ss_m)
                        war_h[(idx, "s")] = n_sil
                        P.wait("v", ss_m, n_sil)
                        P.wait("v", st_m, n_mm)
                        dst = m_hall[:, e, ic, :]
                        P.op("v", lambda E, dst=dst, gpar=gpar, pu=pu: E.tensor_tensor(
                            out=dst, in0=m_gsil[:, gpar, :], in1=pu[:, 0:512], op=ALU.mult), inc=sv_m)
                        P.wait("v", sv_m, sv_m.n)
                        war_h[(idx, "v")] = P.op("v", lambda E, dst=dst: E.tensor_tensor(
                            out=dst, in0=dst, in1=pbc_m[:, 0:512], op=ALU.mult), inc=sv_m)
                P.barrier()
                war_d = {}
                war_de = {}
                for dc in range(NCH):
                    pd = ps[dc % 2]
                    if dc >= 2:
                        P.wait("t", sv_m, war_de[dc - 2])
                    for e in range(NE):
                        idx = dc * NE + e
                        wpar = idx % 2
                        wdsem = S["m_wd"] if wpar == 0 else S["m_wd1"]
                        if idx >= 2:
                            P.wait("y", st_m, war_d[idx - 2])
                        n_w = P.dma("y", m_wdst[:, wpar], gath_moe(e, 2)[:, dc * 128:(dc + 1) * 128].rearrange(
                            "(ic p) d -> p ic d", p=128), wdsem)
                        P.wait("t", wdsem, n_w)
                        for ic in range(NIC):
                            P.op("t", lambda E, e=e, ic=ic, pd=pd, wpar=wpar: E.matmul(
                                pd[:, 0:512], m_wdst[:, wpar, ic, :], m_hall[:, e, ic, :],
                                start=(e == 0 and ic == 0), stop=(e == NE - 1 and ic == NIC - 1)),
                                inc=(st_m if ic == NIC - 1 else None))
                        war_d[idx] = st_m.n
                    n_dmm = st_m.n
                    P.wait("v", st_m, n_dmm)
                    sdst = stream[:, dc, t0:t0 + T2]
                    war_de[dc] = P.op("v", lambda E, sdst=sdst, pd=pd: E.scalar_tensor_tensor(
                        out=sdst, in0=pd[:, 0:512], scalar=0.5, in1=sdst,
                        op0=ALU.mult, op1=ALU.add), inc=sv_m)
                P.barrier()

        run_moe()
        dump_stream(3)

        run_xln(S["cc3"])
        n_sp2 = P.dma("g", spill.ap().rearrange("(cc p) t -> p cc t", p=128), stream[:], S["sp"])

        def run_attn1():
            mk_dmask(a_dm1, FILL_L1)
            aB["dmask"] = a_dm1
            for e in ("y", "g", "s", "t", "v"):
                P.wait(e, S["sp"], S["sp"].n)
            P.wait("y", S["cc3"], CCP)
            P.barrier()
            proj_pass("wqT1_s", 1024,
                      lambda oc, tw: a_q[:, oc, tw * 512:(tw + 1) * 512])
            proj_pass("wkT_s", 256,
                      lambda oc, tw: a_kf[:, oc, tw * 512:(tw + 1) * 512])
            proj_pass("wvT_s", 256,
                      lambda oc, tw: a_vf[:, oc, tw * 512:(tw + 1) * 512])
            for kv in range(2):
                for kc in range(16):
                    tpar = kc % 2
                    src = a_vf[:, kv, kc * 128:(kc + 1) * 128]
                    P.wait("t", ss_a, ss_a.n)
                    n_tp = P.op("t", lambda E, tpar=tpar, src=src: E.transpose(
                        pt[tpar][:, 0:128], src, ident16[:]), inc=st_a)
                    P.wait("s", st_a, n_tp)
                    dst = a_vtm[:, kc, kv * 128:(kv + 1) * 128]
                    P.op("s", lambda E, dst=dst, tpar=tpar: E.copy(
                        out=dst, in_=pt[tpar][:, 0:128]), inc=ss_a)
            P.barrier()
            for h in range(NH_LOC):
                def scores_mm(_h, qt, w, ps_ap, h=h):
                    lhsT = a_q[:, h, qt * 128:(qt + 1) * 128]
                    rhs = a_kf[:, h // 4, w * 512:(w + 1) * 512]
                    P.op("t", lambda E, lhsT=lhsT, rhs=rhs, ps_ap=ps_ap: E.matmul(
                        ps_ap, lhsT, rhs, start=True, stop=True), inc=st_a)
                    return st_a.n

                emit_attn_core(P, aB, asems, scores_mm,
                               lambda _h, kc, h=h: a_vtm[:, kc, (h // 4) * 128:(h // 4 + 1) * 128],
                               lambda _h, quad, src, h=h: P.dma(
                                   "g", y_o.ap()[h * 128:(h + 1) * 128,
                                                 quad * 512:(quad + 1) * 512],
                                   src, so_a[0]),
                               FILL_L1, nheads=1)
                P.barrier()
            P.wait("g", so_a[0], so_a[0].n)
            P.wait("g", so_a[1], so_a[1].n)
            gather(y_o.ap(), yg.ap(), 1024, S["cc4"], [[0, 1], [2, 3], [4, 5], [6, 7]])
            P.barrier()

        run_attn1()

        P.dma("g", stream[:], spill.ap().rearrange("(cc p) t -> p cc t", p=128), S["sp"])
        o_ao = at("o_ao", [128, NCH, T], BF16, AR)
        o_wst = at("o_wst", [128, 2, NCH, 128], BF16, AR + 32768)
        o_stg = at("o_stg", [128, 2, 2, 1024], BF16, AR + 40960)

        def run_outproj():
            P.wait("g", S["cc4"], CCP)
            P.wait("g", S["sp"], S["sp"].n)
            war_st = {}
            for cc in range(NCH):
                par = cc % 2
                r, hcl = divmod(cc, 8)
                src = yg.ap()[r * 1024 + hcl * 128: r * 1024 + (hcl + 1) * 128, :].rearrange(
                    "p (h t) -> p h t", t=T)
                P.wait("g", sv_a, war_st.get(par, 0))
                n_l = P.dma("g", o_stg[:, par], src, S[f"oio{par}"])
                P.wait("v", S[f"oio{par}"], n_l)
                dst = o_ao[:, cc, :]
                P.op("v", lambda E, dst=dst, par=par: E.tensor_scalar(
                    out=dst, in0=o_stg[:, par, 0], scalar1=cfgm[:, 0:1], scalar2=None,
                    op0=ALU.mult), inc=sv_a)
                P.wait("v", sv_a, sv_a.n)
                war_st[par] = P.op("v", lambda E, dst=dst, par=par: E.scalar_tensor_tensor(
                    out=dst, in0=o_stg[:, par, 1], scalar=cfgm[:, 1:2], in1=dst,
                    op0=ALU.mult, op1=ALU.add), inc=sv_a)
            P.barrier()
            wo_r = wl_ap("woT1").rearrange("(dc p) (cc c) -> cc p dc c", p=128, c=128)
            war_w = {}
            war_ev = {}
            for cc in range(NCH):
                par = cc % 2
                if cc >= 2:
                    P.wait("y", st_a, war_w[cc - 2])
                n_w = P.dma("y", o_wst[:, par], wo_r[cc], S[f"o_w{par}"])
                P.wait("t", S[f"o_w{par}"], n_w)
                for tw in range(2):
                    pp = ps[tw % 2]
                    if cc >= 1:
                        P.wait("t", sv_a, war_ev[(cc - 1, tw)])
                    for dc in range(NCH):
                        lhsT = o_wst[:, par, dc, :]
                        rhs = o_ao[:, dc, tw * 512:(tw + 1) * 512]
                        P.op("t", lambda E, lhsT=lhsT, rhs=rhs, dc=dc, pp=pp: E.matmul(
                            pp[:, 0:512], lhsT, rhs, start=(dc == 0), stop=(dc == NCH - 1)),
                            inc=(st_a if dc == NCH - 1 else None))
                    n_mm = st_a.n
                    war_w[cc] = n_mm
                    P.wait("v", st_a, n_mm)
                    P.wait("v", S["sp"], S["sp"].n)
                    sdst = stream[:, cc, tw * 512:(tw + 1) * 512]
                    bslice = bo_sb[:, cc:cc + 1]
                    war_ev[(cc, tw)] = P.op("v", lambda E, sdst=sdst, pp=pp, bslice=bslice: E.scalar_tensor_tensor(
                        out=sdst, in0=pp[:, 0:512], scalar=bslice, in1=sdst,
                        op0=ALU.add, op1=ALU.add), inc=sv_a)
            P.barrier()

        run_outproj()
        dump_stream(4)

        run_moe()

        P.wait("g", S["m_v"], S["m_v"].n)
        n_out = P.dma("g", out_t.rearrange("(cc p) t -> p cc t", p=128), stream[:], S["io"])
        P.wait("g", S["io"], n_out)

        with nc.Block() as block:
            P.finalize(block)
    return nc


try:
    from prog_fused import WL_PAD, MOE_PAD, build_fused
except ImportError:
    pass
from concourse.bass_utils import run_bass_kernel_spmd

_fused_nc = [None]
_last_results = [None]


def kernel(**inputs):
    f = np.float32
    x = np.asarray(inputs["L0_attn_x"], dtype=f)
    Wq0 = np.asarray(inputs["L0_attn_Wq_w"], f)
    Wdkv = np.asarray(inputs["L0_attn_Wdkv_w"], f)
    Wuk = np.asarray(inputs["L0_attn_Wuk_w"], f)
    Wuv = np.asarray(inputs["L0_attn_Wuv_w"], f)
    Wo0 = np.asarray(inputs["L0_attn_Wo_w"], f)
    Wq1 = np.asarray(inputs["L1_attn_Wq"], f)
    Wk1 = np.asarray(inputs["L1_attn_Wk"], f)
    Wv1 = np.asarray(inputs["L1_attn_Wv"], f)
    Wo1 = np.asarray(inputs["L1_attn_Wo"], f)
    bo = np.asarray(inputs["L1_attn_bo"], f)
    Wr = np.asarray(inputs["L1_ffn_Wr"], f)
    Wg1 = np.asarray(inputs["L1_ffn_Wg"], f)
    Wu1 = np.asarray(inputs["L1_ffn_Wu"], f)
    Wd1 = np.asarray(inputs["L1_ffn_Wd"], f)
    gate_p = np.asarray(inputs["L0_ffn_gate_W_packed"], np.int32)
    gate_s = np.asarray(inputs["L0_ffn_gate_scales"], f)
    up_p = np.asarray(inputs["L0_ffn_up_W_packed"], np.int32)
    up_s = np.asarray(inputs["L0_ffn_up_scales"], f)
    down_p = np.asarray(inputs["L0_ffn_down_W_packed"], np.int32)
    down_s = np.asarray(inputs["L0_ffn_down_scales"], f)

    Wq0T = np.ascontiguousarray(Wq0.T)
    Wo0T = np.ascontiguousarray(Wo0.T)
    WdkvT = np.ascontiguousarray(Wdkv.T)
    Wq1T = np.ascontiguousarray(Wq1.T)
    Wk1T = np.ascontiguousarray(Wk1.T)
    Wv1T = np.ascontiguousarray(Wv1.T)
    Wo1T = np.ascontiguousarray(Wo1.T)

    NLOC = 16384 // 8
    in_maps = []
    for c in range(8):
        b, half = divmod(c, 2)
        hb = half * 8
        kvb = half * 2
        wl_parts = [
            Wq0[:, hb * 128:(hb + 8) * 128],
            Wq0T[:, hb * 128:(hb + 8) * 128],
            Wo0T[:, hb * 128:(hb + 8) * 128],
            Wuk, Wuv, WdkvT,
            Wq1T[:, hb * 128:(hb + 8) * 128],
            Wk1T[:, kvb * 128:(kvb + 2) * 128],
            Wv1T[:, kvb * 128:(kvb + 2) * 128],
            Wo1T, Wr,
        ]
        wl_flat = np.zeros(WL_PAD, f)
        o = 0
        for p_ in wl_parts:
            wl_flat[o:o + p_.size] = np.ascontiguousarray(p_).ravel()
            o += p_.size
        moe_flat = np.zeros(MOE_PAD, f)
        o = 0
        for p_ in (Wg1[c], Wu1[c], Wd1[c]):
            moe_flat[o:o + p_.size] = p_.ravel()
            o += p_.size
        cfg = np.zeros((128, 2), f)
        cfg[:, 0] = 1.0 if half == 0 else 0.0
        cfg[:, 1] = 1.0 - cfg[:, 0]
        in_maps.append(dict(
            x_in=np.ascontiguousarray(x[b, half * 1024:(half + 1) * 1024, :].T),
            wl_f32=wl_flat.reshape(WL_PAD // 512, 512),
            moe_f32=moe_flat.reshape(MOE_PAD // 512, 512),
            gate_p=gate_p[c * NLOC:(c + 1) * NLOC], gate_s=gate_s[c * NLOC:(c + 1) * NLOC],
            up_p=up_p[c * NLOC:(c + 1) * NLOC], up_s=up_s[c * NLOC:(c + 1) * NLOC],
            down_p=down_p[c * 256:(c + 1) * 256], down_s=down_s[c * 256:(c + 1) * 256],
            bo_in=bo, cfg_in=cfg,
        ))

    if _fused_nc[0] is None:
        _fused_nc[0] = build_fused()
    res = run_bass_kernel_spmd(_fused_nc[0], in_maps, core_ids=list(range(8)))
    _last_results[0] = res.results
    out = np.empty((4, 2048, 2048), f)
    for c in range(8):
        b, half = divmod(c, 2)
        out[b, half * 1024:(half + 1) * 1024, :] = np.asarray(res.results[c]["out_t"]).T
    return out


# version 09
# speedup vs baseline: 1.1895x; 1.0043x over previous
import numpy as np
from contextlib import ExitStack
import concourse.bass as bass
import concourse.mybir as mybir

F32 = mybir.dt.float32
BF16 = mybir.dt.bfloat16
I32 = mybir.dt.int32
AF = mybir.ActivationFunctionType
ALU = mybir.AluOpType
AX = mybir.AxisListType

ENGINES = ("y", "v", "s", "t", "g")
ENGINE_ATTR = {"y": "sync", "v": "vector", "s": "scalar", "t": "tensor", "g": "gpsimd"}

C = 2048
NCH = 16
GELU_K = 0.7978845608028654


class Sem:
    __slots__ = ("h", "n", "name")
    def __init__(self, h, name):
        self.h = h
        self.n = 0
        self.name = name
    def bump(self, k=1):
        self.n += k
        return self.n


class Prog:
    def __init__(self, nc, es):
        self.nc = nc
        self.es = es
        self.plan = {e: [] for e in ENGINES}
        self.sems = {}
        self._barn = 0
        self.last_inc = {e: None for e in ENGINES}
        self.dirty = {e: False for e in ENGINES}
        self._dummy = None

    def sem(self, name):
        s = Sem(self.es.enter_context(self.nc.semaphore(name)), name)
        self.sems[name] = s
        return s

    def const_f32(self, *values):
        if not hasattr(self, "_const_sem"):
            self._const_sem = self.sem("constf")
        for v in values:
            t = self.es.enter_context(self.nc.sbuf_tensor(f"cf_{str(v).replace('.','_').replace('-','m')}", [128, 1], F32))
            self.op("v", lambda E, t=t, v=v: E.memset(t[:], float(v)), inc=self._const_sem)
            self.nc.const_aps.register(F32, float(v), t[:])
        return self._const_sem

    def op(self, eng, fn, inc=None, k=1):
        if inc is None:
            self.plan[eng].append(fn)
            self.dirty[eng] = True
            return None
        n = inc.bump(k)
        self.plan[eng].append(lambda E, fn=fn: fn(E).then_inc(inc.h, k))
        self.last_inc[eng] = (inc, n)
        self.dirty[eng] = False
        return n

    def wait(self, eng, sem, n):
        if n > 0:
            self.plan[eng].append(lambda E: E.wait_ge(sem.h, n))

    def coll(self, fn, sem):
        n = sem.bump(1)
        self.plan["g"].append(lambda E, fn=fn: fn(E).then_inc(sem.h))
        self.last_inc["g"] = (sem, n)
        self.dirty["g"] = False
        return n

    def dma(self, eng, out, in_, sem, **kw):
        n = sem.bump(16)
        self.plan[eng].append(
            lambda E: E.dma_start(out=out, in_=in_, **kw).then_inc(sem.h, 16))
        self.last_inc[eng] = (sem, n)
        self.dirty[eng] = False
        return n

    def barrier(self, engs=("v", "s", "t", "g")):
        if self._dummy is None:
            self._dummy = self.es.enter_context(
                self.nc.sbuf_tensor("bar_dummy", [128, 8], F32))
            self._bars = {e: self.sem(f"bar_{e}") for e in ENGINES}
        d = self._dummy
        for e in engs:
            if self.dirty[e]:
                s = self._bars[e]
                if e == "v":
                    self.op("v", lambda E: E.memset(d[:, 0:2], 0.0), inc=s)
                elif e == "s":
                    self.op("s", lambda E: E.mul(out=d[:, 2:4], in_=d[:, 2:4], mul=1.0), inc=s)
                elif e == "g":
                    self.op("g", lambda E: E.memset(d[:, 4:6], 0.0), inc=s)
                else:
                    raise AssertionError(f"dirty engine {e} needs a real inc'd op before barrier")
        self._barn += 1
        for e in tuple(engs) + ("y",):
            for o in engs:
                if o != e and self.last_inc[o] is not None:
                    sem, n = self.last_inc[o]
                    self.wait(e, sem, n)

    def finalize(self, block):
        for e in ENGINES:
            fns = self.plan[e]
            if not fns:
                continue
            def runner(E, fns=fns):
                for f in fns:
                    f(E)
            getattr(block, ENGINE_ATTR[e])(runner)


class Bufs:
    def __init__(self, nc, es, T_CORE, want=("ffn",)):
        sb = lambda name, shape, dt=F32: es.enter_context(nc.sbuf_tensor(name, shape, dt))
        ps = lambda name, shape, dt=F32: es.enter_context(nc.psum_tensor(name, shape, dt))
        T = T_CORE
        self.T = T
        self.T2 = T // 2
        self.stream = sb("stream", [128, NCH, T])
        self.x_ln = sb("x_ln", [128, NCH, T], BF16)
        self.big_a = sb("big_a", [128, NCH, T], BF16)
        self.rows = sb("rows", [1, 4, self.T2])
        self.ones_col16 = sb("ones_col16", [128, 1], BF16)
        self.ones_row32 = sb("ones_row32", [1, 128])
        self.ident16 = sb("ident16", [128, 128], BF16)
        self.xb16 = self.big_a[:, :, 0:self.T2]
        self.xsq16 = self.big_a[:, :, self.T2:2 * self.T2]
        if "ffn" in want:
            self.wg_buf = sb("wg_buf", [128, 2, NCH, 128], BF16)
            self.wu_buf = sb("wu_buf", [128, 2, NCH, 128], BF16)
            self.wd_buf = sb("wd_buf", [128, 2, NCH, 128], BF16)
            self.gelu_sb = sb("gelu_sb", [128, 2, self.T2], BF16)
            self.ds_acc = sb("ds_acc", [128, NCH, self.T2])
        self.ps = [ps(f"ps{i}", [128, 512]) for i in range(6)]
        self.ps_small = ps("ps_small", [1, 512])
        self.pt = ps("pt", [128, 1024], BF16)


def emit_consts(P, B):
    sg = P.sem("const_g")
    sv = P.sem("const_v")
    P.op("v", lambda E: E.memset(B.ones_col16[:], 1.0))
    P.op("v", lambda E: E.memset(B.ones_row32[:], 1.0), inc=sv)
    P.op("g", lambda E: E.memset(B.ident16[:], 0.0), inc=sg)
    P.wait("g", sg, 1)
    P.op("g", lambda E: E.affine_select(
        out=B.ident16[:], in_=B.ident16[:], compare_op=ALU.not_equal,
        fill=1.0, base=0, pattern=[[-1, 128]], channel_multiplier=1), inc=sg)
    P.wait("t", sv, 1)
    P.wait("t", sg, 2)
    return sg, sv


def emit_ln(P, B, half, sems, war=(), dst_t0=None):
    T2 = B.T2
    t0 = half * T2
    dt0 = t0 if dst_t0 is None else dst_t0
    ss, sv, st = sems["ln_s"], sems["ln_v"], sems["ln_t"]
    src = B.stream[:, :, t0:t0 + T2]
    dst = B.x_ln[:, :, dt0:dt0 + T2]
    negmean = B.rows[:, 0, :]
    msq = B.rows[:, 1, :]
    var = B.rows[:, 2, :]
    rstd = B.rows[:, 3, :]
    lnv = B.rows[:, 1, :]
    pbc = B.ps[5][:, 0:T2]
    bc = pbc.unsqueeze(1).broadcast_to([128, NCH, T2])

    P.wait("s", st, st.n)
    P.wait("t", sv, sv.n)
    for (sem, n) in war:
        P.wait("s", sem, n)
        P.wait("v", sem, n)

    n_cast = P.op("s", lambda E: E.copy(out=B.xb16, in_=src), inc=ss)
    n_sq = P.op("s", lambda E: E.activation(out=B.xsq16, in_=src, func=AF.Square), inc=ss)

    P.wait("t", ss, n_cast)
    for i in range(NCH):
        rhs = B.xb16[:, i, :]
        P.op("t", lambda E, rhs=rhs, i=i: E.matmul(
            B.ps_small[:, 0:T2], B.ones_col16[:], rhs,
            start=(i == 0), stop=(i == NCH - 1)), inc=(st if i == NCH - 1 else None))
    n_sums = st.n
    P.wait("v", st, n_sums)
    n_nm = P.op("v", lambda E: E.tensor_single_scalar(
        out=negmean, in_=B.ps_small[:, 0:T2], scalar=-1.0 / C, op=ALU.mult), inc=sv)
    P.wait("t", sv, n_nm)
    P.wait("t", ss, n_sq)
    for i in range(NCH):
        rhs = B.xsq16[:, i, :]
        P.op("t", lambda E, rhs=rhs, i=i: E.matmul(
            B.ps_small[:, 0:T2], B.ones_col16[:], rhs,
            start=(i == 0), stop=(i == NCH - 1)), inc=(st if i == NCH - 1 else None))
    n_sumsq = st.n
    P.wait("v", st, n_sumsq)
    P.op("v", lambda E: E.tensor_scalar(
        out=msq, in0=B.ps_small[:, 0:T2], scalar1=1.0 / C, scalar2=1e-5,
        op0=ALU.mult, op1=ALU.add), inc=sv)
    P.wait("v", sv, sv.n)
    P.op("v", lambda E: E.tensor_tensor(out=var, in0=negmean, in1=negmean, op=ALU.mult), inc=sv)
    P.wait("v", sv, sv.n)
    n_var = P.op("v", lambda E: E.tensor_sub(out=var, in0=msq, in1=var), inc=sv)
    P.wait("s", sv, n_var)
    P.op("s", lambda E: E.activation(out=lnv, in_=var, func=AF.Ln), inc=ss)
    P.wait("s", ss, ss.n)
    n_rstd = P.op("s", lambda E: E.activation(out=rstd, in_=lnv, func=AF.Exp, scale=-0.5), inc=ss)
    P.wait("t", sv, n_nm)
    n_bc_m = P.op("t", lambda E: E.matmul(
        pbc, B.ones_row32[:], negmean, start=True, stop=True), inc=st)
    P.wait("v", st, n_bc_m)
    n_xm = P.op("v", lambda E: E.tensor_tensor(out=dst, in0=src, in1=bc, op=ALU.add), inc=sv)
    P.wait("t", sv, n_xm)
    P.wait("t", ss, n_rstd)
    n_bc_r = P.op("t", lambda E: E.matmul(
        pbc, B.ones_row32[:], rstd, start=True, stop=True), inc=st)
    P.wait("v", st, n_bc_r)
    n_out = P.op("v", lambda E: E.tensor_tensor(out=dst, in0=dst, in1=bc, op=ALU.mult), inc=sv)
    return n_out


def emit_ffn0(P, B, w, sems, n_mlp=16384, xln_half=False):
    NC = n_mlp // 128
    NGRP = NC // NCH
    T2 = B.T2
    ss, sv, st = sems["f_s"], sems["f_v"], sems["f_t"]
    wsem = sems["f_w"]
    dsem = sems["f_d"]
    lnv = sems["ln_v"]

    for half in (0, 1):
        n_ln = emit_ln(P, B, half, sems, war=[(st, st.n)],
                       dst_t0=(0 if xln_half else None))
        P.wait("t", lnv, n_ln)
        t0 = half * T2
        x0 = 0 if xln_half else t0
        w_ready = {}
        mm_done = {}
        gel_done = {}
        ev_done = {}
        dmm_done = {}
        dev_done = {}

        def load_w(jg):
            if "wg_gate" in sems:
                sems["wg_gate"](jg)
            P.dma("y", B.wg_buf[:, jg % 2], w["wg_t"](jg), wsem[jg % 2])
            w_ready[jg] = P.dma("y", B.wu_buf[:, jg % 2], w["wu_t"](jg), wsem[jg % 2])

        for g in range(NGRP):
            for j in range(NCH):
                jg = g * NCH + j
                par = jg % 2
                if jg == 0:
                    load_w(0)
                    if NC > 1:
                        load_w(1)
                psg, psu = B.ps[0 + par][:, 0:T2], B.ps[2 + par][:, 0:T2]
                P.wait("t", wsem[jg % 2], w_ready[jg])
                if jg >= 2:
                    P.wait("t", sv, ev_done[jg - 2])
                    P.wait("t", ss, gel_done[jg - 2])
                for cc in range(NCH):
                    lhsT = B.wg_buf[:, par, cc, :]
                    rhs = B.x_ln[:, cc, x0:x0 + T2]
                    P.op("t", lambda E, lhsT=lhsT, rhs=rhs, cc=cc, psg=psg: E.matmul(
                        psg, lhsT, rhs, start=(cc == 0), stop=(cc == NCH - 1)))
                for cc in range(NCH):
                    lhsT = B.wu_buf[:, par, cc, :]
                    rhs = B.x_ln[:, cc, x0:x0 + T2]
                    P.op("t", lambda E, lhsT=lhsT, rhs=rhs, cc=cc, psu=psu: E.matmul(
                        psu, lhsT, rhs, start=(cc == 0), stop=(cc == NCH - 1)),
                        inc=(st if cc == NCH - 1 else None))
                mm_done[jg] = st.n
                if jg + 2 < NC:
                    P.wait("y", st, mm_done[jg])
                    load_w(jg + 2)
                P.wait("s", st, mm_done[jg])
                if jg >= 2:
                    P.wait("s", sv, ev_done[jg - 2])
                gdst = B.gelu_sb[:, par, :]
                gel_done[jg] = P.op("s", lambda E, gdst=gdst, psg=psg: E.activation(
                    out=gdst, in_=psg, func=AF.Gelu_apprx_tanh), inc=ss)
                P.wait("v", ss, gel_done[jg])
                P.wait("v", st, mm_done[jg])
                if g > 0 and j == 0:
                    P.wait("v", st, dmm_done[NCH - 1])
                idst = B.big_a[:, j, t0:t0 + T2]
                ev_done[jg] = P.op("v", lambda E, idst=idst, gdst=gdst, psu=psu: E.tensor_tensor(
                    out=idst, in0=gdst, in1=psu, op=ALU.mult), inc=sv)
                if "jg_feeder" in sems:
                    sems["jg_feeder"]()
            if "wd_gate" in sems:
                gsem, gn = sems.pop("wd_gate")
                P.wait("y", gsem, gn)
            n_grp_ev = sv.n
            for dc in range(NCH):
                par = dc % 2
                psd = B.ps[4][:, 0:T2]
                if dc >= 2:
                    P.wait("y", st, dmm_done[dc - 2])
                ndw = P.dma("y", B.wd_buf[:, par], w["wd_t"](dc, g), dsem[par])
                P.wait("t", dsem[par], ndw)
                P.wait("t", sv, n_grp_ev)
                if dc >= 1:
                    P.wait("t", sv, dev_done[dc - 1])
                for j in range(NCH):
                    lhsT = B.wd_buf[:, par, j, :]
                    rhs = B.big_a[:, j, t0:t0 + T2]
                    P.op("t", lambda E, lhsT=lhsT, rhs=rhs, j=j, psd=psd: E.matmul(
                        psd, lhsT, rhs, start=(j == 0), stop=(j == NCH - 1)),
                        inc=(st if j == NCH - 1 else None))
                dmm_done[dc] = st.n
                P.wait("v", st, dmm_done[dc])
                ddst = B.ds_acc[:, dc, :]
                if g == 0:
                    dev_done[dc] = P.op("v", lambda E, ddst=ddst, psd=psd: E.tensor_copy(
                        out=ddst, in_=psd), inc=sv)
                else:
                    dev_done[dc] = P.op("v", lambda E, ddst=ddst, psd=psd: E.tensor_tensor(
                        out=ddst, in0=ddst, in1=psd, op=ALU.add), inc=sv)
        P.wait("v", sv, sv.n)
        sdst = B.stream[:, :, t0:t0 + T2]
        P.op("v", lambda E, sdst=sdst: E.scalar_tensor_tensor(
            out=sdst, in0=B.ds_acc[:], scalar=0.5, in1=sdst,
            op0=ALU.mult, op1=ALU.add), inc=sv)
    return sv.n


def make_ffn0_sems(P):
    return {
        "ln_s": P.sem("ln_s"), "ln_v": P.sem("ln_v"), "ln_t": P.sem("ln_t"),
        "f_s": P.sem("f_s"), "f_v": P.sem("f_v"), "f_t": P.sem("f_t"),
        "f_w": [P.sem("f_w0"), P.sem("f_w1")],
        "f_d": [P.sem("f_d0"), P.sem("f_d1")],
    }


"""attn0a: MLA attention, head-sharded. Per core: 8 heads x full 2048-token causal.
in: xln_all [2, 2048, 1024] bf16 ([t-half][c][t_local]); weight slices bf16.
out: y_mine [1024, 2048] bf16 (my 8 head-chunks' output rows, all tokens)."""

FILL_L0 = -1e30

NH_LOC = 8
NQT = 16
TFULL = 2048
RSQ = 1.0 / np.sqrt(128.0)


def emit_attn_core(P, B, sems, scores_mm, av_lhsT, out_dma, fill, nheads=NH_LOC):
    ss, sv, st, sg = sems["a_s"], sems["a_v"], sems["a_t"], sems["a_g"]
    so = sems["a_o"]
    scores = B["scores"]
    attn_b = B["attn_b"]
    attnT = B["attnT"]
    rowm = B["rowm"]
    pt = B["pt"]
    sc_ps = B["sc_ps"]
    y_ps = B["y_ps"]
    ystage = B["ystage"]
    dmask = B["dmask"]
    war_pt = B.setdefault("_war_pt", {})
    war_sc = B.setdefault("_war_sc", {})
    war_y = B.setdefault("_war_y", {})
    for h in range(nheads):
        for quad in range(4):
            nw = quad + 1
            nk = nw * 512
            for j in range(4):
                qt = quad * 4 + j
                P.wait("s", sv, sv.n)
                for w in range(nw):
                    par = w % 2
                    ps = sc_ps[par]
                    we, wn = war_sc.get(par, (None, 0))
                    if we is not None:
                        P.wait("t", we, wn)
                    n_mm = scores_mm(h, qt, w, ps[:, 0:512])
                    dst = scores[:, w * 512:(w + 1) * 512]
                    if w == quad:
                        P.wait("v", st, n_mm)
                        msk = dmask[:, qt % 4, :]
                        n_e = P.op("v", lambda E, dst=dst, ps=ps, msk=msk: E.tensor_tensor(
                            out=dst, in0=ps[:, 0:512], in1=msk, op=ALU.add), inc=sv)
                        war_sc[par] = (sv, n_e)
                    else:
                        P.wait("s", st, n_mm)
                        n_e = P.op("s", lambda E, dst=dst, ps=ps: E.copy(
                            out=dst, in_=ps[:, 0:512]), inc=ss)
                        war_sc[par] = (ss, n_e)
                P.wait("v", sv, sv.n)
                P.wait("v", ss, ss.n)
                n_max = P.op("v", lambda E, nk=nk: E.reduce_max(
                    out=rowm[:, 0:1], in_=scores[:, 0:nk], axis=AX.X, negate=True), inc=sv)
                P.wait("s", sv, n_max)
                P.wait("s", st, st.n)
                n_exp = P.op("s", lambda E, nk=nk: E.activation(
                    out=attn_b[:, 0:nk], in_=scores[:, 0:nk], func=AF.Exp,
                    bias=rowm[:, 0:1], accum_out=rowm[:, 1:2]), inc=ss)
                P.wait("v", ss, n_exp)
                n_rec = P.op("v", lambda E: E.reciprocal(out=rowm[:, 1:2], in_=rowm[:, 1:2]), inc=sv)
                P.wait("v", sv, n_rec)
                n_nrm = P.op("v", lambda E, nk=nk: E.tensor_scalar_mul(
                    out=attn_b[:, 0:nk], in0=attn_b[:, 0:nk], scalar1=rowm[:, 1:2]), inc=sv)
                for kc in range(nw * 4):
                    tpar = kc % 2
                    src = attn_b[:, kc * 128:(kc + 1) * 128]
                    P.wait("t", sv, n_nrm)
                    P.wait("t", ss, war_pt.get(tpar, 0))
                    n_tp = P.op("t", lambda E, tpar=tpar, src=src: E.transpose(
                        pt[tpar][:, 0:128], src, B["ident16"][:]), inc=st)
                    P.wait("s", st, n_tp)
                    dstT = attnT[:, kc, j * 128:(j + 1) * 128]
                    war_pt[tpar] = P.op("s", lambda E, dstT=dstT, tpar=tpar: E.copy(
                        out=dstT, in_=pt[tpar][:, 0:128]), inc=ss)
            ypar = (quad % 2) if not B.get("y_single") else 0
            yp = y_ps[ypar]
            P.wait("t", ss, ss.n)
            P.wait("t", so[ypar], war_y.get(ypar, 0))
            for kc in range(nw * 4):
                lhsT = av_lhsT(h, kc)
                rhs = attnT[:, kc, :]
                P.op("t", lambda E, lhsT=lhsT, rhs=rhs, kc=kc, yp=yp, nw=nw: E.matmul(
                    yp[:, 0:512], lhsT, rhs, start=(kc == 0), stop=(kc == nw * 4 - 1)),
                    inc=(st if kc == nw * 4 - 1 else None))
            n_av = st.n
            P.wait("s", st, n_av)
            P.wait("s", so[ypar], war_y.get(ypar, 0))
            n_ye = P.op("s", lambda E, ypar=ypar, yp=yp: E.copy(
                out=ystage[:, ypar, :], in_=yp[:, 0:512]), inc=ss)
            P.wait("g", ss, n_ye)
            war_y[ypar] = out_dma(h, quad, ystage[:, ypar, :])
    return


def build_attn0(T_CORE=1024):
    TF = 2 * T_CORE
    nc = bass.Bass("TRN2", target_bir_lowering=False, debug=False)
    xln_all = nc.dram_tensor("xln_all", [2, 2048, T_CORE], BF16, kind="ExternalInput").ap()
    wq_nat_s = nc.dram_tensor("wq_nat_s", [2048, 1024], BF16, kind="ExternalInput").ap()
    wuk = nc.dram_tensor("wuk", [2048, 512], BF16, kind="ExternalInput").ap()
    wqT_s = nc.dram_tensor("wqT_s", [2048, 1024], BF16, kind="ExternalInput").ap()
    wuv = nc.dram_tensor("wuv", [2048, 512], BF16, kind="ExternalInput").ap()
    woT_s = nc.dram_tensor("woT_s", [2048, 1024], BF16, kind="ExternalInput").ap()
    wdkvT = nc.dram_tensor("wdkvT", [2048, 512], BF16, kind="ExternalInput").ap()
    y_mine = nc.dram_tensor("y_mine", [1024, TF], BF16, kind="ExternalOutput").ap()

    with ExitStack() as es:
        P = Prog(nc, es)
        sb = lambda name, shape, dt=F32: es.enter_context(nc.sbuf_tensor(name, shape, dt))
        ps = lambda name, shape, dt=F32: es.enter_context(nc.psum_tensor(name, shape, dt))
        xla = sb("xla", [128, NCH, TF], BF16)
        big1 = sb("big1", [128, NCH, TF // 2], BF16)
        cv = sb("cv", [128, NCH, 1024], BF16)
        ckv = sb("ckv", [128, 4, TF], BF16)
        keff = sb("keff", [128, NH_LOC, 512], BF16)
        qlat = sb("qlat", [128, 4, TF], BF16)
        scores = sb("scores", [128, 2048])
        attn_b = sb("attn_b", [128, 2048], BF16)
        attnT = sb("attnT", [128, 16, 512], BF16)
        rowm = sb("rowm", [128, 2])
        wst = sb("wst", [128, 1, NCH, 128], BF16)
        wbig = cv[:, :, 0:512]
        ystage = sb("ystage", [128, 2, 512], BF16)
        ident16 = sb("ident16", [128, 128], BF16)
        dmask = sb("dmask", [128, 4, 512], BF16)
        sc_ps = [ps("sc_ps0", [128, 512]), ps("sc_ps1", [128, 512])]
        y_ps = [ps("y_ps0", [128, 512]), ps("y_ps1", [128, 512])]
        pt = [ps("pt0", [128, 1024], BF16), ps("pt1", [128, 1024], BF16)]

        sems = {k: P.sem(k) for k in ("a_s", "a_v", "a_t", "a_g",
                                      "w0", "w1", "io")}
        sems["a_o"] = [P.sem("a_o0"), P.sem("a_o1")]
        ss, sv, st, sg = sems["a_s"], sems["a_v"], sems["a_t"], sems["a_g"]
        so = sems["a_o"]
        io = sems["io"]
        wsem = [sems["w0"], sems["w1"]]

        csg = P.sem("csg")
        P.op("g", lambda E: E.memset(ident16[:], 0.0), inc=csg)
        P.op("g", lambda E: E.memset(dmask[:], 0.0), inc=csg)
        P.wait("g", csg, 2)
        P.op("g", lambda E: E.affine_select(
            out=ident16[:], in_=ident16[:], compare_op=ALU.not_equal,
            fill=1.0, base=0, pattern=[[-1, 128]], channel_multiplier=1), inc=csg)
        for m in range(4):
            sl = dmask[:, m, :]
            P.op("g", lambda E, sl=sl, m=m: E.affine_select(
                out=sl, in_=sl, compare_op=ALU.is_ge, fill=FILL_L0,
                base=m * 128, pattern=[[-1, 512]], channel_multiplier=1), inc=csg)
        P.wait("t", csg, 7)
        P.wait("v", csg, 7)

        for r in range(2):
            P.dma("y", xla[:, :, r * T_CORE:(r + 1) * T_CORE],
                  xln_all[r].rearrange("(cc p) t -> p cc t", p=128), io)
        n_x = io.n
        n_wuk = P.dma("y", wbig, wuk.rearrange("(cc p) l -> p cc l", p=128), io)
        P.wait("t", io, n_wuk)

        wq_r = wq_nat_s.rearrange("(dc p) (sc s) -> sc p dc s", p=128, s=128)
        war_w = {}
        war_sc_ev = {}
        for sc in range(NH_LOC):
            if sc >= 1:
                P.wait("y", st, war_w[sc - 1])
            n_w = P.dma("y", wst[:, 0], wq_r[sc], wsem[0])
            P.wait("t", wsem[0], n_w)
            pp = sc_ps[sc % 2]
            if sc >= 2:
                P.wait("t", ss, war_sc_ev[sc - 2])
            for dc in range(NCH):
                P.op("t", lambda E, dc=dc, pp=pp: E.matmul(
                    pp[:, 0:512], wst[:, 0, dc, :], wbig[:, dc, :],
                    start=(dc == 0), stop=(dc == NCH - 1)),
                    inc=(st if dc == NCH - 1 else None))
            war_w[sc] = st.n
            P.wait("s", st, st.n)
            war_sc_ev[sc] = P.op("s", lambda E, sc=sc, pp=pp: E.mul(
                out=keff[:, sc, :], in_=pp[:, 0:512], mul=RSQ), inc=ss)
        P.barrier()

        P.wait("y", st, st.n)
        n_wuv = P.dma("y", wbig, wuv.rearrange("(dc p) l -> p dc l", p=128), io)
        woT_r = woT_s.rearrange("(dc p) (cw c) -> cw p dc c", p=128, c=512)
        veff = big1[:, 12:16, :].rearrange("p a t -> p a t")
        wo_stage = big1[:, 0:8, :].rearrange("p a t -> p (a t)").rearrange(
            "p (dc c) -> p dc c", c=512)
        P.wait("t", io, n_wuv)
        for cw in range(2):
            P.wait("y", st, st.n)
            n_w = P.dma("y", wo_stage, woT_r[cw], wsem[cw % 2])
            P.wait("t", wsem[cw % 2], n_w)
            for lc in range(4):
                pp = sc_ps[lc % 2]
                P.wait("t", ss, ss.n)
                for dc in range(NCH):
                    P.op("t", lambda E, dc=dc, lc=lc, pp=pp: E.matmul(
                        pp[:, 0:512], wbig[:, dc, lc * 128:(lc + 1) * 128],
                        wo_stage[:, dc, :], start=(dc == 0), stop=(dc == NCH - 1)),
                        inc=(st if dc == NCH - 1 else None))
                P.wait("s", st, st.n)
                dst = veff[:, lc, cw * 512:(cw + 1) * 512]
                P.op("s", lambda E, dst=dst, pp=pp: E.copy(out=dst, in_=pp[:, 0:512]), inc=ss)
            P.barrier()

        P.wait("y", st, st.n)
        n_wk = P.dma("y", wbig, wdkvT.rearrange("(cc p) l -> p cc l", p=128), io)
        P.wait("t", io, n_wk)
        for lc in range(4):
            for tw in range(4):
                pp = sc_ps[tw % 2]
                P.wait("t", ss, ss.n)
                for cc in range(NCH):
                    P.op("t", lambda E, cc=cc, lc=lc, tw=tw, pp=pp: E.matmul(
                        pp[:, 0:512], wbig[:, cc, lc * 128:(lc + 1) * 128],
                        xla[:, cc, tw * 512:(tw + 1) * 512],
                        start=(cc == 0), stop=(cc == NCH - 1)),
                        inc=(st if cc == NCH - 1 else None))
                P.wait("s", st, st.n)
                dst = ckv[:, lc, tw * 512:(tw + 1) * 512]
                P.op("s", lambda E, dst=dst, pp=pp: E.copy(out=dst, in_=pp[:, 0:512]), inc=ss)
        P.barrier()

        for kc in range(16):
            for cwi in range(2):
                pp = sc_ps[cwi]
                P.wait("t", ss, ss.n)
                for lc in range(4):
                    P.op("t", lambda E, lc=lc, kc=kc, cwi=cwi, pp=pp: E.matmul(
                        pp[:, 0:512], ckv[:, lc, kc * 128:(kc + 1) * 128],
                        veff[:, lc, cwi * 512:(cwi + 1) * 512],
                        start=(lc == 0), stop=(lc == 3)),
                        inc=(st if lc == 3 else None))
                P.wait("s", st, st.n)
                dst = cv[:, kc, cwi * 512:(cwi + 1) * 512]
                P.op("s", lambda E, dst=dst, pp=pp: E.copy(out=dst, in_=pp[:, 0:512]), inc=ss)
        P.barrier()

        q = big1[:].rearrange("p a t -> p (a t)").rearrange("p (h t) -> p h t", t=TF)
        wqT_r = wqT_s.rearrange("(cc p) (dq d) -> dq p cc d", p=128, d=128)
        for dqc in range(NH_LOC):
            P.wait("y", st, st.n)
            n_w = P.dma("y", wst[:, 0], wqT_r[dqc], wsem[0])
            P.wait("t", wsem[0], n_w)
            for tw in range(4):
                pp = sc_ps[tw % 2]
                P.wait("t", ss, ss.n)
                for cc in range(NCH):
                    P.op("t", lambda E, cc=cc, tw=tw, pp=pp: E.matmul(
                        pp[:, 0:512], wst[:, 0, cc, :],
                        xla[:, cc, tw * 512:(tw + 1) * 512],
                        start=(cc == 0), stop=(cc == NCH - 1)),
                        inc=(st if cc == NCH - 1 else None))
                P.wait("s", st, st.n)
                dst = q[:, dqc, tw * 512:(tw + 1) * 512]
                P.op("s", lambda E, dst=dst, pp=pp: E.copy(out=dst, in_=pp[:, 0:512]), inc=ss)
            P.barrier()

        def mk_qlat(h):
            for lc in range(4):
                for tw in range(4):
                    pp = sc_ps[tw % 2]
                    P.wait("t", ss, ss.n)
                    P.op("t", lambda E, lc=lc, tw=tw, h=h, pp=pp: E.matmul(
                        pp[:, 0:512], keff[:, h, lc * 128:(lc + 1) * 128],
                        q[:, h, tw * 512:(tw + 1) * 512],
                        start=True, stop=True), inc=st)
                    P.wait("s", st, st.n)
                    dst = qlat[:, lc, tw * 512:(tw + 1) * 512]
                    P.op("s", lambda E, dst=dst, pp=pp: E.copy(out=dst, in_=pp[:, 0:512]), inc=ss)

        B = dict(scores=scores[:], attn_b=attn_b[:], attnT=attnT,
                 rowm=rowm, pt=pt, sc_ps=sc_ps, y_ps=y_ps, ystage=ystage,
                 ident16=ident16, dmask=dmask)

        def scores_mm(h, qt, w, ps_ap):
            for lc in range(4):
                P.op("t", lambda E, lc=lc, qt=qt, w=w, ps_ap=ps_ap: E.matmul(
                    ps_ap, qlat[:, lc, qt * 128:(qt + 1) * 128],
                    ckv[:, lc, w * 512:(w + 1) * 512],
                    start=(lc == 0), stop=(lc == 3)),
                    inc=(st if lc == 3 else None))
            return st.n

        def av_lhsT(h, kc):
            return cv[:, kc, h * 128:(h + 1) * 128]

        def out_dma(h, quad, src):
            return P.dma("g", y_mine[h * 128:(h + 1) * 128,
                                     quad * 512:(quad + 1) * 512], src, so[quad % 2])

        asems = dict(a_s=ss, a_v=sv, a_t=st, a_g=sg, a_o=so)
        for h in range(NH_LOC):
            mk_qlat(h)
            P.barrier()
            _run_head(P, B, asems, h, scores_mm, av_lhsT, out_dma, -1e30)
            P.barrier()

        P.wait("g", so[0], so[0].n)
        P.wait("g", so[1], so[1].n)
        with nc.Block() as block:
            P.finalize(block)
    return nc


def _run_head(P, B, sems, h, scores_mm, av_lhsT, out_dma, fill):
    emit_attn_core(P, B, sems,
                   lambda _h, qt, w, ps: scores_mm(h, qt, w, ps),
                   lambda _h, kc: av_lhsT(h, kc),
                   lambda _h, quad, src: out_dma(h, quad, src),
                   fill, nheads=1)


"""attn1a: GQA attention, head-sharded. Per core: 8 q-heads (2 kv-heads), full causal.
in: xln_all [2, 2048, 1024] bf16; wqT_s [2048,1024], wkT_s [2048,256], wvT_s [2048,256] bf16
out: ao_mine [1024, 2048] bf16 (pre-out-projection attention output rows)."""

NH_LOC = 8
FILL_L1 = -3.0e38


def build_attn1(T_CORE=1024):
    TF = 2 * T_CORE
    nc = bass.Bass("TRN2", target_bir_lowering=False, debug=False)
    xln_all = nc.dram_tensor("xln_all", [2, 2048, T_CORE], BF16, kind="ExternalInput").ap()
    wqT_s = nc.dram_tensor("wqT_s", [2048, 1024], BF16, kind="ExternalInput").ap()
    wkT_s = nc.dram_tensor("wkT_s", [2048, 256], BF16, kind="ExternalInput").ap()
    wvT_s = nc.dram_tensor("wvT_s", [2048, 256], BF16, kind="ExternalInput").ap()
    ao_mine = nc.dram_tensor("ao_mine", [1024, TF], BF16, kind="ExternalOutput").ap()

    with ExitStack() as es:
        P = Prog(nc, es)
        sb = lambda name, shape, dt=F32: es.enter_context(nc.sbuf_tensor(name, shape, dt))
        ps = lambda name, shape, dt=F32: es.enter_context(nc.psum_tensor(name, shape, dt))
        xla = sb("xla", [128, NCH, TF], BF16)
        q = sb("q", [128, NH_LOC, TF], BF16)
        kf = sb("kf", [128, 2, TF], BF16)
        vf = sb("vf", [128, 2, TF], BF16)
        v_tm = sb("v_tm", [128, 16, 256], BF16)
        scores = sb("scores", [128, 2048])
        attn_b = sb("attn_b", [128, 2048], BF16)
        attnT = sb("attnT", [128, 16, 512], BF16)
        rowm = sb("rowm", [128, 2])
        wst = sb("wst", [128, 1, NCH, 128], BF16)
        ystage = sb("ystage", [128, 2, 512], BF16)
        ident16 = sb("ident16", [128, 128], BF16)
        dmask = sb("dmask", [128, 4, 512], BF16)
        sc_ps = [ps("sc_ps0", [128, 512]), ps("sc_ps1", [128, 512])]
        y_ps = [ps("y_ps0", [128, 512]), ps("y_ps1", [128, 512])]
        pt = [ps("pt0", [128, 1024], BF16), ps("pt1", [128, 1024], BF16)]

        sems = {k: P.sem(k) for k in ("a_s", "a_v", "a_t", "a_g", "w0", "io")}
        sems["a_o"] = [P.sem("a_o0"), P.sem("a_o1")]
        ss, sv, st, sg = sems["a_s"], sems["a_v"], sems["a_t"], sems["a_g"]
        so = sems["a_o"]
        io = sems["io"]
        wsem = sems["w0"]

        csg = P.sem("csg")
        P.op("g", lambda E: E.memset(ident16[:], 0.0), inc=csg)
        P.op("g", lambda E: E.memset(dmask[:], 0.0), inc=csg)
        P.wait("g", csg, 2)
        P.op("g", lambda E: E.affine_select(
            out=ident16[:], in_=ident16[:], compare_op=ALU.not_equal,
            fill=1.0, base=0, pattern=[[-1, 128]], channel_multiplier=1), inc=csg)
        for m in range(4):
            sl = dmask[:, m, :]
            P.op("g", lambda E, sl=sl, m=m: E.affine_select(
                out=sl, in_=sl, compare_op=ALU.is_ge, fill=FILL_L1,
                base=m * 128, pattern=[[-1, 512]], channel_multiplier=1), inc=csg)
        P.wait("t", csg, 7)
        P.wait("v", csg, 7)

        for r in range(2):
            P.dma("y", xla[:, :, r * T_CORE:(r + 1) * T_CORE],
                  xln_all[r].rearrange("(cc p) t -> p cc t", p=128), io)
        n_x = io.n
        P.wait("t", io, n_x)

        wqT_r = wqT_s.rearrange("(cc p) (dq d) -> dq p cc d", p=128, d=128)
        for dqc in range(NH_LOC):
            P.wait("y", st, st.n)
            n_w = P.dma("y", wst[:, 0], wqT_r[dqc], wsem)
            P.wait("t", wsem, n_w)
            for tw in range(4):
                pp = sc_ps[tw % 2]
                P.wait("t", ss, ss.n)
                for cc in range(NCH):
                    P.op("t", lambda E, cc=cc, tw=tw, pp=pp: E.matmul(
                        pp[:, 0:512], wst[:, 0, cc, :],
                        xla[:, cc, tw * 512:(tw + 1) * 512],
                        start=(cc == 0), stop=(cc == NCH - 1)),
                        inc=(st if cc == NCH - 1 else None))
                P.wait("s", st, st.n)
                dst = q[:, dqc, tw * 512:(tw + 1) * 512]
                P.op("s", lambda E, dst=dst, pp=pp: E.copy(out=dst, in_=pp[:, 0:512]), inc=ss)
            P.barrier()

        wkT_r = wkT_s.rearrange("(cc p) (kv d) -> kv p cc d", p=128, d=128)
        wvT_r = wvT_s.rearrange("(cc p) (kv d) -> kv p cc d", p=128, d=128)
        for (wr, dstb) in ((wkT_r, kf), (wvT_r, vf)):
            for kv in range(2):
                P.wait("y", st, st.n)
                n_w = P.dma("y", wst[:, 0], wr[kv], wsem)
                P.wait("t", wsem, n_w)
                for tw in range(4):
                    pp = sc_ps[tw % 2]
                    P.wait("t", ss, ss.n)
                    for cc in range(NCH):
                        P.op("t", lambda E, cc=cc, tw=tw, pp=pp: E.matmul(
                            pp[:, 0:512], wst[:, 0, cc, :],
                            xla[:, cc, tw * 512:(tw + 1) * 512],
                            start=(cc == 0), stop=(cc == NCH - 1)),
                            inc=(st if cc == NCH - 1 else None))
                    P.wait("s", st, st.n)
                    dst = dstb[:, kv, tw * 512:(tw + 1) * 512]
                    P.op("s", lambda E, dst=dst, pp=pp: E.copy(out=dst, in_=pp[:, 0:512]), inc=ss)
                P.barrier()

        for kv in range(2):
            for kc in range(16):
                tpar = kc % 2
                src = vf[:, kv, kc * 128:(kc + 1) * 128]
                P.wait("t", ss, ss.n)
                n_tp = P.op("t", lambda E, tpar=tpar, src=src: E.transpose(
                    pt[tpar][:, 0:128], src, ident16[:]), inc=st)
                P.wait("s", st, n_tp)
                dst = v_tm[:, kc, kv * 128:(kv + 1) * 128]
                P.op("s", lambda E, dst=dst, tpar=tpar: E.copy(
                    out=dst, in_=pt[tpar][:, 0:128]), inc=ss)
        P.barrier()

        B = dict(scores=scores[:], attn_b=attn_b[:], attnT=attnT,
                 rowm=rowm, pt=pt, sc_ps=sc_ps, y_ps=y_ps, ystage=ystage,
                 ident16=ident16, dmask=dmask)

        def scores_mm(h, qt, w, ps_ap):
            P.op("t", lambda E, h=h, qt=qt, w=w, ps_ap=ps_ap: E.matmul(
                ps_ap, q[:, h, qt * 128:(qt + 1) * 128],
                kf[:, h // 4, w * 512:(w + 1) * 512],
                start=True, stop=True), inc=st)
            return st.n

        def av_lhsT(h, kc):
            return v_tm[:, kc, (h // 4) * 128:(h // 4 + 1) * 128]

        def out_dma(h, quad, src):
            return P.dma("g", ao_mine[h * 128:(h + 1) * 128,
                                      quad * 512:(quad + 1) * 512], src, so[quad % 2])

        asems = dict(a_s=ss, a_v=sv, a_t=st, a_g=sg, a_o=so)
        for h in range(NH_LOC):
            emit_attn_core(P, B, asems,
                           lambda _h, qt, w, ps, h=h: scores_mm(h, qt, w, ps),
                           lambda _h, kc, h=h: av_lhsT(h, kc),
                           lambda _h, quad, src, h=h: out_dma(h, quad, src),
                           FILL_L1, nheads=1)
            P.barrier()

        P.wait("g", so[0], so[0].n)
        P.wait("g", so[1], so[1].n)
        with nc.Block() as block:
            P.finalize(block)
    return nc


"""FUSED megakernel: the whole macaron-d2 forward in ONE program per core.

Phases: prep (dequant+casts+AllGather) | ffn0 | xln->gather | attn0 (stream
spilled) | y-add | ffn0 | moe | xln->gather | attn1 (spilled) | outproj | moe
| store. Residual stream stays in SBUF except during the two attention phases
(spilled to private DRAM; reloaded). All cross-core traffic via pair/all-8
AllGather collectives. Program is identical on all 8 cores; per-core behavior
comes only from input data (incl. 0/1 half-selection masks cfgm).

SBUF map (bytes/partition): [16384,81920) stream | [81920,215808) phase arena
| [215808,229344) perm smalls. PSUM: ps0..ps4+ps_small (left,f32), pt0/pt1
(right,bf16).
"""
from types import SimpleNamespace

NH_LOC = 8
NE = 8
NIC = 11
T = 1024
T2 = 512
TF = 2048

WL_SPECS = [
    ("wq_nat_s", (2048, 1024)), ("wqT_s", (2048, 1024)), ("woT_s", (2048, 1024)),
    ("wuk", (2048, 512)), ("wuv", (2048, 512)), ("wdkvT", (2048, 512)),
    ("wqT1_s", (2048, 1024)), ("wkT_s", (2048, 256)), ("wvT_s", (2048, 256)),
    ("woT1", (2048, 2048)), ("wr", (2048, 8)),
]
WL_OFF = {}
_off = 0
for _n, _s in WL_SPECS:
    WL_OFF[_n] = _off
    _off += _s[0] * _s[1]
WL_ELEMS = _off
WL_PAD = ((WL_ELEMS + 262143) // 262144) * 262144

GW_TILE = 128 * 16 * 128
C_WG = 0
C_WU = 16 * GW_TILE
GU_ELEMS = 32 * GW_TILE
CHUNK_G = 8 * GW_TILE
D_ELEMS = 16 * GW_TILE
MOE_ELEMS = 3 * 2048 * 1408
MOE_PAD = ((MOE_ELEMS + 262143) // 262144) * 262144


def build_fused(n_mlp=16384, debug=False, stop=None, unit_groups=False):
    NCHK = n_mlp // 8 // 128
    NGRP = n_mlp // 128 // NCH
    nc = bass.Bass("TRN2", target_bir_lowering=False, debug=False)

    x_in = nc.dram_tensor("x_in", [2048, T], F32, kind="ExternalInput").ap()
    wl_f32 = nc.dram_tensor("wl_f32", [WL_PAD // 512, 512], F32, kind="ExternalInput").ap()
    moe_f32 = nc.dram_tensor("moe_f32", [MOE_PAD // 512, 512], F32, kind="ExternalInput").ap()
    gate_p = nc.dram_tensor("gate_p", [n_mlp // 8, 1024], I32, kind="ExternalInput").ap()
    gate_s = nc.dram_tensor("gate_s", [n_mlp // 8, 64], F32, kind="ExternalInput").ap()
    up_p = nc.dram_tensor("up_p", [n_mlp // 8, 1024], I32, kind="ExternalInput").ap()
    up_s = nc.dram_tensor("up_s", [n_mlp // 8, 64], F32, kind="ExternalInput").ap()
    down_p = nc.dram_tensor("down_p", [256, n_mlp // 2], I32, kind="ExternalInput").ap()
    down_s = nc.dram_tensor("down_s", [256, n_mlp // 32], F32, kind="ExternalInput").ap()
    bo_in = nc.dram_tensor("bo_in", [2048], F32, kind="ExternalInput").ap()
    cfg_in = nc.dram_tensor("cfg_in", [128, 2], F32, kind="ExternalInput").ap()
    out_t = nc.dram_tensor("out_t", [2048, T], F32, kind="ExternalOutput").ap()

    dbg = {}
    if debug:
        for i in range(5):
            dbg[i] = nc.dram_tensor(f"dbg{i}", [2048, T], F32, kind="ExternalOutput").ap()
        dbg["y0"] = nc.dram_tensor("dbg_y0", [1024, TF], BF16, kind="ExternalOutput").ap()
        dbg["xln0"] = nc.dram_tensor("dbg_xln0", [2048, T], BF16, kind="ExternalOutput").ap()
        dbg["gw0"] = nc.dram_tensor("dbg_gw0", [128, 16, 128], BF16, kind="ExternalOutput").ap()
        dbg["gw17"] = nc.dram_tensor("dbg_gw17", [128, 16, 128], BF16, kind="ExternalOutput").ap()
        dbg["cw0"] = nc.dram_tensor("dbg_cw0", [128, 16, 128], BF16, kind="ExternalOutput").ap()
        dbg["gm0"] = nc.dram_tensor("dbg_gm0", [64, 1408], BF16, kind="ExternalOutput").ap()
        dbg["wl0"] = nc.dram_tensor("dbg_wl0", [64, 1024], BF16, kind="ExternalOutput").ap()
    wl_bf = nc.dram_tensor("wl_bf", [WL_PAD // 512, 512], BF16)
    contrib = nc.dram_tensor("contrib", [GU_ELEMS // 512, 512], BF16)
    gath = nc.dram_tensor("gath", [8 * GU_ELEMS // 512, 512], BF16)
    contrib_d = nc.dram_tensor("contrib_d", [D_ELEMS // 512, 512], BF16)
    gath_d = nc.dram_tensor("gath_d", [8 * D_ELEMS // 512, 512], BF16)
    contrib_m = nc.dram_tensor("contrib_m", [MOE_PAD // 512, 512], BF16)
    gath_m = nc.dram_tensor("gath_m", [8 * MOE_PAD // 512, 512], BF16)
    xln_o = nc.dram_tensor("xln_o", [2048, T], BF16)
    xg = nc.dram_tensor("xg", [2 * 2048, T], BF16)
    y_o = nc.dram_tensor("y_o", [1024, TF], BF16)
    yg = nc.dram_tensor("yg", [2 * 1024, TF], BF16)
    spill = nc.dram_tensor("spill", [2048, T], F32)

    flatC = contrib.ap().rearrange("a b -> (a b)")
    flatCD = contrib_d.ap().rearrange("a b -> (a b)")
    flatG = gath.ap().rearrange("a b -> (a b)")
    flatGD = gath_d.ap().rearrange("a b -> (a b)")
    flatGM = gath_m.ap().rearrange("a b -> (a b)")
    flatWL = wl_bf.ap().rearrange("a b -> (a b)")

    def wl_ap(name):
        r, c = dict(WL_SPECS)[name]
        o = WL_OFF[name]
        return flatWL[o:o + r * c].rearrange("(a b) -> a b", b=c)

    def gath_wg(slot, gu):
        g, j = divmod(slot, NCH)
        r = 4 * (g % 2) + j // 4
        jl = 4 * (g // 2) + j % 4
        o = (jl // 4) * 8 * CHUNK_G + r * CHUNK_G + (gu * 4 + jl % 4) * GW_TILE
        return flatG[o:o + GW_TILE].rearrange("(p cc n) -> p cc n", p=128, cc=16)

    def gath_wd(dc, g):
        r, dl = divmod(dc, 2)
        o = r * D_ELEMS + (dl * NGRP + g) * GW_TILE
        return flatGD[o:o + GW_TILE].rearrange("(p cc n) -> p cc n", p=128, cc=16)

    def gath_moe(e, which):
        o = e * MOE_PAD + which * 2048 * 1408
        if which < 2:
            return flatGM[o:o + 2048 * 1408].rearrange("(a b) -> a b", b=1408)
        return flatGM[o:o + 1408 * 2048].rearrange("(a b) -> a b", b=2048)

    CCV = 128 if unit_groups else 1
    CCP = 32 if unit_groups else 1
    with ExitStack() as es:
        P = Prog(nc, es)

        def at(name, shape, dt, off):
            return nc.alloc_sbuf_tensor_at(name, shape, dt, offset=off)

        stream = at("stream", [128, NCH, T], F32, 18432)
        AR = 83968
        SM = 215808
        ones_col16 = at("ones_col16", [128, 1], BF16, SM); SM += 32
        ones_row32 = at("ones_row32", [1, 128], F32, SM); SM += 512
        ones_row16 = at("ones_row16", [1, 128], BF16, SM); SM += 256
        ident16 = at("ident16", [128, 128], BF16, SM); SM += 256
        ident32 = at("ident32", [128, 128], F32, SM); SM += 512
        wr_sb = at("wr_sb", [128, NCH, 8], BF16, SM); SM += 256
        bo_sb = at("bo_sb", [128, NCH], F32, SM); SM += 64
        rowm = at("rowm", [128, 2], F32, SM); SM += 32
        ystage = at("ystage", [128, 2, 512], BF16, SM); SM += 2048
        ce_all = at("ce_all", [8, 512], BF16, SM); SM += 1024
        ce_rows = at("ce_rows", [1, 8, 512], BF16, SM); SM += 8192
        er = at("er", [128, 4, 8], F32, SM); SM += 128
        rsc = at("rsc", [128, 8], F32, SM); SM += 32
        comb = at("comb", [128, 8], F32, SM); SM += 32
        bar_dummy = at("bar_dummy", [128, 8], F32, SM); SM += 32
        cfgm = at("cfgm", [128, 2], F32, SM); SM += 32
        cf_a = at("cf_a", [128, 1], F32, SM); SM += 32
        cf_b = at("cf_b", [128, 1], F32, SM); SM += 32
        assert SM <= 229344, SM
        P._dummy = bar_dummy
        P._bars = {e: P.sem(f"bar_{e}") for e in ("y", "v", "s", "t", "g")}
        nc.const_aps.register(F32, -2.0, cf_a[:])
        nc.const_aps.register(F32, -2.5, cf_b[:])

        ps = [es.enter_context(nc.psum_tensor(f"ps{i}", [128, 512], F32)) for i in range(5)]
        ps_small = es.enter_context(nc.psum_tensor("ps_small", [1, 512], F32))
        pt = [es.enter_context(nc.psum_tensor("pt0", [128, 1024], BF16, side="right")),
              es.enter_context(nc.psum_tensor("pt1", [128, 1024], BF16, side="right"))]

        S = {k: P.sem(k) for k in (
            "io", "sp",
            "d_v", "d_g", "d_s", "d_t", "dio0", "dio1", "dso0", "dso1",
            "cA_in0", "cA_in1", "cA_out0", "cA_out1", "cA_s",
            "cc0", "cc01", "cc02", "cc03", "cc0b", "cc0d", "cc1", "cc2", "cc3", "cc4",
            "ln_s", "ln_v", "ln_t",
            "f_s", "f_v", "f_t", "f_w0", "f_w1", "f_d0", "f_d1",
            "a_s", "a_v", "a_t", "a_g", "a_o0", "a_o1", "a_w",
            "m_s", "m_v", "m_t", "m_g", "m_w0", "m_w1", "m_wd", "m_wd1", "m_ce",
            "o_w0", "o_w1", "yio0", "yio1", "oio0", "oio1",
        )}
        csg = P.sem("csg")
        csv = P.sem("csv")


        def gather(in_ap, out_full_ap, rows_in, sem, groups):
            if unit_groups:
                nrep = out_full_ap.shape[0] // rows_in
                for r in range(nrep - 1):
                    P.dma("g", out_full_ap[(r + 1) * rows_in:(r + 2) * rows_in], in_ap, sem)
                P.dma("g", out_full_ap[0:rows_in], in_ap, sem)
            else:
                P.coll(lambda E, in_ap=in_ap, out_full_ap=out_full_ap, groups=groups:
                       E.collective_compute("AllGather", ALU.bypass,
                                            replica_groups=groups,
                                            ins=[in_ap], outs=[out_full_ap]), sem)

        P.op("v", lambda E: E.memset(ones_col16[:], 1.0))
        P.op("v", lambda E: E.memset(ones_row32[:], 1.0))
        P.op("v", lambda E: E.memset(ones_row16[:], 1.0), inc=csv)
        P.op("v", lambda E: E.memset(cf_a[:], -2.0))
        P.op("v", lambda E: E.memset(cf_b[:], -2.5), inc=csv)
        P.op("g", lambda E: E.memset(ident16[:], 0.0))
        P.op("g", lambda E: E.memset(ident32[:], 0.0), inc=csg)
        P.wait("g", csg, 1)
        P.op("g", lambda E: E.affine_select(
            out=ident16[:], in_=ident16[:], compare_op=ALU.not_equal,
            fill=1.0, base=0, pattern=[[-1, 128]], channel_multiplier=1), inc=csg)
        P.op("g", lambda E: E.affine_select(
            out=ident32[:], in_=ident32[:], compare_op=ALU.not_equal,
            fill=1.0, base=0, pattern=[[-1, 128]], channel_multiplier=1), inc=csg)
        P.wait("t", csg, 3)
        P.wait("t", csv, 2)
        P.wait("s", csv, 2)
        n_x = P.dma("g", stream[:], x_in.rearrange("(cc p) t -> p cc t", p=128), S["io"])
        P.dma("g", cfgm[:], cfg_in, S["io"])
        for cc in range(NCH):
            P.dma("g", bo_sb[:, cc:cc + 1], bo_in[cc * 128:(cc + 1) * 128].unsqueeze(1), S["io"])
        n_io0 = S["io"].n

        pk_b = at("pk_b", [128, 2, 2048], I32, AR)
        sc_b = at("sc_b", [128, 2, 128], F32, AR + 16384)
        c4 = at("c4", [128, 4096], I32, AR + 17408)
        s4 = at("s4", [128, 4096], I32, AR + 33792)
        b4 = at("b4", [128, 4096], I32, AR + 50176)
        x4 = at("x4", [128, 4096], I32, AR + 66560)
        val16 = at("val16", [128, 2, 4096], BF16, AR + 82944)
        wtile = at("wtile", [128, 2, 512], BF16, AR + 99328)
        cbuf32 = at("cbuf32", [128, 2, 1024], F32, AR + 116736)
        cbuf16 = at("cbuf16", [128, 2, 1024], BF16, AR + 124928)
        assert AR + 124928 + 4096 <= 215808

        _war_cast = {}

        _cast_par = [0]

        def emit_cast_chunk(src_flat, dst_flat, i):
            c_s = S["cA_s"]
            war_cast = _war_cast
            par = _cast_par[0] % 2
            _cast_par[0] += 1
            in_sem = S[f"cA_in{par}"]
            out_sem = S[f"cA_out{par}"]
            r0 = i * 256
            vi = src_flat[r0:r0 + 256].rearrange("(a p) w -> p a w", p=128)
            vo = dst_flat[r0:r0 + 256].rearrange("(a p) w -> p a w", p=128)
            P.wait("g", out_sem, out_sem.n)
            P.wait("g", c_s, war_cast.get(par, 0))
            n_in = P.dma("g", cbuf32[:, par].rearrange("p (a w) -> p a w", w=512), vi, in_sem)
            P.wait("s", in_sem, n_in)
            n_c = P.op("s", lambda E, par=par: E.copy(
                out=cbuf16[:, par], in_=cbuf32[:, par]), inc=c_s)
            war_cast[par] = n_c
            P.wait("g", c_s, n_c)
            P.dma("g", vo, cbuf16[:, par].rearrange("p (a w) -> p a w", w=512), out_sem)

        sv_d, sg_d, ss_d, st_d = S["d_v"], S["d_g"], S["d_s"], S["d_t"]
        war_tp = {}
        war_decode = {}

        def emit_decode(pk, par, W, lo, hi, sc_bc):
            cW, cLo, cHi = c4[:, 0:2 * W], c4[:, 0:W], c4[:, W:2 * W]
            sW, bW, xW = s4[:, 0:2 * W], b4[:, 0:2 * W], x4[:, 0:2 * W]
            fbL = x4.bitcast(F32)[:, 0:W].rearrange("p (b k) -> p b k", k=16)
            fbH = x4.bitcast(F32)[:, W:2 * W].rearrange("p (b k) -> p b k", k=16)

            def vop(fn):
                P.wait("v", sv_d, sv_d.n)
                return P.op("v", fn, inc=sv_d)
            vop(lambda E, cLo=cLo, pk=pk: E.tensor_single_scalar(
                out=cLo, in_=pk, scalar=15, op=ALU.bitwise_and))
            vop(lambda E, cHi=cHi, pk=pk: E.tensor_single_scalar(
                out=cHi, in_=pk, scalar=4, op=ALU.logical_shift_right))
            vop(lambda E, cHi=cHi: E.tensor_single_scalar(
                out=cHi, in_=cHi, scalar=15, op=ALU.bitwise_and))
            vop(lambda E, sW=sW, cW=cW: E.tensor_single_scalar(
                out=sW, in_=cW, scalar=3, op=ALU.logical_shift_right))
            vop(lambda E, sW=sW: E.tensor_single_scalar(
                out=sW, in_=sW, scalar=31, op=ALU.logical_shift_left))
            vop(lambda E, cW=cW: E.tensor_single_scalar(
                out=cW, in_=cW, scalar=7, op=ALU.bitwise_and))
            vop(lambda E, xW=xW, cW=cW: E.tensor_single_scalar(
                out=xW, in_=cW, scalar=1, op=ALU.min))
            vop(lambda E, bW=bW, cW=cW: E.tensor_single_scalar(
                out=bW, in_=cW, scalar=1, op=ALU.logical_shift_right))
            vop(lambda E, bW=bW: E.tensor_single_scalar(
                out=bW, in_=bW, scalar=1, op=ALU.min))
            vop(lambda E, bW=bW, cW=cW: E.tensor_tensor(
                out=bW, in0=cW, in1=bW, op=ALU.mult))
            vop(lambda E, bW=bW: E.tensor_single_scalar(
                out=bW, in_=bW, scalar=22, op=ALU.logical_shift_left))
            vop(lambda E, xW=xW: E.tensor_single_scalar(
                out=xW, in_=xW, scalar=0x3F000000, op=ALU.mult))
            vop(lambda E, xW=xW, bW=bW: E.tensor_tensor(
                out=xW, in0=xW, in1=bW, op=ALU.add))
            vop(lambda E, xW=xW, sW=sW: E.tensor_tensor(
                out=xW, in0=xW, in1=sW, op=ALU.bitwise_or))
            vop(lambda E, lo=lo, fbL=fbL: E.tensor_tensor(
                out=lo, in0=fbL, in1=sc_bc, op=ALU.mult))
            n_out = vop(lambda E, hi=hi, fbH=fbH: E.tensor_tensor(
                out=hi, in0=fbH, in1=sc_bc, op=ALU.mult))
            war_decode[("v", par)] = n_out
            return n_out

        def do_dequant(pk_ap, sc_ap, nchk, pw, out_tile_fn4, W):
            nsub = pw // W
            nbl = W // 16
            for ch in range(nchk):
                for sub in range(nsub):
                    blk = ch * nsub + sub
                    par = blk % 2
                    iosem = S[f"dio{par}"]
                    P.wait("y", sv_d, war_decode.get(("v", par), 0))
                    P.dma("y", pk_b[:, par, 0:W],
                          pk_ap[ch * 128:(ch + 1) * 128, sub * W:(sub + 1) * W], iosem)
                    n_sc = P.dma("y", sc_b[:, par, 0:nbl],
                                 sc_ap[ch * 128:(ch + 1) * 128, sub * nbl:(sub + 1) * nbl], iosem)
                    P.wait("v", iosem, n_sc)
                    pk = pk_b[:, par, 0:W]
                    vv = val16[:, par, 0:2 * W].rearrange("p (a b) -> p a b", b=2)
                    sc_bc = sc_b[:, par, 0:nbl].unsqueeze(2).broadcast_to([128, nbl, 16])
                    lo = vv[:, :, 0].rearrange("p (b k) -> p b k", k=16)
                    hi = vv[:, :, 1].rearrange("p (b k) -> p b k", k=16)
                    P.wait("v", st_d, war_tp.get(("val", par), 0))
                    n_dec = emit_decode(pk, par, W, lo, hi, sc_bc)
                    nq4 = 2 * W // 512
                    for q4 in range(nq4):
                        tpar = (blk * nq4 + q4) % 2
                        P.wait("t", sv_d, n_dec)
                        P.wait("t", ss_d, war_tp.get(tpar, 0))
                        for u in range(4):
                            q = q4 * 4 + u
                            src = val16[:, par, q * 128:(q + 1) * 128]
                            n_tp = P.op("t", lambda E, tpar=tpar, src=src, u=u: E.transpose(
                                pt[tpar][:, u * 128:(u + 1) * 128], src, ident16[:]),
                                inc=(st_d if u == 3 else None))
                        war_tp[("val", par)] = n_tp
                        P.wait("s", st_d, n_tp)
                        sosem = S[f"dso{tpar}"]
                        P.wait("s", sosem, sosem.n)
                        n_ev = P.op("s", lambda E, tpar=tpar: E.copy(
                            out=wtile[:, tpar], in_=pt[tpar][:, 0:512]), inc=ss_d)
                        war_tp[tpar] = n_ev
                        P.wait("y", ss_d, n_ev)
                        P.dma("y", out_tile_fn4(ch, sub, q4), wtile[:, tpar].rearrange(
                            "p (a n) -> p a n", n=128), sosem)

        def vCK4(k, gu, ch, q4):
            o = k * CHUNK_G + (gu * 4 + ch) * GW_TILE
            return flatC[o:o + GW_TILE].rearrange(
                "(c cc n) -> c cc n", c=128, cc=16)[:, q4 * 4:(q4 + 1) * 4, :]

        vdn = flatCD[0:16 * GW_TILE].rearrange(
            "(dl g c cc n) -> dl g c cc n", g=NGRP, c=128, cc=16, n=128)

        def vdn_slot4(ch, sub, q4):
            r_t = 2 * sub + q4 // 4
            g = 2 * (q4 % 4) + r_t // 4
            j0 = 4 * (r_t % 4)
            return vdn[ch, g, :, j0:j0 + 4, :]
        for k in range(4):
            do_dequant(gate_p[512 * k:512 * (k + 1)], gate_s[512 * k:512 * (k + 1)],
                       4, 1024, lambda ch, sub, q4, k=k: vCK4(k, 0, ch, q4), 1024)
            do_dequant(up_p[512 * k:512 * (k + 1)], up_s[512 * k:512 * (k + 1)],
                       4, 1024, lambda ch, sub, q4, k=k: vCK4(k, 1, ch, q4), 1024)
            for s in ("dso0", "dso1"):
                P.wait("g", S[s], S[s].n)
            gather(contrib.ap()[k * (CHUNK_G // 512):(k + 1) * (CHUNK_G // 512)],
                   gath.ap()[k * (8 * CHUNK_G // 512):(k + 1) * (8 * CHUNK_G // 512)],
                   CHUNK_G // 512, S["cc0" if k == 0 else f"cc0{k}"], [list(range(8))])
        do_dequant(down_p, down_s, 2, n_mlp // 2, vdn_slot4, 2048)
        for s in ("dso0", "dso1"):
            P.wait("g", S[s], S[s].n)
        gather(contrib_d.ap(), gath_d.ap(), D_ELEMS // 512, S["cc0d"], [list(range(8))])
        def job_gather_m():
            for s in ("cA_out0", "cA_out1"):
                P.wait("g", S[s], S[s].n)
            gather(contrib_m.ap(), gath_m.ap(), MOE_PAD // 512, S["cc0b"], [list(range(8))])
        cast_jobs = []
        for (cs, cd, rt) in ((moe_f32, contrib_m.ap(), MOE_PAD // 512),):
            cast_jobs += [(cs, cd, i) for i in range(rt // 256)]
        cast_jobs.append(job_gather_m)
        for (cs, cd, rt) in ((wl_f32, wl_bf.ap(), WL_PAD // 512),):
            cast_jobs += [(cs, cd, i) for i in range(rt // 256)]
        cast_pos = [0]

        def cast_feeder(n=1):
            for _ in range(n):
                if cast_pos[0] >= len(cast_jobs):
                    return
                j = cast_jobs[cast_pos[0]]
                cast_pos[0] += 1
                if callable(j):
                    j()
                else:
                    emit_cast_chunk(j[0], j[1], j[2])

        ffnB = SimpleNamespace(
            stream=stream, T=T, T2=T2,
            x_ln=at("f_xln", [128, NCH, T2], BF16, AR),
            big_a=at("f_biga", [128, NCH, 1024], BF16, AR + 16384),
            wg_buf=at("f_wg", [128, 2, NCH, 128], BF16, AR + 49152),
            wu_buf=at("f_wu", [128, 2, NCH, 128], BF16, AR + 57344),
            wd_buf=at("f_wd", [128, 2, NCH, 128], BF16, AR + 65536),
            gelu_sb=at("f_gelu", [128, 2, T2], BF16, AR + 73728),
            ds_acc=at("f_ds", [128, NCH, T2], F32, AR + 75776),
            rows=at("f_rows", [1, 4, T2], F32, AR + 108544),
            ones_col16=ones_col16, ones_row32=ones_row32,
            ps=[ps[0], ps[1], ps[2], ps[3], ps[4], ps[4]],
            ps_small=ps_small,
        )
        ffnB.xb16 = ffnB.big_a[:, :, 0:T2]
        ffnB.xsq16 = ffnB.big_a[:, :, T2:1024]
        fw = dict(wg_t=lambda jg: gath_wg(jg, 0),
                  wu_t=lambda jg: gath_wg(jg, 1),
                  wd_t=gath_wd)
        fsems = dict(ln_s=S["ln_s"], ln_v=S["ln_v"], ln_t=S["ln_t"],
                     f_s=S["f_s"], f_v=S["f_v"], f_t=S["f_t"],
                     f_w=[S["f_w0"], S["f_w1"]], f_d=[S["f_d0"], S["f_d1"]])

        if debug:
            P.wait("g", S["cc0"], CCV)
            P.wait("g", S["cc0b"], CCV)
            vC0 = flatC[0:GW_TILE].rearrange("(p cc n) -> p cc n", p=128, cc=16)
            for (tag, src_ap) in (("gw0", gath_wg(0, 0)), ("gw17", gath_wg(17, 0)),
                                  ("cw0", vC0)):
                n_d = P.dma("g", dbg[tag], src_ap, S["io"])
            n_d = P.dma("g", dbg["gm0"], gath_moe(0, 0)[0:64, :], S["io"])
            n_d = P.dma("g", dbg["wl0"], wl_ap("wq_nat_s")[0:64, :], S["io"])
            P.wait("g", S["io"], n_d)

        def run_ffn():
            for e in ("v", "s", "t", "y"):
                P.wait(e, S["dso0"], S["dso0"].n)
                P.wait(e, S["dso1"], S["dso1"].n)
            fsems["jg_feeder"] = cast_feeder
            P.wait("y", S["cc0"], CCV)
            fsems["wg_gate"] = lambda jg: P.wait(
                "y", S["cc0" if jg < 32 else f"cc0{min(jg // 32, 3)}"], CCV)
            fsems["wd_gate"] = (S["cc0d"], CCV)
            P.wait("s", S["io"], n_io0)
            P.wait("v", S["io"], n_io0)
            emit_ffn0(P, ffnB, fw, fsems, n_mlp=n_mlp, xln_half=True)
            P.barrier()

        run_ffn()
        cast_feeder(len(cast_jobs))
        ncA0, ncA1 = S["cA_out0"].n, S["cA_out1"].n

        def dump_stream(i):
            if debug:
                P.wait("g", S["f_v"], S["f_v"].n)
                P.wait("g", S["m_v"], S["m_v"].n)
                P.wait("g", S["a_v"], S["a_v"].n)
                n_d = P.dma("g", dbg[i].rearrange("(cc p) t -> p cc t", p=128), stream[:], S["io"])
                P.wait("g", S["io"], n_d)
        dump_stream(0)
        if stop == "x1":
            P.wait("g", S["f_v"], S["f_v"].n)
            n_o = P.dma("g", out_t.rearrange("(cc p) t -> p cc t", p=128), stream[:], S["io"])
            P.wait("g", S["io"], n_o)
            with nc.Block() as block:
                P.finalize(block)
            return nc

        for e in ("y", "g"):
            P.wait(e, S["cA_out0"], ncA0)
            P.wait(e, S["cA_out1"], ncA1)
        xlnB = SimpleNamespace(
            stream=stream, T=T, T2=T2,
            x_ln=ffnB.x_ln, rows=ffnB.rows,
            xb16=ffnB.xb16, xsq16=ffnB.xsq16,
            ones_col16=ones_col16, ones_row32=ones_row32,
            ps=[None, None, None, None, None, ps[4]], ps_small=ps_small,
        )
        lsems = dict(ln_s=S["ln_s"], ln_v=S["ln_v"], ln_t=S["ln_t"])

        def run_xln(ccsem):
            for half in (0, 1):
                n_ln = emit_ln(P, xlnB, half, lsems, dst_t0=0,
                               war=[(S["io"], S["io"].n)])
                P.wait("g", lsems["ln_v"], n_ln)
                P.dma("g", xln_o.ap()[:, half * T2:(half + 1) * T2].rearrange(
                    "(cc p) t -> p cc t", p=128), xlnB.x_ln[:], S["io"])
            P.wait("g", S["io"], S["io"].n)
            gather(xln_o.ap(), xg.ap(), 2048, ccsem, [[0, 1], [2, 3], [4, 5], [6, 7]])
            P.barrier()

        run_xln(S["cc1"])
        if debug:
            n_d = P.dma("g", dbg["xln0"].rearrange("(cc p) t -> p cc t", p=128),
                        xln_o.ap().rearrange("(cc p) t -> p cc t", p=128), S["io"])
            P.wait("g", S["io"], n_d)

        n_sp = P.dma("g", spill.ap().rearrange("(cc p) t -> p cc t", p=128), stream[:], S["sp"])

        AB = 18432
        a_xst = at("a_xst", [128, NCH, T2], BF16, AB)
        a_wf = at("a_wf", [128, NCH, 1024], BF16, AB + 16384)
        a_q = at("a_q", [128, NH_LOC, TF], BF16, AB + 49152)
        a_cv = at("a_cv", [128, NCH, 1024], BF16, AB + 81920)
        a_ckv = at("a_ckv", [128, 4, TF], BF16, AB + 114688)
        a_keff = at("a_keff", [128, NH_LOC, 512], BF16, AB + 131072)
        a_qlat = at("a_qlat", [128, 4, TF], BF16, AB + 139264)
        a_scores = at("a_scores", [128, 2048], F32, AB + 155648)
        a_attnb = at("a_attnb", [128, 2048], BF16, AB + 163840)
        a_attnT = at("a_attnT", [128, 16, 512], BF16, AB + 167936)
        a_dm0 = at("a_dm0", [128, 4, 512], BF16, AB + 184320)
        a_veff = at("a_veff", [128, 4, 1024], BF16, AB + 188416)
        a_kf = at("a_kf", [128, 2, TF], BF16, AB + 81920)
        a_vf = at("a_vf", [128, 2, TF], BF16, AB + 90112)
        a_vtm = at("a_vtm", [128, 16, 256], BF16, AB + 98304)
        a_dm1 = at("a_dm1", [128, 4, 512], BF16, AB + 106496)

        ss_a, sv_a, st_a, sg_a = S["a_s"], S["a_v"], S["a_t"], S["a_g"]
        so_a = [S["a_o0"], S["a_o1"]]
        aw = S["a_w"]

        war_pp = {}
        aB = dict(scores=a_scores[:], attn_b=a_attnb[:], attnT=a_attnT,
                  rowm=rowm, pt=pt, sc_ps=[ps[0], ps[1]], y_ps=[ps[2], ps[2]],
                  ystage=ystage, ident16=ident16, y_single=True)
        asems = dict(a_s=ss_a, a_v=sv_a, a_t=st_a, a_g=sg_a, a_o=so_a)

        def mk_dmask(dm, fill):
            nmk = P.op("g", lambda E, dm=dm: E.memset(dm[:], 0.0), inc=sg_a)
            P.wait("g", sg_a, nmk)
            for m in range(4):
                sl = dm[:, m, :]
                nmk = P.op("g", lambda E, sl=sl, m=m, fill=fill: E.affine_select(
                    out=sl, in_=sl, compare_op=ALU.is_ge, fill=fill,
                    base=m * 128, pattern=[[-1, 512]], channel_multiplier=1), inc=sg_a)
            P.wait("t", sg_a, sg_a.n)
            P.wait("v", sg_a, sg_a.n)

        def load_xst(tw):
            r, twl = divmod(tw, 2)
            src = xg.ap()[r * 2048:(r + 1) * 2048, twl * T2:(twl + 1) * T2].rearrange(
                "(cc p) t -> p cc t", p=128)
            return P.dma("y", a_xst[:], src, aw)

        def proj_pass(wl_name, ncols, dst_fn, extra_lhs_rearr=None):
            nocc = ncols // 128
            wv = wl_ap(wl_name).rearrange("(cc p) d -> p cc d", p=128)
            P.wait("y", st_a, st_a.n)
            n_w = P.dma("y", a_wf[:, :, 0:ncols], wv, aw)
            for tw in range(4):
                P.wait("y", st_a, st_a.n)
                n_x = load_xst(tw)
                P.wait("t", aw, n_x)
                for oc in range(nocc):
                    pp = ps[oc % 2]
                    P.wait("t", ss_a, war_pp.get(id(pp), 0))
                    for cc in range(NCH):
                        lhsT = a_wf[:, cc, oc * 128:(oc + 1) * 128]
                        rhs = a_xst[:, cc, :]
                        P.op("t", lambda E, lhsT=lhsT, rhs=rhs, cc=cc, pp=pp: E.matmul(
                            pp[:, 0:512], lhsT, rhs, start=(cc == 0), stop=(cc == NCH - 1)),
                            inc=(st_a if cc == NCH - 1 else None))
                    P.wait("s", st_a, st_a.n)
                    dst = dst_fn(oc, tw)
                    war_pp[id(pp)] = P.op("s", lambda E, dst=dst, pp=pp: E.copy(out=dst, in_=pp[:, 0:512]), inc=ss_a)
            P.barrier()

        def run_attn0():
            mk_dmask(a_dm0, FILL_L0)
            aB["dmask"] = a_dm0
            for e in ("y", "g", "s", "t", "v"):
                P.wait(e, S["sp"], n_sp)
            P.wait("y", S["cc1"], CCP)
            P.wait("y", S["cA_out0"], S["cA_out0"].n)
            P.wait("y", S["cA_out1"], S["cA_out1"].n)
            P.barrier()
            n_wuk = P.dma("y", a_xst[:], wl_ap("wuk").rearrange("(dc p) l -> p dc l", p=128), aw)
            wqv = wl_ap("wq_nat_s").rearrange("(dc p) s -> p dc s", p=128)
            n_wq = P.dma("y", a_wf[:, :, 0:1024], wqv, aw)
            P.wait("t", aw, n_wq)
            for sc in range(NH_LOC):
                pp = ps[sc % 2]
                P.wait("t", ss_a, war_pp.get(id(pp), 0))
                for dc in range(NCH):
                    lhsT = a_wf[:, dc, sc * 128:(sc + 1) * 128]
                    P.op("t", lambda E, lhsT=lhsT, dc=dc, pp=pp: E.matmul(
                        pp[:, 0:512], lhsT, a_xst[:, dc, :],
                        start=(dc == 0), stop=(dc == NCH - 1)),
                        inc=(st_a if dc == NCH - 1 else None))
                P.wait("s", st_a, st_a.n)
                dst = a_keff[:, sc, :]
                war_pp[id(pp)] = P.op("s", lambda E, dst=dst, pp=pp: E.mul(out=dst, in_=pp[:, 0:512], mul=RSQ), inc=ss_a)
            P.barrier()
            n_wuv = P.dma("y", a_xst[:], wl_ap("wuv").rearrange("(dc p) l -> p dc l", p=128), aw)
            wov = wl_ap("woT_s").rearrange("(dc p) c -> p dc c", p=128)
            n_wo = P.dma("y", a_wf[:, :, 0:1024], wov, aw)
            P.wait("t", aw, n_wo)
            for cw in range(2):
                for lc in range(4):
                    pp = ps[lc % 2]
                    P.wait("t", ss_a, war_pp.get(id(pp), 0))
                    for dc in range(NCH):
                        lhsT = a_xst[:, dc, lc * 128:(lc + 1) * 128]
                        rhs = a_wf[:, dc, cw * 512:(cw + 1) * 512]
                        P.op("t", lambda E, lhsT=lhsT, rhs=rhs, dc=dc, pp=pp: E.matmul(
                            pp[:, 0:512], lhsT, rhs, start=(dc == 0), stop=(dc == NCH - 1)),
                            inc=(st_a if dc == NCH - 1 else None))
                    P.wait("s", st_a, st_a.n)
                    dst = a_veff[:, lc, cw * 512:(cw + 1) * 512]
                    war_pp[id(pp)] = P.op("s", lambda E, dst=dst, pp=pp: E.copy(out=dst, in_=pp[:, 0:512]), inc=ss_a)
            P.barrier()
            n_wk = P.dma("y", a_wf[:, :, 0:512], wl_ap("wdkvT").rearrange(
                "(cc p) l -> p cc l", p=128), aw)
            for tw in range(4):
                P.wait("y", st_a, st_a.n)
                n_x = load_xst(tw)
                P.wait("t", aw, n_x)
                for lc in range(4):
                    pp = ps[lc % 2]
                    P.wait("t", ss_a, war_pp.get(id(pp), 0))
                    for cc in range(NCH):
                        lhsT = a_wf[:, cc, lc * 128:(lc + 1) * 128]
                        P.op("t", lambda E, lhsT=lhsT, cc=cc, pp=pp: E.matmul(
                            pp[:, 0:512], lhsT, a_xst[:, cc, :],
                            start=(cc == 0), stop=(cc == NCH - 1)),
                            inc=(st_a if cc == NCH - 1 else None))
                    P.wait("s", st_a, st_a.n)
                    dst = a_ckv[:, lc, tw * 512:(tw + 1) * 512]
                    war_pp[id(pp)] = P.op("s", lambda E, dst=dst, pp=pp: E.copy(out=dst, in_=pp[:, 0:512]), inc=ss_a)
            P.barrier()
            for kc in range(16):
                for cwi in range(2):
                    pp = ps[cwi]
                    P.wait("t", ss_a, war_pp.get(id(pp), 0))
                    for lc in range(4):
                        lhsT = a_ckv[:, lc, kc * 128:(kc + 1) * 128]
                        rhs = a_veff[:, lc, cwi * 512:(cwi + 1) * 512]
                        P.op("t", lambda E, lhsT=lhsT, rhs=rhs, lc=lc, pp=pp: E.matmul(
                            pp[:, 0:512], lhsT, rhs, start=(lc == 0), stop=(lc == 3)),
                            inc=(st_a if lc == 3 else None))
                    P.wait("s", st_a, st_a.n)
                    dst = a_cv[:, kc, cwi * 512:(cwi + 1) * 512]
                    war_pp[id(pp)] = P.op("s", lambda E, dst=dst, pp=pp: E.copy(out=dst, in_=pp[:, 0:512]), inc=ss_a)
            P.barrier()
            proj_pass("wqT_s", 1024,
                      lambda oc, tw: a_q[:, oc, tw * 512:(tw + 1) * 512])
            for h in range(NH_LOC):
                for lc in range(4):
                    for tw in range(4):
                        pp = ps[tw % 2]
                        P.wait("t", ss_a, war_pp.get(id(pp), 0))
                        lhsT = a_keff[:, h, lc * 128:(lc + 1) * 128]
                        rhs = a_q[:, h, tw * 512:(tw + 1) * 512]
                        P.op("t", lambda E, lhsT=lhsT, rhs=rhs, pp=pp: E.matmul(
                            pp[:, 0:512], lhsT, rhs, start=True, stop=True), inc=st_a)
                        P.wait("s", st_a, st_a.n)
                        dst = a_qlat[:, lc, tw * 512:(tw + 1) * 512]
                        war_pp[id(pp)] = P.op("s", lambda E, dst=dst, pp=pp: E.copy(out=dst, in_=pp[:, 0:512]), inc=ss_a)
                P.barrier()

                def scores_mm(_h, qt, w, ps_ap):
                    for lc in range(4):
                        lhsT = a_qlat[:, lc, qt * 128:(qt + 1) * 128]
                        rhs = a_ckv[:, lc, w * 512:(w + 1) * 512]
                        P.op("t", lambda E, lhsT=lhsT, rhs=rhs, lc=lc, ps_ap=ps_ap: E.matmul(
                            ps_ap, lhsT, rhs, start=(lc == 0), stop=(lc == 3)),
                            inc=(st_a if lc == 3 else None))
                    return st_a.n

                emit_attn_core(P, aB, asems, scores_mm,
                               lambda _h, kc, h=h: a_cv[:, kc, h * 128:(h + 1) * 128],
                               lambda _h, quad, src, h=h: P.dma(
                                   "g", y_o.ap()[h * 128:(h + 1) * 128,
                                                 quad * 512:(quad + 1) * 512],
                                   src, so_a[0]),
                               FILL_L0, nheads=1)
                P.barrier()
            P.wait("g", so_a[0], so_a[0].n)
            P.wait("g", so_a[1], so_a[1].n)
            gather(y_o.ap(), yg.ap(), 1024, S["cc2"], [[0, 1], [2, 3], [4, 5], [6, 7]])
            P.barrier()

        run_attn0()

        n_rl = P.dma("g", stream[:], spill.ap().rearrange("(cc p) t -> p cc t", p=128), S["sp"])
        ystg = at("ystg", [128, 2, 2, 1024], BF16, AR + 16384)
        war_add = {}

        def run_yadd(ccsem):
            P.wait("g", ccsem, CCP)
            P.wait("g", S["sp"], S["sp"].n)
            for cc in range(NCH):
                par = cc % 2
                r, hcl = divmod(cc, 8)
                src = yg.ap()[r * 1024 + hcl * 128: r * 1024 + (hcl + 1) * 128, :].rearrange(
                    "p (h t) -> p h t", t=T)
                P.wait("g", S["f_v"], war_add.get(par, 0))
                n_l = P.dma("g", ystg[:, par], src, S[f"yio{par}"])
                P.wait("v", S[f"yio{par}"], n_l)
                P.wait("v", S["sp"], S["sp"].n)
                sdst = stream[:, cc, :]
                P.op("v", lambda E, sdst=sdst, par=par: E.scalar_tensor_tensor(
                    out=sdst, in0=ystg[:, par, 0], scalar=cfgm[:, 0:1], in1=sdst,
                    op0=ALU.mult, op1=ALU.add), inc=S["f_v"])
                P.wait("v", S["f_v"], S["f_v"].n)
                war_add[par] = P.op("v", lambda E, sdst=sdst, par=par: E.scalar_tensor_tensor(
                    out=sdst, in0=ystg[:, par, 1], scalar=cfgm[:, 1:2], in1=sdst,
                    op0=ALU.mult, op1=ALU.add), inc=S["f_v"])
            P.barrier()

        if debug:
            P.wait("g", S["cc2"], CCP)
            n_d = P.dma("g", dbg["y0"], y_o.ap(), S["io"])
            P.wait("g", S["io"], n_d)
        run_yadd(S["cc2"])
        dump_stream(1)

        run_ffn()
        dump_stream(2)

        m_xln = ffnB.x_ln
        m_hall = at("m_hall", [128, NE, NIC, T2], BF16, AR + 16384)
        m_wgst = at("m_wgst", [128, 2, NCH, 128], BF16, AR + 106496)
        m_wust = at("m_wust", [128, 2, NCH, 128], BF16, AR + 114688)
        m_wdst = at("m_wdst", [128, 2, NIC, 128], BF16, AR + 122880)
        m_gsil = at("m_gsil", [128, 2, T2], BF16, AR + 122880)
        m_rows = at("m_rows", [1, 4, T2], F32, AR + 106496)
        assert AR + 125696 <= 215808

        moeB = SimpleNamespace(
            stream=stream, T=T, T2=T2, x_ln=m_xln, rows=m_rows,
            xb16=m_hall[:, 0:2].rearrange("p e a t -> p (e a t)")[:, 0:16 * T2].rearrange(
                "p (a t) -> p a t", t=T2),
            xsq16=m_hall[:, 2:4].rearrange("p e a t -> p (e a t)")[:, 0:16 * T2].rearrange(
                "p (a t) -> p a t", t=T2),
            ones_col16=ones_col16, ones_row32=ones_row32,
            ps=[None, None, None, None, None, ps[3]], ps_small=ps_small,
        )
        msems = dict(ln_s=S["ln_s"], ln_v=S["ln_v"], ln_t=S["ln_t"])
        ss_m, sv_m, st_m = S["m_s"], S["m_v"], S["m_t"]

        P.wait("g", S["cA_out0"], S["cA_out0"].n)
        P.wait("g", S["cA_out1"], S["cA_out1"].n)
        n_wr = P.dma("g", wr_sb[:], wl_ap("wr").rearrange("(cc p) e -> p cc e", p=128), S["io"])

        def run_moe():
            P.wait("s", S["io"], n_wr)
            P.wait("v", S["io"], n_wr)
            P.wait("t", S["io"], n_wr)
            pbc_m = ps[3]
            ps_r = ps[0]
            for half in (0, 1):
                t0 = half * T2
                n_ln = emit_ln(P, moeB, half, msems, war=[(st_m, st_m.n), (sv_m, sv_m.n)], dst_t0=0)
                P.wait("t", msems["ln_v"], n_ln)
                for tc in range(4):
                    P.wait("t", sv_m, sv_m.n)
                    P.wait("t", ss_m, ss_m.n)
                    for cc in range(NCH):
                        lhsT = m_xln[:, cc, tc * 128:(tc + 1) * 128]
                        P.op("t", lambda E, lhsT=lhsT, cc=cc: E.matmul(
                            ps_r[:, 0:8], lhsT, wr_sb[:, cc, :],
                            start=(cc == 0), stop=(cc == NCH - 1)),
                            inc=(st_m if cc == NCH - 1 else None))
                    n_lg = st_m.n
                    P.wait("v", st_m, st_m.n)
                    P.wait("v", ss_m, ss_m.n)
                    er0, er2 = er[:, 0, :], er[:, 1, :]
                    msk1, msk2 = er[:, 2, :], er[:, 3, :]
                    m1 = rsc[:, 0:1]; m2 = rsc[:, 1:2]; s12 = rsc[:, 2:3]
                    w1 = rsc[:, 4:5]; w2 = rsc[:, 5:6]; nmx = rsc[:, 6:7]
                    P.op("v", lambda E: E.reduce_max(out=nmx, in_=ps_r[:, 0:8], axis=AX.X, negate=True), inc=sv_m)
                    P.wait("s", sv_m, sv_m.n)
                    P.op("s", lambda E: E.activation(out=er0, in_=ps_r[:, 0:8], func=AF.Exp, bias=nmx), inc=ss_m)
                    P.wait("v", ss_m, ss_m.n)
                    P.op("v", lambda E: E.reduce_max(out=m1, in_=er0, axis=AX.X), inc=sv_m)
                    P.wait("v", sv_m, sv_m.n)
                    P.op("v", lambda E: E.tensor_scalar(out=msk1, in0=er0, scalar1=m1, scalar2=None, op0=ALU.is_equal), inc=sv_m)
                    P.wait("v", sv_m, sv_m.n)
                    P.op("v", lambda E: E.scalar_tensor_tensor(out=er2, in0=msk1, scalar=-4.0, in1=er0, op0=ALU.mult, op1=ALU.add), inc=sv_m)
                    P.wait("v", sv_m, sv_m.n)
                    P.op("v", lambda E: E.reduce_max(out=m2, in_=er2, axis=AX.X), inc=sv_m)
                    P.wait("v", sv_m, sv_m.n)
                    P.op("v", lambda E: E.tensor_scalar(out=msk2, in0=er2, scalar1=m2, scalar2=None, op0=ALU.is_equal), inc=sv_m)
                    P.op("v", lambda E: E.tensor_tensor(out=s12, in0=m1, in1=m2, op=ALU.add), inc=sv_m)
                    P.wait("v", sv_m, sv_m.n)
                    P.op("v", lambda E: E.reciprocal(out=s12, in_=s12), inc=sv_m)
                    P.wait("v", sv_m, sv_m.n)
                    P.op("v", lambda E: E.tensor_tensor(out=w1, in0=m1, in1=s12, op=ALU.mult), inc=sv_m)
                    P.op("v", lambda E: E.tensor_tensor(out=w2, in0=m2, in1=s12, op=ALU.mult), inc=sv_m)
                    P.wait("v", sv_m, sv_m.n)
                    P.op("v", lambda E: E.tensor_scalar(out=msk1, in0=msk1, scalar1=w1, scalar2=None, op0=ALU.mult), inc=sv_m)
                    P.op("v", lambda E: E.tensor_scalar(out=msk2, in0=msk2, scalar1=w2, scalar2=None, op0=ALU.mult), inc=sv_m)
                    P.wait("v", sv_m, sv_m.n)
                    n_cmb = P.op("v", lambda E: E.tensor_tensor(out=comb[:], in0=msk1, in1=msk2, op=ALU.add), inc=sv_m)
                    P.wait("t", sv_m, n_cmb)
                    n_tp = P.op("t", lambda E: E.transpose(ps_r[0:8, 0:128], comb[:], ident32[:]), inc=st_m)
                    P.wait("s", st_m, n_tp)
                    P.wait("s", S["m_ce"], S["m_ce"].n)
                    P.op("s", lambda E, tc=tc: E.copy(
                        out=ce_all[0:8, tc * 128:(tc + 1) * 128], in_=ps_r[0:8, 0:128]), inc=ss_m)
                P.wait("g", ss_m, ss_m.n)
                for e in range(NE):
                    P.dma("g", ce_rows[:, e, :], ce_all[e:e + 1, :], S["m_ce"])
                P.wait("t", S["m_ce"], S["m_ce"].n)
                war_w = {}
                war_h = {}
                P.wait("y", st_m, st_m.n)
                P.wait("y", S["ln_v"], S["ln_v"].n)
                P.wait("y", S["ln_s"], S["ln_s"].n)
                P.wait("y", S["cc0b"], CCV)
                for e in range(NE):
                    P.wait("t", sv_m, sv_m.n)
                    n_bc = P.op("t", lambda E, e=e: E.matmul(
                        pbc_m[:, 0:512], ones_row16[:], ce_rows[:, e, :],
                        start=True, stop=True), inc=st_m)
                    for ic in range(NIC):
                        par = (e * NIC + ic) % 2
                        idx = e * NIC + ic
                        wgsem = S[f"m_w{par}"]
                        if idx >= 2:
                            P.wait("y", st_m, war_w[idx - 2])
                        P.dma("y", m_wgst[:, par], gath_moe(e, 0)[:, ic * 128:(ic + 1) * 128].rearrange(
                            "(cc p) i -> p cc i", p=128), wgsem)
                        n_w = P.dma("y", m_wust[:, par], gath_moe(e, 1)[:, ic * 128:(ic + 1) * 128].rearrange(
                            "(cc p) i -> p cc i", p=128), wgsem)
                        P.wait("t", wgsem, n_w)
                        pg, pu = (ps[0], ps[2]) if idx % 2 == 0 else (ps[1], ps[4])
                        if idx >= 2:
                            P.wait("t", ss_m, war_h.get((idx - 2, "s"), 0))
                            P.wait("t", sv_m, war_h.get((idx - 2, "v"), 0))
                        for cc in range(NCH):
                            P.op("t", lambda E, cc=cc, par=par, pg=pg: E.matmul(
                                pg[:, 0:512], m_wgst[:, par, cc, :], m_xln[:, cc, 0:T2],
                                start=(cc == 0), stop=(cc == NCH - 1)))
                        for cc in range(NCH):
                            P.op("t", lambda E, cc=cc, par=par, pu=pu: E.matmul(
                                pu[:, 0:512], m_wust[:, par, cc, :], m_xln[:, cc, 0:T2],
                                start=(cc == 0), stop=(cc == NCH - 1)),
                                inc=(st_m if cc == NCH - 1 else None))
                        n_mm = st_m.n
                        war_w[idx] = n_mm
                        P.wait("s", st_m, n_mm)
                        if idx >= 2:
                            P.wait("s", sv_m, war_h.get((idx - 2, "v"), 0))
                        gpar = idx % 2
                        n_sil = P.op("s", lambda E, gpar=gpar, pg=pg: E.activation(
                            out=m_gsil[:, gpar, :], in_=pg[:, 0:512], func=AF.Silu), inc=ss_m)
                        war_h[(idx, "s")] = n_sil
                        P.wait("v", ss_m, n_sil)
                        P.wait("v", st_m, n_mm)
                        dst = m_hall[:, e, ic, :]
                        P.op("v", lambda E, dst=dst, gpar=gpar, pu=pu: E.tensor_tensor(
                            out=dst, in0=m_gsil[:, gpar, :], in1=pu[:, 0:512], op=ALU.mult), inc=sv_m)
                        P.wait("v", sv_m, sv_m.n)
                        war_h[(idx, "v")] = P.op("v", lambda E, dst=dst: E.tensor_tensor(
                            out=dst, in0=dst, in1=pbc_m[:, 0:512], op=ALU.mult), inc=sv_m)
                P.barrier()
                war_d = {}
                war_de = {}
                for dc in range(NCH):
                    pd = ps[dc % 2]
                    if dc >= 2:
                        P.wait("t", sv_m, war_de[dc - 2])
                    for e in range(NE):
                        idx = dc * NE + e
                        wpar = idx % 2
                        wdsem = S["m_wd"] if wpar == 0 else S["m_wd1"]
                        if idx >= 2:
                            P.wait("y", st_m, war_d[idx - 2])
                        n_w = P.dma("y", m_wdst[:, wpar], gath_moe(e, 2)[:, dc * 128:(dc + 1) * 128].rearrange(
                            "(ic p) d -> p ic d", p=128), wdsem)
                        P.wait("t", wdsem, n_w)
                        for ic in range(NIC):
                            P.op("t", lambda E, e=e, ic=ic, pd=pd, wpar=wpar: E.matmul(
                                pd[:, 0:512], m_wdst[:, wpar, ic, :], m_hall[:, e, ic, :],
                                start=(e == 0 and ic == 0), stop=(e == NE - 1 and ic == NIC - 1)),
                                inc=(st_m if ic == NIC - 1 else None))
                        war_d[idx] = st_m.n
                    n_dmm = st_m.n
                    P.wait("v", st_m, n_dmm)
                    sdst = stream[:, dc, t0:t0 + T2]
                    war_de[dc] = P.op("v", lambda E, sdst=sdst, pd=pd: E.scalar_tensor_tensor(
                        out=sdst, in0=pd[:, 0:512], scalar=0.5, in1=sdst,
                        op0=ALU.mult, op1=ALU.add), inc=sv_m)
                P.barrier()

        run_moe()
        dump_stream(3)

        run_xln(S["cc3"])
        n_sp2 = P.dma("g", spill.ap().rearrange("(cc p) t -> p cc t", p=128), stream[:], S["sp"])

        def run_attn1():
            mk_dmask(a_dm1, FILL_L1)
            aB["dmask"] = a_dm1
            for e in ("y", "g", "s", "t", "v"):
                P.wait(e, S["sp"], S["sp"].n)
            P.wait("y", S["cc3"], CCP)
            P.barrier()
            proj_pass("wqT1_s", 1024,
                      lambda oc, tw: a_q[:, oc, tw * 512:(tw + 1) * 512])
            proj_pass("wkT_s", 256,
                      lambda oc, tw: a_kf[:, oc, tw * 512:(tw + 1) * 512])
            proj_pass("wvT_s", 256,
                      lambda oc, tw: a_vf[:, oc, tw * 512:(tw + 1) * 512])
            for kv in range(2):
                for kc in range(16):
                    tpar = kc % 2
                    src = a_vf[:, kv, kc * 128:(kc + 1) * 128]
                    P.wait("t", ss_a, ss_a.n)
                    n_tp = P.op("t", lambda E, tpar=tpar, src=src: E.transpose(
                        pt[tpar][:, 0:128], src, ident16[:]), inc=st_a)
                    P.wait("s", st_a, n_tp)
                    dst = a_vtm[:, kc, kv * 128:(kv + 1) * 128]
                    P.op("s", lambda E, dst=dst, tpar=tpar: E.copy(
                        out=dst, in_=pt[tpar][:, 0:128]), inc=ss_a)
            P.barrier()
            for h in range(NH_LOC):
                def scores_mm(_h, qt, w, ps_ap, h=h):
                    lhsT = a_q[:, h, qt * 128:(qt + 1) * 128]
                    rhs = a_kf[:, h // 4, w * 512:(w + 1) * 512]
                    P.op("t", lambda E, lhsT=lhsT, rhs=rhs, ps_ap=ps_ap: E.matmul(
                        ps_ap, lhsT, rhs, start=True, stop=True), inc=st_a)
                    return st_a.n

                emit_attn_core(P, aB, asems, scores_mm,
                               lambda _h, kc, h=h: a_vtm[:, kc, (h // 4) * 128:(h // 4 + 1) * 128],
                               lambda _h, quad, src, h=h: P.dma(
                                   "g", y_o.ap()[h * 128:(h + 1) * 128,
                                                 quad * 512:(quad + 1) * 512],
                                   src, so_a[0]),
                               FILL_L1, nheads=1)
                P.barrier()
            P.wait("g", so_a[0], so_a[0].n)
            P.wait("g", so_a[1], so_a[1].n)
            gather(y_o.ap(), yg.ap(), 1024, S["cc4"], [[0, 1], [2, 3], [4, 5], [6, 7]])
            P.barrier()

        run_attn1()

        P.dma("g", stream[:], spill.ap().rearrange("(cc p) t -> p cc t", p=128), S["sp"])
        o_ao = at("o_ao", [128, NCH, T], BF16, AR)
        o_wst = at("o_wst", [128, 2, NCH, 128], BF16, AR + 32768)
        o_stg = at("o_stg", [128, 2, 2, 1024], BF16, AR + 40960)

        def run_outproj():
            P.wait("g", S["cc4"], CCP)
            P.wait("g", S["sp"], S["sp"].n)
            war_st = {}
            for cc in range(NCH):
                par = cc % 2
                r, hcl = divmod(cc, 8)
                src = yg.ap()[r * 1024 + hcl * 128: r * 1024 + (hcl + 1) * 128, :].rearrange(
                    "p (h t) -> p h t", t=T)
                P.wait("g", sv_a, war_st.get(par, 0))
                n_l = P.dma("g", o_stg[:, par], src, S[f"oio{par}"])
                P.wait("v", S[f"oio{par}"], n_l)
                dst = o_ao[:, cc, :]
                P.op("v", lambda E, dst=dst, par=par: E.tensor_scalar(
                    out=dst, in0=o_stg[:, par, 0], scalar1=cfgm[:, 0:1], scalar2=None,
                    op0=ALU.mult), inc=sv_a)
                P.wait("v", sv_a, sv_a.n)
                war_st[par] = P.op("v", lambda E, dst=dst, par=par: E.scalar_tensor_tensor(
                    out=dst, in0=o_stg[:, par, 1], scalar=cfgm[:, 1:2], in1=dst,
                    op0=ALU.mult, op1=ALU.add), inc=sv_a)
            P.barrier()
            wo_r = wl_ap("woT1").rearrange("(dc p) (cc c) -> cc p dc c", p=128, c=128)
            war_w = {}
            war_ev = {}
            for cc in range(NCH):
                par = cc % 2
                if cc >= 2:
                    P.wait("y", st_a, war_w[cc - 2])
                n_w = P.dma("y", o_wst[:, par], wo_r[cc], S[f"o_w{par}"])
                P.wait("t", S[f"o_w{par}"], n_w)
                for tw in range(2):
                    pp = ps[tw % 2]
                    if cc >= 1:
                        P.wait("t", sv_a, war_ev[(cc - 1, tw)])
                    for dc in range(NCH):
                        lhsT = o_wst[:, par, dc, :]
                        rhs = o_ao[:, dc, tw * 512:(tw + 1) * 512]
                        P.op("t", lambda E, lhsT=lhsT, rhs=rhs, dc=dc, pp=pp: E.matmul(
                            pp[:, 0:512], lhsT, rhs, start=(dc == 0), stop=(dc == NCH - 1)),
                            inc=(st_a if dc == NCH - 1 else None))
                    n_mm = st_a.n
                    war_w[cc] = n_mm
                    P.wait("v", st_a, n_mm)
                    P.wait("v", S["sp"], S["sp"].n)
                    sdst = stream[:, cc, tw * 512:(tw + 1) * 512]
                    bslice = bo_sb[:, cc:cc + 1]
                    war_ev[(cc, tw)] = P.op("v", lambda E, sdst=sdst, pp=pp, bslice=bslice: E.scalar_tensor_tensor(
                        out=sdst, in0=pp[:, 0:512], scalar=bslice, in1=sdst,
                        op0=ALU.add, op1=ALU.add), inc=sv_a)
            P.barrier()

        run_outproj()
        dump_stream(4)

        run_moe()

        P.wait("g", S["m_v"], S["m_v"].n)
        n_out = P.dma("g", out_t.rearrange("(cc p) t -> p cc t", p=128), stream[:], S["io"])
        P.wait("g", S["io"], n_out)

        with nc.Block() as block:
            P.finalize(block)
    return nc


try:
    from prog_fused import WL_PAD, MOE_PAD, build_fused
except ImportError:
    pass
from concourse.bass_utils import run_bass_kernel_spmd

_fused_nc = [None]
_last_results = [None]


def kernel(**inputs):
    f = np.float32
    x = np.asarray(inputs["L0_attn_x"], dtype=f)
    Wq0 = np.asarray(inputs["L0_attn_Wq_w"], f)
    Wdkv = np.asarray(inputs["L0_attn_Wdkv_w"], f)
    Wuk = np.asarray(inputs["L0_attn_Wuk_w"], f)
    Wuv = np.asarray(inputs["L0_attn_Wuv_w"], f)
    Wo0 = np.asarray(inputs["L0_attn_Wo_w"], f)
    Wq1 = np.asarray(inputs["L1_attn_Wq"], f)
    Wk1 = np.asarray(inputs["L1_attn_Wk"], f)
    Wv1 = np.asarray(inputs["L1_attn_Wv"], f)
    Wo1 = np.asarray(inputs["L1_attn_Wo"], f)
    bo = np.asarray(inputs["L1_attn_bo"], f)
    Wr = np.asarray(inputs["L1_ffn_Wr"], f)
    Wg1 = np.asarray(inputs["L1_ffn_Wg"], f)
    Wu1 = np.asarray(inputs["L1_ffn_Wu"], f)
    Wd1 = np.asarray(inputs["L1_ffn_Wd"], f)
    gate_p = np.asarray(inputs["L0_ffn_gate_W_packed"], np.int32)
    gate_s = np.asarray(inputs["L0_ffn_gate_scales"], f)
    up_p = np.asarray(inputs["L0_ffn_up_W_packed"], np.int32)
    up_s = np.asarray(inputs["L0_ffn_up_scales"], f)
    down_p = np.asarray(inputs["L0_ffn_down_W_packed"], np.int32)
    down_s = np.asarray(inputs["L0_ffn_down_scales"], f)

    Wq0T = np.ascontiguousarray(Wq0.T)
    Wo0T = np.ascontiguousarray(Wo0.T)
    WdkvT = np.ascontiguousarray(Wdkv.T)
    Wq1T = np.ascontiguousarray(Wq1.T)
    Wk1T = np.ascontiguousarray(Wk1.T)
    Wv1T = np.ascontiguousarray(Wv1.T)
    Wo1T = np.ascontiguousarray(Wo1.T)

    NLOC = 16384 // 8
    in_maps = []
    for c in range(8):
        b, half = divmod(c, 2)
        hb = half * 8
        kvb = half * 2
        wl_parts = [
            Wq0[:, hb * 128:(hb + 8) * 128],
            Wq0T[:, hb * 128:(hb + 8) * 128],
            Wo0T[:, hb * 128:(hb + 8) * 128],
            Wuk, Wuv, WdkvT,
            Wq1T[:, hb * 128:(hb + 8) * 128],
            Wk1T[:, kvb * 128:(kvb + 2) * 128],
            Wv1T[:, kvb * 128:(kvb + 2) * 128],
            Wo1T, Wr,
        ]
        wl_flat = np.zeros(WL_PAD, f)
        o = 0
        for p_ in wl_parts:
            wl_flat[o:o + p_.size] = np.ascontiguousarray(p_).ravel()
            o += p_.size
        moe_flat = np.zeros(MOE_PAD, f)
        o = 0
        for p_ in (Wg1[c], Wu1[c], Wd1[c]):
            moe_flat[o:o + p_.size] = p_.ravel()
            o += p_.size
        cfg = np.zeros((128, 2), f)
        cfg[:, 0] = 1.0 if half == 0 else 0.0
        cfg[:, 1] = 1.0 - cfg[:, 0]
        in_maps.append(dict(
            x_in=np.ascontiguousarray(x[b, half * 1024:(half + 1) * 1024, :].T),
            wl_f32=wl_flat.reshape(WL_PAD // 512, 512),
            moe_f32=moe_flat.reshape(MOE_PAD // 512, 512),
            gate_p=gate_p[c * NLOC:(c + 1) * NLOC], gate_s=gate_s[c * NLOC:(c + 1) * NLOC],
            up_p=up_p[c * NLOC:(c + 1) * NLOC], up_s=up_s[c * NLOC:(c + 1) * NLOC],
            down_p=down_p[c * 256:(c + 1) * 256], down_s=down_s[c * 256:(c + 1) * 256],
            bo_in=bo, cfg_in=cfg,
        ))

    if _fused_nc[0] is None:
        _fused_nc[0] = build_fused()
    res = run_bass_kernel_spmd(_fused_nc[0], in_maps, core_ids=list(range(8)))
    _last_results[0] = res.results
    out = np.empty((4, 2048, 2048), f)
    for c in range(8):
        b, half = divmod(c, 2)
        out[b, half * 1024:(half + 1) * 1024, :] = np.asarray(res.results[c]["out_t"]).T
    return out
